# version 06
# baseline (speedup 1.0000x reference)
import contextlib
import numpy as np
import concourse.bass as bass
import concourse.mybir as mybir
from concourse.bass_utils import run_bass_kernel_spmd

F32 = mybir.dt.float32
BF16 = mybir.dt.bfloat16
ALU = mybir.AluOpType
AF = mybir.ActivationFunctionType
AX = mybir.AxisListType.X

B, T, C = 4, 8192, 1024
I32 = mybir.dt.int32
CAP = 1536
NSLOT = 8 * CAP
TH = T // 2
G = C // 128
NC512 = T // 512
PAIRS = [[0, 1], [2, 3], [4, 5], [6, 7]]
EPS = 1e-5


class Sem:

    def __init__(self, h):
        self.h = h
        self.v = 0

    def inc(self, instr, k=1):
        instr.then_inc(self.h, k)
        self.v += k
        return instr

    def wait(self, eng):
        eng.wait_ge(self.h, self.v)


def interleave(*chains):
    import itertools
    for ops in itertools.zip_longest(*chains):
        for op in ops:
            if op is not None:
                op()


def vec_pg(a):
    return np.ascontiguousarray(np.asarray(a, np.float32).reshape(G, 128).T)



def emit_s1(nc, D, S):
    es = contextlib.ExitStack()
    xT = D["xT"]; Wk = D["Wk"]; Wv = D["Wv"]; Wr = D["Wr"]; Wo = D["Wo"]
    vecs = {nm: D[nm] for nm in ["ln1_s", "ln1_b", "tmk", "tmv", "tmr",
                                 "w_decay", "u_first", "w_router"]}
    xn0_d, xn0bf_d, ln1bf_d = D["xn0"], D["xn0bf"], D["ln1bf"]
    scores_d, tmout_d = D["scores"], D["tmout"]

    def sb(stack, name, shape, dtype=F32):
        return stack.enter_context(nc.sbuf_tensor("s1_" + name, shape, dtype))

    def psum(stack, name, shape, dtype=F32):
        return stack.enter_context(nc.psum_tensor("s1_" + name, shape, dtype))

    vt = {nm: sb(es, "v_" + nm, [128, G]) for nm in vecs}
    aux = sb(es, "aux", [128, 4 * G])
    lam_tile = sb(es, "lam_tile", [128, G * 512])
    onesc = sb(es, "onesc", [128, 1])
    onesr = sb(es, "onesr", [1, 128])
    zeros1 = sb(es, "zeros1", [128, G], BF16)
    carry = sb(es, "carry", [128, 4 * G])
    wkf = sb(es, "wkf", [128, G * 2 * C], BF16)
    wvf = sb(es, "wvf", [128, G * 2 * C], BF16)
    wrf = sb(es, "wrf", [128, G * 2 * C], BF16)
    wo_b = sb(es, "wo_b", [128, G * C], BF16)
    cst_eps = sb(es, "cst_eps", [128, 1])

    ld, sA, sM, sV = S

    with nc.Block() as blk:
        @blk.sync
        def _(sync):
            for nm in vecs:
                ld.inc(sync.dma_start(out=vt[nm][:, :], in_=vecs[nm]), 16)
            ld.wait(sync)

        @blk.gpsimd
        def _(g_):
            ld.inc(g_.dma_start(
                out=wo_b[:, :].rearrange("p (kc o) -> p kc o", o=C),
                in_=Wo.rearrange("(kc p) o -> p kc o", p=128)), 16)

    with nc.Block() as blk:
        @blk.vector
        def _(v):
            v.memset(onesc[:, :], 1.0)
            v.memset(onesr[:, :], 1.0)
            v.memset(zeros1[:, :], 0.0)
            v.memset(carry[:, :], 0.0)
            v.memset(lam_tile[:, :], 1.0)
            v.memset(cst_eps[:, :], EPS)
            for i, nm in enumerate(["tmk", "tmv", "tmr"]):
                v.tensor_scalar(aux[:, i * G:(i + 1) * G], vt[nm][:, :], -1.0, 1.0,
                                ALU.mult, ALU.add)

        @blk.scalar
        def _(s):
            s.activation(aux[:, 3 * G:4 * G], vt["u_first"][:, :], AF.Exp)
            s.activation(vt["w_decay"][:, :], vt["w_decay"][:, :], AF.Exp)
            s.drain()
            s.activation(vt["w_decay"][:, :], vt["w_decay"][:, :], AF.Exp,
                         scale=-1.0)

    with nc.Block() as blk:
        @blk.vector
        def _(v):
            for g in range(G):
                sl = lam_tile[:, g * 512:(g + 1) * 512]
                v.tensor_scalar_mul(sl, sl, vt["w_decay"][:, g:g + 1])

    with contextlib.ExitStack() as es0:
        wtmp = sb(es0, "wtmp", [128, C])
        for proj, (wd, dst, mixv) in enumerate(
                [(Wk, wkf, "tmk"), (Wv, wvf, "tmv"), (Wr, wrf, "tmr")]):
            for kc in range(G):
                with nc.Block() as blk:
                    @blk.sync
                    def _(sync, wd=wd, kc=kc):
                        ld.inc(sync.dma_start(
                            out=wtmp[:, :],
                            in_=wd[kc * 128:(kc + 1) * 128, :]), 16)
                        ld.wait(sync)

                    @blk.vector
                    def _(v, dst=dst, mixv=mixv, proj=proj, kc=kc):
                        ld.wait(v)
                        base = kc * 2 * C
                        v.tensor_scalar_mul(dst[:, base:base + C], wtmp[:, :],
                                            vt[mixv][:, kc:kc + 1])
                        v.tensor_scalar_mul(
                            dst[:, base + C:base + 2 * C], wtmp[:, :],
                            aux[:, proj * G + kc:proj * G + kc + 1])

    with nc.Block() as blk:
        @blk.sync
        def _(sync):
            ld.inc(sync.dma_start(
                out=ln1bf_d[:, 0:1].rearrange("(g p) t -> p g t", p=128),
                in_=zeros1[:, :].rearrange("p (g o) -> p g o", o=1),
                allow_slow_non_contiguous=True), 16)

    with contextlib.ExitStack() as es1:
        xc = sb(es1, "xc", [128, G * 512])
        sqf = sb(es1, "sqf", [128, G * 512])
        xnc = sb(es1, "xnc", [128, G * 512])
        xnbfc = sb(es1, "xnbfc", [128, G * 512], BF16)
        ln1cw = sb(es1, "ln1cw", [128, G * 512], BF16)
        row = sb(es1, "row", [1, 5 * 512])
        ps_sum = psum(es1, "ps_sum", [1, 512])
        ps_sq = psum(es1, "ps_sq", [1, 512])
        ps_rb = psum(es1, "ps_rb", [128, 512])
        ps_mrb = psum(es1, "ps_mrb", [128, 512])
        ps_sc = psum(es1, "ps_sc", [1, 512])

        def ln_block(src, out_cb, scores_too):
            with nc.Block() as blk:
                @blk.scalar
                def _(s):
                    for g in range(G):
                        s.activation(sqf[:, g * 512:(g + 1) * 512],
                                     src[:, g * 512:(g + 1) * 512], AF.Square)
                    sA.inc(s.drain())

                @blk.tensor
                def _(tn):
                    for g in range(G):
                        tn.matmul(ps_sum[:, :], onesc[:, 0:1],
                                  src[:, g * 512:(g + 1) * 512],
                                  start=(g == 0), stop=(g == G - 1))
                    sA.wait(tn)
                    for g in range(G):
                        last = (g == G - 1) and not scores_too
                        mm = tn.matmul(ps_sq[:, :], onesc[:, 0:1],
                                       sqf[:, g * 512:(g + 1) * 512],
                                       start=(g == 0), stop=(g == G - 1))
                    if scores_too:
                        for g in range(G):
                            mm = tn.matmul(ps_sc[:, :],
                                           vt["w_router"][:, g:g + 1],
                                           src[:, g * 512:(g + 1) * 512],
                                           start=(g == 0), stop=(g == G - 1))
                    sM.inc(mm)

                @blk.vector
                def _(v):
                    sM.wait(v)
                    mean, var, rstd, mr = (row[:, 0:512], row[:, 512:1024],
                                           row[:, 1024:1536], row[:, 1536:2048])
                    v.tensor_scalar_mul(mean, ps_sum[:, :], 1.0 / C)
                    v.nop(cycle_cnt=16)
                    v.tensor_mul(var, mean, mean)
                    v.nop(cycle_cnt=16)
                    v.scalar_tensor_tensor(var, ps_sq[:, :], 1.0 / C, var,
                                           ALU.mult, ALU.subtract)
                    sV.inc(v.drain())

                @blk.scalar
                def _(s):
                    sV.wait(s)
                    s.activation(row[:, 1024:1536], row[:, 512:1024], AF.Sqrt,
                                 bias=cst_eps[0:1, 0:1])
                    sA.inc(s.drain())

                @blk.vector
                def _(v):
                    sA.wait(v)
                    v.reciprocal(row[:, 1024:1536], row[:, 1024:1536])
                    v.nop(cycle_cnt=32)
                    v.tensor_mul(row[:, 1536:2048], row[:, 0:512],
                                 row[:, 1024:1536])
                    sV.inc(v.drain())

                @blk.tensor
                def _(tn):
                    sV.wait(tn)
                    tn.matmul(ps_rb[:, :], onesr[:, :], row[:, 1024:1536],
                              start=True, stop=True)
                    sM.inc(tn.matmul(ps_mrb[:, :], onesr[:, :],
                                     row[:, 1536:2048], start=True, stop=True))

                @blk.vector
                def _(v):
                    sM.wait(v)
                    for g in range(0, G, 2):
                        interleave(out_cb(g), out_cb(g + 1))
                    sV.inc(v.drain())

        for n in range(NC512):
            c0 = 1 + n * 512
            with nc.Block() as blk:
                @blk.sync
                def _(sync):
                    ld.inc(sync.dma_start(
                        out=xc[:, :].rearrange("p (g t) -> p g t", t=512),
                        in_=xT[:, c0:c0 + 512].rearrange("(g p) t -> p g t",
                                                         p=128)), 16)
                    ld.wait(sync)

            def cb1(g):
                def f(v=None):
                    pass
                xg = xc[:, g * 512:(g + 1) * 512]
                og = xnc[:, g * 512:(g + 1) * 512]
                bg = xnbfc[:, g * 512:(g + 1) * 512]
                vv = nc.vector
                return [
                    lambda: vv.tensor_mul(og, xg, ps_rb[:, :]),
                    lambda: vv.tensor_sub(og, og, ps_mrb[:, :]),
                    lambda: vv.tensor_copy(bg, og),
                ]
            ln_block(xc, cb1, scores_too=False)

            def cb2(g):
                xg = xnc[:, g * 512:(g + 1) * 512]
                tg = xc[:, g * 512:(g + 1) * 512]
                og = ln1cw[:, g * 512:(g + 1) * 512]
                vv = nc.vector
                return [
                    lambda: vv.tensor_mul(tg, xg, ps_rb[:, :]),
                    lambda: vv.tensor_sub(tg, tg, ps_mrb[:, :]),
                    lambda: vv.tensor_scalar(og, tg, vt["ln1_s"][:, g:g + 1],
                                             vt["ln1_b"][:, g:g + 1],
                                             ALU.mult, ALU.add),
                ]
            ln_block(xnc, cb2, scores_too=True)

            with nc.Block() as blk:
                @blk.scalar
                def _(s):
                    s.copy(row[:, 2048:2560], ps_sc[:, :])
                    sA.inc(s.drain())

                @blk.sync
                def _(sync):
                    sA.wait(sync)
                    ld.inc(sync.dma_start(
                        out=xn0_d[:, c0:c0 + 512].rearrange(
                            "(g p) t -> p g t", p=128),
                        in_=xnc[:, :].rearrange("p (g t) -> p g t", t=512)), 16)
                    ld.inc(sync.dma_start(
                        out=xn0bf_d[:, c0:c0 + 512].rearrange(
                            "(g p) t -> p g t", p=128),
                        in_=xnbfc[:, :].rearrange("p (g t) -> p g t", t=512)), 16)
                    ld.inc(sync.dma_start(
                        out=ln1bf_d[:, c0:c0 + 512].rearrange(
                            "(g p) t -> p g t", p=128),
                        in_=ln1cw[:, :].rearrange("p (g t) -> p g t", t=512)), 16)
                    ld.inc(sync.dma_start(
                        out=scores_d[:, n * 512:(n + 1) * 512],
                        in_=row[:, 2048:2560]), 16)
                    ld.wait(sync)

    with contextlib.ExitStack() as es2:
        ln1c = sb(es2, "ln1c", [128, G * 513], BF16)
        z_sb = sb(es2, "z_sb", [128, G * 512], BF16)
        tm_sb = sb(es2, "tm_sb", [128, 2 * 512])
        wk_t = {}
        par_t = {}
        for nm in ["ek", "vv", "sr", "ekv", "euk", "ukv"]:
            par_t[nm] = [sb(es2, nm + "A", [128, 512]),
                         sb(es2, nm + "B", [128, 512])]
        PQ = {}
        for nm in ["P", "Q"]:
            PQ[nm] = [sb(es2, nm + "A", [128, 513]),
                      sb(es2, nm + "B", [128, 513])]
        ps_k = [psum(es2, f"ps_k{i}", [128, 512]) for i in range(2)]
        ps_v = [psum(es2, f"ps_v{i}", [128, 512]) for i in range(2)]
        ps_r = [psum(es2, f"ps_r{i}", [128, 512]) for i in range(2)]

        for n in range(NC512):
            c0 = 1 + n * 512
            with nc.Block() as blk:
                @blk.sync
                def _(sync):
                    ld.inc(sync.dma_start(
                        out=ln1c[:, :].rearrange("p (g t) -> p g t", t=513),
                        in_=ln1bf_d[:, c0 - 1:c0 + 512].rearrange(
                            "(g p) t -> p g t", p=128)), 16)
                    ld.wait(sync)

            sA_b, sM_b, sV_b = sA.v, sM.v, sV.v
            with nc.Block() as blk:
                @blk.tensor
                def _(tn):
                    for g in range(G):
                        if g >= 2:
                            tn.wait_ge(sA.h, sA_b + g - 1)
                        for wf, pp in [(wkf, ps_k), (wvf, ps_v), (wrf, ps_r)]:
                            for kc in range(G):
                                lw = wf[:, (kc * 2) * C + g * 128:
                                        (kc * 2) * C + g * 128 + 128]
                                ls = wf[:, (kc * 2 + 1) * C + g * 128:
                                        (kc * 2 + 1) * C + g * 128 + 128]
                                cur = ln1c[:, kc * 513 + 1:kc * 513 + 513]
                                shf = ln1c[:, kc * 513 + 0:kc * 513 + 512]
                                tn.matmul(pp[g % 2][:, :], lw, cur,
                                          start=(kc == 0), stop=False)
                                mm = tn.matmul(pp[g % 2][:, :], ls, shf,
                                               start=False, stop=(kc == G - 1))
                        sM.inc(mm)

                @blk.scalar
                def _(s):
                    for g in range(G):
                        par = g % 2
                        s.wait_ge(sM.h, sM_b + g + 1)
                        if g >= 2:
                            s.wait_ge(sV.h, sV_b + (g - 2) // 2 + 1)
                        s.activation(par_t["ek"][par][:, :], ps_k[par][:, :],
                                     AF.Exp)
                        s.copy(par_t["vv"][par][:, :], ps_v[par][:, :])
                        s.activation(par_t["sr"][par][:, :], ps_r[par][:, :],
                                     AF.Sigmoid)
                        sA.inc(s.drain())

                @blk.vector
                def _(v):
                    for g0 in range(0, G, 2):
                        v.wait_ge(sA.h, sA_b + g0 + 2)

                        def mk(g):
                            par = g % 2
                            ekg = par_t["ek"][par]
                            vg = par_t["vv"][par]
                            srg = par_t["sr"][par]
                            ekvg = par_t["ekv"][par]
                            eukg = par_t["euk"][par]
                            ukvg = par_t["ukv"][par]
                            Pg, Qg = PQ["P"][par], PQ["Q"][par]
                            lamg = lam_tile[:, g * 512:(g + 1) * 512]
                            zg = z_sb[:, g * 512:(g + 1) * 512]
                            cP = carry[:, g:g + 1]
                            cQ = carry[:, 2 * G + g:2 * G + g + 1]
                            return [
                                lambda: v.tensor_mul(ekvg[:, :], ekg[:, :],
                                                     vg[:, :]),
                                lambda: v.tensor_scalar_mul(
                                    eukg[:, :], ekg[:, :],
                                    aux[:, 3 * G + g:3 * G + g + 1]),
                                lambda: v.tensor_mul(ukvg[:, :], eukg[:, :],
                                                     vg[:, :]),
                                lambda: v.tensor_copy(Pg[:, 0:1], cP),
                                lambda: v.tensor_copy(Qg[:, 0:1], cQ),
                                lambda: v.tensor_tensor_scan(
                                    Pg[:, 1:513], lamg, ekvg[:, :], cP,
                                    ALU.mult, ALU.add),
                                lambda: v.tensor_tensor_scan(
                                    Qg[:, 1:513], lamg, ekg[:, :], cQ,
                                    ALU.mult, ALU.add),
                                lambda: v.tensor_add(ukvg[:, :], ukvg[:, :],
                                                     Pg[:, 0:512]),
                                lambda: v.tensor_add(eukg[:, :], eukg[:, :],
                                                     Qg[:, 0:512]),
                                lambda: v.tensor_copy(cP, Pg[:, 512:513]),
                                lambda: v.tensor_copy(cQ, Qg[:, 512:513]),
                                lambda: v.reciprocal(eukg[:, :], eukg[:, :]),
                                lambda: v.tensor_mul(ukvg[:, :], ukvg[:, :],
                                                     eukg[:, :]),
                                lambda: v.tensor_mul(zg, ukvg[:, :], srg[:, :]),
                            ]
                        interleave(mk(g0), mk(g0 + 1))
                        sV.inc(v.drain())

            sA_b, sM_b = sA.v, sM.v
            with nc.Block() as blk:
                @blk.tensor
                def _(tn):
                    for m in range(G):
                        if m >= 2:
                            tn.wait_ge(sA.h, sA_b + m - 1)
                        for g in range(G):
                            lw = wo_b[:, g * C + m * 128:g * C + m * 128 + 128]
                            mm = tn.matmul(ps_k[m % 2][:, :], lw,
                                           z_sb[:, g * 512:(g + 1) * 512],
                                           start=(g == 0), stop=(g == G - 1))
                        sM.inc(mm)

                @blk.scalar
                def _(s):
                    for m in range(G):
                        s.wait_ge(sM.h, sM_b + m + 1)
                        if m >= 2:
                            s.wait_ge(ld.h, ld.v + 16 * (m - 1))
                        s.copy(tm_sb[:, (m % 2) * 512:(m % 2) * 512 + 512],
                               ps_k[m % 2][:, :])
                        sA.inc(s.drain())

                @blk.sync
                def _(sync):
                    for m in range(G):
                        sync.wait_ge(sA.h, sA_b + m + 1)
                        ld.inc(sync.dma_start(
                            out=tmout_d[m * 128:(m + 1) * 128,
                                        n * 512:(n + 1) * 512],
                            in_=tm_sb[:, (m % 2) * 512:(m % 2) * 512 + 512]), 16)
                ld.v = ld.v

    with nc.Block() as blk:
        @blk.sync
        def _(sync):
            ld.wait(sync)

    es.close()


def s1_inmap(inp, b):
    x = np.zeros((C, T + 1), np.float32)
    x[:, 1:] = np.asarray(inp["L0_attn_x"][b], np.float32).T
    return {
        "xT": x,
        "Wk": np.asarray(inp["L0_attn_Wk"], np.float32),
        "Wv": np.asarray(inp["L0_attn_Wv"], np.float32),
        "Wr": np.asarray(inp["L0_attn_Wr"], np.float32),
        "Wo": np.asarray(inp["L0_attn_Wo"], np.float32),
        "ln1_s": vec_pg(inp["L0_attn_ln1_s"]),
        "ln1_b": vec_pg(inp["L0_attn_ln1_b"]),
        "tmk": vec_pg(inp["L0_attn_tmk"]),
        "tmv": vec_pg(inp["L0_attn_tmv"]),
        "tmr": vec_pg(inp["L0_attn_tmr"]),
        "w_decay": vec_pg(inp["L0_attn_w_decay"]),
        "u_first": vec_pg(inp["L0_attn_u_first"]),
        "w_router": vec_pg(inp["L0_ffn_w_router"]),
    }





def emit_lif(nc, S, base_name, src_d, base_d, out_d, zero_col0=False):
    es = contextlib.ExitStack()

    def sb(name, shape, dtype=F32):
        return es.enter_context(nc.sbuf_tensor(base_name + name, shape, dtype))

    CH = 1024
    NCH = T // CH
    tmc = [sb("tmc0", [128, G * CH]), sb("tmc1", [128, G * CH])]
    wbuf = sb("wbuf", [128, G * CH])
    bsc = sb("bsc", [128, G * CH])
    lifu = sb("lifu", [128, G])
    ld, sA, sM, sV = S
    if zero_col0:
        zt = sb("zc", [128, G])
        with nc.Block() as blk:
            @blk.vector
            def _(v):
                v.memset(zt[:, :], 0.0)
                sV.inc(v.drain())

            @blk.gpsimd
            def _(g_):
                sV.wait(g_)
                ld.inc(g_.dma_start(
                    out=out_d[:, 0:1].rearrange("(g p) t -> p g t", p=128),
                    in_=zt[:, :].rearrange("p (g o) -> p g o", o=1),
                    allow_slow_non_contiguous=True), 16)

    with nc.Block() as blk:
        @blk.vector
        def _(v):
            v.memset(lifu[:, :], 0.0)

    with nc.Block() as blk:
        @blk.sync
        def _(sync):
            ld.inc(sync.dma_start(
                out=tmc[0][:, :].rearrange("p (g t) -> p g t", t=CH),
                in_=src_d[:, 0:CH].rearrange("(g p) t -> p g t", p=128)), 16)

    for ch in range(NCH):
        buf = tmc[ch % 2]
        nxt = tmc[(ch + 1) % 2]
        ld_target = ld.v
        with nc.Block() as blk:
            @blk.sync
            def _(sync, ch=ch, nxt=nxt):
                sync.wait_ge(ld.h, ld.v)
                if ch + 1 < NCH:
                    ld.inc(sync.dma_start(
                        out=nxt[:, :].rearrange("p (g t) -> p g t", t=CH),
                        in_=src_d[:, (ch + 1) * CH:(ch + 2) * CH].rearrange(
                            "(g p) t -> p g t", p=128)), 16)
                ld.inc(sync.dma_start(
                    out=bsc[:, :].rearrange("p (g t) -> p g t", t=CH),
                    in_=base_d[:, 1 + ch * CH:1 + (ch + 1) * CH].rearrange(
                        "(g p) t -> p g t", p=128)), 16)

            @blk.vector
            def _(v, buf=buf, ch=ch, ld_target=ld_target):
                v.wait_ge(ld.h, ld_target)
                bt = buf[:, :].rearrange("p (g t) -> p g t", t=CH)
                wt = wbuf[:, :].rearrange("p (g t) -> p g t", t=CH)
                for t in range(CH):
                    v.scalar_tensor_tensor(wt[:, :, t], lifu[:, :], 0.5,
                                           bt[:, :, t], ALU.mult, ALU.add)
                    v.nop(cycle_cnt=16)
                    v.scalar_tensor_tensor(lifu[:, :], wt[:, :, t], 2.0,
                                           wt[:, :, t], ALU.is_lt, ALU.mult)
                    v.nop(cycle_cnt=16)
                ld.wait(v)
                for g in range(0, G, 2):
                    for gg in (g, g + 1):
                        v.scalar_tensor_tensor(
                            bsc[:, gg * CH:(gg + 1) * CH],
                            wbuf[:, gg * CH:(gg + 1) * CH], 2.0,
                            bsc[:, gg * CH:(gg + 1) * CH],
                            ALU.is_ge, ALU.add)
                sV.inc(v.drain())

            @blk.gpsimd
            def _(g_, ch=ch):
                sV.wait(g_)
                ld.inc(g_.dma_start(
                    out=out_d[:, 1 + ch * CH:1 + (ch + 1) * CH].rearrange(
                        "(g p) t -> p g t", p=128),
                    in_=bsc[:, :].rearrange("p (g t) -> p g t", t=CH)), 16)

    with nc.Block() as blk:
        @blk.sync
        def _(sync):
            ld.wait(sync)
    es.close()





def emit_s3(nc, D, S):
    es = contextlib.ExitStack()
    th0 = 0

    xmid_d = D["xmid"]
    fWk_d, fWr_d, fWv_d = D["fWk"], D["fWr"], D["fWv"]
    vecs = {nm: D[nm] for nm in ["ln2_s", "ln2_b", "f_tmk", "f_tmr"]}
    cmh_d = D["cmf"]
    ln2bf_d = D["ln2bf"]

    def sb(stack, name, shape, dtype=F32):
        return stack.enter_context(nc.sbuf_tensor("s3_" + name, shape, dtype))

    def psum(stack, name, shape, dtype=F32):
        return stack.enter_context(nc.psum_tensor("s3_" + name, shape, dtype))

    vt = {nm: sb(es, "v_" + nm, [128, G]) for nm in vecs}
    aux = sb(es, "aux", [128, 2 * G])
    onesc = sb(es, "onesc", [128, 1])
    onesr = sb(es, "onesr", [1, 128])
    cst_eps = sb(es, "cst_eps", [128, 1])
    zeros1 = sb(es, "zeros1", [128, G], BF16)
    fwr_b = sb(es, "fwr_b", [128, G * C], BF16)

    ld, sA, sM, sV = S

    with nc.Block() as blk:
        @blk.sync
        def _(sync):
            for nm in vecs:
                ld.inc(sync.dma_start(out=vt[nm][:, :], in_=vecs[nm]), 16)
            ld.wait(sync)

        @blk.gpsimd
        def _(g_):
            ld.inc(g_.dma_start(
                out=fwr_b[:, :].rearrange("p (kc o) -> p kc o", o=C),
                in_=fWr_d.rearrange("(kc p) o -> p kc o", p=128)), 16)

    with nc.Block() as blk:
        @blk.vector
        def _(v):
            v.memset(onesc[:, :], 1.0)
            v.memset(onesr[:, :], 1.0)
            v.memset(cst_eps[:, :], EPS)
            v.memset(zeros1[:, :], 0.0)
            for i, nm in enumerate(["f_tmk", "f_tmr"]):
                v.tensor_scalar(aux[:, i * G:(i + 1) * G], vt[nm][:, :], -1.0,
                                1.0, ALU.mult, ALU.add)

    with contextlib.ExitStack() as es1:
        xc = sb(es1, "xc", [128, G * 512])
        sqf = sb(es1, "sqf", [128, G * 512])
        lnw = sb(es1, "lnw", [128, G * 512], BF16)
        row = sb(es1, "row", [1, 4 * 512])
        ps_sum = psum(es1, "ps_sum", [1, 512])
        ps_sq = psum(es1, "ps_sq", [1, 512])
        ps_rb = psum(es1, "ps_rb", [128, 512])
        ps_mrb = psum(es1, "ps_mrb", [128, 512])

        widths = [512] * (T // 512) + [1]
        col = th0
        for n, W in enumerate(widths):
            lcol = col - th0
            with nc.Block() as blk:
                @blk.sync
                def _(sync, col=col, W=W):
                    sync.wait_ge(ld.h, ld.v)
                    ld.inc(sync.dma_start(
                        out=xc[:, 0:G * W].rearrange("p (g t) -> p g t", t=W),
                        in_=xmid_d[:, col:col + W].rearrange(
                            "(g p) t -> p g t", p=128),
                        allow_slow_non_contiguous=(W == 1)), 16)
                    ld.wait(sync)

            with nc.Block() as blk:
                @blk.scalar
                def _(s, W=W):
                    for g in range(G):
                        s.activation(sqf[:, g * W:(g + 1) * W],
                                     xc[:, g * W:(g + 1) * W], AF.Square)
                    sA.inc(s.drain())

                @blk.tensor
                def _(tn, W=W):
                    for g in range(G):
                        tn.matmul(ps_sum[:, 0:W], onesc[:, 0:1],
                                  xc[:, g * W:(g + 1) * W],
                                  start=(g == 0), stop=(g == G - 1))
                    sA.wait(tn)
                    for g in range(G):
                        mm = tn.matmul(ps_sq[:, 0:W], onesc[:, 0:1],
                                       sqf[:, g * W:(g + 1) * W],
                                       start=(g == 0), stop=(g == G - 1))
                    sM.inc(mm)

                @blk.vector
                def _(v, W=W):
                    sM.wait(v)
                    mean, var = row[:, 0:W], row[:, 512:512 + W]
                    v.tensor_scalar_mul(mean, ps_sum[:, 0:W], 1.0 / C)
                    v.nop(cycle_cnt=16)
                    v.tensor_mul(var, mean, mean)
                    v.nop(cycle_cnt=16)
                    v.scalar_tensor_tensor(var, ps_sq[:, 0:W], 1.0 / C, var,
                                           ALU.mult, ALU.subtract)
                    sV.inc(v.drain())

                @blk.scalar
                def _(s, W=W):
                    sV.wait(s)
                    s.activation(row[:, 1024:1024 + W], row[:, 512:512 + W],
                                 AF.Sqrt, bias=cst_eps[0:1, 0:1])
                    sA.inc(s.drain())

                @blk.vector
                def _(v, W=W):
                    sA.wait(v)
                    v.reciprocal(row[:, 1024:1024 + W], row[:, 1024:1024 + W])
                    v.nop(cycle_cnt=32)
                    v.tensor_mul(row[:, 1536:1536 + W], row[:, 0:W],
                                 row[:, 1024:1024 + W])
                    sV.inc(v.drain())

                @blk.tensor
                def _(tn, W=W):
                    sV.wait(tn)
                    tn.matmul(ps_rb[:, 0:W], onesr[:, :], row[:, 1024:1024 + W],
                              start=True, stop=True)
                    sM.inc(tn.matmul(ps_mrb[:, 0:W], onesr[:, :],
                                     row[:, 1536:1536 + W], start=True,
                                     stop=True))

                @blk.vector
                def _(v, W=W):
                    sM.wait(v)

                    def mk(g):
                        xg = xc[:, g * W:(g + 1) * W]
                        tg = sqf[:, g * W:(g + 1) * W]
                        og = lnw[:, g * W:(g + 1) * W]
                        return [
                            lambda: v.tensor_mul(tg, xg, ps_rb[:, 0:W]),
                            lambda: v.tensor_sub(tg, tg, ps_mrb[:, 0:W]),
                            lambda: v.tensor_scalar(og, tg,
                                                    vt["ln2_s"][:, g:g + 1],
                                                    vt["ln2_b"][:, g:g + 1],
                                                    ALU.mult, ALU.add),
                        ]
                    for g in range(0, G, 2):
                        interleave(mk(g), mk(g + 1))
                    sV.inc(v.drain())

                @blk.sync
                def _(sync, W=W, lcol=lcol):
                    sV.wait(sync)
                    ld.inc(sync.dma_start(
                        out=ln2bf_d[:, lcol:lcol + W].rearrange(
                            "(g p) t -> p g t", p=128),
                        in_=lnw[:, 0:G * W].rearrange("p (g t) -> p g t", t=W),
                        allow_slow_non_contiguous=(W == 1)), 16)
            col += W

    if True:
        with nc.Block() as blk:
            @blk.sync
            def _(sync):
                ld.inc(sync.dma_start(
                    out=ln2bf_d[:, 0:1].rearrange("(g p) t -> p g t", p=128),
                    in_=zeros1[:, :].rearrange("p (g o) -> p g o", o=1),
                    allow_slow_non_contiguous=True), 16)

    with contextlib.ExitStack() as es2:
        fwk_b = sb(es2, "fwk_b", [128, G * 4 * C], BF16)
        fwv_t = [sb(es2, "fwv_t0", [128, 32 * 128], BF16),
                 sb(es2, "fwv_t1", [128, 32 * 128], BF16)]
        ln2c = sb(es2, "ln2c", [128, G * 513], BF16)
        xk2 = sb(es2, "xk2", [128, G * 512], BF16)
        xr2 = sb(es2, "xr2", [128, G * 512], BF16)
        sig = sb(es2, "sig", [128, G * 512])
        h2 = sb(es2, "h2", [128, 32 * 512], BF16)
        relu_t = sb(es2, "relu_t", [128, 2 * 512])
        cmo = sb(es2, "cmo", [128, G * 512])
        ps_h = [psum(es2, f"ps_h{i}", [128, 512]) for i in range(2)]
        ps_o = [psum(es2, f"ps_o{i}", [128, 512]) for i in range(2)]

        with nc.Block() as blk:
            @blk.gpsimd
            def _(g_):
                ld.inc(g_.dma_start(
                    out=fwk_b[:, :].rearrange("p (kc o) -> p kc o", o=4 * C),
                    in_=fWk_d.rearrange("(kc p) o -> p kc o", p=128)), 16)
                ld.wait(g_)

        for n in range(T // 512):
            with nc.Block() as blk:
                @blk.sync
                def _(sync, n=n):
                    sync.wait_ge(ld.h, ld.v)
                    ld.inc(sync.dma_start(
                        out=ln2c[:, :].rearrange("p (g t) -> p g t", t=513),
                        in_=ln2bf_d[:, n * 512:n * 512 + 513].rearrange(
                            "(g p) t -> p g t", p=128)), 16)
                    ld.wait(sync)

            sA_b, sM_b, sV_b = sA.v, sM.v, sV.v
            with nc.Block() as blk:
                @blk.vector
                def _(v):
                    def mk(g, dst, mixv):
                        cur = ln2c[:, g * 513 + 1:g * 513 + 513]
                        shf = ln2c[:, g * 513 + 0:g * 513 + 512]
                        og = dst[:, g * 512:(g + 1) * 512]
                        tg = relu_t[:, (g % 2) * 512:(g % 2) * 512 + 512]
                        return [
                            lambda: v.tensor_tensor(tg, cur, shf, ALU.subtract),
                            lambda: v.scalar_tensor_tensor(
                                og, tg, vt[mixv][:, g:g + 1], shf,
                                ALU.mult, ALU.add),
                        ]
                    for g in range(0, G, 2):
                        interleave(mk(g, xk2, "f_tmk"), mk(g + 1, xk2, "f_tmk"))
                    for g in range(0, G, 2):
                        interleave(mk(g, xr2, "f_tmr"), mk(g + 1, xr2, "f_tmr"))
                    sV.inc(v.drain())

                @blk.tensor
                def _(tn):
                    sV.wait(tn)
                    for m in range(G):
                        if m >= 2:
                            tn.wait_ge(sA.h, sA_b + m - 1)
                        for kc in range(G):
                            mm = tn.matmul(
                                ps_o[m % 2][:, :],
                                fwr_b[:, kc * C + m * 128:kc * C + m * 128 + 128],
                                xr2[:, kc * 512:(kc + 1) * 512],
                                start=(kc == 0), stop=(kc == G - 1))
                        sM.inc(mm)

                @blk.scalar
                def _(s):
                    for m in range(G):
                        s.wait_ge(sM.h, sM_b + m + 1)
                        s.activation(sig[:, m * 512:(m + 1) * 512],
                                     ps_o[m % 2][:, :], AF.Sigmoid)
                        sA.inc(s.drain())

            sA_b, sM_b = sA.v, sM.v
            with nc.Block() as blk:
                @blk.tensor
                def _(tn):
                    for hc in range(32):
                        if hc >= 2:
                            tn.wait_ge(sA.h, sA_b + 2 * (hc - 1))
                        for kc in range(G):
                            mm = tn.matmul(
                                ps_h[hc % 2][:, :],
                                fwk_b[:, kc * 4 * C + hc * 128:
                                      kc * 4 * C + hc * 128 + 128],
                                xk2[:, kc * 512:(kc + 1) * 512],
                                start=(kc == 0), stop=(kc == G - 1))
                        sM.inc(mm)

                @blk.scalar
                def _(s):
                    for hc in range(32):
                        s.wait_ge(sM.h, sM_b + hc + 1)
                        rt = relu_t[:, (hc % 2) * 512:(hc % 2) * 512 + 512]
                        s.activation(rt, ps_h[hc % 2][:, :], AF.Relu)
                        s.drain()
                        s.activation(h2[:, hc * 512:(hc + 1) * 512], rt,
                                     AF.Square)
                        sA.inc(s.drain(), 2)

            sM_b2, sV_b2, ld_b2 = sM.v, sV.v, ld.v
            with nc.Block() as blk:
                @blk.gpsimd
                def _(g_):
                    for m in range(G):
                        if m >= 2:
                            g_.wait_ge(sM.h, sM_b2 + m - 1)
                        ld.inc(g_.dma_start(
                            out=fwv_t[m % 2][:, :].rearrange(
                                "p (hc o) -> p hc o", o=128),
                            in_=fWv_d[:, m * 128:(m + 1) * 128].rearrange(
                                "(hc p) o -> p hc o", p=128)), 16)

                @blk.tensor
                def _(tn):
                    for m in range(G):
                        tn.wait_ge(ld.h, ld_b2 + 16 * (m + 1))
                        if m >= 2:
                            tn.wait_ge(sV.h, sV_b2 + m - 1)
                        for hc in range(32):
                            mm = tn.matmul(
                                ps_o[m % 2][:, :],
                                fwv_t[m % 2][:, hc * 128:(hc + 1) * 128],
                                h2[:, hc * 512:(hc + 1) * 512],
                                start=(hc == 0), stop=(hc == 31))
                        sM.inc(mm)

                @blk.vector
                def _(v):
                    for m in range(G):
                        v.wait_ge(sM.h, sM_b2 + m + 1)
                        v.tensor_mul(cmo[:, m * 512:(m + 1) * 512],
                                     sig[:, m * 512:(m + 1) * 512],
                                     ps_o[m % 2][:, :])
                        sV.inc(v.drain())

                @blk.sync
                def _(sync, n=n):
                    sV.wait(sync)
                    ld.inc(sync.dma_start(
                        out=cmh_d[:, n * 512:(n + 1) * 512].rearrange(
                            "(g p) t -> p g t", p=128),
                        in_=cmo[:, :].rearrange("p (g t) -> p g t", t=512)), 16)

    with nc.Block() as blk:
        @blk.sync
        def _(sync):
            ld.wait(sync)
    es.close()





def emit_s5(nc, D, S):
    es = contextlib.ExitStack()
    th0 = 0

    xT_d, xn0_d, xn0bf_d = D["xT"], D["xn0"], D["xn0bf"]
    sout_d, scores_d = D["sout"], D["scores"]
    W1_d, W2_d, b1_d, b2_d = D["W1"], D["W2"], D["b1"], D["b2"]
    idn_d = D["idn"]
    x1h_d = D["x1"]

    def sb(stack, name, shape, dtype=F32):
        return stack.enter_context(nc.sbuf_tensor("s5_" + name, shape, dtype))

    def psum(stack, name, shape, dtype=F32):
        return stack.enter_context(nc.psum_tensor("s5_" + name, shape, dtype))

    b1t = sb(es, "b1t", [128, 32])
    b2t = sb(es, "b2t", [128, G])
    onesc = sb(es, "onesc", [128, 1])
    onesr = sb(es, "onesr", [1, 128])
    idf = sb(es, "idf", [128, 128])
    scp = sb(es, "scp", [128, 64])
    msk64 = sb(es, "msk64", [128, 64])
    bis = sb(es, "bis", [128, 8])
    rowx = sb(es, "rowx", [1, 132])
    w1b = sb(es, "w1b", [128, G * 4 * C], BF16)
    w2t = [sb(es, "w2t0", [128, 32 * 128], BF16),
           sb(es, "w2t1", [128, 32 * 128], BF16)]

    ld, sA, sM, sV = S

    with nc.Block() as blk:
        @blk.sync
        def _(sync):
            ld.inc(sync.dma_start(out=b1t[:, :], in_=b1_d), 16)
            ld.inc(sync.dma_start(out=b2t[:, :], in_=b2_d), 16)
            ld.inc(sync.dma_start(
                out=scp[:, :],
                in_=scores_d.rearrange("o (p f) -> p (o f)", p=128)), 16)
            ld.wait(sync)

        @blk.gpsimd
        def _(g_):
            ld.inc(g_.dma_start(
                out=w1b[:, :].rearrange("p (kc o) -> p kc o", o=4 * C),
                in_=W1_d.rearrange("(kc p) o -> p kc o", p=128)), 16)

    with nc.Block() as blk:
        @blk.vector
        def _(v):
            v.memset(onesc[:, :], 1.0)
            v.memset(onesr[:, :], 1.0)

        @blk.sync
        def _(sync):
            ld.inc(sync.dma_start(out=idf[:, :], in_=idn_d), 16)
            ld.wait(sync)

    ps_t = psum(es, "ps_t", [1, 128])
    ps_c = psum(es, "ps_c", [1, 128])
    ps_bc = psum(es, "ps_bc", [128, 64])

    with nc.Block() as blk:
        @blk.vector
        def _(v):
            v.reduce_max(bis[:, 5:6], scp[:, :], axis=AX)
            v.nop(cycle_cnt=32)
            v.tensor_reduce(bis[:, 6:7], scp[:, :], axis=AX, op=ALU.min)
            sV.inc(v.drain())

        @blk.tensor
        def _(tn):
            sV.wait(tn)
            tn.transpose(ps_t[:, :], bis[:, 5:6], idf[:, :])
            sM.inc(tn.transpose(ps_c[:, :], bis[:, 6:7], idf[:, :]))

        @blk.vector
        def _(v):
            sM.wait(v)
            v.reduce_max(rowx[0:1, 0:1], ps_t[0:1, :], axis=AX)
            v.tensor_reduce(rowx[0:1, 1:2], ps_c[0:1, :], axis=AX, op=ALU.min)
            v.nop(cycle_cnt=32)
            v.tensor_scalar(rowx[0:1, 0:1], rowx[0:1, 0:1], 1.0, None, ALU.add)
            v.tensor_scalar(rowx[0:1, 1:2], rowx[0:1, 1:2], -1.0, None, ALU.add)
            sV.inc(v.drain())

        @blk.tensor
        def _(tn):
            sV.wait(tn)
            sM.inc(tn.matmul(ps_bc[:, 0:2], onesr[:, :], rowx[0:1, 0:2],
                             start=True, stop=True))

        @blk.vector
        def _(v):
            sM.wait(v)
            v.tensor_copy(bis[:, 1:2], ps_bc[:, 0:1])
            v.tensor_copy(bis[:, 0:1], ps_bc[:, 1:2])
            v.nop(cycle_cnt=32)
            v.tensor_add(bis[:, 2:3], bis[:, 0:1], bis[:, 1:2])
            v.nop(cycle_cnt=32)
            v.tensor_scalar_mul(bis[:, 2:3], bis[:, 2:3], 0.5)

    for it in range(44):
        with nc.Block() as blk:
            @blk.vector
            def _(v):
                v.tensor_scalar(msk64[:, :], scp[:, :], bis[:, 2:3], None,
                                ALU.is_ge)
                v.nop(cycle_cnt=32)
                v.reduce_sum(bis[:, 4:5], msk64[:, :], axis=AX)
                sV.inc(v.drain())

            @blk.tensor
            def _(tn):
                sV.wait(tn)
                sM.inc(tn.matmul(ps_t[0:1, 0:1], onesc[:, 0:1], bis[:, 4:5],
                                 start=True, stop=True))

            @blk.scalar
            def _(s):
                sM.wait(s)
                s.copy(rowx[0:1, 2:3], ps_t[0:1, 0:1])
                sA.inc(s.drain())

            @blk.tensor
            def _(tn):
                sA.wait(tn)
                sM.inc(tn.matmul(ps_bc[:, 0:1], onesr[:, :], rowx[0:1, 2:3],
                                 start=True, stop=True))

            @blk.vector
            def _(v):
                sM.wait(v)
                lo, hi, th, sel = (bis[:, 0:1], bis[:, 1:2], bis[:, 2:3],
                                   bis[:, 3:4])
                t1, t2 = bis[:, 5:6], bis[:, 6:7]
                v.tensor_scalar(sel, ps_bc[:, 0:1], float(TH), None, ALU.is_ge)
                v.tensor_sub(t1, th, lo)
                v.tensor_sub(t2, th, hi)
                v.nop(cycle_cnt=32)
                v.tensor_mul(t1, t1, sel)
                v.tensor_scalar(bis[:, 7:8], sel, -1.0, 1.0, ALU.mult, ALU.add)
                v.nop(cycle_cnt=32)
                v.tensor_add(lo, lo, t1)
                v.tensor_mul(t2, t2, bis[:, 7:8])
                v.nop(cycle_cnt=32)
                v.tensor_add(hi, hi, t2)
                v.nop(cycle_cnt=32)
                v.tensor_add(th, lo, hi)
                v.nop(cycle_cnt=32)
                v.tensor_scalar_mul(th, th, 0.5)

    with nc.Block() as blk:
        @blk.vector
        def _(v):
            v.tensor_copy(msk64[:, 0:1], bis[:, 0:1])

    with contextlib.ExitStack() as es2:
        xnbfc = sb(es2, "xnbfc", [128, G * 512], BF16)
        xnc = sb(es2, "xnc", [128, G * 512])
        soc = sb(es2, "soc", [128, G * 512])
        h2 = sb(es2, "h2", [128, 32 * 512], BF16)
        x1c = sb(es2, "x1c", [128, G * 512])
        srow = sb(es2, "srow", [1, 3 * 512])
        gm = sb(es2, "gm", [128, G * 512])
        ps_h = [psum(es2, f"ps_h{i}", [128, 512]) for i in range(2)]
        ps_o = [psum(es2, f"ps_o{i}", [128, 512]) for i in range(2)]
        ps_w = psum(es2, "ps_w", [128, 512])

        for n in range(T // 512):
            c0 = 1 + th0 + n * 512
            with nc.Block() as blk:
                @blk.sync
                def _(sync, n=n, c0=c0):
                    sync.wait_ge(ld.h, ld.v)
                    ld.inc(sync.dma_start(
                        out=xnbfc[:, :].rearrange("p (g t) -> p g t", t=512),
                        in_=xn0bf_d[:, c0:c0 + 512].rearrange(
                            "(g p) t -> p g t", p=128)), 16)
                    ld.inc(sync.dma_start(
                        out=xnc[:, :].rearrange("p (g t) -> p g t", t=512),
                        in_=xn0_d[:, c0:c0 + 512].rearrange(
                            "(g p) t -> p g t", p=128)), 16)
                    ld.inc(sync.dma_start(
                        out=x1c[:, :].rearrange("p (g t) -> p g t", t=512),
                        in_=xT_d[:, c0:c0 + 512].rearrange(
                            "(g p) t -> p g t", p=128)), 16)
                    ld.inc(sync.dma_start(
                        out=soc[:, :].rearrange("p (g t) -> p g t", t=512),
                        in_=sout_d[:, c0:c0 + 512].rearrange(
                            "(g p) t -> p g t", p=128)), 16)
                    ld.inc(sync.dma_start(
                        out=srow[0:1, 0:512],
                        in_=scores_d[:, th0 + n * 512:th0 + (n + 1) * 512]), 16)
                    ld.wait(sync)

            sA_b0, sV_b0, sM_b0 = sA.v, sV.v, sM.v
            with nc.Block() as blk:
                @blk.scalar
                def _(s):
                    s.activation(srow[0:1, 512:1024], srow[0:1, 0:512],
                                 AF.Sigmoid)
                    sA.inc(s.drain())

                @blk.vector
                def _(v):
                    sA.wait(v)
                    v.tensor_scalar(srow[0:1, 1024:1536], srow[0:1, 0:512],
                                    bis[0:1, 0:1], None, ALU.is_ge)
                    v.nop(cycle_cnt=32)
                    v.tensor_mul(srow[0:1, 1024:1536], srow[0:1, 1024:1536],
                                 srow[0:1, 512:1024])
                    sV.inc(v.drain())

                @blk.tensor
                def _(tn):
                    sV.wait(tn)
                    sM.inc(tn.matmul(ps_w[:, :], onesr[:, :],
                                     srow[0:1, 1024:1536], start=True,
                                     stop=True))

            sA_b, sM_b = sA.v, sM.v
            with nc.Block() as blk:
                @blk.tensor
                def _(tn):
                    for hc in range(32):
                        if hc >= 2:
                            tn.wait_ge(sA.h, sA_b + hc - 1)
                        for kc in range(G):
                            mm = tn.matmul(
                                ps_h[hc % 2][:, :],
                                w1b[:, kc * 4 * C + hc * 128:
                                    kc * 4 * C + hc * 128 + 128],
                                xnbfc[:, kc * 512:(kc + 1) * 512],
                                start=(kc == 0), stop=(kc == G - 1))
                        sM.inc(mm)

                @blk.scalar
                def _(s):
                    for hc in range(32):
                        s.wait_ge(sM.h, sM_b + hc + 1)
                        s.activation(h2[:, hc * 512:(hc + 1) * 512],
                                     ps_h[hc % 2][:, :], AF.Gelu_apprx_tanh,
                                     bias=b1t[:, hc:hc + 1])
                        sA.inc(s.drain())

            sM_b2, sV_b2, sA_b2, ld_b2 = sM.v, sV.v, sA.v, ld.v
            with nc.Block() as blk:
                @blk.gpsimd
                def _(g_):
                    for m in range(G):
                        if m >= 2:
                            g_.wait_ge(sM.h, sM_b2 + m - 1)
                        ld.inc(g_.dma_start(
                            out=w2t[m % 2][:, :].rearrange(
                                "p (hc o) -> p hc o", o=128),
                            in_=W2_d[:, m * 128:(m + 1) * 128].rearrange(
                                "(hc p) o -> p hc o", p=128)), 16)

                @blk.tensor
                def _(tn):
                    for m in range(G):
                        tn.wait_ge(ld.h, ld_b2 + 16 * (m + 1))
                        if m >= 2:
                            tn.wait_ge(sV.h, sV_b2 + m - 1)
                        for hc in range(32):
                            mm = tn.matmul(
                                ps_o[m % 2][:, :],
                                w2t[m % 2][:, hc * 128:(hc + 1) * 128],
                                h2[:, hc * 512:(hc + 1) * 512],
                                start=(hc == 0), stop=(hc == 31))
                        sM.inc(mm)

                @blk.vector
                def _(v):
                    for m in range(G):
                        v.wait_ge(sM.h, sM_b2 + m + 1)
                        sl = slice(m * 512, (m + 1) * 512)
                        par = m % 2
                        gsl = gm[:, sl]
                        v.tensor_scalar(gsl, ps_o[par][:, :],
                                        b2t[:, m:m + 1], None, ALU.add)
                        v.nop(cycle_cnt=32)
                        v.tensor_mul(gsl, gsl, ps_w[:, :])
                        v.nop(cycle_cnt=32)
                        v.tensor_add(gsl, gsl, xnc[:, sl])
                        sV.inc(v.drain())

                @blk.scalar
                def _(s):
                    for m in range(G):
                        s.wait_ge(sV.h, sV_b2 + m + 1)
                        sl = slice(m * 512, (m + 1) * 512)
                        s.activation(gm[:, sl], gm[:, sl], AF.Silu)
                        sA.inc(s.drain())

                @blk.vector
                def _(v):
                    for m in range(G):
                        v.wait_ge(sA.h, sA_b2 + m + 1)
                        sl = slice(m * 512, (m + 1) * 512)
                        v.tensor_mul(soc[:, sl], soc[:, sl], gm[:, sl])
                        v.nop(cycle_cnt=32)
                        v.tensor_add(x1c[:, sl], x1c[:, sl], soc[:, sl])
                        sV.inc(v.drain())

                @blk.sync
                def _(sync, n=n):
                    sV.wait(sync)
                    ld.inc(sync.dma_start(
                        out=x1h_d[:, n * 512:(n + 1) * 512].rearrange(
                            "(g p) t -> p g t", p=128),
                        in_=x1c[:, :].rearrange("p (g t) -> p g t", t=512)), 16)

    with nc.Block() as blk:
        @blk.sync
        def _(sync):
            ld.wait(sync)
    es.close()





def emit_s6(nc, D, S):
    es = contextlib.ExitStack()

    x1_d = D["x1"]
    wq_d, wk_d, wv_d = D["wq"], D["wk"], D["wv"]
    wr_d, br_d = D["wr"], D["br"]
    cos_d, sin_d, ps2_d, idn_d = D["cosb"], D["sinb"], D["ps2"], D["idn"]
    vecs = {nm: D[nm] for nm in ["ln_g", "ln_b"]}

    xn1bf_d = D["xn1bf"]
    qr_d, kr_d, sq_d, sk_d = D["qrh"], D["krh"], D["sqh"], D["skh"]
    sktm_d, vtm_d = D["sktm"], D["vtm"]
    sgate_d, slot_d = D["sgate"], D["slotint"]
    iota8_d, iotac_d, ut128_d = D["iota8"], D["iotac"], D["ut128"]
    trasht_d = D["trasht"]

    def sb(stack, name, shape, dtype=F32):
        return stack.enter_context(nc.sbuf_tensor("s6_" + name, shape, dtype))

    def psum(stack, name, shape, dtype=F32):
        return stack.enter_context(nc.psum_tensor("s6_" + name, shape, dtype))

    vt = {nm: sb(es, "v_" + nm, [128, G]) for nm in vecs}
    onesc = sb(es, "onesc", [128, 1])
    onesr = sb(es, "onesr", [1, 128])
    cst_eps = sb(es, "cst_eps", [128, 1])
    idf = sb(es, "idf", [128, 128])
    ps2b = sb(es, "ps2b", [128, 128], BF16)
    wqb = sb(es, "wqb", [128, G * 512], BF16)
    wkb = sb(es, "wkb", [128, G * 512], BF16)
    wvb = sb(es, "wvb", [128, G * 512], BF16)
    wrt = sb(es, "wrt", [128, G * 8])
    brt = sb(es, "brt", [1, 8])
    iota8 = sb(es, "iota8", [128, 8])
    iotac = sb(es, "iotac", [128, 8])
    ut128 = sb(es, "ut128", [128, 128])
    trasht = sb(es, "trasht", [128, 1])
    cnt8 = sb(es, "cnt8", [1, 8])
    islot6 = sb(es, "islot6", [128, 64], I32)
    rank6 = sb(es, "rank6", [128, 64])
    e16 = sb(es, "e16", [128, 64])
    offrow = sb(es, "offrow", [1, 8])
    lt8t = sb(es, "lt8t", [8, 8])
    r8 = sb(es, "r8", [1, 32])
    i8a = sb(es, "i8a", [1, 8], I32)
    t88 = sb(es, "t88", [8, 8])
    off8c = sb(es, "off8c", [8, 1])

    ld, sA, sM, sV = S

    with nc.Block() as blk:
        @blk.sync
        def _(sync):
            for nm in vecs:
                ld.inc(sync.dma_start(out=vt[nm][:, :], in_=vecs[nm]), 16)
            ld.inc(sync.dma_start(out=idf[:, :], in_=idn_d), 16)
            ld.inc(sync.dma_start(
                out=wrt[:, :].rearrange("p (kc o) -> p kc o", o=8),
                in_=wr_d.rearrange("(kc p) o -> p kc o", p=128)), 16)
            ld.inc(sync.dma_start(out=brt[:, :], in_=br_d), 16)
            ld.inc(sync.dma_start(out=iota8[:, :], in_=iota8_d), 16)
            ld.inc(sync.dma_start(out=iotac[:, :], in_=iotac_d), 16)
            ld.inc(sync.dma_start(out=ut128[:, :], in_=ut128_d), 16)
            ld.inc(sync.dma_start(out=trasht[:, :], in_=trasht_d), 16)
            ld.inc(sync.dma_start(out=lt8t[:, :], in_=D["lt8"]), 16)
            ld.wait(sync)

        @blk.gpsimd
        def _(g_):
            ld.inc(g_.dma_start(out=ps2b[:, :], in_=ps2_d), 16)
            for wd, wb in [(wq_d, wqb), (wk_d, wkb), (wv_d, wvb)]:
                ld.inc(g_.dma_start(
                    out=wb[:, :].rearrange("p (kc o) -> p kc o", o=512),
                    in_=wd.rearrange("(kc p) o -> p kc o", p=128)), 16)

    with nc.Block() as blk:
        @blk.vector
        def _(v):
            v.memset(onesc[:, :], 1.0)
            v.memset(onesr[:, :], 1.0)
            v.memset(cst_eps[:, :], EPS)
            v.memset(cnt8[:, :], 0.0)

    with contextlib.ExitStack() as e1:
        xc = sb(e1, "xc", [128, G * 512])
        sqf = sb(e1, "sqf", [128, G * 512])
        xnc = sb(e1, "xnc", [128, G * 512])
        xnbfc = sb(e1, "xnbfc", [128, G * 512], BF16)
        lac = sb(e1, "lac", [128, G * 512], BF16)
        row = sb(e1, "row", [1, 4 * 512])
        cosc = sb(e1, "cosc", [128, 512])
        sinc = sb(e1, "sinc", [128, 512])
        qb = sb(e1, "qb", [128, 4 * 512], BF16)
        kb = sb(e1, "kb", [128, 4 * 512], BF16)
        ob = sb(e1, "ob", [128, 4 * 512], BF16)
        ob2 = sb(e1, "ob2", [128, 4 * 512], BF16)
        tt8 = sb(e1, "tt8", [128, 8 * 512])
        lg = sb(e1, "lg", [128, 48])
        sg8 = sb(e1, "sg8", [8, 128])
        ps_sum = psum(e1, "ps_sum", [1, 512])
        ps_sq = psum(e1, "ps_sq", [1, 512])
        ps_rb = psum(e1, "ps_rb", [128, 512])
        ps_mrb = psum(e1, "ps_mrb", [128, 512])
        ps_q = [psum(e1, f"ps_q{i}", [128, 512]) for i in range(2)]
        ps_g = psum(e1, "ps_g", [128, 8])
        ps_gt = psum(e1, "ps_gt", [8, 128])

        for n in range(NC512):
            c0 = n * 512
            sV_chunk_top = sV.v
            with nc.Block() as blk:
                @blk.sync
                def _(sync, c0=c0):
                    sync.wait_ge(ld.h, ld.v)
                    ld.inc(sync.dma_start(
                        out=xc[:, :].rearrange("p (g t) -> p g t", t=512),
                        in_=x1_d[:, c0:c0 + 512].rearrange(
                            "(g p) t -> p g t", p=128)), 16)
                    ld.inc(sync.dma_start(out=cosc[:, :],
                                          in_=cos_d[:, c0:c0 + 512]), 16)
                    ld.inc(sync.dma_start(out=sinc[:, :],
                                          in_=sin_d[:, c0:c0 + 512]), 16)
                    ld.wait(sync)

            with nc.Block() as blk:
                @blk.scalar
                def _(s):
                    for g in range(G):
                        s.activation(sqf[:, g * 512:(g + 1) * 512],
                                     xc[:, g * 512:(g + 1) * 512], AF.Square)
                    sA.inc(s.drain())

                @blk.tensor
                def _(tn, sV_chunk_top=sV_chunk_top):
                    tn.wait_ge(sV.h, sV_chunk_top)
                    for g in range(G):
                        tn.matmul(ps_sum[:, :], onesc[:, 0:1],
                                  xc[:, g * 512:(g + 1) * 512],
                                  start=(g == 0), stop=(g == G - 1))
                    sA.wait(tn)
                    for g in range(G):
                        mm = tn.matmul(ps_sq[:, :], onesc[:, 0:1],
                                       sqf[:, g * 512:(g + 1) * 512],
                                       start=(g == 0), stop=(g == G - 1))
                    sM.inc(mm)

                @blk.vector
                def _(v):
                    sM.wait(v)
                    mean, var = row[:, 0:512], row[:, 512:1024]
                    v.tensor_scalar_mul(mean, ps_sum[:, :], 1.0 / C)
                    v.nop(cycle_cnt=16)
                    v.tensor_mul(var, mean, mean)
                    v.nop(cycle_cnt=16)
                    v.scalar_tensor_tensor(var, ps_sq[:, :], 1.0 / C, var,
                                           ALU.mult, ALU.subtract)
                    sV.inc(v.drain())

                @blk.scalar
                def _(s):
                    sV.wait(s)
                    s.activation(row[:, 1024:1536], row[:, 512:1024], AF.Sqrt,
                                 bias=cst_eps[0:1, 0:1])
                    sA.inc(s.drain())

                @blk.vector
                def _(v):
                    sA.wait(v)
                    v.reciprocal(row[:, 1024:1536], row[:, 1024:1536])
                    v.nop(cycle_cnt=32)
                    v.tensor_mul(row[:, 1536:2048], row[:, 0:512],
                                 row[:, 1024:1536])
                    sV.inc(v.drain())

                @blk.tensor
                def _(tn):
                    sV.wait(tn)
                    tn.matmul(ps_rb[:, :], onesr[:, :], row[:, 1024:1536],
                              start=True, stop=True)
                    sM.inc(tn.matmul(ps_mrb[:, :], onesr[:, :],
                                     row[:, 1536:2048], start=True, stop=True))

                @blk.vector
                def _(v):
                    sM.wait(v)

                    def mk(g):
                        xg = xc[:, g * 512:(g + 1) * 512]
                        og = xnc[:, g * 512:(g + 1) * 512]
                        bg = xnbfc[:, g * 512:(g + 1) * 512]
                        ag = lac[:, g * 512:(g + 1) * 512]
                        return [
                            lambda: v.tensor_mul(og, xg, ps_rb[:, :]),
                            lambda: v.tensor_sub(og, og, ps_mrb[:, :]),
                            lambda: v.tensor_copy(bg, og),
                            lambda: v.tensor_scalar(ag, og,
                                                    vt["ln_g"][:, g:g + 1],
                                                    vt["ln_b"][:, g:g + 1],
                                                    ALU.mult, ALU.add),
                        ]
                    for g in range(0, G, 2):
                        interleave(mk(g), mk(g + 1))
                    sV.inc(v.drain())

                @blk.sync
                def _(sync, c0=c0):
                    sV.wait(sync)
                    ld.inc(sync.dma_start(
                        out=xn1bf_d[:, c0:c0 + 512].rearrange(
                            "(g p) t -> p g t", p=128),
                        in_=xnbfc[:, :].rearrange("p (g t) -> p g t", t=512)),
                        16)

            sA_b, sM_b = sA.v, sM.v
            with nc.Block() as blk:
                @blk.tensor
                def _(tn):
                    for idx in range(8):
                        wb = wqb if idx < 4 else wkb
                        m = idx % 4
                        if idx >= 2:
                            tn.wait_ge(sA.h, sA_b + idx - 1)
                        for kc in range(G):
                            mm = tn.matmul(
                                ps_q[idx % 2][:, :],
                                wb[:, kc * 512 + m * 128:
                                   kc * 512 + m * 128 + 128],
                                lac[:, kc * 512:(kc + 1) * 512],
                                start=(kc == 0), stop=(kc == G - 1))
                        sM.inc(mm)

                @blk.scalar
                def _(s):
                    for idx in range(8):
                        s.wait_ge(sM.h, sM_b + idx + 1)
                        dst = qb if idx < 4 else kb
                        m = idx % 4
                        s.copy(dst[:, m * 512:(m + 1) * 512],
                               ps_q[idx % 2][:, :])
                        sA.inc(s.drain())

            sM_b, sV_b = sM.v, sV.v
            with nc.Block() as blk:
                @blk.tensor
                def _(tn):
                    for idx in range(8):
                        src = qb if idx < 4 else kb
                        m = idx % 4
                        if idx >= 2:
                            tn.wait_ge(sV.h, sV_b + idx - 1)
                        sM.inc(tn.matmul(ps_q[idx % 2][:, :], ps2b[:, :],
                                         src[:, m * 512:(m + 1) * 512],
                                         start=True, stop=True))

                @blk.vector
                def _(v):
                    for idx in range(8):
                        src = qb if idx < 4 else kb
                        dst = ob if idx < 4 else ob2
                        m = idx % 4
                        ssl = slice(m * 512, (m + 1) * 512)
                        t0 = tt8[:, (idx % 2) * 512:(idx % 2) * 512 + 512]
                        t1 = tt8[:, (2 + idx % 2) * 512:(2 + idx % 2) * 512 + 512]
                        v.wait_ge(sM.h, sM_b + idx + 1)
                        v.tensor_mul(t0, ps_q[idx % 2][:, :], sinc[:, :])
                        v.tensor_mul(t1, src[:, ssl], cosc[:, :])
                        v.nop(cycle_cnt=32)
                        v.tensor_add(dst[:, ssl], t0, t1)
                        sV.inc(v.drain())

                @blk.sync
                def _(sync, c0=c0):
                    for idx in range(8):
                        sync.wait_ge(sV.h, sV_b + idx + 1)
                        dd = qr_d if idx < 4 else kr_d
                        src = ob if idx < 4 else ob2
                        m = idx % 4
                        ld.inc(sync.dma_start(
                            out=dd[m * 128:(m + 1) * 128, c0:c0 + 512],
                            in_=src[:, m * 512:(m + 1) * 512]), 16)

            sA_b, sV_b, ld_b = sA.v, sV.v, ld.v
            with nc.Block() as blk:
                @blk.vector
                def _(v):
                    for idx in range(8):
                        src = qb if idx < 4 else kb
                        m = idx % 4
                        ssl = slice(m * 512, (m + 1) * 512)
                        tt = tt8[:, idx * 512:(idx + 1) * 512]
                        v.tensor_scalar_min(tt, src[:, ssl], 0.0)
                        sV.inc(v.drain())

                @blk.scalar
                def _(s):
                    for idx in range(8):
                        s.wait_ge(sV.h, sV_b + idx + 1)
                        tt = tt8[:, idx * 512:(idx + 1) * 512]
                        s.activation(tt, tt, AF.Exp)
                        sA.inc(s.drain())

                @blk.vector
                def _(v):
                    for idx in range(8):
                        v.wait_ge(sA.h, sA_b + idx + 1)
                        src = qb if idx < 4 else kb
                        dst = ob if idx < 4 else ob2
                        m = idx % 4
                        ssl = slice(m * 512, (m + 1) * 512)
                        tt = tt8[:, idx * 512:(idx + 1) * 512]
                        v.scalar_tensor_tensor(dst[:, ssl], src[:, ssl], 1.0,
                                               tt, ALU.add, ALU.max)
                        sV.inc(v.drain())

                @blk.sync
                def _(sync, c0=c0):
                    for idx in range(8):
                        sync.wait_ge(sV.h, sV_b + 8 + idx + 1)
                        dd = sq_d if idx < 4 else sk_d
                        src = ob if idx < 4 else ob2
                        m = idx % 4
                        ld.inc(sync.dma_start(
                            out=dd[m * 128:(m + 1) * 128, c0:c0 + 512],
                            in_=src[:, m * 512:(m + 1) * 512]), 16)

            sA_b, sM_b, sV_b = sA.v, sM.v, sV.v
            with nc.Block() as blk:
                @blk.tensor
                def _(tn):
                    for idx in range(8):
                        sl4 = idx // 2
                        wb = wvb if idx % 2 == 0 else wkb
                        if idx >= 2:
                            tn.wait_ge(sA.h, sA_b + idx - 1)
                        for kc in range(G):
                            mm = tn.matmul(
                                ps_q[idx % 2][:, :],
                                lac[:, kc * 512 + sl4 * 128:
                                    kc * 512 + sl4 * 128 + 128],
                                wb[:, kc * 512:(kc + 1) * 512],
                                start=(kc == 0), stop=(kc == G - 1))
                        sM.inc(mm)

                @blk.scalar
                def _(s):
                    for idx in range(8):
                        s.wait_ge(sM.h, sM_b + idx + 1)
                        sl4 = idx // 2
                        if idx % 2 == 0:
                            s.copy(ob[:, sl4 * 512:(sl4 + 1) * 512],
                                   ps_q[idx % 2][:, :])
                        else:
                            s.copy(tt8[:, sl4 * 512:(sl4 + 1) * 512],
                                   ps_q[idx % 2][:, :])
                        sA.inc(s.drain())

                @blk.vector
                def _(v):
                    for sl4 in range(4):
                        v.wait_ge(sA.h, sA_b + 2 * sl4 + 2)
                        tt = tt8[:, sl4 * 512:(sl4 + 1) * 512]
                        mn = tt8[:, (4 + sl4) * 512:(5 + sl4) * 512]
                        v.tensor_scalar_min(mn, tt, 0.0)
                        sV.inc(v.drain())

                @blk.scalar
                def _(s):
                    for sl4 in range(4):
                        s.wait_ge(sV.h, sV_b + sl4 + 1)
                        mn = tt8[:, (4 + sl4) * 512:(5 + sl4) * 512]
                        s.activation(mn, mn, AF.Exp)
                        sA.inc(s.drain())

                @blk.vector
                def _(v):
                    for sl4 in range(4):
                        v.wait_ge(sA.h, sA_b + 8 + sl4 + 1)
                        tt = tt8[:, sl4 * 512:(sl4 + 1) * 512]
                        mn = tt8[:, (4 + sl4) * 512:(5 + sl4) * 512]
                        v.scalar_tensor_tensor(
                            ob2[:, sl4 * 512:(sl4 + 1) * 512], tt, 1.0, mn,
                            ALU.add, ALU.max)
                        sV.inc(v.drain())

                @blk.sync
                def _(sync, c0=c0):
                    for sl4 in range(4):
                        sync.wait_ge(sA.h, sA_b + 2 * sl4 + 1)
                        ld.inc(sync.dma_start(
                            out=vtm_d[c0 + sl4 * 128:c0 + (sl4 + 1) * 128, :],
                            in_=ob[:, sl4 * 512:(sl4 + 1) * 512]), 16)
                    for sl4 in range(4):
                        sync.wait_ge(sV.h, sV_b + 4 + sl4 + 1)
                        ld.inc(sync.dma_start(
                            out=sktm_d[c0 + sl4 * 128:c0 + (sl4 + 1) * 128, :],
                            in_=ob2[:, sl4 * 512:(sl4 + 1) * 512]), 16)

            for sl4 in range(4):
                gcol = c0 + sl4 * 128
                gidx = n * 4 + sl4
                rM0, rV0, rA0, rL0 = sM.v, sV.v, sA.v, ld.v
                with nc.Block() as blk:
                    @blk.tensor
                    def _(tn, sl4=sl4):
                        for kc in range(G):
                            tn.matmul(ps_g[:, :],
                                      xnc[:, kc * 512 + sl4 * 128:
                                          kc * 512 + sl4 * 128 + 128],
                                      wrt[:, kc * 8:(kc + 1) * 8],
                                      start=(kc == 0), stop=False)
                        sM.inc(tn.matmul(ps_g[:, :], onesr[:, :],
                                         brt[:, :], start=False,
                                         stop=True))

                    @blk.vector
                    def _(v, rM0=rM0):
                        v.wait_ge(sM.h, rM0 + 1)
                        v.tensor_copy(lg[:, 0:8], ps_g[:, :])
                        v.nop(cycle_cnt=32)
                        v.max(out=lg[:, 8:16], in_=lg[:, 0:8])
                        v.drain()
                        v.tensor_sub(lg[:, 16:17], lg[:, 8:9], lg[:, 9:10])
                        sV.inc(v.drain())

                    @blk.scalar
                    def _(s, rV0=rV0):
                        s.wait_ge(sV.h, rV0 + 1)
                        s.activation(lg[:, 17:18], lg[:, 16:17], AF.Sigmoid)
                        sA.inc(s.drain())

                    @blk.vector
                    def _(v, rA0=rA0):
                        v.wait_ge(sA.h, rA0 + 1)
                        v.tensor_scalar(lg[:, 20:28], lg[:, 0:8],
                                        lg[:, 8:9], None, ALU.is_equal)
                        v.nop(cycle_cnt=32)
                        v.tensor_mul(lg[:, 28:36], lg[:, 20:28], iota8[:, :])
                        v.tensor_scalar(lg[:, 36:44], lg[:, 20:28], -9.0, 9.0,
                                        ALU.mult, ALU.add)
                        v.nop(cycle_cnt=32)
                        v.tensor_add(lg[:, 28:36], lg[:, 28:36], lg[:, 36:44])
                        v.nop(cycle_cnt=32)
                        v.tensor_reduce(lg[:, 18:19], lg[:, 28:36], axis=AX,
                                        op=ALU.min)
                        v.nop(cycle_cnt=32)
                        v.tensor_scalar(lg[:, 20:28], iota8[:, :],
                                        lg[:, 18:19], None, ALU.is_equal)
                        v.nop(cycle_cnt=32)
                        v.tensor_scalar(lg[:, 28:36], lg[:, 20:28],
                                        lg[:, 17:18], None, ALU.mult)
                        sV.inc(v.drain())

                    @blk.tensor
                    def _(tn, rV0=rV0, rA0=rA0):
                        tn.wait_ge(sV.h, rV0 + 2)
                        tn.wait_ge(sA.h, rA0)
                        tn.matmul(ps_g[:, :], ut128[:, :], lg[:, 20:28],
                                  start=True, stop=False)
                        tn.matmul(ps_g[:, :], onesr[:, :], cnt8[:, :],
                                  start=False, stop=True)
                        sM.inc(tn.matmul(ps_sum[0:1, 0:8], onesc[:, 0:1],
                                         lg[:, 20:28], start=True, stop=True))
                        sM.inc(tn.transpose(ps_gt[:, :], lg[:, 28:36],
                                            idf[:, :]))

                    @blk.vector
                    def _(v, gidx=gidx, rM0=rM0):
                        v.wait_ge(sM.h, rM0 + 2)
                        v.tensor_tensor(lg[:, 36:44], ps_g[:, :],
                                        lg[:, 20:28], ALU.mult)
                        v.tensor_tensor(cnt8[:, :], cnt8[:, :],
                                        ps_sum[0:1, 0:8], ALU.add)
                        v.nop(cycle_cnt=32)
                        v.reduce_sum(rank6[:, gidx:gidx + 1], lg[:, 36:44],
                                     axis=AX)
                        v.tensor_copy(e16[:, gidx:gidx + 1], lg[:, 18:19])
                        sV.inc(v.drain())

                    @blk.scalar
                    def _(s, rM0=rM0, rL0=rL0):
                        s.wait_ge(sM.h, rM0 + 3)
                        s.wait_ge(ld.h, rL0)
                        s.copy(sg8[:, :], ps_gt[:, :])
                        sA.inc(s.drain())

                    @blk.sync
                    def _(sync, gcol=gcol, rA0=rA0):
                        sync.wait_ge(sA.h, rA0 + 2)
                        ld.inc(sync.dma_start(
                            out=sgate_d[:, gcol:gcol + 128],
                            in_=sg8[:, :]), 16)

        pM0, pV0 = sM.v, sV.v
        with nc.Block() as blk:
            @blk.vector
            def _(v):
                v.tensor_scalar_mul(r8[0:1, 0:8], cnt8[:, :], 1.0 / 512.0)
                v.nop(cycle_cnt=32)
                v.tensor_copy(i8a[:, :], r8[0:1, 0:8])
                v.nop(cycle_cnt=32)
                v.tensor_copy(r8[0:1, 8:16], i8a[:, :])
                v.nop(cycle_cnt=32)
                v.tensor_tensor(r8[0:1, 16:24], r8[0:1, 0:8], r8[0:1, 8:16],
                                ALU.is_gt)
                v.nop(cycle_cnt=32)
                v.tensor_add(r8[0:1, 8:16], r8[0:1, 8:16], r8[0:1, 16:24])
                v.nop(cycle_cnt=32)
                v.tensor_scalar_mul(r8[0:1, 8:16], r8[0:1, 8:16], 512.0)
                sV.inc(v.drain())

            @blk.tensor
            def _(tn, pV0=pV0):
                tn.wait_ge(sV.h, pV0 + 1)
                sM.inc(tn.matmul(ps_g[0:8, 0:8], onesr[0:1, 0:8],
                                 r8[0:1, 8:16], start=True, stop=True))

            @blk.vector
            def _(v, pM0=pM0):
                v.wait_ge(sM.h, pM0 + 1)
                v.tensor_tensor(t88[:, :], ps_g[0:8, 0:8], lt8t[:, :],
                                ALU.mult)
                v.nop(cycle_cnt=32)
                v.reduce_sum(off8c[:, :], t88[:, :], axis=AX)
                sV.inc(v.drain())

            @blk.tensor
            def _(tn, pV0=pV0):
                tn.wait_ge(sV.h, pV0 + 2)
                sM.inc(tn.transpose(ps_sum[0:1, 0:8], off8c[:, :],
                                    idf[0:8, 0:8]))

            @blk.vector
            def _(v, pM0=pM0):
                v.wait_ge(sM.h, pM0 + 2)
                v.tensor_copy(offrow[:, :], ps_sum[0:1, 0:8])
                sV.inc(v.drain())

            @blk.sync
            def _(sync, pV0=pV0):
                sync.wait_ge(sV.h, pV0 + 3)
                ld.inc(sync.dma_start(out=D["offs"], in_=offrow[:, :]), 16)

            @blk.tensor
            def _(tn, pV0=pV0):
                tn.wait_ge(sV.h, pV0 + 3)
                sM.inc(tn.matmul(ps_g[:, :], onesr[:, :], offrow[:, :],
                                 start=True, stop=True))

            @blk.vector
            def _(v, pM0=pM0):
                v.wait_ge(sM.h, pM0 + 3)
                for gidx in range(64):
                    v.tensor_scalar(lg[:, 20:28], iota8[:, :],
                                    e16[:, gidx:gidx + 1], None, ALU.is_equal)
                    v.nop(cycle_cnt=32)
                    v.tensor_tensor(lg[:, 28:36], ps_g[:, :], lg[:, 20:28],
                                    ALU.mult)
                    v.nop(cycle_cnt=32)
                    v.reduce_sum(lg[:, 19:20], lg[:, 28:36], axis=AX)
                    v.nop(cycle_cnt=32)
                    v.tensor_add(lg[:, 19:20], lg[:, 19:20],
                                 rank6[:, gidx:gidx + 1])
                    v.nop(cycle_cnt=32)
                    v.tensor_copy(islot6[:, gidx:gidx + 1], lg[:, 19:20])
                sV.inc(v.drain())

        with nc.Block() as blk:
            @blk.sync
            def _(sync):
                sV.wait(sync)
                ld.inc(sync.dma_start(out=slot_d, in_=islot6[:, :]), 16)

    with nc.Block() as blk:
        @blk.sync
        def _(sync):
            ld.wait(sync)
    es.close()


def rope_tables():
    d = 64
    pos = np.arange(T, dtype=np.float32)
    freqs = (10000.0 ** (-np.arange(0, d, 2, dtype=np.float32) / d))
    ang = pos[:, None] * freqs
    cos = np.repeat(np.cos(ang), 2, axis=1).T.astype(np.float32)
    sin = np.repeat(np.sin(ang), 2, axis=1).T.astype(np.float32)
    cosb = np.concatenate([cos, cos], 0)
    sinb = np.concatenate([sin, sin], 0)
    p1 = np.zeros((64, 64), np.float32)
    for i in range(32):
        p1[2 * i + 1, 2 * i] = -1.0
        p1[2 * i, 2 * i + 1] = 1.0
    ps2 = np.zeros((128, 128), np.float32)
    ps2[0:64, 0:64] = p1
    ps2[64:128, 64:128] = p1
    return cosb, sinb, ps2


def emit_s7(nc, D, S, njg=64, nheads=8):
    _pad = 0
    es = contextlib.ExitStack()

    qr_d, kr_d, sq_d, sk_d = D["qrh"], D["krh"], D["sqh"], D["skh"]
    sktm_d, vtm_d = D["sktm"], D["vtm"]
    mb_d, msk_d, idn_d = D["mbeta"], D["mask4"], D["idn"]
    o_d = D["oh"]
    dbg = False

    def sb(stack, name, shape, dtype=F32):
        return stack.enter_context(nc.sbuf_tensor("s7_" + name, shape, dtype))

    def psum(stack, name, shape, dtype=F32):
        return stack.enter_context(nc.psum_tensor("s7_" + name, shape, dtype))

    qrh = sb(es, "qrt", [64, T], BF16)
    krh = sb(es, "krt", [64, T], BF16)
    sqh = sb(es, "sqt", [64, T], BF16)
    skh = sb(es, "skt", [64, T], BF16)
    sktmh = sb(es, "sktmt", [32, 256 * 64], BF16)
    vtmh = sb(es, "vtmt", [32, 256 * 64], BF16)
    mask4 = sb(es, "mask4t", [32, 128])
    idf = sb(es, "idft", [128, 128])
    idb = sb(es, "idbt", [128, 128], BF16)
    idb32 = sb(es, "idb32", [32, 32], BF16)
    mbt = sb(es, "mbt", [1, 8])
    ones1r = sb(es, "ones1r", [1, 32])
    bt32 = sb(es, "bt32", [32, 1])
    mz = sb(es, "mz", [64, 65])
    mz_bf = sb(es, "mz_bf", [64, 65], BF16)
    sc_sb = sb(es, "sc_sb", [32, 128])
    e_sb = sb(es, "e_sb", [32, 128])
    e_bf = sb(es, "e_bf", [32, 128], BF16)
    pT = sb(es, "pT", [32, 128], BF16)
    nmx = sb(es, "nmx", [32, 4])
    rs = sb(es, "rs", [32, 4])
    den1 = sb(es, "den1", [32, 2])
    mo_sb = sb(es, "mo_sb", [32, 64])
    pv_sb = sb(es, "pv_sb", [32, 64], BF16)
    dv65 = sb(es, "dv65", [32, 65], BF16)
    t_sb = sb(es, "t_sb", [32, 64])
    otm = sb(es, "otm", [32, 64], BF16)
    obuf = [sb(es, "obuf0", [64, 512], BF16), sb(es, "obuf1", [64, 512], BF16)]

    ps_sc = psum(es, "ps_sc", [32, 128])
    ps_sd = psum(es, "ps_sd", [32, 256])
    ps_nd = psum(es, "ps_nd", [32, 65])
    ps_pv = psum(es, "ps_pv", [32, 65])
    ps_mu = psum(es, "ps_mu", [64, 65])
    ps_ot = [psum(es, "ps_ot0", [64, 32], BF16),
             psum(es, "ps_ot1", [64, 32], BF16)]
    ps_bb = psum(es, "ps_bb", [32, 1])

    ld, sA, sM, sV = S
    ld_s7_base = ld.v

    with nc.Block() as blk:
        @blk.sync
        def _(sync):
            sync.wait_ge(ld.h, ld.v)
            ld.inc(sync.dma_start(out=mask4[:, :], in_=msk_d), 16)
            ld.inc(sync.dma_start(out=idf[:, :], in_=idn_d), 16)
            ld.inc(sync.dma_start(out=mbt[:, :], in_=mb_d), 16)
            ld.wait(sync)

        @blk.vector
        def _(v):
            v.wait_ge(ld.h, ld_s7_base + 48)
            v.tensor_copy(idb[:, :], idf[:, :])
            v.tensor_copy(idb32[:, :], idf[0:32, 0:32])
            v.memset(ones1r[:, :], 1.0)

        @blk.scalar
        def _(s):
            ld.wait(s)
            for _ in range(_pad):
                s.nop(cycle_cnt=7)
            s.activation(mbt[:, :], mbt[:, :], AF.Sigmoid)

    NJG = njg
    for hh in range(nheads):
        with nc.Block() as blk:
            @blk.sync
            def _(sync, hh=hh):
                sync.wait_ge(ld.h, ld.v)
                r0 = hh * 64
                ld.inc(sync.dma_start(out=qrh[:, :], in_=qr_d[r0:r0 + 64, :]), 16)
                ld.inc(sync.dma_start(out=krh[:, :], in_=kr_d[r0:r0 + 64, :]), 16)
                ld.inc(sync.dma_start(out=sqh[:, :], in_=sq_d[r0:r0 + 64, :]), 16)
                ld.inc(sync.dma_start(out=skh[:, :], in_=sk_d[r0:r0 + 64, :]), 16)
                ld.inc(sync.dma_start(
                    out=sktmh[:, :].rearrange("p (s d) -> p s d", d=64),
                    in_=sktm_d[:, r0:r0 + 64].rearrange("(s p) d -> p s d",
                                                        p=32)), 16)
                ld.inc(sync.dma_start(
                    out=vtmh[:, :].rearrange("p (s d) -> p s d", d=64),
                    in_=vtm_d[:, r0:r0 + 64].rearrange("(s p) d -> p s d",
                                                       p=32)), 16)
                ld.wait(sync)

            @blk.tensor
            def _(tn, hh=hh):
                sM.inc(tn.matmul(ps_bb[:, :], ones1r[:, :],
                                 mbt[:, hh:hh + 1], start=True, stop=True))

            @blk.vector
            def _(v):
                sM.wait(v)
                v.memset(mz[:, 0:64], 0.0)
                v.memset(mz[:, 64:65], 1e-6)
                v.memset(mz_bf[:, 0:64], 0.0)
                v.memset(mz_bf[:, 64:65], 1e-6)
                v.memset(dv65[:, 64:65], 1.0)
                v.tensor_copy(bt32[:, :], ps_bb[:, :])

        M0, V0, A0, L0 = sM.v, sV.v, sA.v, ld.v
        with nc.Block() as blk:
            @blk.tensor
            def _(tn, hh=hh):
                for J in range(NJG):
                    if J >= 1:
                        tn.wait_ge(sV.h, V0 + (J - 1) * 10 + 1)
                    for s4 in range(4):
                        j = J * 4 + s4
                        mm = tn.matmul(ps_sc[:, s4 * 32:(s4 + 1) * 32],
                                       qrh[:, j * 32:(j + 1) * 32],
                                       krh[:, j * 32:(j + 1) * 32],
                                       start=True, stop=True)
                    sM.inc(mm)
                    tn.wait_ge(sV.h, V0 + J * 10 + 2)
                    for s4 in range(4):
                        j = J * 4 + s4
                        mm = tn.matmul(ps_sd[:, s4 * 64:(s4 + 1) * 64],
                                       pT[:, s4 * 32:(s4 + 1) * 32],
                                       vtmh[:, j * 64:(j + 1) * 64],
                                       start=True, stop=True)
                    sM.inc(mm)
                    for s4 in range(4):
                        j = J * 4 + s4
                        if s4 >= 1:
                            tn.wait_ge(sV.h, V0 + J * 10 + 2 * s4 + 2)
                        tn.matmul(ps_nd[:, :], sqh[:, j * 32:(j + 1) * 32],
                                  mz_bf[:, :], start=True, stop=True)
                        sM.inc(tn.matmul(ps_pv[:, :],
                                         skh[:, j * 32:(j + 1) * 32],
                                         mz_bf[:, :], start=True, stop=True))
                        tn.wait_ge(sV.h, V0 + J * 10 + 3 + 2 * s4)
                        sM.inc(tn.matmul(ps_mu[:, :],
                                         sktmh[:, j * 64:(j + 1) * 64],
                                         dv65[:, :], start=True, stop=True))
                        if s4 >= 2:
                            tn.wait_ge(sA.h, A0 + J * 5 + 1 + s4)
                        elif J >= 1:
                            tn.wait_ge(sA.h, A0 + (J - 1) * 5 + 4 + s4)
                        sM.inc(tn.transpose(ps_ot[s4 % 2][:, 0:32], otm[:, :],
                                            idb32[:, :]))

            @blk.vector
            def _(v, hh=hh):
                for J in range(NJG):
                    if J >= 1:
                        v.wait_ge(sA.h, A0 + (J - 1) * 5 + 1)
                    v.wait_ge(sM.h, M0 + J * 14 + 1)
                    v.scalar_tensor_tensor(sc_sb[:, :], ps_sc[:, :], 0.125,
                                           mask4[:, :], ALU.mult, ALU.add)
                    v.nop(cycle_cnt=32)
                    for s4 in range(4):
                        v.tensor_reduce(nmx[:, s4:s4 + 1],
                                        sc_sb[:, s4 * 32:(s4 + 1) * 32],
                                        axis=AX, op=ALU.max)
                    v.nop(cycle_cnt=32)
                    for s4 in range(4):
                        v.tensor_scalar(sc_sb[:, s4 * 32:(s4 + 1) * 32],
                                        sc_sb[:, s4 * 32:(s4 + 1) * 32],
                                        nmx[:, s4:s4 + 1], None, ALU.subtract)
                    sV.inc(v.drain())
                    v.wait_ge(sA.h, A0 + J * 5 + 1)
                    for s4 in range(4):
                        v.reduce_sum(rs[:, s4:s4 + 1],
                                     e_sb[:, s4 * 32:(s4 + 1) * 32], axis=AX)
                    for s4 in range(4):
                        v.reciprocal(rs[:, s4:s4 + 1], rs[:, s4:s4 + 1])
                    for s4 in range(4):
                        v.tensor_scalar(e_bf[:, s4 * 32:(s4 + 1) * 32],
                                        e_sb[:, s4 * 32:(s4 + 1) * 32],
                                        rs[:, s4:s4 + 1], None, ALU.mult)
                    v.drain()
                    v.transpose(pT[:, :], e_bf[:, :])
                    sV.inc(v.drain())
                    for s4 in range(4):
                        j = J * 4 + s4
                        v.wait_ge(sM.h, M0 + J * 14 + 3 + 3 * s4)
                        v.tensor_scalar(den1[:, 0:1], ps_nd[:, 64:65], 1e-6,
                                        None, ALU.add)
                        v.reciprocal(den1[:, 1:2], ps_pv[:, 64:65])
                        v.nop(cycle_cnt=32)
                        v.reciprocal(den1[:, 0:1], den1[:, 0:1])
                        v.tensor_scalar_mul(pv_sb[:, :], ps_pv[:, 0:64],
                                            den1[:, 1:2])
                        v.nop(cycle_cnt=32)
                        v.tensor_scalar_mul(mo_sb[:, :], ps_nd[:, 0:64],
                                            den1[:, 0:1])
                        v.tensor_sub(dv65[:, 0:64],
                                     vtmh[:, j * 64:(j + 1) * 64], pv_sb[:, :])
                        v.nop(cycle_cnt=32)
                        v.tensor_sub(t_sb[:, :], mo_sb[:, :],
                                     ps_sd[:, s4 * 64:(s4 + 1) * 64])
                        v.nop(cycle_cnt=32)
                        v.tensor_scalar_mul(t_sb[:, :], t_sb[:, :],
                                            bt32[:, :])
                        v.nop(cycle_cnt=32)
                        v.tensor_add(otm[:, :], t_sb[:, :],
                                     ps_sd[:, s4 * 64:(s4 + 1) * 64])
                        sV.inc(v.drain())
                        v.wait_ge(sM.h, M0 + J * 14 + 4 + 3 * s4)
                        v.tensor_add(mz[:, :], mz[:, :], ps_mu[:, :])
                        v.nop(cycle_cnt=32)
                        v.tensor_copy(mz_bf[:, :], mz[:, :])
                        sV.inc(v.drain())

            @blk.scalar
            def _(s, hh=hh):
                for J in range(NJG):
                    s.wait_ge(sV.h, V0 + J * 10 + 1)
                    s.activation(e_sb[:, :], sc_sb[:, :], AF.Exp)
                    sA.inc(s.drain())
                    for s4 in range(4):
                        cb = J // 4
                        if J % 4 == 0 and s4 == 0 and cb >= 2:
                            s.wait_ge(ld.h, L0 + 16 * (cb - 1))
                        s.wait_ge(sM.h, M0 + J * 14 + 5 + 3 * s4)
                        col = (J % 4) * 128 + s4 * 32
                        s.copy(obuf[cb % 2][:, col:col + 32],
                               ps_ot[s4 % 2][:, 0:32])
                        sA.inc(s.drain())

            @blk.sync
            def _(sync, hh=hh):
                for cb in range(NJG // 4):
                    sync.wait_ge(sA.h, A0 + (4 * cb + 3) * 5 + 5)
                    ld.inc(sync.dma_start(
                        out=o_d[hh * 64:hh * 64 + 64,
                                cb * 512:(cb + 1) * 512],
                        in_=obuf[cb % 2][:, :]), 16)

    with nc.Block() as blk:
        @blk.sync
        def _(sync):
            ld.wait(sync)
    es.close()










def emit_exchange(nc, D, S, cc):
    ld, sA, sM, sV = S
    with nc.Block() as blk:
        @blk.gpsimd
        def _(g_):
            g_.wait_ge(ld.h, ld.v)
            g_.collective_compute(
                "AllGather", ALU.bypass,
                replica_groups=[[0, 1], [2, 3], [4, 5], [6, 7]],
                ins=[D["oh"].opt()],
                outs=[D["og"].opt()],
            ).then_inc(cc.h)



def emit_s8(nc, D, S, cc):
    ld, sA, sM, sV = S
    og_d, x1_d, xn1bf_d = D["og"], D["x1"], D["xn1bf"]
    sg_d, slot_d = D["sgate"], D["slotint"]
    gbuf_d, gout_d = D["gbuf"], D["gout"]
    wout_d, w1_d, w2_d = D["wout"], D["mw1"], D["mw2"]
    b1_d, b2_d, idn_d = D["mb1"], D["mb2"], D["idn"]
    out_d = D["out"]
    es = contextlib.ExitStack()

    def sb(stack, name, shape, dtype=F32):
        return stack.enter_context(nc.sbuf_tensor("s8_" + name, shape, dtype))

    def psum(stack, name, shape, dtype=F32):
        return stack.enter_context(nc.psum_tensor("s8_" + name, shape, dtype))

    with contextlib.ExitStack() as ea:
        xnbfc = sb(ea, "xnbfc", [128, G * 512], BF16)
        xtok = sb(ea, "xtok", [128, 2 * C], BF16)
        islot = sb(ea, "islot", [128, 64], I32)
        idf = sb(ea, "idfa", [128, 128])
        idb = sb(ea, "idba", [128, 128], BF16)
        ps_t = [psum(ea, f"ps_ta{i}", [128, 128], BF16) for i in range(2)]

        with nc.Block() as blk:
            @blk.sync
            def _(sync):
                sync.wait_ge(ld.h, ld.v)
                ld.inc(sync.dma_start(out=islot[:, :], in_=slot_d), 16)
                ld.inc(sync.dma_start(out=idf[:, :], in_=idn_d), 16)
                ld.wait(sync)

            @blk.vector
            def _(v):
                ld.wait(v)
                v.tensor_copy(idb[:, :], idf[:, :])
                sV.inc(v.drain())
        sV_idb = sV.v

        last_sc_ld = [0, 0]
        for n in range(16):
            c0 = n * 512
            with nc.Block() as blk:
                @blk.sync
                def _(sync, c0=c0):
                    sync.wait_ge(ld.h, ld.v)
                    ld.inc(sync.dma_start(
                        out=xnbfc[:, :].rearrange("p (g t) -> p g t", t=512),
                        in_=xn1bf_d[:, c0:c0 + 512].rearrange(
                            "(g p) t -> p g t", p=128)), 16)
                    ld.wait(sync)
            ldL = ld.v
            for sl4 in range(4):
                par = sl4 % 2
                aM0, aA0 = sM.v, sA.v
                prev_ld = last_sc_ld[par]
                with nc.Block() as blk:
                    @blk.tensor
                    def _(tn, sl4=sl4, ldL=ldL, aA0=aA0):
                        tn.wait_ge(ld.h, ldL)
                        tn.wait_ge(sV.h, sV_idb)
                        for cg in range(G):
                            if cg >= 2:
                                tn.wait_ge(sA.h, aA0 + cg - 1)
                            sM.inc(tn.transpose(
                                ps_t[cg % 2][:, :],
                                xnbfc[:, cg * 512 + sl4 * 128:
                                      cg * 512 + sl4 * 128 + 128],
                                idb[:, :]))

                    @blk.scalar
                    def _(s, par=par, aM0=aM0, prev_ld=prev_ld):
                        if prev_ld:
                            s.wait_ge(ld.h, prev_ld)
                        for cg in range(G):
                            s.wait_ge(sM.h, aM0 + cg + 1)
                            s.copy(xtok[:, par * C + cg * 128:
                                        par * C + (cg + 1) * 128],
                                   ps_t[cg % 2][:, :])
                            sA.inc(s.drain())

                    @blk.gpsimd
                    def _(g_, n=n, sl4=sl4, par=par, aA0=aA0):
                        g_.wait_ge(sA.h, aA0 + 8)
                        ld.inc(g_.indirect_dma_start(
                            out=gbuf_d,
                            out_offset=bass.IndirectOffsetOnAxis(
                                islot[:, n * 4 + sl4:n * 4 + sl4 + 1], 0),
                            in_=xtok[:, par * C:(par + 1) * C],
                            in_offset=None), 16)
                last_sc_ld[par] = ld.v

    NQ = 24
    with contextlib.ExitStack() as eb:
        w1t = sb(eb, "w1t", [128, G * 4 * C], BF16)
        w2r = sb(eb, "w2r", [128, 32 * C], BF16)
        b1c = sb(eb, "b1c", [128, 32])
        gtok = sb(eb, "gtok", [128, 4 * C], BF16)
        xe = sb(eb, "xe", [128, G * 512], BF16)
        u_bf = sb(eb, "u_bf", [128, 32 * 512], BF16)
        oe = sb(eb, "oe", [128, 512])
        otok = sb(eb, "otok", [128, 4 * C])
        idfb = sb(eb, "idfb", [128, 128])
        idbb = sb(eb, "idbb", [128, 128], BF16)
        offr2 = sb(eb, "offr2", [1, 8])
        qcolt = sb(eb, "qcolt", [24, 1])
        piota = sb(eb, "piota", [128, 1])
        ge24 = sb(eb, "ge24", [24, 8])
        ec24 = sb(eb, "ec24", [24, 1])
        ecrow = sb(eb, "ecrow", [1, 24])
        ecb = sb(eb, "ecb", [128, 24])
        wf = sb(eb, "wf", [128, 24])
        w0 = sb(eb, "w0", [128, 24])
        iwidx = sb(eb, "iwidx", [128, 8 * 24], I32)
        iw2 = sb(eb, "iw2", [128, 32 * 24], I32)
        ib1 = sb(eb, "ib1", [128, 24], I32)
        ones1b = sb(eb, "ones1b", [1, 128])
        ps_h = [psum(eb, f"ps_hb{i}", [128, 512]) for i in range(2)]
        ps_o = [psum(eb, f"ps_ob{i}", [128, 512]) for i in range(2)]
        ps_tb = [psum(eb, f"ps_tbb{i}", [128, 128], BF16) for i in range(2)]
        ps_tf = [psum(eb, f"ps_tfb{i}", [128, 128]) for i in range(2)]

        kB0 = sM.v
        with nc.Block() as blk:
            @blk.sync
            def _(sync):
                sync.wait_ge(ld.h, ld.v)
                ld.inc(sync.dma_start(out=idfb[:, :], in_=idn_d), 16)
                ld.inc(sync.dma_start(out=offr2[:, :], in_=D["offs"]), 16)
                ld.inc(sync.dma_start(out=qcolt[:, :], in_=D["qcol"]), 16)
                ld.inc(sync.dma_start(out=piota[:, :], in_=D["piota"]), 16)
                ld.wait(sync)

            @blk.vector
            def _(v):
                ld.wait(v)
                v.tensor_copy(idbb[:, :], idfb[:, :])
                v.memset(otok[:, :], 0.0)
                v.memset(ones1b[:, :], 1.0)
                sV.inc(v.drain())

            @blk.gpsimd
            def _(g_):
                sV.wait(g_)
                ld.inc(g_.dma_start(
                    out=gout_d[NSLOT:NSLOT + 128, :],
                    in_=otok[:, 0:C]), 16)
        sV_idb2 = sV.v

        kM0, kV0 = sM.v, sV.v
        with nc.Block() as blk:
            @blk.tensor
            def _(tn):
                sM.inc(tn.matmul(ge24[0:24, 0:8] if False else ps_h[0][0:24, 0:8],
                                 ones1b[0:1, 0:24], offr2[:, :],
                                 start=True, stop=True))

            @blk.vector
            def _(v, kM0=kM0):
                v.wait_ge(sM.h, kM0 + 1)
                v.tensor_scalar(ge24[:, :], ps_h[0][0:24, 0:8],
                                qcolt[:, 0:1], None, ALU.is_le)
                v.nop(cycle_cnt=32)
                v.reduce_sum(ec24[:, :], ge24[:, :], axis=AX)
                v.nop(cycle_cnt=32)
                v.tensor_scalar(ec24[:, :], ec24[:, :], -1.0, None, ALU.add)
                sV.inc(v.drain())

            @blk.tensor
            def _(tn, kV0=kV0):
                tn.wait_ge(sV.h, kV0 + 1)
                sM.inc(tn.transpose(ps_tf[0][0:1, 0:24], ec24[:, :],
                                    idfb[0:24, 0:24]))

            @blk.scalar
            def _(s, kM0=kM0):
                s.wait_ge(sM.h, kM0 + 2)
                s.copy(ecrow[:, :], ps_tf[0][0:1, 0:24])
                sA.inc(s.drain())

            @blk.tensor
            def _(tn, kA0=sA.v):
                tn.wait_ge(sA.h, kA0)
                sM.inc(tn.matmul(ps_h[1][0:128, 0:24], ones1b[0:1, :],
                                 ecrow[:, :], start=True, stop=True))

            @blk.vector
            def _(v, kM0=kM0):
                v.wait_ge(sM.h, kM0 + 3)
                v.tensor_copy(ecb[:, :], ps_h[1][0:128, 0:24])
                sV.inc(v.drain())

            @blk.vector
            def _(v):
                v.tensor_scalar_mul(w0[:, :], ecb[:, :], 1024.0)
                v.nop(cycle_cnt=32)
                v.tensor_scalar(w0[:, :], w0[:, :], piota[:, 0:1], None,
                                ALU.add)
                v.nop(cycle_cnt=32)
                for kc in range(G):
                    v.tensor_scalar(wf[:, :], w0[:, :], float(kc * 128), None,
                                    ALU.add)
                    v.nop(cycle_cnt=32)
                    v.tensor_copy(iwidx[:, kc * 24:(kc + 1) * 24], wf[:, :])
                    v.nop(cycle_cnt=32)
                v.tensor_scalar_mul(w0[:, :], ecb[:, :], 4096.0)
                v.nop(cycle_cnt=32)
                v.tensor_scalar(w0[:, :], w0[:, :], piota[:, 0:1], None,
                                ALU.add)
                v.nop(cycle_cnt=32)
                for hc in range(32):
                    v.tensor_scalar(wf[:, :], w0[:, :], float(hc * 128), None,
                                    ALU.add)
                    v.nop(cycle_cnt=32)
                    v.tensor_copy(iw2[:, hc * 24:(hc + 1) * 24], wf[:, :])
                    v.nop(cycle_cnt=32)
                v.tensor_scalar_mul(w0[:, :], ecb[:, :], 128.0)
                v.nop(cycle_cnt=32)
                v.tensor_scalar(w0[:, :], w0[:, :], piota[:, 0:1], None,
                                ALU.add)
                v.nop(cycle_cnt=32)
                v.tensor_copy(ib1[:, :], w0[:, :])
                sV.inc(v.drain())

        for q in range(NQ):
            r0 = q * 512
            prevM = sM.v
            with nc.Block() as blk:
                @blk.gpsimd
                def _(g_, q=q, prevM=prevM):
                    g_.wait_ge(sM.h, prevM)
                    for kc in range(G):
                        ld.inc(g_.indirect_dma_start(
                            out=w1t[:, kc * 4 * C:(kc + 1) * 4 * C],
                            out_offset=None,
                            in_=w1_d,
                            in_offset=bass.IndirectOffsetOnAxis(
                                iwidx[:, kc * 24 + q:kc * 24 + q + 1], 0)),
                            16)
                    for hc in range(32):
                        ld.inc(g_.indirect_dma_start(
                            out=w2r[:, hc * C:(hc + 1) * C],
                            out_offset=None,
                            in_=w2_d,
                            in_offset=bass.IndirectOffsetOnAxis(
                                iw2[:, hc * 24 + q:hc * 24 + q + 1], 0)),
                            16)
                    ld.inc(g_.indirect_dma_start(
                        out=b1c[:, :],
                        out_offset=None,
                        in_=D["b1pr"],
                        in_offset=bass.IndirectOffsetOnAxis(
                            ib1[:, q:q + 1], 0)), 16)

                @blk.sync
                def _(sync, r0=r0):
                    sync.wait_ge(ld.h, ld.v)
                    ld.inc(sync.dma_start(
                        out=gtok[:, :].rearrange("p (gr c) -> p gr c", c=C),
                        in_=gbuf_d[r0:r0 + 512, :].rearrange(
                            "(gr p) c -> p gr c", p=128)), 16)
                    ld.wait(sync)
            ld_w = ld.v

            bM0, bV0 = sM.v, sV.v
            with nc.Block() as blk:
                @blk.tensor
                def _(tn, ld_w=ld_w, bV0=bV0):
                    tn.wait_ge(ld.h, ld_w)
                    tn.wait_ge(sV.h, sV_idb2)
                    for i in range(32):
                        gr, cg = divmod(i, 8)
                        if i >= 2:
                            tn.wait_ge(sV.h, bV0 + i - 1)
                        sM.inc(tn.transpose(
                            ps_tb[i % 2][:, :],
                            gtok[:, gr * C + cg * 128:
                                 gr * C + (cg + 1) * 128],
                            idbb[:, :]))

                @blk.vector
                def _(v, bM0=bM0):
                    for i in range(32):
                        gr, cg = divmod(i, 8)
                        v.wait_ge(sM.h, bM0 + i + 1)
                        v.tensor_copy(xe[:, cg * 512 + gr * 128:
                                         cg * 512 + (gr + 1) * 128],
                                      ps_tb[i % 2][:, :])
                        sV.inc(v.drain())

            cM0, cA0 = sM.v, sA.v
            with nc.Block() as blk:
                @blk.tensor
                def _(tn, cA0=cA0, bV0=bV0):
                    tn.wait_ge(sV.h, bV0 + 32)
                    for hc in range(32):
                        if hc >= 2:
                            tn.wait_ge(sA.h, cA0 + hc - 1)
                        for kc in range(G):
                            mm = tn.matmul(
                                ps_h[hc % 2][:, :],
                                w1t[:, kc * 4 * C + hc * 128:
                                    kc * 4 * C + hc * 128 + 128],
                                xe[:, kc * 512:(kc + 1) * 512],
                                start=(kc == 0), stop=(kc == G - 1))
                        sM.inc(mm)

                @blk.scalar
                def _(s, cM0=cM0):
                    for hc in range(32):
                        s.wait_ge(sM.h, cM0 + hc + 1)
                        s.activation(u_bf[:, hc * 512:(hc + 1) * 512],
                                     ps_h[hc % 2][:, :], AF.Gelu,
                                     bias=b1c[:, hc:hc + 1])
                        sA.inc(s.drain())

            dM0, dV0, dA0, dL0 = sM.v, sV.v, sA.v, ld.v
            with nc.Block() as blk:
                @blk.tensor
                def _(tn, dM0=dM0, dA0=dA0, dV0=dV0):
                    for m in range(G):
                        if m >= 2:
                            tn.wait_ge(sA.h, dA0 + m - 1)
                        for hc in range(32):
                            mm = tn.matmul(
                                ps_o[m % 2][:, :],
                                w2r[:, hc * C + m * 128:
                                    hc * C + (m + 1) * 128],
                                u_bf[:, hc * 512:(hc + 1) * 512],
                                start=(hc == 0), stop=(hc == 31))
                        sM.inc(mm)
                        tn.wait_ge(sA.h, dA0 + m + 1)
                        for gr in range(4):
                            if m * 4 + gr >= 2:
                                tn.wait_ge(sV.h, dV0 + m * 4 + gr - 1)
                            sM.inc(tn.transpose(
                                ps_tf[gr % 2][:, :],
                                oe[:, gr * 128:(gr + 1) * 128],
                                idfb[:, :]))

                @blk.scalar
                def _(s, dM0=dM0):
                    for m in range(G):
                        s.wait_ge(sM.h, dM0 + m * 5 + 1)
                        s.copy(oe[:, :], ps_o[m % 2][:, :])
                        sA.inc(s.drain())

                @blk.vector
                def _(v, dM0=dM0, dL0=dL0):
                    for m in range(G):
                        for gr in range(4):
                            if m == 0 and gr == 0:
                                v.wait_ge(ld.h, dL0)
                            v.wait_ge(sM.h, dM0 + m * 5 + 2 + gr)
                            v.tensor_copy(otok[:, gr * C + m * 128:
                                               gr * C + (m + 1) * 128],
                                          ps_tf[gr % 2][:, :])
                            sV.inc(v.drain())

                @blk.sync
                def _(sync, r0=r0, dV0=dV0):
                    sync.wait_ge(sV.h, dV0 + 32)
                    ld.inc(sync.dma_start(
                        out=gout_d[r0:r0 + 512, :].rearrange(
                            "(gr p) c -> p gr c", p=128),
                        in_=otok[:, :].rearrange("p (gr c) -> p gr c",
                                                 c=C)), 16)

    with contextlib.ExitStack() as ec:
        wout_b = sb(ec, "wout_b", [128, G * C], BF16)
        oc = sb(ec, "oc", [128, G * 512], BF16)
        x1c = sb(ec, "x1c", [128, G * 512])
        s1_sb = sb(ec, "s1sb", [128, G * 512])
        mtok = sb(ec, "mtok", [128, 4 * C])
        macc = sb(ec, "macc", [128, G * 512])
        sgc8 = sb(ec, "sgc8", [8, 512])
        b2t = sb(ec, "b2t2", [8, C])
        rowg = sb(ec, "rowg", [1, 512])
        ones8 = sb(ec, "ones8", [8, 1])
        onesr = sb(ec, "onesr2", [1, 128])
        idf3 = sb(ec, "idf3", [128, 128])
        islot2 = sb(ec, "islot2", [128, 64], I32)
        ps_h2 = [psum(ec, f"ps_hc{i}", [128, 512]) for i in range(2)]
        ps_tf2 = [psum(ec, f"ps_tfc{i}", [128, 128]) for i in range(2)]
        ps_w = psum(ec, "ps_wc", [128, 512])
        ps_b2 = psum(ec, "ps_b2c", [128, 512])
        ps_row = psum(ec, "ps_rowc", [1, 512])

        with nc.Block() as blk:
            @blk.sync
            def _(sync):
                sync.wait_ge(ld.h, ld.v)
                sync.wait_ge(cc.h, 1)
                ld.inc(sync.dma_start(out=b2t[:, :], in_=b2_d), 16)
                ld.inc(sync.dma_start(out=islot2[:, :], in_=slot_d), 16)
                ld.inc(sync.dma_start(out=idf3[:, :], in_=idn_d), 16)
                ld.wait(sync)

            @blk.gpsimd
            def _(g_):
                ld.inc(g_.dma_start(
                    out=wout_b[:, :].rearrange("p (kc o) -> p kc o", o=C),
                    in_=wout_d.rearrange("(kc p) o -> p kc o", p=128)), 16)

            @blk.vector
            def _(v):
                v.memset(ones8[:, :], 1.0)
                v.memset(onesr[:, :], 1.0)
                sV.inc(v.drain())
        wout_ld = ld.v

        for n in range(16):
            c0 = n * 512
            eL0 = ld.v
            with nc.Block() as blk:
                @blk.sync
                def _(sync, c0=c0):
                    sync.wait_ge(ld.h, ld.v)
                    ld.inc(sync.dma_start(
                        out=oc[:, :].rearrange("p (g t) -> p g t", t=512),
                        in_=og_d[:, c0:c0 + 512].rearrange(
                            "(g p) t -> p g t", p=128)), 16)
                    ld.inc(sync.dma_start(
                        out=x1c[:, :].rearrange("p (g t) -> p g t", t=512),
                        in_=x1_d[:, c0:c0 + 512].rearrange(
                            "(g p) t -> p g t", p=128)), 16)
                    ld.inc(sync.dma_start(out=sgc8[:, :],
                                          in_=sg_d[:, c0:c0 + 512]), 16)
                    ld.wait(sync)
            ld_chunk = ld.v

            fA0, fM0 = sA.v, sM.v
            with nc.Block() as blk:
                @blk.tensor
                def _(tn, fA0=fA0, ld_chunk=ld_chunk):
                    tn.wait_ge(ld.h, wout_ld)
                    tn.wait_ge(ld.h, ld_chunk)
                    for m in range(G):
                        if m >= 2:
                            tn.wait_ge(sA.h, fA0 + m - 1)
                        for kc in range(G):
                            mm = tn.matmul(
                                ps_h2[m % 2][:, :],
                                wout_b[:, kc * C + m * 128:
                                       kc * C + m * 128 + 128],
                                oc[:, kc * 512:(kc + 1) * 512],
                                start=(kc == 0), stop=(kc == G - 1))
                        sM.inc(mm)

                @blk.scalar
                def _(s, fM0=fM0):
                    for m in range(G):
                        s.wait_ge(sM.h, fM0 + m + 1)
                        s.copy(s1_sb[:, m * 512:(m + 1) * 512],
                               ps_h2[m % 2][:, :])
                        sA.inc(s.drain())

            with nc.Block() as blk:
                @blk.gpsimd
                def _(g_, n=n):
                    for tg in range(4):
                        ld.inc(g_.indirect_dma_start(
                            out=mtok[:, tg * C:(tg + 1) * C],
                            out_offset=None,
                            in_=gout_d,
                            in_offset=bass.IndirectOffsetOnAxis(
                                islot2[:, n * 4 + tg:n * 4 + tg + 1], 0)),
                            16)
            ld_gather = ld.v

            hM0, hV0 = sM.v, sV.v
            with nc.Block() as blk:
                @blk.tensor
                def _(tn, hV0=hV0, ld_gather=ld_gather):
                    tn.wait_ge(ld.h, ld_gather)
                    for i in range(32):
                        tg, cg = divmod(i, 8)
                        if i >= 2:
                            tn.wait_ge(sV.h, hV0 + i - 1)
                        sM.inc(tn.transpose(
                            ps_tf2[i % 2][:, :],
                            mtok[:, tg * C + cg * 128:
                                 tg * C + (cg + 1) * 128],
                            idf3[:, :]))

                @blk.vector
                def _(v, hM0=hM0):
                    for i in range(32):
                        tg, cg = divmod(i, 8)
                        v.wait_ge(sM.h, hM0 + i + 1)
                        v.tensor_copy(macc[:, cg * 512 + tg * 128:
                                           cg * 512 + tg * 128 + 128],
                                      ps_tf2[i % 2][:, :])
                        sV.inc(v.drain())

            iM0, iV0, iA0 = sM.v, sV.v, sA.v
            with nc.Block() as blk:
                @blk.tensor
                def _(tn):
                    sM.inc(tn.matmul(ps_row[:, :], ones8[:, 0:1],
                                     sgc8[:, :], start=True, stop=True))

                @blk.scalar
                def _(s, iM0=iM0):
                    s.wait_ge(sM.h, iM0 + 1)
                    s.copy(rowg[:, :], ps_row[:, :])
                    sA.inc(s.drain())

                @blk.tensor
                def _(tn, iA0=iA0):
                    tn.wait_ge(sA.h, iA0 + 1)
                    sM.inc(tn.matmul(ps_w[:, :], onesr[:, :], rowg[:, :],
                                     start=True, stop=True))

                @blk.vector
                def _(v, iM0=iM0):
                    v.wait_ge(sM.h, iM0 + 2)
                    for m in range(G):
                        sl = slice(m * 512, (m + 1) * 512)
                        v.tensor_mul(macc[:, sl], macc[:, sl], ps_w[:, :])
                    sV.inc(v.drain())
            jM0, jV0, jA0 = sM.v, sV.v, sA.v
            with nc.Block() as blk:
                @blk.tensor
                def _(tn, jV0=jV0):
                    for m in range(G):
                        if m >= 1:
                            tn.wait_ge(sV.h, jV0 + m)
                        sM.inc(tn.matmul(ps_b2[:, :],
                                         b2t[:, m * 128:(m + 1) * 128],
                                         sgc8[:, :], start=True, stop=True))

                @blk.vector
                def _(v, jM0=jM0):
                    for m in range(G):
                        v.wait_ge(sM.h, jM0 + m + 1)
                        sl = slice(m * 512, (m + 1) * 512)
                        v.tensor_add(macc[:, sl], macc[:, sl], ps_b2[:, :])
                        sV.inc(v.drain())

                @blk.scalar
                def _(s, jV0=jV0):
                    for m in range(G):
                        s.wait_ge(sV.h, jV0 + m + 1)
                        sl = slice(m * 512, (m + 1) * 512)
                        s.activation(macc[:, sl], macc[:, sl], AF.Silu)
                        sA.inc(s.drain())

                @blk.vector
                def _(v, jA0=jA0):
                    for m in range(G):
                        v.wait_ge(sA.h, jA0 + m + 1)
                        sl = slice(m * 512, (m + 1) * 512)
                        v.tensor_mul(s1_sb[:, sl], s1_sb[:, sl], macc[:, sl])
                        v.nop(cycle_cnt=32)
                        v.tensor_add(x1c[:, sl], x1c[:, sl], s1_sb[:, sl])
                        sV.inc(v.drain(), 2)

                @blk.sync
                def _(sync, n=n, jV0=jV0):
                    sync.wait_ge(sV.h, jV0 + G + 2 * G)
                    ld.inc(sync.dma_start(
                        out=out_d[:, n * 512:(n + 1) * 512].rearrange(
                            "(g p) t -> p g t", p=128),
                        in_=x1c[:, :].rearrange("p (g t) -> p g t", t=512)),
                        16)

    with nc.Block() as blk:
        @blk.sync
        def _(sync):
            ld.wait(sync)
    es.close()


def emit_s8_dense(nc, D, S, cc, nchunks=T // 512, nexperts=8):
    SKIP = ''
    es = contextlib.ExitStack()
    th0 = 0

    o_d = D["og"]
    wout_d, x1_d, xn1bf_d = D["wout"], D["x1"], D["xn1bf"]
    sg_d = D["sgate"]
    w1_d, w2_d, b1_d, b2_d = D["mw1"], D["mw2"], D["mb1"], D["mb2"]
    out_d = D["out"]

    def sb(stack, name, shape, dtype=F32):
        return stack.enter_context(nc.sbuf_tensor("s8d_" + name, shape, dtype))

    def psum(stack, name, shape, dtype=F32):
        return stack.enter_context(nc.psum_tensor("s8d_" + name, shape, dtype))

    wout_b = sb(es, "wout_b", [128, G * C], BF16)
    b1t = sb(es, "b1t", [128, 8 * 32])
    b2t = sb(es, "b2t", [8, C])
    w1t = sb(es, "w1t", [128, G * 4 * C], BF16)
    w2t = [sb(es, "w2t0", [128, 32 * 128], BF16),
           sb(es, "w2t1", [128, 32 * 128], BF16)]
    x1c = sb(es, "x1c", [128, G * 512])
    xnbfc = sb(es, "xnbfc", [128, G * 512], BF16)
    s1_sb = sb(es, "s1_sb", [128, G * 512])
    acc = sb(es, "acc", [128, G * 512])
    u_bf = sb(es, "u_bf", [128, 32 * 512], BF16)
    oc = u_bf
    glt = sb(es, "glt", [128, 2 * 512])
    sgc8 = sb(es, "sgc8", [8, 512])
    sgc1 = sb(es, "sgc1", [1, 512])

    ps_h = [psum(es, f"ps_h{i}", [128, 512]) for i in range(2)]
    ps_o = [psum(es, f"ps_o{i}", [128, 512]) for i in range(2)]
    ps_w = psum(es, "ps_w", [128, 512])
    ps_b2 = psum(es, "ps_b2", [128, 512])

    ld, sA, sM, sV = S
    onesr = sb(es, "onesr", [1, 128])

    with nc.Block() as blk:
        @blk.sync
        def _(sync):
            sync.wait_ge(ld.h, ld.v)
            sync.wait_ge(cc.h, 1)
            ld.inc(sync.dma_start(out=b1t[:, :], in_=b1_d), 16)
            ld.inc(sync.dma_start(out=b2t[:, :], in_=b2_d), 16)
            ld.wait(sync)

        @blk.gpsimd
        def _(g_):
            ld.inc(g_.dma_start(
                out=wout_b[:, :].rearrange("p (kc o) -> p kc o", o=C),
                in_=wout_d.rearrange("(kc p) o -> p kc o", p=128)), 16)

        @blk.vector
        def _(v):
            v.memset(onesr[:, :], 1.0)

    for n in range(nchunks):
        c0 = th0 + n * 512
        with nc.Block() as blk:
            @blk.sync
            def _(sync, n=n, c0=c0):
                sync.wait_ge(ld.h, ld.v)
                ld.inc(sync.dma_start(
                    out=oc[:, 0:G * 512].rearrange("p (g t) -> p g t", t=512),
                    in_=o_d[:, c0:c0 + 512].rearrange("(g p) t -> p g t",
                                                      p=128)), 16)
                ld.inc(sync.dma_start(
                    out=x1c[:, :].rearrange("p (g t) -> p g t", t=512),
                    in_=x1_d[:, c0:c0 + 512].rearrange("(g p) t -> p g t",
                                                       p=128)), 16)
                ld.inc(sync.dma_start(
                    out=xnbfc[:, :].rearrange("p (g t) -> p g t", t=512),
                    in_=xn1bf_d[:, c0:c0 + 512].rearrange("(g p) t -> p g t",
                                                          p=128)), 16)
                ld.inc(sync.dma_start(out=sgc8[:, :],
                                      in_=sg_d[:, n * 512:(n + 1) * 512]), 16)
                ld.wait(sync)

        sA_b, sM_b = sA.v, sM.v
        if 'wout' in SKIP:
            pass
        else:
         with nc.Block() as blk:
            @blk.tensor
            def _(tn):
                for m in range(G):
                    if m >= 2:
                        tn.wait_ge(sA.h, sA_b + m - 1)
                    for kc in range(G):
                        mm = tn.matmul(
                            ps_h[m % 2][:, :],
                            wout_b[:, kc * C + m * 128:kc * C + m * 128 + 128],
                            oc[:, kc * 512:(kc + 1) * 512],
                            start=(kc == 0), stop=(kc == G - 1))
                    sM.inc(mm)

            @blk.scalar
            def _(s):
                for m in range(G):
                    s.wait_ge(sM.h, sM_b + m + 1)
                    s.copy(s1_sb[:, m * 512:(m + 1) * 512], ps_h[m % 2][:, :])
                    sA.inc(s.drain())

        with nc.Block() as blk:
            @blk.vector
            def _(v):
                v.memset(acc[:, :], 0.0)

        for e in range(nexperts):
            with nc.Block() as blk:
                @blk.gpsimd
                def _(g_, e=e):
                    ld.inc(g_.dma_start(
                        out=w1t[:, :].rearrange("p (kc o) -> p kc o",
                                                o=4 * C),
                        in_=w1_d[e * C:(e + 1) * C, :].rearrange(
                            "(kc p) o -> p kc o", p=128)), 16)
                    ld.wait(g_)

                @blk.sync
                def _(sync, e=e, n=n):
                    ld.inc(sync.dma_start(
                        out=sgc1[:, :],
                        in_=sg_d[e:e + 1, n * 512:(n + 1) * 512]), 16)
                    ld.wait(sync)

                @blk.tensor
                def _(tn, e=e):
                    ld.wait(tn)
                    sM.inc(tn.matmul(ps_w[:, :], onesr[:, :], sgc1[:, :],
                                     start=True, stop=True))

            sA_b, sM_b, sV_b = sA.v, sM.v, sV.v
            with nc.Block() as blk:
                @blk.tensor
                def _(tn, e=e):
                    for hc in range(32):
                        if hc >= 2:
                            tn.wait_ge(sV.h, sV_b + hc - 1)
                        for kc in range(G):
                            mm = tn.matmul(
                                ps_h[hc % 2][:, :],
                                w1t[:, kc * 4 * C + hc * 128:
                                    kc * 4 * C + hc * 128 + 128],
                                xnbfc[:, kc * 512:(kc + 1) * 512],
                                start=(kc == 0), stop=(kc == G - 1))
                        sM.inc(mm)

                @blk.scalar
                def _(s, e=e):
                    for hc in range(32):
                        s.wait_ge(sM.h, sM_b + hc + 1)
                        if hc >= 2:
                            s.wait_ge(sV.h, sV_b + hc - 1)
                        gl = glt[:, (hc % 2) * 512:(hc % 2) * 512 + 512]
                        s.activation(gl, ps_h[hc % 2][:, :], AF.Gelu,
                                     bias=b1t[:, e * 32 + hc:e * 32 + hc + 1])
                        sA.inc(s.drain())

                @blk.vector
                def _(v, e=e):
                    for hc in range(32):
                        v.wait_ge(sA.h, sA_b + hc + 1)
                        gl = glt[:, (hc % 2) * 512:(hc % 2) * 512 + 512]
                        v.tensor_mul(u_bf[:, hc * 512:(hc + 1) * 512], gl,
                                     ps_w[:, :])
                        sV.inc(v.drain())

            sM_b2, sV_b2, ld_b2 = sM.v, sV.v, ld.v
            with nc.Block() as blk:
                @blk.gpsimd
                def _(g_, e=e):
                    for m in range(G):
                        if m >= 2:
                            g_.wait_ge(sM.h, sM_b2 + m - 1)
                        ld.inc(g_.dma_start(
                            out=w2t[m % 2][:, :].rearrange(
                                "p (hc o) -> p hc o", o=128),
                            in_=w2_d[e * 4 * C:(e + 1) * 4 * C,
                                     m * 128:(m + 1) * 128].rearrange(
                                "(hc p) o -> p hc o", p=128)), 16)

                @blk.tensor
                def _(tn, e=e):
                    for m in range(G):
                        tn.wait_ge(ld.h, ld_b2 + 16 * (m + 1))
                        if m >= 2:
                            tn.wait_ge(sV.h, sV_b2 + m - 1)
                        for hc in range(32):
                            mm = tn.matmul(
                                ps_o[m % 2][:, :],
                                w2t[m % 2][:, hc * 128:(hc + 1) * 128],
                                u_bf[:, hc * 512:(hc + 1) * 512],
                                start=(hc == 0), stop=(hc == 31))
                        sM.inc(mm)

                @blk.vector
                def _(v, e=e):
                    for m in range(G):
                        v.wait_ge(sM.h, sM_b2 + m + 1)
                        sl = slice(m * 512, (m + 1) * 512)
                        v.tensor_add(acc[:, sl], acc[:, sl], ps_o[m % 2][:, :])
                        sV.inc(v.drain())

        sA_b, sM_b, sV_b = sA.v, sM.v, sV.v
        if 'comb' in SKIP:
            continue
        with nc.Block() as blk:
            @blk.tensor
            def _(tn):
                for m in range(G):
                    if m >= 1:
                        tn.wait_ge(sV.h, sV_b + m)
                    sM.inc(tn.matmul(ps_b2[:, :],
                                     b2t[:, m * 128:(m + 1) * 128],
                                     sgc8[:, :], start=True, stop=True))

            @blk.vector
            def _(v):
                for m in range(G):
                    v.wait_ge(sM.h, sM_b + m + 1)
                    sl = slice(m * 512, (m + 1) * 512)
                    v.tensor_add(acc[:, sl], acc[:, sl], ps_b2[:, :])
                    sV.inc(v.drain())

            @blk.scalar
            def _(s):
                for m in range(G):
                    s.wait_ge(sV.h, sV_b + m + 1)
                    sl = slice(m * 512, (m + 1) * 512)
                    s.activation(acc[:, sl], acc[:, sl], AF.Silu)
                    sA.inc(s.drain())

            @blk.vector
            def _(v):
                for m in range(G):
                    v.wait_ge(sA.h, sA_b + m + 1)
                    sl = slice(m * 512, (m + 1) * 512)
                    v.tensor_mul(s1_sb[:, sl], s1_sb[:, sl], acc[:, sl])
                    v.nop(cycle_cnt=32)
                    v.tensor_add(x1c[:, sl], x1c[:, sl], s1_sb[:, sl])
                    sV.inc(v.drain(), 2)

            @blk.sync
            def _(sync, n=n):
                sV.wait(sync)
                ld.inc(sync.dma_start(
                    out=out_d[:, n * 512:(n + 1) * 512].rearrange(
                        "(g p) t -> p g t", p=128),
                    in_=x1c[:, :].rearrange("p (g t) -> p g t", t=512)), 16)

    with nc.Block() as blk:
        @blk.sync
        def _(sync):
            ld.wait(sync)
    es.close()
    return nc




def stage_begin(nc, S):
    ld, sA, sM, sV = S
    with nc.Block() as blk:
        for eng in ("sync", "vector", "scalar", "tensor", "gpsimd"):
            def mk(e, _vals=(ld.v, sA.v, sM.v, sV.v)):
                e.wait_ge(ld.h, _vals[0])
                e.wait_ge(sA.h, _vals[1])
                e.wait_ge(sM.h, _vals[2])
                e.wait_ge(sV.h, _vals[3])
            getattr(blk, eng)(mk)


def build_fused():
    nc = bass.Bass("TRN2", target_bir_lowering=False)
    es = contextlib.ExitStack()
    D = {}

    def ein(name, shape, dt=F32):
        D[name] = nc.dram_tensor(name, shape, dt, kind="ExternalInput").ap()

    def intr(name, shape, dt=F32):
        D[name] = nc.dram_tensor(name, shape, dt, kind="Internal").ap()

    ein("xT", [C, T + 1])
    for nm in ("Wk", "Wv", "Wr", "Wo"):
        ein(nm, [C, C])
    for nm in ("ln1_s", "ln1_b", "tmk", "tmv", "tmr", "w_decay", "u_first",
               "w_router", "ln2_s", "ln2_b", "f_tmk", "f_tmr",
               "ln_g", "ln_b"):
        ein(nm, [128, G])
    ein("fWk", [C, 4 * C]); ein("fWr", [C, C]); ein("fWv", [4 * C, C])
    ein("W1", [C, 4 * C]); ein("W2", [4 * C, C])
    ein("b1", [128, 32]); ein("b2", [128, G]); ein("idn", [128, 128])
    ein("wq", [C, 512]); ein("wk", [C, 512]); ein("wv", [C, 512])
    ein("wr", [C, 8]); ein("br", [1, 8])
    ein("cosb", [128, T]); ein("sinb", [128, T]); ein("ps2", [128, 128])
    ein("mbeta", [1, 8]); ein("mask4", [32, 128])
    ein("wout", [C, C]); ein("mw1", [8 * C, 4 * C]); ein("mw2", [32 * C, C])
    ein("mb1", [128, 8 * 32]); ein("mb2", [8, C])
    ein("iota8", [128, 8]); ein("iotac", [128, 8]); ein("ut128", [128, 128])
    ein("trasht", [128, 1])
    ein("lt8", [8, 8]); ein("qcol", [24, 1]); ein("piota", [128, 1])
    ein("b1pr", [8 * 128, 32])

    intr("xn0", [C, T + 1]); intr("xn0bf", [C, T + 1], BF16)
    intr("ln1bf", [C, T + 1], BF16)
    intr("scores", [1, T]); intr("tmout", [C, T])
    intr("xmid", [C, T + 1]); intr("ln2bf", [C, T + 1], BF16)
    intr("cmf", [C, T]); intr("sout", [C, T + 1]); intr("x1", [C, T])
    intr("xn1bf", [C, T], BF16)
    intr("sgate", [8, T]); intr("slotint", [128, 64], I32)
    intr("offs", [1, 8])
    for nm in ("qrh", "krh", "sqh", "skh"):
        intr(nm, [512, T], BF16)
    intr("sktm", [T, 512], BF16); intr("vtm", [T, 512], BF16)
    intr("oh", [512, T], BF16); intr("og", [1024, T], BF16)
    intr("gbuf", [NSLOT + 128, C], BF16); intr("gout", [NSLOT + 128, C])
    D["out"] = nc.dram_tensor("out", [C, T], F32, kind="ExternalOutput").ap()

    S = tuple(Sem(es.enter_context(nc.semaphore(n)))
              for n in ("ld", "sA", "sM", "sV"))
    cc = Sem(es.enter_context(nc.semaphore("cc")))

    emit_s1(nc, D, S)
    stage_begin(nc, S)
    emit_lif(nc, S, "l1_", D["tmout"], D["xn0"], D["xmid"], zero_col0=True)
    stage_begin(nc, S)
    emit_s3(nc, D, S)
    stage_begin(nc, S)
    emit_lif(nc, S, "l2_", D["cmf"], D["xmid"], D["sout"])
    stage_begin(nc, S)
    emit_s5(nc, D, S)
    stage_begin(nc, S)
    emit_s6(nc, D, S)
    stage_begin(nc, S)
    emit_s7(nc, D, S)
    stage_begin(nc, S)
    emit_exchange(nc, D, S, cc)
    emit_s8(nc, D, S, cc)

    ld = S[0]
    with nc.Block() as blk:
        @blk.sync
        def _(sync):
            sync.wait_ge(ld.h, ld.v)
            sync.wait_ge(cc.h, 1)
    es.close()
    return nc


def kernel(**inputs):
    inp = {k: np.asarray(v) for k, v in inputs.items()}
    nc = build_fused()

    wqkv = np.asarray(inp["L1_attn_w_qkv"], np.float32)
    cosb, sinb, ps2 = rope_tables()
    eye = np.eye(128, dtype=np.float32)
    mb = np.asarray(inp["L1_attn_mem_beta"], np.float32).reshape(1, 16)
    mask_add = np.where(np.tril(np.ones((32, 32), np.float32)) > 0, 0.0,
                        -1e30).astype(np.float32)
    iota8 = np.tile(np.arange(8, dtype=np.float32), (128, 1))
    iota8 = np.ascontiguousarray(iota8)
    iotac = np.ascontiguousarray(iota8 * float(CAP))
    ut128 = np.ascontiguousarray(
        np.triu(np.ones((128, 128), np.float32), 1))

    shared = {
        "fWk": np.asarray(inp["L0_attn_fWk"], np.float32),
        "fWr": np.asarray(inp["L0_attn_fWr"], np.float32),
        "fWv": np.asarray(inp["L0_attn_fWv"], np.float32),
        "ln2_s": vec_pg(inp["L0_attn_ln2_s"]),
        "ln2_b": vec_pg(inp["L0_attn_ln2_b"]),
        "f_tmk": vec_pg(inp["L0_attn_f_tmk"]),
        "f_tmr": vec_pg(inp["L0_attn_f_tmr"]),
        "W1": np.asarray(inp["L0_ffn_W1"], np.float32),
        "W2": np.asarray(inp["L0_ffn_W2"], np.float32),
        "b1": np.ascontiguousarray(
            np.asarray(inp["L0_ffn_b1"], np.float32).reshape(32, 128).T),
        "b2": vec_pg(inp["L0_ffn_b2"]),
        "idn": eye,
        "wr": np.asarray(inp["L1_ffn_wr"], np.float32),
        "br": np.asarray(inp["L1_ffn_br"], np.float32).reshape(1, 8),
        "cosb": cosb, "sinb": sinb, "ps2": ps2,
        "ln_g": vec_pg(inp["L1_attn_ln_g"]),
        "ln_b": vec_pg(inp["L1_attn_ln_b"]),
        "mask4": np.ascontiguousarray(np.tile(mask_add, (1, 4))),
        "wout": np.asarray(inp["L1_attn_w_out"], np.float32),
        "mw1": np.asarray(inp["L1_ffn_w1"], np.float32).reshape(8 * C, 4 * C),
        "mw2": np.asarray(inp["L1_ffn_w2"], np.float32).reshape(8 * 4 * C, C),
        "mb1": np.concatenate(
            [np.asarray(inp["L1_ffn_b1"], np.float32)[e].reshape(32, 128).T
             for e in range(8)], axis=1),
        "mb2": np.asarray(inp["L1_ffn_b2"], np.float32),
        "iota8": iota8, "iotac": iotac, "ut128": ut128,
        "trasht": np.ascontiguousarray(
            (NSLOT + np.arange(128, dtype=np.float32)).reshape(128, 1)),
        "lt8": np.ascontiguousarray(
            np.tril(np.ones((8, 8), np.float32), -1)),
        "qcol": np.ascontiguousarray(
            (512.0 * np.arange(24, dtype=np.float32)).reshape(24, 1)),
        "piota": np.ascontiguousarray(
            np.arange(128, dtype=np.float32).reshape(128, 1)),
        "b1pr": np.ascontiguousarray(np.stack(
            [np.asarray(inp["L1_ffn_b1"], np.float32)[e].reshape(32, 128).T
             for e in range(8)], axis=0).reshape(8 * 128, 32)),
    }

    maps = []
    for c in range(8):
        b, hs = c // 2, c % 2
        m = dict(s1_inmap(inp, b))
        m.update(shared)
        m.update({
            "wq": np.ascontiguousarray(wqkv[:, hs * 512:hs * 512 + 512]),
            "wk": np.ascontiguousarray(
                wqkv[:, 1024 + hs * 512:1024 + hs * 512 + 512]),
            "wv": np.ascontiguousarray(
                wqkv[:, 2048 + hs * 512:2048 + hs * 512 + 512]),
            "mbeta": np.ascontiguousarray(mb[:, hs * 8:hs * 8 + 8]),
        })
        maps.append(m)

    res = run_bass_kernel_spmd(nc, maps, core_ids=list(range(8))).results
    out = np.empty((B, T, C), np.float32)
    for b in range(B):
        out[b] = res[2 * b]["out"].T
    return out


# version 07
# speedup vs baseline: 1.0456x; 1.0456x over previous
import contextlib
import numpy as np
import concourse.bass as bass
import concourse.mybir as mybir
from concourse.bass_utils import run_bass_kernel_spmd

F32 = mybir.dt.float32
BF16 = mybir.dt.bfloat16
ALU = mybir.AluOpType
AF = mybir.ActivationFunctionType
AX = mybir.AxisListType.X

B, T, C = 4, 8192, 1024
I32 = mybir.dt.int32
CAP = 1536
NSLOT = 8 * CAP
TH = T // 2
G = C // 128
NC512 = T // 512
PAIRS = [[0, 1], [2, 3], [4, 5], [6, 7]]
EPS = 1e-5


class Sem:

    def __init__(self, h):
        self.h = h
        self.v = 0

    def inc(self, instr, k=1):
        instr.then_inc(self.h, k)
        self.v += k
        return instr

    def wait(self, eng):
        eng.wait_ge(self.h, self.v)


def interleave(*chains):
    import itertools
    for ops in itertools.zip_longest(*chains):
        for op in ops:
            if op is not None:
                op()


def vec_pg(a):
    return np.ascontiguousarray(np.asarray(a, np.float32).reshape(G, 128).T)



def emit_s1(nc, D, S):
    es = contextlib.ExitStack()
    xT = D["xT"]; Wk = D["Wk"]; Wv = D["Wv"]; Wr = D["Wr"]; Wo = D["Wo"]
    vecs = {nm: D[nm] for nm in ["ln1_s", "ln1_b", "tmk", "tmv", "tmr",
                                 "w_decay", "u_first", "w_router"]}
    xn0_d, xn0bf_d, ln1bf_d = D["xn0"], D["xn0bf"], D["ln1bf"]
    scores_d, tmout_d = D["scores"], D["tmout"]

    def sb(stack, name, shape, dtype=F32):
        return stack.enter_context(nc.sbuf_tensor("s1_" + name, shape, dtype))

    def psum(stack, name, shape, dtype=F32):
        return stack.enter_context(nc.psum_tensor("s1_" + name, shape, dtype))

    vt = {nm: sb(es, "v_" + nm, [128, G]) for nm in vecs}
    aux = sb(es, "aux", [128, 4 * G])
    lam_tile = sb(es, "lam_tile", [128, G * 512])
    onesc = sb(es, "onesc", [128, 1])
    onesr = sb(es, "onesr", [1, 128])
    zeros1 = sb(es, "zeros1", [128, G], BF16)
    carry = sb(es, "carry", [128, 4 * G])
    wkf = sb(es, "wkf", [128, G * 2 * C], BF16)
    wvf = sb(es, "wvf", [128, G * 2 * C], BF16)
    wrf = sb(es, "wrf", [128, G * 2 * C], BF16)
    wo_b = sb(es, "wo_b", [128, G * C], BF16)
    cst_eps = sb(es, "cst_eps", [128, 1])

    ld, sA, sM, sV = S

    with nc.Block() as blk:
        @blk.sync
        def _(sync):
            for nm in vecs:
                ld.inc(sync.dma_start(out=vt[nm][:, :], in_=vecs[nm]), 16)
            ld.wait(sync)

        @blk.gpsimd
        def _(g_):
            ld.inc(g_.dma_start(
                out=wo_b[:, :].rearrange("p (kc o) -> p kc o", o=C),
                in_=Wo.rearrange("(kc p) o -> p kc o", p=128)), 16)

    with nc.Block() as blk:
        @blk.vector
        def _(v):
            v.memset(onesc[:, :], 1.0)
            v.memset(onesr[:, :], 1.0)
            v.memset(zeros1[:, :], 0.0)
            v.memset(carry[:, :], 0.0)
            v.memset(lam_tile[:, :], 1.0)
            v.memset(cst_eps[:, :], EPS)
            for i, nm in enumerate(["tmk", "tmv", "tmr"]):
                v.tensor_scalar(aux[:, i * G:(i + 1) * G], vt[nm][:, :], -1.0, 1.0,
                                ALU.mult, ALU.add)

        @blk.scalar
        def _(s):
            s.activation(aux[:, 3 * G:4 * G], vt["u_first"][:, :], AF.Exp)
            s.activation(vt["w_decay"][:, :], vt["w_decay"][:, :], AF.Exp)
            s.drain()
            s.activation(vt["w_decay"][:, :], vt["w_decay"][:, :], AF.Exp,
                         scale=-1.0)

    with nc.Block() as blk:
        @blk.vector
        def _(v):
            for g in range(G):
                sl = lam_tile[:, g * 512:(g + 1) * 512]
                v.tensor_scalar_mul(sl, sl, vt["w_decay"][:, g:g + 1])

    with contextlib.ExitStack() as es0:
        wtmp = sb(es0, "wtmp", [128, C])
        for proj, (wd, dst, mixv) in enumerate(
                [(Wk, wkf, "tmk"), (Wv, wvf, "tmv"), (Wr, wrf, "tmr")]):
            for kc in range(G):
                with nc.Block() as blk:
                    @blk.sync
                    def _(sync, wd=wd, kc=kc):
                        ld.inc(sync.dma_start(
                            out=wtmp[:, :],
                            in_=wd[kc * 128:(kc + 1) * 128, :]), 16)
                        ld.wait(sync)

                    @blk.vector
                    def _(v, dst=dst, mixv=mixv, proj=proj, kc=kc):
                        ld.wait(v)
                        base = kc * 2 * C
                        v.tensor_scalar_mul(dst[:, base:base + C], wtmp[:, :],
                                            vt[mixv][:, kc:kc + 1])
                        v.tensor_scalar_mul(
                            dst[:, base + C:base + 2 * C], wtmp[:, :],
                            aux[:, proj * G + kc:proj * G + kc + 1])

    with nc.Block() as blk:
        @blk.sync
        def _(sync):
            ld.inc(sync.dma_start(
                out=ln1bf_d[:, 0:1].rearrange("(g p) t -> p g t", p=128),
                in_=zeros1[:, :].rearrange("p (g o) -> p g o", o=1),
                allow_slow_non_contiguous=True), 16)

    with contextlib.ExitStack() as es1:
        xc = sb(es1, "xc", [128, G * 512])
        sqf = sb(es1, "sqf", [128, G * 512])
        xnc = sb(es1, "xnc", [128, G * 512])
        xnbfc = sb(es1, "xnbfc", [128, G * 512], BF16)
        ln1cw = sb(es1, "ln1cw", [128, G * 512], BF16)
        row = sb(es1, "row", [1, 5 * 512])
        ps_sum = psum(es1, "ps_sum", [1, 512])
        ps_sq = psum(es1, "ps_sq", [1, 512])
        ps_rb = psum(es1, "ps_rb", [128, 512])
        ps_mrb = psum(es1, "ps_mrb", [128, 512])
        ps_sc = psum(es1, "ps_sc", [1, 512])

        def ln_block(src, out_cb, scores_too):
            with nc.Block() as blk:
                @blk.scalar
                def _(s):
                    for g in range(G):
                        s.activation(sqf[:, g * 512:(g + 1) * 512],
                                     src[:, g * 512:(g + 1) * 512], AF.Square)
                    sA.inc(s.drain())

                @blk.tensor
                def _(tn):
                    for g in range(G):
                        tn.matmul(ps_sum[:, :], onesc[:, 0:1],
                                  src[:, g * 512:(g + 1) * 512],
                                  start=(g == 0), stop=(g == G - 1))
                    sA.wait(tn)
                    for g in range(G):
                        last = (g == G - 1) and not scores_too
                        mm = tn.matmul(ps_sq[:, :], onesc[:, 0:1],
                                       sqf[:, g * 512:(g + 1) * 512],
                                       start=(g == 0), stop=(g == G - 1))
                    if scores_too:
                        for g in range(G):
                            mm = tn.matmul(ps_sc[:, :],
                                           vt["w_router"][:, g:g + 1],
                                           src[:, g * 512:(g + 1) * 512],
                                           start=(g == 0), stop=(g == G - 1))
                    sM.inc(mm)

                @blk.vector
                def _(v):
                    sM.wait(v)
                    mean, var, rstd, mr = (row[:, 0:512], row[:, 512:1024],
                                           row[:, 1024:1536], row[:, 1536:2048])
                    v.tensor_scalar_mul(mean, ps_sum[:, :], 1.0 / C)
                    v.nop(cycle_cnt=16)
                    v.tensor_mul(var, mean, mean)
                    v.nop(cycle_cnt=16)
                    v.scalar_tensor_tensor(var, ps_sq[:, :], 1.0 / C, var,
                                           ALU.mult, ALU.subtract)
                    sV.inc(v.drain())

                @blk.scalar
                def _(s):
                    sV.wait(s)
                    s.activation(row[:, 1024:1536], row[:, 512:1024], AF.Sqrt,
                                 bias=cst_eps[0:1, 0:1])
                    sA.inc(s.drain())

                @blk.vector
                def _(v):
                    sA.wait(v)
                    v.reciprocal(row[:, 1024:1536], row[:, 1024:1536])
                    v.nop(cycle_cnt=32)
                    v.tensor_mul(row[:, 1536:2048], row[:, 0:512],
                                 row[:, 1024:1536])
                    sV.inc(v.drain())

                @blk.tensor
                def _(tn):
                    sV.wait(tn)
                    tn.matmul(ps_rb[:, :], onesr[:, :], row[:, 1024:1536],
                              start=True, stop=True)
                    sM.inc(tn.matmul(ps_mrb[:, :], onesr[:, :],
                                     row[:, 1536:2048], start=True, stop=True))

                @blk.vector
                def _(v):
                    sM.wait(v)
                    for g in range(0, G, 2):
                        interleave(out_cb(g), out_cb(g + 1))
                    sV.inc(v.drain())

        for n in range(NC512):
            c0 = 1 + n * 512
            with nc.Block() as blk:
                @blk.sync
                def _(sync):
                    ld.inc(sync.dma_start(
                        out=xc[:, :].rearrange("p (g t) -> p g t", t=512),
                        in_=xT[:, c0:c0 + 512].rearrange("(g p) t -> p g t",
                                                         p=128)), 16)
                    ld.wait(sync)

            def cb1(g):
                def f(v=None):
                    pass
                xg = xc[:, g * 512:(g + 1) * 512]
                og = xnc[:, g * 512:(g + 1) * 512]
                bg = xnbfc[:, g * 512:(g + 1) * 512]
                vv = nc.vector
                return [
                    lambda: vv.tensor_mul(og, xg, ps_rb[:, :]),
                    lambda: vv.tensor_sub(og, og, ps_mrb[:, :]),
                    lambda: vv.tensor_copy(bg, og),
                ]
            ln_block(xc, cb1, scores_too=False)

            def cb2(g):
                xg = xnc[:, g * 512:(g + 1) * 512]
                tg = xc[:, g * 512:(g + 1) * 512]
                og = ln1cw[:, g * 512:(g + 1) * 512]
                vv = nc.vector
                return [
                    lambda: vv.tensor_mul(tg, xg, ps_rb[:, :]),
                    lambda: vv.tensor_sub(tg, tg, ps_mrb[:, :]),
                    lambda: vv.tensor_scalar(og, tg, vt["ln1_s"][:, g:g + 1],
                                             vt["ln1_b"][:, g:g + 1],
                                             ALU.mult, ALU.add),
                ]
            ln_block(xnc, cb2, scores_too=True)

            with nc.Block() as blk:
                @blk.scalar
                def _(s):
                    s.copy(row[:, 2048:2560], ps_sc[:, :])
                    sA.inc(s.drain())

                @blk.sync
                def _(sync):
                    sA.wait(sync)
                    ld.inc(sync.dma_start(
                        out=xn0_d[:, c0:c0 + 512].rearrange(
                            "(g p) t -> p g t", p=128),
                        in_=xnc[:, :].rearrange("p (g t) -> p g t", t=512)), 16)
                    ld.inc(sync.dma_start(
                        out=xn0bf_d[:, c0:c0 + 512].rearrange(
                            "(g p) t -> p g t", p=128),
                        in_=xnbfc[:, :].rearrange("p (g t) -> p g t", t=512)), 16)
                    ld.inc(sync.dma_start(
                        out=ln1bf_d[:, c0:c0 + 512].rearrange(
                            "(g p) t -> p g t", p=128),
                        in_=ln1cw[:, :].rearrange("p (g t) -> p g t", t=512)), 16)
                    ld.inc(sync.dma_start(
                        out=scores_d[:, n * 512:(n + 1) * 512],
                        in_=row[:, 2048:2560]), 16)
                    ld.wait(sync)

    with contextlib.ExitStack() as es2:
        ln1c = sb(es2, "ln1c", [128, G * 513], BF16)
        z_sb = sb(es2, "z_sb", [128, G * 512], BF16)
        tm_sb = sb(es2, "tm_sb", [128, 2 * 512])
        wk_t = {}
        par_t = {}
        for nm in ["ek", "vv", "sr", "ekv", "euk", "ukv"]:
            par_t[nm] = [sb(es2, nm + "A", [128, 512]),
                         sb(es2, nm + "B", [128, 512])]
        PQ = {}
        for nm in ["P", "Q"]:
            PQ[nm] = [sb(es2, nm + "A", [128, 513]),
                      sb(es2, nm + "B", [128, 513])]
        ps_k = [psum(es2, f"ps_k{i}", [128, 512]) for i in range(2)]
        ps_v = [psum(es2, f"ps_v{i}", [128, 512]) for i in range(2)]
        ps_r = [psum(es2, f"ps_r{i}", [128, 512]) for i in range(2)]

        for n in range(NC512):
            c0 = 1 + n * 512
            with nc.Block() as blk:
                @blk.sync
                def _(sync):
                    ld.inc(sync.dma_start(
                        out=ln1c[:, :].rearrange("p (g t) -> p g t", t=513),
                        in_=ln1bf_d[:, c0 - 1:c0 + 512].rearrange(
                            "(g p) t -> p g t", p=128)), 16)
                    ld.wait(sync)

            sA_b, sM_b, sV_b = sA.v, sM.v, sV.v
            with nc.Block() as blk:
                @blk.tensor
                def _(tn):
                    for g in range(G):
                        if g >= 2:
                            tn.wait_ge(sA.h, sA_b + g - 1)
                        for wf, pp in [(wkf, ps_k), (wvf, ps_v), (wrf, ps_r)]:
                            for kc in range(G):
                                lw = wf[:, (kc * 2) * C + g * 128:
                                        (kc * 2) * C + g * 128 + 128]
                                ls = wf[:, (kc * 2 + 1) * C + g * 128:
                                        (kc * 2 + 1) * C + g * 128 + 128]
                                cur = ln1c[:, kc * 513 + 1:kc * 513 + 513]
                                shf = ln1c[:, kc * 513 + 0:kc * 513 + 512]
                                tn.matmul(pp[g % 2][:, :], lw, cur,
                                          start=(kc == 0), stop=False)
                                mm = tn.matmul(pp[g % 2][:, :], ls, shf,
                                               start=False, stop=(kc == G - 1))
                        sM.inc(mm)

                @blk.scalar
                def _(s):
                    for g in range(G):
                        par = g % 2
                        s.wait_ge(sM.h, sM_b + g + 1)
                        if g >= 2:
                            s.wait_ge(sV.h, sV_b + (g - 2) // 2 + 1)
                        s.activation(par_t["ek"][par][:, :], ps_k[par][:, :],
                                     AF.Exp)
                        s.copy(par_t["vv"][par][:, :], ps_v[par][:, :])
                        s.activation(par_t["sr"][par][:, :], ps_r[par][:, :],
                                     AF.Sigmoid)
                        sA.inc(s.drain())

                @blk.vector
                def _(v):
                    for g0 in range(0, G, 2):
                        v.wait_ge(sA.h, sA_b + g0 + 2)

                        def mk(g):
                            par = g % 2
                            ekg = par_t["ek"][par]
                            vg = par_t["vv"][par]
                            srg = par_t["sr"][par]
                            ekvg = par_t["ekv"][par]
                            eukg = par_t["euk"][par]
                            ukvg = par_t["ukv"][par]
                            Pg, Qg = PQ["P"][par], PQ["Q"][par]
                            lamg = lam_tile[:, g * 512:(g + 1) * 512]
                            zg = z_sb[:, g * 512:(g + 1) * 512]
                            cP = carry[:, g:g + 1]
                            cQ = carry[:, 2 * G + g:2 * G + g + 1]
                            return [
                                lambda: v.tensor_mul(ekvg[:, :], ekg[:, :],
                                                     vg[:, :]),
                                lambda: v.tensor_scalar_mul(
                                    eukg[:, :], ekg[:, :],
                                    aux[:, 3 * G + g:3 * G + g + 1]),
                                lambda: v.tensor_mul(ukvg[:, :], eukg[:, :],
                                                     vg[:, :]),
                                lambda: v.tensor_copy(Pg[:, 0:1], cP),
                                lambda: v.tensor_copy(Qg[:, 0:1], cQ),
                                lambda: v.tensor_tensor_scan(
                                    Pg[:, 1:513], lamg, ekvg[:, :], cP,
                                    ALU.mult, ALU.add),
                                lambda: v.tensor_tensor_scan(
                                    Qg[:, 1:513], lamg, ekg[:, :], cQ,
                                    ALU.mult, ALU.add),
                                lambda: v.tensor_add(ukvg[:, :], ukvg[:, :],
                                                     Pg[:, 0:512]),
                                lambda: v.tensor_add(eukg[:, :], eukg[:, :],
                                                     Qg[:, 0:512]),
                                lambda: v.tensor_copy(cP, Pg[:, 512:513]),
                                lambda: v.tensor_copy(cQ, Qg[:, 512:513]),
                                lambda: v.reciprocal(eukg[:, :], eukg[:, :]),
                                lambda: v.tensor_mul(ukvg[:, :], ukvg[:, :],
                                                     eukg[:, :]),
                                lambda: v.tensor_mul(zg, ukvg[:, :], srg[:, :]),
                            ]
                        interleave(mk(g0), mk(g0 + 1))
                        sV.inc(v.drain())

            sA_b, sM_b = sA.v, sM.v
            with nc.Block() as blk:
                @blk.tensor
                def _(tn):
                    for m in range(G):
                        if m >= 2:
                            tn.wait_ge(sA.h, sA_b + m - 1)
                        for g in range(G):
                            lw = wo_b[:, g * C + m * 128:g * C + m * 128 + 128]
                            mm = tn.matmul(ps_k[m % 2][:, :], lw,
                                           z_sb[:, g * 512:(g + 1) * 512],
                                           start=(g == 0), stop=(g == G - 1))
                        sM.inc(mm)

                @blk.scalar
                def _(s):
                    for m in range(G):
                        s.wait_ge(sM.h, sM_b + m + 1)
                        if m >= 2:
                            s.wait_ge(ld.h, ld.v + 16 * (m - 1))
                        s.copy(tm_sb[:, (m % 2) * 512:(m % 2) * 512 + 512],
                               ps_k[m % 2][:, :])
                        sA.inc(s.drain())

                @blk.sync
                def _(sync):
                    for m in range(G):
                        sync.wait_ge(sA.h, sA_b + m + 1)
                        ld.inc(sync.dma_start(
                            out=tmout_d[m * 128:(m + 1) * 128,
                                        n * 512:(n + 1) * 512],
                            in_=tm_sb[:, (m % 2) * 512:(m % 2) * 512 + 512]), 16)
                ld.v = ld.v

    with nc.Block() as blk:
        @blk.sync
        def _(sync):
            ld.wait(sync)

    es.close()


def s1_inmap(inp, b):
    x = np.zeros((C, T + 1), np.float32)
    x[:, 1:] = np.asarray(inp["L0_attn_x"][b], np.float32).T
    return {
        "xT": x,
        "Wk": np.asarray(inp["L0_attn_Wk"], np.float32),
        "Wv": np.asarray(inp["L0_attn_Wv"], np.float32),
        "Wr": np.asarray(inp["L0_attn_Wr"], np.float32),
        "Wo": np.asarray(inp["L0_attn_Wo"], np.float32),
        "ln1_s": vec_pg(inp["L0_attn_ln1_s"]),
        "ln1_b": vec_pg(inp["L0_attn_ln1_b"]),
        "tmk": vec_pg(inp["L0_attn_tmk"]),
        "tmv": vec_pg(inp["L0_attn_tmv"]),
        "tmr": vec_pg(inp["L0_attn_tmr"]),
        "w_decay": vec_pg(inp["L0_attn_w_decay"]),
        "u_first": vec_pg(inp["L0_attn_u_first"]),
        "w_router": vec_pg(inp["L0_ffn_w_router"]),
    }





def emit_lif(nc, S, base_name, src_d, base_d, out_d, zero_col0=False):
    es = contextlib.ExitStack()

    def sb(name, shape, dtype=F32):
        return es.enter_context(nc.sbuf_tensor(base_name + name, shape, dtype))

    CH = 1024
    NCH = T // CH
    tmc = [sb("tmc0", [128, G * CH]), sb("tmc1", [128, G * CH])]
    wbuf = sb("wbuf", [128, G * CH])
    bsc = sb("bsc", [128, G * CH])
    lifu = sb("lifu", [128, G])
    ld, sA, sM, sV = S
    if zero_col0:
        zt = sb("zc", [128, G])
        with nc.Block() as blk:
            @blk.vector
            def _(v):
                v.memset(zt[:, :], 0.0)
                sV.inc(v.drain())

            @blk.gpsimd
            def _(g_):
                sV.wait(g_)
                ld.inc(g_.dma_start(
                    out=out_d[:, 0:1].rearrange("(g p) t -> p g t", p=128),
                    in_=zt[:, :].rearrange("p (g o) -> p g o", o=1),
                    allow_slow_non_contiguous=True), 16)

    with nc.Block() as blk:
        @blk.vector
        def _(v):
            v.memset(lifu[:, :], 0.0)

    with nc.Block() as blk:
        @blk.sync
        def _(sync):
            ld.inc(sync.dma_start(
                out=tmc[0][:, :].rearrange("p (g t) -> p g t", t=CH),
                in_=src_d[:, 0:CH].rearrange("(g p) t -> p g t", p=128)), 16)

    for ch in range(NCH):
        buf = tmc[ch % 2]
        nxt = tmc[(ch + 1) % 2]
        ld_target = ld.v
        with nc.Block() as blk:
            @blk.sync
            def _(sync, ch=ch, nxt=nxt):
                sync.wait_ge(ld.h, ld.v)
                if ch + 1 < NCH:
                    ld.inc(sync.dma_start(
                        out=nxt[:, :].rearrange("p (g t) -> p g t", t=CH),
                        in_=src_d[:, (ch + 1) * CH:(ch + 2) * CH].rearrange(
                            "(g p) t -> p g t", p=128)), 16)
                ld.inc(sync.dma_start(
                    out=bsc[:, :].rearrange("p (g t) -> p g t", t=CH),
                    in_=base_d[:, 1 + ch * CH:1 + (ch + 1) * CH].rearrange(
                        "(g p) t -> p g t", p=128)), 16)

            @blk.vector
            def _(v, buf=buf, ch=ch, ld_target=ld_target):
                v.wait_ge(ld.h, ld_target)
                bt = buf[:, :].rearrange("p (g t) -> p g t", t=CH)
                wt = wbuf[:, :].rearrange("p (g t) -> p g t", t=CH)
                for t in range(CH):
                    v.scalar_tensor_tensor(wt[:, :, t], lifu[:, :], 0.5,
                                           bt[:, :, t], ALU.mult, ALU.add)
                    v.nop(cycle_cnt=16)
                    v.scalar_tensor_tensor(lifu[:, :], wt[:, :, t], 2.0,
                                           wt[:, :, t], ALU.is_lt, ALU.mult)
                    v.nop(cycle_cnt=16)
                ld.wait(v)
                for g in range(0, G, 2):
                    for gg in (g, g + 1):
                        v.scalar_tensor_tensor(
                            bsc[:, gg * CH:(gg + 1) * CH],
                            wbuf[:, gg * CH:(gg + 1) * CH], 2.0,
                            bsc[:, gg * CH:(gg + 1) * CH],
                            ALU.is_ge, ALU.add)
                sV.inc(v.drain())

            @blk.gpsimd
            def _(g_, ch=ch):
                sV.wait(g_)
                ld.inc(g_.dma_start(
                    out=out_d[:, 1 + ch * CH:1 + (ch + 1) * CH].rearrange(
                        "(g p) t -> p g t", p=128),
                    in_=bsc[:, :].rearrange("p (g t) -> p g t", t=CH)), 16)

    with nc.Block() as blk:
        @blk.sync
        def _(sync):
            ld.wait(sync)
    es.close()





def emit_s3(nc, D, S):
    es = contextlib.ExitStack()
    th0 = 0

    xmid_d = D["xmid"]
    fWk_d, fWr_d, fWv_d = D["fWk"], D["fWr"], D["fWv"]
    vecs = {nm: D[nm] for nm in ["ln2_s", "ln2_b", "f_tmk", "f_tmr"]}
    cmh_d = D["cmp"]
    ln2bf_d = D["ln2bf"]

    def sb(stack, name, shape, dtype=F32):
        return stack.enter_context(nc.sbuf_tensor("s3_" + name, shape, dtype))

    def psum(stack, name, shape, dtype=F32):
        return stack.enter_context(nc.psum_tensor("s3_" + name, shape, dtype))

    vt = {nm: sb(es, "v_" + nm, [128, G]) for nm in vecs}
    aux = sb(es, "aux", [128, 2 * G])
    onesc = sb(es, "onesc", [128, 1])
    onesr = sb(es, "onesr", [1, 128])
    cst_eps = sb(es, "cst_eps", [128, 1])
    zeros1 = sb(es, "zeros1", [128, G], BF16)
    fwr_b = sb(es, "fwr_b", [128, G * C], BF16)

    ld, sA, sM, sV = S

    with nc.Block() as blk:
        @blk.sync
        def _(sync):
            for nm in vecs:
                ld.inc(sync.dma_start(out=vt[nm][:, :], in_=vecs[nm]), 16)
            ld.wait(sync)

        @blk.gpsimd
        def _(g_):
            ld.inc(g_.dma_start(
                out=fwr_b[:, :].rearrange("p (kc o) -> p kc o", o=C),
                in_=fWr_d.rearrange("(kc p) o -> p kc o", p=128)), 16)

    with nc.Block() as blk:
        @blk.vector
        def _(v):
            v.memset(onesc[:, :], 1.0)
            v.memset(onesr[:, :], 1.0)
            v.memset(cst_eps[:, :], EPS)
            v.memset(zeros1[:, :], 0.0)
            for i, nm in enumerate(["f_tmk", "f_tmr"]):
                v.tensor_scalar(aux[:, i * G:(i + 1) * G], vt[nm][:, :], -1.0,
                                1.0, ALU.mult, ALU.add)

    with contextlib.ExitStack() as es1:
        xc = sb(es1, "xc", [128, G * 512])
        sqf = sb(es1, "sqf", [128, G * 512])
        lnw = sb(es1, "lnw", [128, G * 512], BF16)
        row = sb(es1, "row", [1, 4 * 512])
        ps_sum = psum(es1, "ps_sum", [1, 512])
        ps_sq = psum(es1, "ps_sq", [1, 512])
        ps_rb = psum(es1, "ps_rb", [128, 512])
        ps_mrb = psum(es1, "ps_mrb", [128, 512])

        widths = [512] * (T // 512) + [1]
        col = th0
        for n, W in enumerate(widths):
            lcol = col - th0
            with nc.Block() as blk:
                @blk.sync
                def _(sync, col=col, W=W):
                    sync.wait_ge(ld.h, ld.v)
                    ld.inc(sync.dma_start(
                        out=xc[:, 0:G * W].rearrange("p (g t) -> p g t", t=W),
                        in_=xmid_d[:, col:col + W].rearrange(
                            "(g p) t -> p g t", p=128),
                        allow_slow_non_contiguous=(W == 1)), 16)
                    ld.wait(sync)

            with nc.Block() as blk:
                @blk.scalar
                def _(s, W=W):
                    for g in range(G):
                        s.activation(sqf[:, g * W:(g + 1) * W],
                                     xc[:, g * W:(g + 1) * W], AF.Square)
                    sA.inc(s.drain())

                @blk.tensor
                def _(tn, W=W):
                    for g in range(G):
                        tn.matmul(ps_sum[:, 0:W], onesc[:, 0:1],
                                  xc[:, g * W:(g + 1) * W],
                                  start=(g == 0), stop=(g == G - 1))
                    sA.wait(tn)
                    for g in range(G):
                        mm = tn.matmul(ps_sq[:, 0:W], onesc[:, 0:1],
                                       sqf[:, g * W:(g + 1) * W],
                                       start=(g == 0), stop=(g == G - 1))
                    sM.inc(mm)

                @blk.vector
                def _(v, W=W):
                    sM.wait(v)
                    mean, var = row[:, 0:W], row[:, 512:512 + W]
                    v.tensor_scalar_mul(mean, ps_sum[:, 0:W], 1.0 / C)
                    v.nop(cycle_cnt=16)
                    v.tensor_mul(var, mean, mean)
                    v.nop(cycle_cnt=16)
                    v.scalar_tensor_tensor(var, ps_sq[:, 0:W], 1.0 / C, var,
                                           ALU.mult, ALU.subtract)
                    sV.inc(v.drain())

                @blk.scalar
                def _(s, W=W):
                    sV.wait(s)
                    s.activation(row[:, 1024:1024 + W], row[:, 512:512 + W],
                                 AF.Sqrt, bias=cst_eps[0:1, 0:1])
                    sA.inc(s.drain())

                @blk.vector
                def _(v, W=W):
                    sA.wait(v)
                    v.reciprocal(row[:, 1024:1024 + W], row[:, 1024:1024 + W])
                    v.nop(cycle_cnt=32)
                    v.tensor_mul(row[:, 1536:1536 + W], row[:, 0:W],
                                 row[:, 1024:1024 + W])
                    sV.inc(v.drain())

                @blk.tensor
                def _(tn, W=W):
                    sV.wait(tn)
                    tn.matmul(ps_rb[:, 0:W], onesr[:, :], row[:, 1024:1024 + W],
                              start=True, stop=True)
                    sM.inc(tn.matmul(ps_mrb[:, 0:W], onesr[:, :],
                                     row[:, 1536:1536 + W], start=True,
                                     stop=True))

                @blk.vector
                def _(v, W=W):
                    sM.wait(v)

                    def mk(g):
                        xg = xc[:, g * W:(g + 1) * W]
                        tg = sqf[:, g * W:(g + 1) * W]
                        og = lnw[:, g * W:(g + 1) * W]
                        return [
                            lambda: v.tensor_mul(tg, xg, ps_rb[:, 0:W]),
                            lambda: v.tensor_sub(tg, tg, ps_mrb[:, 0:W]),
                            lambda: v.tensor_scalar(og, tg,
                                                    vt["ln2_s"][:, g:g + 1],
                                                    vt["ln2_b"][:, g:g + 1],
                                                    ALU.mult, ALU.add),
                        ]
                    for g in range(0, G, 2):
                        interleave(mk(g), mk(g + 1))
                    sV.inc(v.drain())

                @blk.sync
                def _(sync, W=W, lcol=lcol):
                    sV.wait(sync)
                    ld.inc(sync.dma_start(
                        out=ln2bf_d[:, lcol:lcol + W].rearrange(
                            "(g p) t -> p g t", p=128),
                        in_=lnw[:, 0:G * W].rearrange("p (g t) -> p g t", t=W),
                        allow_slow_non_contiguous=(W == 1)), 16)
            col += W

    if True:
        with nc.Block() as blk:
            @blk.sync
            def _(sync):
                ld.inc(sync.dma_start(
                    out=ln2bf_d[:, 0:1].rearrange("(g p) t -> p g t", p=128),
                    in_=zeros1[:, :].rearrange("p (g o) -> p g o", o=1),
                    allow_slow_non_contiguous=True), 16)

    with contextlib.ExitStack() as es2:
        fwk_b = sb(es2, "fwk_b", [128, G * 2 * C], BF16)
        fwv_t = [sb(es2, "fwv_t0", [128, 16 * 128], BF16),
                 sb(es2, "fwv_t1", [128, 16 * 128], BF16)]
        ln2c = sb(es2, "ln2c", [128, G * 513], BF16)
        xk2 = sb(es2, "xk2", [128, G * 512], BF16)
        xr2 = sb(es2, "xr2", [128, G * 512], BF16)
        sig = sb(es2, "sig", [128, G * 512])
        h2 = sb(es2, "h2", [128, 16 * 512], BF16)
        relu_t = sb(es2, "relu_t", [128, 2 * 512])
        cmo = sb(es2, "cmo", [128, G * 512])
        ps_h = [psum(es2, f"ps_h{i}", [128, 512]) for i in range(2)]
        ps_o = [psum(es2, f"ps_o{i}", [128, 512]) for i in range(2)]

        with nc.Block() as blk:
            @blk.gpsimd
            def _(g_):
                ld.inc(g_.dma_start(
                    out=fwk_b[:, :].rearrange("p (kc o) -> p kc o", o=2 * C),
                    in_=fWk_d.rearrange("(kc p) o -> p kc o", p=128)), 16)
                ld.wait(g_)

        for n in range(T // 512):
            with nc.Block() as blk:
                @blk.sync
                def _(sync, n=n):
                    sync.wait_ge(ld.h, ld.v)
                    ld.inc(sync.dma_start(
                        out=ln2c[:, :].rearrange("p (g t) -> p g t", t=513),
                        in_=ln2bf_d[:, n * 512:n * 512 + 513].rearrange(
                            "(g p) t -> p g t", p=128)), 16)
                    ld.wait(sync)

            sA_b, sM_b, sV_b = sA.v, sM.v, sV.v
            with nc.Block() as blk:
                @blk.vector
                def _(v):
                    def mk(g, dst, mixv):
                        cur = ln2c[:, g * 513 + 1:g * 513 + 513]
                        shf = ln2c[:, g * 513 + 0:g * 513 + 512]
                        og = dst[:, g * 512:(g + 1) * 512]
                        tg = relu_t[:, (g % 2) * 512:(g % 2) * 512 + 512]
                        return [
                            lambda: v.tensor_tensor(tg, cur, shf, ALU.subtract),
                            lambda: v.scalar_tensor_tensor(
                                og, tg, vt[mixv][:, g:g + 1], shf,
                                ALU.mult, ALU.add),
                        ]
                    for g in range(0, G, 2):
                        interleave(mk(g, xk2, "f_tmk"), mk(g + 1, xk2, "f_tmk"))
                    for g in range(0, G, 2):
                        interleave(mk(g, xr2, "f_tmr"), mk(g + 1, xr2, "f_tmr"))
                    sV.inc(v.drain())

                @blk.tensor
                def _(tn):
                    sV.wait(tn)
                    for m in range(G):
                        if m >= 2:
                            tn.wait_ge(sA.h, sA_b + m - 1)
                        for kc in range(G):
                            mm = tn.matmul(
                                ps_o[m % 2][:, :],
                                fwr_b[:, kc * C + m * 128:kc * C + m * 128 + 128],
                                xr2[:, kc * 512:(kc + 1) * 512],
                                start=(kc == 0), stop=(kc == G - 1))
                        sM.inc(mm)

                @blk.scalar
                def _(s):
                    for m in range(G):
                        s.wait_ge(sM.h, sM_b + m + 1)
                        s.activation(sig[:, m * 512:(m + 1) * 512],
                                     ps_o[m % 2][:, :], AF.Sigmoid)
                        sA.inc(s.drain())

            sA_b, sM_b = sA.v, sM.v
            with nc.Block() as blk:
                @blk.tensor
                def _(tn):
                    for hc in range(16):
                        if hc >= 2:
                            tn.wait_ge(sA.h, sA_b + 2 * (hc - 1))
                        for kc in range(G):
                            mm = tn.matmul(
                                ps_h[hc % 2][:, :],
                                fwk_b[:, kc * 2 * C + hc * 128:
                                      kc * 2 * C + hc * 128 + 128],
                                xk2[:, kc * 512:(kc + 1) * 512],
                                start=(kc == 0), stop=(kc == G - 1))
                        sM.inc(mm)

                @blk.scalar
                def _(s):
                    for hc in range(16):
                        s.wait_ge(sM.h, sM_b + hc + 1)
                        rt = relu_t[:, (hc % 2) * 512:(hc % 2) * 512 + 512]
                        s.activation(rt, ps_h[hc % 2][:, :], AF.Relu)
                        s.drain()
                        s.activation(h2[:, hc * 512:(hc + 1) * 512], rt,
                                     AF.Square)
                        sA.inc(s.drain(), 2)

            sM_b2, sV_b2, ld_b2 = sM.v, sV.v, ld.v
            with nc.Block() as blk:
                @blk.gpsimd
                def _(g_):
                    for m in range(G):
                        if m >= 2:
                            g_.wait_ge(sM.h, sM_b2 + m - 1)
                        ld.inc(g_.dma_start(
                            out=fwv_t[m % 2][:, :].rearrange(
                                "p (hc o) -> p hc o", o=128),
                            in_=fWv_d[:, m * 128:(m + 1) * 128].rearrange(
                                "(hc p) o -> p hc o", p=128)), 16)

                @blk.tensor
                def _(tn):
                    for m in range(G):
                        tn.wait_ge(ld.h, ld_b2 + 16 * (m + 1))
                        if m >= 2:
                            tn.wait_ge(sV.h, sV_b2 + m - 1)
                        for hc in range(16):
                            mm = tn.matmul(
                                ps_o[m % 2][:, :],
                                fwv_t[m % 2][:, hc * 128:(hc + 1) * 128],
                                h2[:, hc * 512:(hc + 1) * 512],
                                start=(hc == 0), stop=(hc == 15))
                        sM.inc(mm)

                @blk.vector
                def _(v):
                    for m in range(G):
                        v.wait_ge(sM.h, sM_b2 + m + 1)
                        v.tensor_mul(cmo[:, m * 512:(m + 1) * 512],
                                     sig[:, m * 512:(m + 1) * 512],
                                     ps_o[m % 2][:, :])
                        sV.inc(v.drain())

                @blk.sync
                def _(sync, n=n):
                    sV.wait(sync)
                    ld.inc(sync.dma_start(
                        out=cmh_d[:, n * 512:(n + 1) * 512].rearrange(
                            "(g p) t -> p g t", p=128),
                        in_=cmo[:, :].rearrange("p (g t) -> p g t", t=512)), 16)

    with nc.Block() as blk:
        @blk.sync
        def _(sync):
            ld.wait(sync)
    es.close()





def emit_s5(nc, D, S):
    es = contextlib.ExitStack()
    th0 = 0

    xT_d, xn0_d, xn0bf_d = D["xT"], D["xn0"], D["xn0bf"]
    sout_d, scores_d = D["sout"], D["scores"]
    W1_d, W2_d, b1_d, b2_d = D["W1"], D["W2"], D["b1"], D["b2"]
    idn_d = D["idn"]
    x1h_d = D["x1"]

    def sb(stack, name, shape, dtype=F32):
        return stack.enter_context(nc.sbuf_tensor("s5_" + name, shape, dtype))

    def psum(stack, name, shape, dtype=F32):
        return stack.enter_context(nc.psum_tensor("s5_" + name, shape, dtype))

    b1t = sb(es, "b1t", [128, 32])
    b2t = sb(es, "b2t", [128, G])
    onesc = sb(es, "onesc", [128, 1])
    onesr = sb(es, "onesr", [1, 128])
    idf = sb(es, "idf", [128, 128])
    scp = sb(es, "scp", [128, 64])
    msk64 = sb(es, "msk64", [128, 64])
    bis = sb(es, "bis", [128, 8])
    rowx = sb(es, "rowx", [1, 132])
    w1b = sb(es, "w1b", [128, G * 4 * C], BF16)
    w2t = [sb(es, "w2t0", [128, 32 * 128], BF16),
           sb(es, "w2t1", [128, 32 * 128], BF16)]

    ld, sA, sM, sV = S

    with nc.Block() as blk:
        @blk.sync
        def _(sync):
            ld.inc(sync.dma_start(out=b1t[:, :], in_=b1_d), 16)
            ld.inc(sync.dma_start(out=b2t[:, :], in_=b2_d), 16)
            ld.inc(sync.dma_start(
                out=scp[:, :],
                in_=scores_d.rearrange("o (p f) -> p (o f)", p=128)), 16)
            ld.wait(sync)

        @blk.gpsimd
        def _(g_):
            ld.inc(g_.dma_start(
                out=w1b[:, :].rearrange("p (kc o) -> p kc o", o=4 * C),
                in_=W1_d.rearrange("(kc p) o -> p kc o", p=128)), 16)

    with nc.Block() as blk:
        @blk.vector
        def _(v):
            v.memset(onesc[:, :], 1.0)
            v.memset(onesr[:, :], 1.0)

        @blk.sync
        def _(sync):
            ld.inc(sync.dma_start(out=idf[:, :], in_=idn_d), 16)
            ld.wait(sync)

    ps_t = psum(es, "ps_t", [1, 128])
    ps_c = psum(es, "ps_c", [1, 128])
    ps_bc = psum(es, "ps_bc", [128, 64])

    with nc.Block() as blk:
        @blk.vector
        def _(v):
            v.reduce_max(bis[:, 5:6], scp[:, :], axis=AX)
            v.nop(cycle_cnt=32)
            v.tensor_reduce(bis[:, 6:7], scp[:, :], axis=AX, op=ALU.min)
            sV.inc(v.drain())

        @blk.tensor
        def _(tn):
            sV.wait(tn)
            tn.transpose(ps_t[:, :], bis[:, 5:6], idf[:, :])
            sM.inc(tn.transpose(ps_c[:, :], bis[:, 6:7], idf[:, :]))

        @blk.vector
        def _(v):
            sM.wait(v)
            v.reduce_max(rowx[0:1, 0:1], ps_t[0:1, :], axis=AX)
            v.tensor_reduce(rowx[0:1, 1:2], ps_c[0:1, :], axis=AX, op=ALU.min)
            v.nop(cycle_cnt=32)
            v.tensor_scalar(rowx[0:1, 0:1], rowx[0:1, 0:1], 1.0, None, ALU.add)
            v.tensor_scalar(rowx[0:1, 1:2], rowx[0:1, 1:2], -1.0, None, ALU.add)
            sV.inc(v.drain())

        @blk.tensor
        def _(tn):
            sV.wait(tn)
            sM.inc(tn.matmul(ps_bc[:, 0:2], onesr[:, :], rowx[0:1, 0:2],
                             start=True, stop=True))

        @blk.vector
        def _(v):
            sM.wait(v)
            v.tensor_copy(bis[:, 1:2], ps_bc[:, 0:1])
            v.tensor_copy(bis[:, 0:1], ps_bc[:, 1:2])
            v.nop(cycle_cnt=32)
            v.tensor_add(bis[:, 2:3], bis[:, 0:1], bis[:, 1:2])
            v.nop(cycle_cnt=32)
            v.tensor_scalar_mul(bis[:, 2:3], bis[:, 2:3], 0.5)

    for it in range(44):
        with nc.Block() as blk:
            @blk.vector
            def _(v):
                v.tensor_scalar(msk64[:, :], scp[:, :], bis[:, 2:3], None,
                                ALU.is_ge)
                v.nop(cycle_cnt=32)
                v.reduce_sum(bis[:, 4:5], msk64[:, :], axis=AX)
                sV.inc(v.drain())

            @blk.tensor
            def _(tn):
                sV.wait(tn)
                sM.inc(tn.matmul(ps_t[0:1, 0:1], onesc[:, 0:1], bis[:, 4:5],
                                 start=True, stop=True))

            @blk.scalar
            def _(s):
                sM.wait(s)
                s.copy(rowx[0:1, 2:3], ps_t[0:1, 0:1])
                sA.inc(s.drain())

            @blk.tensor
            def _(tn):
                sA.wait(tn)
                sM.inc(tn.matmul(ps_bc[:, 0:1], onesr[:, :], rowx[0:1, 2:3],
                                 start=True, stop=True))

            @blk.vector
            def _(v):
                sM.wait(v)
                lo, hi, th, sel = (bis[:, 0:1], bis[:, 1:2], bis[:, 2:3],
                                   bis[:, 3:4])
                t1, t2 = bis[:, 5:6], bis[:, 6:7]
                v.tensor_scalar(sel, ps_bc[:, 0:1], float(TH), None, ALU.is_ge)
                v.tensor_sub(t1, th, lo)
                v.tensor_sub(t2, th, hi)
                v.nop(cycle_cnt=32)
                v.tensor_mul(t1, t1, sel)
                v.tensor_scalar(bis[:, 7:8], sel, -1.0, 1.0, ALU.mult, ALU.add)
                v.nop(cycle_cnt=32)
                v.tensor_add(lo, lo, t1)
                v.tensor_mul(t2, t2, bis[:, 7:8])
                v.nop(cycle_cnt=32)
                v.tensor_add(hi, hi, t2)
                v.nop(cycle_cnt=32)
                v.tensor_add(th, lo, hi)
                v.nop(cycle_cnt=32)
                v.tensor_scalar_mul(th, th, 0.5)

    with nc.Block() as blk:
        @blk.vector
        def _(v):
            v.tensor_copy(msk64[:, 0:1], bis[:, 0:1])

    with contextlib.ExitStack() as es2:
        xnbfc = sb(es2, "xnbfc", [128, G * 512], BF16)
        xnc = sb(es2, "xnc", [128, G * 512])
        soc = sb(es2, "soc", [128, G * 512])
        h2 = sb(es2, "h2", [128, 32 * 512], BF16)
        x1c = sb(es2, "x1c", [128, G * 512])
        srow = sb(es2, "srow", [1, 3 * 512])
        gm = sb(es2, "gm", [128, G * 512])
        ps_h = [psum(es2, f"ps_h{i}", [128, 512]) for i in range(2)]
        ps_o = [psum(es2, f"ps_o{i}", [128, 512]) for i in range(2)]
        ps_w = psum(es2, "ps_w", [128, 512])

        for n in range(T // 512):
            c0 = 1 + th0 + n * 512
            with nc.Block() as blk:
                @blk.sync
                def _(sync, n=n, c0=c0):
                    sync.wait_ge(ld.h, ld.v)
                    ld.inc(sync.dma_start(
                        out=xnbfc[:, :].rearrange("p (g t) -> p g t", t=512),
                        in_=xn0bf_d[:, c0:c0 + 512].rearrange(
                            "(g p) t -> p g t", p=128)), 16)
                    ld.inc(sync.dma_start(
                        out=xnc[:, :].rearrange("p (g t) -> p g t", t=512),
                        in_=xn0_d[:, c0:c0 + 512].rearrange(
                            "(g p) t -> p g t", p=128)), 16)
                    ld.inc(sync.dma_start(
                        out=x1c[:, :].rearrange("p (g t) -> p g t", t=512),
                        in_=xT_d[:, c0:c0 + 512].rearrange(
                            "(g p) t -> p g t", p=128)), 16)
                    ld.inc(sync.dma_start(
                        out=soc[:, :].rearrange("p (g t) -> p g t", t=512),
                        in_=sout_d[:, c0:c0 + 512].rearrange(
                            "(g p) t -> p g t", p=128)), 16)
                    ld.inc(sync.dma_start(
                        out=srow[0:1, 0:512],
                        in_=scores_d[:, th0 + n * 512:th0 + (n + 1) * 512]), 16)
                    ld.wait(sync)

            sA_b0, sV_b0, sM_b0 = sA.v, sV.v, sM.v
            with nc.Block() as blk:
                @blk.scalar
                def _(s):
                    s.activation(srow[0:1, 512:1024], srow[0:1, 0:512],
                                 AF.Sigmoid)
                    sA.inc(s.drain())

                @blk.vector
                def _(v):
                    sA.wait(v)
                    v.tensor_scalar(srow[0:1, 1024:1536], srow[0:1, 0:512],
                                    bis[0:1, 0:1], None, ALU.is_ge)
                    v.nop(cycle_cnt=32)
                    v.tensor_mul(srow[0:1, 1024:1536], srow[0:1, 1024:1536],
                                 srow[0:1, 512:1024])
                    sV.inc(v.drain())

                @blk.tensor
                def _(tn):
                    sV.wait(tn)
                    sM.inc(tn.matmul(ps_w[:, :], onesr[:, :],
                                     srow[0:1, 1024:1536], start=True,
                                     stop=True))

            sA_b, sM_b = sA.v, sM.v
            with nc.Block() as blk:
                @blk.tensor
                def _(tn):
                    for hc in range(32):
                        if hc >= 2:
                            tn.wait_ge(sA.h, sA_b + hc - 1)
                        for kc in range(G):
                            mm = tn.matmul(
                                ps_h[hc % 2][:, :],
                                w1b[:, kc * 4 * C + hc * 128:
                                    kc * 4 * C + hc * 128 + 128],
                                xnbfc[:, kc * 512:(kc + 1) * 512],
                                start=(kc == 0), stop=(kc == G - 1))
                        sM.inc(mm)

                @blk.scalar
                def _(s):
                    for hc in range(32):
                        s.wait_ge(sM.h, sM_b + hc + 1)
                        s.activation(h2[:, hc * 512:(hc + 1) * 512],
                                     ps_h[hc % 2][:, :], AF.Gelu_apprx_tanh,
                                     bias=b1t[:, hc:hc + 1])
                        sA.inc(s.drain())

            sM_b2, sV_b2, sA_b2, ld_b2 = sM.v, sV.v, sA.v, ld.v
            with nc.Block() as blk:
                @blk.gpsimd
                def _(g_):
                    for m in range(G):
                        if m >= 2:
                            g_.wait_ge(sM.h, sM_b2 + m - 1)
                        ld.inc(g_.dma_start(
                            out=w2t[m % 2][:, :].rearrange(
                                "p (hc o) -> p hc o", o=128),
                            in_=W2_d[:, m * 128:(m + 1) * 128].rearrange(
                                "(hc p) o -> p hc o", p=128)), 16)

                @blk.tensor
                def _(tn):
                    for m in range(G):
                        tn.wait_ge(ld.h, ld_b2 + 16 * (m + 1))
                        if m >= 2:
                            tn.wait_ge(sV.h, sV_b2 + m - 1)
                        for hc in range(32):
                            mm = tn.matmul(
                                ps_o[m % 2][:, :],
                                w2t[m % 2][:, hc * 128:(hc + 1) * 128],
                                h2[:, hc * 512:(hc + 1) * 512],
                                start=(hc == 0), stop=(hc == 31))
                        sM.inc(mm)

                @blk.vector
                def _(v):
                    for m in range(G):
                        v.wait_ge(sM.h, sM_b2 + m + 1)
                        sl = slice(m * 512, (m + 1) * 512)
                        par = m % 2
                        gsl = gm[:, sl]
                        v.tensor_scalar(gsl, ps_o[par][:, :],
                                        b2t[:, m:m + 1], None, ALU.add)
                        v.nop(cycle_cnt=32)
                        v.tensor_mul(gsl, gsl, ps_w[:, :])
                        v.nop(cycle_cnt=32)
                        v.tensor_add(gsl, gsl, xnc[:, sl])
                        sV.inc(v.drain())

                @blk.scalar
                def _(s):
                    for m in range(G):
                        s.wait_ge(sV.h, sV_b2 + m + 1)
                        sl = slice(m * 512, (m + 1) * 512)
                        s.activation(gm[:, sl], gm[:, sl], AF.Silu)
                        sA.inc(s.drain())

                @blk.vector
                def _(v):
                    for m in range(G):
                        v.wait_ge(sA.h, sA_b2 + m + 1)
                        sl = slice(m * 512, (m + 1) * 512)
                        v.tensor_mul(soc[:, sl], soc[:, sl], gm[:, sl])
                        v.nop(cycle_cnt=32)
                        v.tensor_add(x1c[:, sl], x1c[:, sl], soc[:, sl])
                        sV.inc(v.drain())

                @blk.sync
                def _(sync, n=n):
                    sV.wait(sync)
                    ld.inc(sync.dma_start(
                        out=x1h_d[:, n * 512:(n + 1) * 512].rearrange(
                            "(g p) t -> p g t", p=128),
                        in_=x1c[:, :].rearrange("p (g t) -> p g t", t=512)), 16)

    with nc.Block() as blk:
        @blk.sync
        def _(sync):
            ld.wait(sync)
    es.close()





def emit_s6(nc, D, S):
    es = contextlib.ExitStack()

    x1_d = D["x1"]
    wq_d, wk_d, wv_d = D["wq"], D["wk"], D["wv"]
    wr_d, br_d = D["wr"], D["br"]
    cos_d, sin_d, ps2_d, idn_d = D["cosb"], D["sinb"], D["ps2"], D["idn"]
    vecs = {nm: D[nm] for nm in ["ln_g", "ln_b"]}

    xn1bf_d = D["xn1bf"]
    qr_d, kr_d, sq_d, sk_d = D["qrh"], D["krh"], D["sqh"], D["skh"]
    sktm_d, vtm_d = D["sktm"], D["vtm"]
    sgate_d, slot_d = D["sgate"], D["slotint"]
    iota8_d, iotac_d, ut128_d = D["iota8"], D["iotac"], D["ut128"]
    trasht_d = D["trasht"]

    def sb(stack, name, shape, dtype=F32):
        return stack.enter_context(nc.sbuf_tensor("s6_" + name, shape, dtype))

    def psum(stack, name, shape, dtype=F32):
        return stack.enter_context(nc.psum_tensor("s6_" + name, shape, dtype))

    vt = {nm: sb(es, "v_" + nm, [128, G]) for nm in vecs}
    onesc = sb(es, "onesc", [128, 1])
    onesr = sb(es, "onesr", [1, 128])
    cst_eps = sb(es, "cst_eps", [128, 1])
    idf = sb(es, "idf", [128, 128])
    ps2b = sb(es, "ps2b", [128, 128], BF16)
    wqb = sb(es, "wqb", [128, G * 512], BF16)
    wkb = sb(es, "wkb", [128, G * 512], BF16)
    wvb = sb(es, "wvb", [128, G * 512], BF16)
    wrt = sb(es, "wrt", [128, G * 8])
    brt = sb(es, "brt", [1, 8])
    iota8 = sb(es, "iota8", [128, 8])
    iotac = sb(es, "iotac", [128, 8])
    ut128 = sb(es, "ut128", [128, 128])
    trasht = sb(es, "trasht", [128, 1])
    cnt8 = sb(es, "cnt8", [1, 8])
    islot6 = sb(es, "islot6", [128, 64], I32)
    rank6 = sb(es, "rank6", [128, 64])
    e16 = sb(es, "e16", [128, 64])
    offrow = sb(es, "offrow", [1, 8])
    lt8t = sb(es, "lt8t", [8, 8])
    r8 = sb(es, "r8", [1, 32])
    i8a = sb(es, "i8a", [1, 8], I32)
    t88 = sb(es, "t88", [8, 8])
    off8c = sb(es, "off8c", [8, 1])

    ld, sA, sM, sV = S

    with nc.Block() as blk:
        @blk.sync
        def _(sync):
            for nm in vecs:
                ld.inc(sync.dma_start(out=vt[nm][:, :], in_=vecs[nm]), 16)
            ld.inc(sync.dma_start(out=idf[:, :], in_=idn_d), 16)
            ld.inc(sync.dma_start(
                out=wrt[:, :].rearrange("p (kc o) -> p kc o", o=8),
                in_=wr_d.rearrange("(kc p) o -> p kc o", p=128)), 16)
            ld.inc(sync.dma_start(out=brt[:, :], in_=br_d), 16)
            ld.inc(sync.dma_start(out=iota8[:, :], in_=iota8_d), 16)
            ld.inc(sync.dma_start(out=iotac[:, :], in_=iotac_d), 16)
            ld.inc(sync.dma_start(out=ut128[:, :], in_=ut128_d), 16)
            ld.inc(sync.dma_start(out=trasht[:, :], in_=trasht_d), 16)
            ld.inc(sync.dma_start(out=lt8t[:, :], in_=D["lt8"]), 16)
            ld.wait(sync)

        @blk.gpsimd
        def _(g_):
            ld.inc(g_.dma_start(out=ps2b[:, :], in_=ps2_d), 16)
            for wd, wb in [(wq_d, wqb), (wk_d, wkb), (wv_d, wvb)]:
                ld.inc(g_.dma_start(
                    out=wb[:, :].rearrange("p (kc o) -> p kc o", o=512),
                    in_=wd.rearrange("(kc p) o -> p kc o", p=128)), 16)

    with nc.Block() as blk:
        @blk.vector
        def _(v):
            v.memset(onesc[:, :], 1.0)
            v.memset(onesr[:, :], 1.0)
            v.memset(cst_eps[:, :], EPS)
            v.memset(cnt8[:, :], 0.0)

    with contextlib.ExitStack() as e1:
        xc = sb(e1, "xc", [128, G * 512])
        sqf = sb(e1, "sqf", [128, G * 512])
        xnc = sb(e1, "xnc", [128, G * 512])
        xnbfc = sb(e1, "xnbfc", [128, G * 512], BF16)
        lac = sb(e1, "lac", [128, G * 512], BF16)
        row = sb(e1, "row", [1, 4 * 512])
        cosc = sb(e1, "cosc", [128, 512])
        sinc = sb(e1, "sinc", [128, 512])
        qb = sb(e1, "qb", [128, 4 * 512], BF16)
        kb = sb(e1, "kb", [128, 4 * 512], BF16)
        ob = sb(e1, "ob", [128, 4 * 512], BF16)
        ob2 = sb(e1, "ob2", [128, 4 * 512], BF16)
        tt8 = sb(e1, "tt8", [128, 8 * 512])
        lg = sb(e1, "lg", [128, 48])
        sg8 = sb(e1, "sg8", [8, 128])
        ps_sum = psum(e1, "ps_sum", [1, 512])
        ps_sq = psum(e1, "ps_sq", [1, 512])
        ps_rb = psum(e1, "ps_rb", [128, 512])
        ps_mrb = psum(e1, "ps_mrb", [128, 512])
        ps_q = [psum(e1, f"ps_q{i}", [128, 512]) for i in range(2)]
        ps_g = psum(e1, "ps_g", [128, 8])
        ps_gt = psum(e1, "ps_gt", [8, 128])

        for n in range(NC512):
            c0 = n * 512
            sV_chunk_top = sV.v
            with nc.Block() as blk:
                @blk.sync
                def _(sync, c0=c0):
                    sync.wait_ge(ld.h, ld.v)
                    ld.inc(sync.dma_start(
                        out=xc[:, :].rearrange("p (g t) -> p g t", t=512),
                        in_=x1_d[:, c0:c0 + 512].rearrange(
                            "(g p) t -> p g t", p=128)), 16)
                    ld.inc(sync.dma_start(out=cosc[:, :],
                                          in_=cos_d[:, c0:c0 + 512]), 16)
                    ld.inc(sync.dma_start(out=sinc[:, :],
                                          in_=sin_d[:, c0:c0 + 512]), 16)
                    ld.wait(sync)

            with nc.Block() as blk:
                @blk.scalar
                def _(s):
                    for g in range(G):
                        s.activation(sqf[:, g * 512:(g + 1) * 512],
                                     xc[:, g * 512:(g + 1) * 512], AF.Square)
                    sA.inc(s.drain())

                @blk.tensor
                def _(tn, sV_chunk_top=sV_chunk_top):
                    tn.wait_ge(sV.h, sV_chunk_top)
                    for g in range(G):
                        tn.matmul(ps_sum[:, :], onesc[:, 0:1],
                                  xc[:, g * 512:(g + 1) * 512],
                                  start=(g == 0), stop=(g == G - 1))
                    sA.wait(tn)
                    for g in range(G):
                        mm = tn.matmul(ps_sq[:, :], onesc[:, 0:1],
                                       sqf[:, g * 512:(g + 1) * 512],
                                       start=(g == 0), stop=(g == G - 1))
                    sM.inc(mm)

                @blk.vector
                def _(v):
                    sM.wait(v)
                    mean, var = row[:, 0:512], row[:, 512:1024]
                    v.tensor_scalar_mul(mean, ps_sum[:, :], 1.0 / C)
                    v.nop(cycle_cnt=16)
                    v.tensor_mul(var, mean, mean)
                    v.nop(cycle_cnt=16)
                    v.scalar_tensor_tensor(var, ps_sq[:, :], 1.0 / C, var,
                                           ALU.mult, ALU.subtract)
                    sV.inc(v.drain())

                @blk.scalar
                def _(s):
                    sV.wait(s)
                    s.activation(row[:, 1024:1536], row[:, 512:1024], AF.Sqrt,
                                 bias=cst_eps[0:1, 0:1])
                    sA.inc(s.drain())

                @blk.vector
                def _(v):
                    sA.wait(v)
                    v.reciprocal(row[:, 1024:1536], row[:, 1024:1536])
                    v.nop(cycle_cnt=32)
                    v.tensor_mul(row[:, 1536:2048], row[:, 0:512],
                                 row[:, 1024:1536])
                    sV.inc(v.drain())

                @blk.tensor
                def _(tn):
                    sV.wait(tn)
                    tn.matmul(ps_rb[:, :], onesr[:, :], row[:, 1024:1536],
                              start=True, stop=True)
                    sM.inc(tn.matmul(ps_mrb[:, :], onesr[:, :],
                                     row[:, 1536:2048], start=True, stop=True))

                @blk.vector
                def _(v):
                    sM.wait(v)

                    def mk(g):
                        xg = xc[:, g * 512:(g + 1) * 512]
                        og = xnc[:, g * 512:(g + 1) * 512]
                        bg = xnbfc[:, g * 512:(g + 1) * 512]
                        ag = lac[:, g * 512:(g + 1) * 512]
                        return [
                            lambda: v.tensor_mul(og, xg, ps_rb[:, :]),
                            lambda: v.tensor_sub(og, og, ps_mrb[:, :]),
                            lambda: v.tensor_copy(bg, og),
                            lambda: v.tensor_scalar(ag, og,
                                                    vt["ln_g"][:, g:g + 1],
                                                    vt["ln_b"][:, g:g + 1],
                                                    ALU.mult, ALU.add),
                        ]
                    for g in range(0, G, 2):
                        interleave(mk(g), mk(g + 1))
                    sV.inc(v.drain())

                @blk.sync
                def _(sync, c0=c0):
                    sV.wait(sync)
                    ld.inc(sync.dma_start(
                        out=xn1bf_d[:, c0:c0 + 512].rearrange(
                            "(g p) t -> p g t", p=128),
                        in_=xnbfc[:, :].rearrange("p (g t) -> p g t", t=512)),
                        16)

            sA_b, sM_b = sA.v, sM.v
            with nc.Block() as blk:
                @blk.tensor
                def _(tn):
                    for idx in range(8):
                        wb = wqb if idx < 4 else wkb
                        m = idx % 4
                        if idx >= 2:
                            tn.wait_ge(sA.h, sA_b + idx - 1)
                        for kc in range(G):
                            mm = tn.matmul(
                                ps_q[idx % 2][:, :],
                                wb[:, kc * 512 + m * 128:
                                   kc * 512 + m * 128 + 128],
                                lac[:, kc * 512:(kc + 1) * 512],
                                start=(kc == 0), stop=(kc == G - 1))
                        sM.inc(mm)

                @blk.scalar
                def _(s):
                    for idx in range(8):
                        s.wait_ge(sM.h, sM_b + idx + 1)
                        dst = qb if idx < 4 else kb
                        m = idx % 4
                        s.copy(dst[:, m * 512:(m + 1) * 512],
                               ps_q[idx % 2][:, :])
                        sA.inc(s.drain())

            sM_b, sV_b = sM.v, sV.v
            with nc.Block() as blk:
                @blk.tensor
                def _(tn):
                    for idx in range(8):
                        src = qb if idx < 4 else kb
                        m = idx % 4
                        if idx >= 2:
                            tn.wait_ge(sV.h, sV_b + idx - 1)
                        sM.inc(tn.matmul(ps_q[idx % 2][:, :], ps2b[:, :],
                                         src[:, m * 512:(m + 1) * 512],
                                         start=True, stop=True))

                @blk.vector
                def _(v):
                    for idx in range(8):
                        src = qb if idx < 4 else kb
                        dst = ob if idx < 4 else ob2
                        m = idx % 4
                        ssl = slice(m * 512, (m + 1) * 512)
                        t0 = tt8[:, (idx % 2) * 512:(idx % 2) * 512 + 512]
                        t1 = tt8[:, (2 + idx % 2) * 512:(2 + idx % 2) * 512 + 512]
                        v.wait_ge(sM.h, sM_b + idx + 1)
                        v.tensor_mul(t0, ps_q[idx % 2][:, :], sinc[:, :])
                        v.tensor_mul(t1, src[:, ssl], cosc[:, :])
                        v.nop(cycle_cnt=32)
                        v.tensor_add(dst[:, ssl], t0, t1)
                        sV.inc(v.drain())

                @blk.sync
                def _(sync, c0=c0):
                    for idx in range(8):
                        sync.wait_ge(sV.h, sV_b + idx + 1)
                        dd = qr_d if idx < 4 else kr_d
                        src = ob if idx < 4 else ob2
                        m = idx % 4
                        ld.inc(sync.dma_start(
                            out=dd[m * 128:(m + 1) * 128, c0:c0 + 512],
                            in_=src[:, m * 512:(m + 1) * 512]), 16)

            sA_b, sV_b, ld_b = sA.v, sV.v, ld.v
            with nc.Block() as blk:
                @blk.vector
                def _(v):
                    for idx in range(8):
                        src = qb if idx < 4 else kb
                        m = idx % 4
                        ssl = slice(m * 512, (m + 1) * 512)
                        tt = tt8[:, idx * 512:(idx + 1) * 512]
                        v.tensor_scalar_min(tt, src[:, ssl], 0.0)
                        sV.inc(v.drain())

                @blk.scalar
                def _(s):
                    for idx in range(8):
                        s.wait_ge(sV.h, sV_b + idx + 1)
                        tt = tt8[:, idx * 512:(idx + 1) * 512]
                        s.activation(tt, tt, AF.Exp)
                        sA.inc(s.drain())

                @blk.vector
                def _(v):
                    for idx in range(8):
                        v.wait_ge(sA.h, sA_b + idx + 1)
                        src = qb if idx < 4 else kb
                        dst = ob if idx < 4 else ob2
                        m = idx % 4
                        ssl = slice(m * 512, (m + 1) * 512)
                        tt = tt8[:, idx * 512:(idx + 1) * 512]
                        v.scalar_tensor_tensor(dst[:, ssl], src[:, ssl], 1.0,
                                               tt, ALU.add, ALU.max)
                        sV.inc(v.drain())

                @blk.sync
                def _(sync, c0=c0):
                    for idx in range(8):
                        sync.wait_ge(sV.h, sV_b + 8 + idx + 1)
                        dd = sq_d if idx < 4 else sk_d
                        src = ob if idx < 4 else ob2
                        m = idx % 4
                        ld.inc(sync.dma_start(
                            out=dd[m * 128:(m + 1) * 128, c0:c0 + 512],
                            in_=src[:, m * 512:(m + 1) * 512]), 16)

            sA_b, sM_b, sV_b = sA.v, sM.v, sV.v
            with nc.Block() as blk:
                @blk.tensor
                def _(tn):
                    for idx in range(8):
                        sl4 = idx // 2
                        wb = wvb if idx % 2 == 0 else wkb
                        if idx >= 2:
                            tn.wait_ge(sA.h, sA_b + idx - 1)
                        for kc in range(G):
                            mm = tn.matmul(
                                ps_q[idx % 2][:, :],
                                lac[:, kc * 512 + sl4 * 128:
                                    kc * 512 + sl4 * 128 + 128],
                                wb[:, kc * 512:(kc + 1) * 512],
                                start=(kc == 0), stop=(kc == G - 1))
                        sM.inc(mm)

                @blk.scalar
                def _(s):
                    for idx in range(8):
                        s.wait_ge(sM.h, sM_b + idx + 1)
                        sl4 = idx // 2
                        if idx % 2 == 0:
                            s.copy(ob[:, sl4 * 512:(sl4 + 1) * 512],
                                   ps_q[idx % 2][:, :])
                        else:
                            s.copy(tt8[:, sl4 * 512:(sl4 + 1) * 512],
                                   ps_q[idx % 2][:, :])
                        sA.inc(s.drain())

                @blk.vector
                def _(v):
                    for sl4 in range(4):
                        v.wait_ge(sA.h, sA_b + 2 * sl4 + 2)
                        tt = tt8[:, sl4 * 512:(sl4 + 1) * 512]
                        mn = tt8[:, (4 + sl4) * 512:(5 + sl4) * 512]
                        v.tensor_scalar_min(mn, tt, 0.0)
                        sV.inc(v.drain())

                @blk.scalar
                def _(s):
                    for sl4 in range(4):
                        s.wait_ge(sV.h, sV_b + sl4 + 1)
                        mn = tt8[:, (4 + sl4) * 512:(5 + sl4) * 512]
                        s.activation(mn, mn, AF.Exp)
                        sA.inc(s.drain())

                @blk.vector
                def _(v):
                    for sl4 in range(4):
                        v.wait_ge(sA.h, sA_b + 8 + sl4 + 1)
                        tt = tt8[:, sl4 * 512:(sl4 + 1) * 512]
                        mn = tt8[:, (4 + sl4) * 512:(5 + sl4) * 512]
                        v.scalar_tensor_tensor(
                            ob2[:, sl4 * 512:(sl4 + 1) * 512], tt, 1.0, mn,
                            ALU.add, ALU.max)
                        sV.inc(v.drain())

                @blk.sync
                def _(sync, c0=c0):
                    for sl4 in range(4):
                        sync.wait_ge(sA.h, sA_b + 2 * sl4 + 1)
                        ld.inc(sync.dma_start(
                            out=vtm_d[c0 + sl4 * 128:c0 + (sl4 + 1) * 128, :],
                            in_=ob[:, sl4 * 512:(sl4 + 1) * 512]), 16)
                    for sl4 in range(4):
                        sync.wait_ge(sV.h, sV_b + 4 + sl4 + 1)
                        ld.inc(sync.dma_start(
                            out=sktm_d[c0 + sl4 * 128:c0 + (sl4 + 1) * 128, :],
                            in_=ob2[:, sl4 * 512:(sl4 + 1) * 512]), 16)

            for sl4 in range(4):
                gcol = c0 + sl4 * 128
                gidx = n * 4 + sl4
                rM0, rV0, rA0, rL0 = sM.v, sV.v, sA.v, ld.v
                with nc.Block() as blk:
                    @blk.tensor
                    def _(tn, sl4=sl4):
                        for kc in range(G):
                            tn.matmul(ps_g[:, :],
                                      xnc[:, kc * 512 + sl4 * 128:
                                          kc * 512 + sl4 * 128 + 128],
                                      wrt[:, kc * 8:(kc + 1) * 8],
                                      start=(kc == 0), stop=False)
                        sM.inc(tn.matmul(ps_g[:, :], onesr[:, :],
                                         brt[:, :], start=False,
                                         stop=True))

                    @blk.vector
                    def _(v, rM0=rM0):
                        v.wait_ge(sM.h, rM0 + 1)
                        v.tensor_copy(lg[:, 0:8], ps_g[:, :])
                        v.nop(cycle_cnt=32)
                        v.max(out=lg[:, 8:16], in_=lg[:, 0:8])
                        v.drain()
                        v.tensor_sub(lg[:, 16:17], lg[:, 8:9], lg[:, 9:10])
                        sV.inc(v.drain())

                    @blk.scalar
                    def _(s, rV0=rV0):
                        s.wait_ge(sV.h, rV0 + 1)
                        s.activation(lg[:, 17:18], lg[:, 16:17], AF.Sigmoid)
                        sA.inc(s.drain())

                    @blk.vector
                    def _(v, rA0=rA0):
                        v.wait_ge(sA.h, rA0 + 1)
                        v.tensor_scalar(lg[:, 20:28], lg[:, 0:8],
                                        lg[:, 8:9], None, ALU.is_equal)
                        v.nop(cycle_cnt=32)
                        v.tensor_mul(lg[:, 28:36], lg[:, 20:28], iota8[:, :])
                        v.tensor_scalar(lg[:, 36:44], lg[:, 20:28], -9.0, 9.0,
                                        ALU.mult, ALU.add)
                        v.nop(cycle_cnt=32)
                        v.tensor_add(lg[:, 28:36], lg[:, 28:36], lg[:, 36:44])
                        v.nop(cycle_cnt=32)
                        v.tensor_reduce(lg[:, 18:19], lg[:, 28:36], axis=AX,
                                        op=ALU.min)
                        v.nop(cycle_cnt=32)
                        v.tensor_scalar(lg[:, 20:28], iota8[:, :],
                                        lg[:, 18:19], None, ALU.is_equal)
                        v.nop(cycle_cnt=32)
                        v.tensor_scalar(lg[:, 28:36], lg[:, 20:28],
                                        lg[:, 17:18], None, ALU.mult)
                        sV.inc(v.drain())

                    @blk.tensor
                    def _(tn, rV0=rV0, rA0=rA0):
                        tn.wait_ge(sV.h, rV0 + 2)
                        tn.wait_ge(sA.h, rA0)
                        tn.matmul(ps_g[:, :], ut128[:, :], lg[:, 20:28],
                                  start=True, stop=False)
                        tn.matmul(ps_g[:, :], onesr[:, :], cnt8[:, :],
                                  start=False, stop=True)
                        sM.inc(tn.matmul(ps_sum[0:1, 0:8], onesc[:, 0:1],
                                         lg[:, 20:28], start=True, stop=True))
                        sM.inc(tn.transpose(ps_gt[:, :], lg[:, 28:36],
                                            idf[:, :]))

                    @blk.vector
                    def _(v, gidx=gidx, rM0=rM0):
                        v.wait_ge(sM.h, rM0 + 2)
                        v.tensor_tensor(lg[:, 36:44], ps_g[:, :],
                                        lg[:, 20:28], ALU.mult)
                        v.tensor_tensor(cnt8[:, :], cnt8[:, :],
                                        ps_sum[0:1, 0:8], ALU.add)
                        v.nop(cycle_cnt=32)
                        v.reduce_sum(rank6[:, gidx:gidx + 1], lg[:, 36:44],
                                     axis=AX)
                        v.tensor_copy(e16[:, gidx:gidx + 1], lg[:, 18:19])
                        sV.inc(v.drain())

                    @blk.scalar
                    def _(s, rM0=rM0, rL0=rL0):
                        s.wait_ge(sM.h, rM0 + 3)
                        s.wait_ge(ld.h, rL0)
                        s.copy(sg8[:, :], ps_gt[:, :])
                        sA.inc(s.drain())

                    @blk.sync
                    def _(sync, gcol=gcol, rA0=rA0):
                        sync.wait_ge(sA.h, rA0 + 2)
                        ld.inc(sync.dma_start(
                            out=sgate_d[:, gcol:gcol + 128],
                            in_=sg8[:, :]), 16)

        pM0, pV0 = sM.v, sV.v
        with nc.Block() as blk:
            @blk.vector
            def _(v):
                v.tensor_scalar_mul(r8[0:1, 0:8], cnt8[:, :], 1.0 / 512.0)
                v.nop(cycle_cnt=32)
                v.tensor_copy(i8a[:, :], r8[0:1, 0:8])
                v.nop(cycle_cnt=32)
                v.tensor_copy(r8[0:1, 8:16], i8a[:, :])
                v.nop(cycle_cnt=32)
                v.tensor_tensor(r8[0:1, 16:24], r8[0:1, 0:8], r8[0:1, 8:16],
                                ALU.is_gt)
                v.nop(cycle_cnt=32)
                v.tensor_add(r8[0:1, 8:16], r8[0:1, 8:16], r8[0:1, 16:24])
                v.nop(cycle_cnt=32)
                v.tensor_scalar_mul(r8[0:1, 8:16], r8[0:1, 8:16], 512.0)
                sV.inc(v.drain())

            @blk.tensor
            def _(tn, pV0=pV0):
                tn.wait_ge(sV.h, pV0 + 1)
                sM.inc(tn.matmul(ps_g[0:8, 0:8], onesr[0:1, 0:8],
                                 r8[0:1, 8:16], start=True, stop=True))

            @blk.vector
            def _(v, pM0=pM0):
                v.wait_ge(sM.h, pM0 + 1)
                v.tensor_tensor(t88[:, :], ps_g[0:8, 0:8], lt8t[:, :],
                                ALU.mult)
                v.nop(cycle_cnt=32)
                v.reduce_sum(off8c[:, :], t88[:, :], axis=AX)
                sV.inc(v.drain())

            @blk.tensor
            def _(tn, pV0=pV0):
                tn.wait_ge(sV.h, pV0 + 2)
                sM.inc(tn.transpose(ps_sum[0:1, 0:8], off8c[:, :],
                                    idf[0:8, 0:8]))

            @blk.vector
            def _(v, pM0=pM0):
                v.wait_ge(sM.h, pM0 + 2)
                v.tensor_copy(offrow[:, :], ps_sum[0:1, 0:8])
                sV.inc(v.drain())

            @blk.sync
            def _(sync, pV0=pV0):
                sync.wait_ge(sV.h, pV0 + 3)
                ld.inc(sync.dma_start(out=D["offs"], in_=offrow[:, :]), 16)

            @blk.tensor
            def _(tn, pV0=pV0):
                tn.wait_ge(sV.h, pV0 + 3)
                sM.inc(tn.matmul(ps_g[:, :], onesr[:, :], offrow[:, :],
                                 start=True, stop=True))

            @blk.vector
            def _(v, pM0=pM0):
                v.wait_ge(sM.h, pM0 + 3)
                for gidx in range(64):
                    v.tensor_scalar(lg[:, 20:28], iota8[:, :],
                                    e16[:, gidx:gidx + 1], None, ALU.is_equal)
                    v.nop(cycle_cnt=32)
                    v.tensor_tensor(lg[:, 28:36], ps_g[:, :], lg[:, 20:28],
                                    ALU.mult)
                    v.nop(cycle_cnt=32)
                    v.reduce_sum(lg[:, 19:20], lg[:, 28:36], axis=AX)
                    v.nop(cycle_cnt=32)
                    v.tensor_add(lg[:, 19:20], lg[:, 19:20],
                                 rank6[:, gidx:gidx + 1])
                    v.nop(cycle_cnt=32)
                    v.tensor_copy(islot6[:, gidx:gidx + 1], lg[:, 19:20])
                sV.inc(v.drain())

        with nc.Block() as blk:
            @blk.sync
            def _(sync):
                sV.wait(sync)
                ld.inc(sync.dma_start(out=slot_d, in_=islot6[:, :]), 16)

    with nc.Block() as blk:
        @blk.sync
        def _(sync):
            ld.wait(sync)
    es.close()


def rope_tables():
    d = 64
    pos = np.arange(T, dtype=np.float32)
    freqs = (10000.0 ** (-np.arange(0, d, 2, dtype=np.float32) / d))
    ang = pos[:, None] * freqs
    cos = np.repeat(np.cos(ang), 2, axis=1).T.astype(np.float32)
    sin = np.repeat(np.sin(ang), 2, axis=1).T.astype(np.float32)
    cosb = np.concatenate([cos, cos], 0)
    sinb = np.concatenate([sin, sin], 0)
    p1 = np.zeros((64, 64), np.float32)
    for i in range(32):
        p1[2 * i + 1, 2 * i] = -1.0
        p1[2 * i, 2 * i + 1] = 1.0
    ps2 = np.zeros((128, 128), np.float32)
    ps2[0:64, 0:64] = p1
    ps2[64:128, 64:128] = p1
    return cosb, sinb, ps2


def emit_s7(nc, D, S, njg=64, nheads=8):
    _pad = 0
    es = contextlib.ExitStack()

    qr_d, kr_d, sq_d, sk_d = D["qrh"], D["krh"], D["sqh"], D["skh"]
    sktm_d, vtm_d = D["sktm"], D["vtm"]
    mb_d, msk_d, idn_d = D["mbeta"], D["mask4"], D["idn"]
    o_d = D["oh"]
    dbg = False

    def sb(stack, name, shape, dtype=F32):
        return stack.enter_context(nc.sbuf_tensor("s7_" + name, shape, dtype))

    def psum(stack, name, shape, dtype=F32):
        return stack.enter_context(nc.psum_tensor("s7_" + name, shape, dtype))

    qrh = sb(es, "qrt", [64, T], BF16)
    krh = sb(es, "krt", [64, T], BF16)
    sqh = sb(es, "sqt", [64, T], BF16)
    skh = sb(es, "skt", [64, T], BF16)
    sktmh = sb(es, "sktmt", [32, 256 * 64], BF16)
    vtmh = sb(es, "vtmt", [32, 256 * 64], BF16)
    mask4 = sb(es, "mask4t", [32, 128])
    idf = sb(es, "idft", [128, 128])
    idb = sb(es, "idbt", [128, 128], BF16)
    idb32 = sb(es, "idb32", [32, 32], BF16)
    mbt = sb(es, "mbt", [1, 8])
    ones1r = sb(es, "ones1r", [1, 32])
    bt32 = sb(es, "bt32", [32, 1])
    mz = sb(es, "mz", [64, 65])
    mz_bf = sb(es, "mz_bf", [64, 65], BF16)
    sc_sb = sb(es, "sc_sb", [32, 128])
    e_sb = sb(es, "e_sb", [32, 128])
    e_bf = sb(es, "e_bf", [32, 128], BF16)
    pT = sb(es, "pT", [32, 128], BF16)
    nmx = sb(es, "nmx", [32, 4])
    rs = sb(es, "rs", [32, 4])
    den1 = sb(es, "den1", [32, 2])
    mo_sb = sb(es, "mo_sb", [32, 64])
    pv_sb = sb(es, "pv_sb", [32, 64], BF16)
    dv65 = sb(es, "dv65", [32, 65], BF16)
    t_sb = sb(es, "t_sb", [32, 64])
    otm = sb(es, "otm", [32, 64], BF16)
    obuf = [sb(es, "obuf0", [64, 512], BF16), sb(es, "obuf1", [64, 512], BF16)]

    ps_sc = psum(es, "ps_sc", [32, 128])
    ps_sd = psum(es, "ps_sd", [32, 256])
    ps_nd = psum(es, "ps_nd", [32, 65])
    ps_pv = psum(es, "ps_pv", [32, 65])
    ps_mu = psum(es, "ps_mu", [64, 65])
    ps_ot = [psum(es, "ps_ot0", [64, 32], BF16),
             psum(es, "ps_ot1", [64, 32], BF16)]
    ps_bb = psum(es, "ps_bb", [32, 1])

    ld, sA, sM, sV = S
    ld_s7_base = ld.v

    with nc.Block() as blk:
        @blk.sync
        def _(sync):
            sync.wait_ge(ld.h, ld.v)
            ld.inc(sync.dma_start(out=mask4[:, :], in_=msk_d), 16)
            ld.inc(sync.dma_start(out=idf[:, :], in_=idn_d), 16)
            ld.inc(sync.dma_start(out=mbt[:, :], in_=mb_d), 16)
            ld.wait(sync)

        @blk.vector
        def _(v):
            v.wait_ge(ld.h, ld_s7_base + 48)
            v.tensor_copy(idb[:, :], idf[:, :])
            v.tensor_copy(idb32[:, :], idf[0:32, 0:32])
            v.memset(ones1r[:, :], 1.0)

        @blk.scalar
        def _(s):
            ld.wait(s)
            for _ in range(_pad):
                s.nop(cycle_cnt=7)
            s.activation(mbt[:, :], mbt[:, :], AF.Sigmoid)

    NJG = njg
    for hh in range(nheads):
        with nc.Block() as blk:
            @blk.sync
            def _(sync, hh=hh):
                sync.wait_ge(ld.h, ld.v)
                r0 = hh * 64
                ld.inc(sync.dma_start(out=qrh[:, :], in_=qr_d[r0:r0 + 64, :]), 16)
                ld.inc(sync.dma_start(out=krh[:, :], in_=kr_d[r0:r0 + 64, :]), 16)
                ld.inc(sync.dma_start(out=sqh[:, :], in_=sq_d[r0:r0 + 64, :]), 16)
                ld.inc(sync.dma_start(out=skh[:, :], in_=sk_d[r0:r0 + 64, :]), 16)
                ld.inc(sync.dma_start(
                    out=sktmh[:, :].rearrange("p (s d) -> p s d", d=64),
                    in_=sktm_d[:, r0:r0 + 64].rearrange("(s p) d -> p s d",
                                                        p=32)), 16)
                ld.inc(sync.dma_start(
                    out=vtmh[:, :].rearrange("p (s d) -> p s d", d=64),
                    in_=vtm_d[:, r0:r0 + 64].rearrange("(s p) d -> p s d",
                                                       p=32)), 16)
                ld.wait(sync)

            @blk.tensor
            def _(tn, hh=hh):
                sM.inc(tn.matmul(ps_bb[:, :], ones1r[:, :],
                                 mbt[:, hh:hh + 1], start=True, stop=True))

            @blk.vector
            def _(v):
                sM.wait(v)
                v.memset(mz[:, 0:64], 0.0)
                v.memset(mz[:, 64:65], 1e-6)
                v.memset(mz_bf[:, 0:64], 0.0)
                v.memset(mz_bf[:, 64:65], 1e-6)
                v.memset(dv65[:, 64:65], 1.0)
                v.tensor_copy(bt32[:, :], ps_bb[:, :])

        M0, V0, A0, L0 = sM.v, sV.v, sA.v, ld.v
        with nc.Block() as blk:
            @blk.tensor
            def _(tn, hh=hh):
                for J in range(NJG):
                    if J >= 1:
                        tn.wait_ge(sV.h, V0 + (J - 1) * 10 + 1)
                    for s4 in range(4):
                        j = J * 4 + s4
                        mm = tn.matmul(ps_sc[:, s4 * 32:(s4 + 1) * 32],
                                       qrh[:, j * 32:(j + 1) * 32],
                                       krh[:, j * 32:(j + 1) * 32],
                                       start=True, stop=True)
                    sM.inc(mm)
                    tn.wait_ge(sV.h, V0 + J * 10 + 2)
                    for s4 in range(4):
                        j = J * 4 + s4
                        mm = tn.matmul(ps_sd[:, s4 * 64:(s4 + 1) * 64],
                                       pT[:, s4 * 32:(s4 + 1) * 32],
                                       vtmh[:, j * 64:(j + 1) * 64],
                                       start=True, stop=True)
                    sM.inc(mm)
                    for s4 in range(4):
                        j = J * 4 + s4
                        if s4 >= 1:
                            tn.wait_ge(sV.h, V0 + J * 10 + 2 * s4 + 2)
                        tn.matmul(ps_nd[:, :], sqh[:, j * 32:(j + 1) * 32],
                                  mz_bf[:, :], start=True, stop=True)
                        sM.inc(tn.matmul(ps_pv[:, :],
                                         skh[:, j * 32:(j + 1) * 32],
                                         mz_bf[:, :], start=True, stop=True))
                        tn.wait_ge(sV.h, V0 + J * 10 + 3 + 2 * s4)
                        sM.inc(tn.matmul(ps_mu[:, :],
                                         sktmh[:, j * 64:(j + 1) * 64],
                                         dv65[:, :], start=True, stop=True))
                        if s4 >= 2:
                            tn.wait_ge(sA.h, A0 + J * 5 + 1 + s4)
                        elif J >= 1:
                            tn.wait_ge(sA.h, A0 + (J - 1) * 5 + 4 + s4)
                        sM.inc(tn.transpose(ps_ot[s4 % 2][:, 0:32], otm[:, :],
                                            idb32[:, :]))

            @blk.vector
            def _(v, hh=hh):
                for J in range(NJG):
                    if J >= 1:
                        v.wait_ge(sA.h, A0 + (J - 1) * 5 + 1)
                    v.wait_ge(sM.h, M0 + J * 14 + 1)
                    v.scalar_tensor_tensor(sc_sb[:, :], ps_sc[:, :], 0.125,
                                           mask4[:, :], ALU.mult, ALU.add)
                    v.nop(cycle_cnt=32)
                    for s4 in range(4):
                        v.tensor_reduce(nmx[:, s4:s4 + 1],
                                        sc_sb[:, s4 * 32:(s4 + 1) * 32],
                                        axis=AX, op=ALU.max)
                    v.nop(cycle_cnt=32)
                    for s4 in range(4):
                        v.tensor_scalar(sc_sb[:, s4 * 32:(s4 + 1) * 32],
                                        sc_sb[:, s4 * 32:(s4 + 1) * 32],
                                        nmx[:, s4:s4 + 1], None, ALU.subtract)
                    sV.inc(v.drain())
                    v.wait_ge(sA.h, A0 + J * 5 + 1)
                    for s4 in range(4):
                        v.reduce_sum(rs[:, s4:s4 + 1],
                                     e_sb[:, s4 * 32:(s4 + 1) * 32], axis=AX)
                    for s4 in range(4):
                        v.reciprocal(rs[:, s4:s4 + 1], rs[:, s4:s4 + 1])
                    for s4 in range(4):
                        v.tensor_scalar(e_bf[:, s4 * 32:(s4 + 1) * 32],
                                        e_sb[:, s4 * 32:(s4 + 1) * 32],
                                        rs[:, s4:s4 + 1], None, ALU.mult)
                    v.drain()
                    v.transpose(pT[:, :], e_bf[:, :])
                    sV.inc(v.drain())
                    for s4 in range(4):
                        j = J * 4 + s4
                        v.wait_ge(sM.h, M0 + J * 14 + 3 + 3 * s4)
                        v.tensor_scalar(den1[:, 0:1], ps_nd[:, 64:65], 1e-6,
                                        None, ALU.add)
                        v.reciprocal(den1[:, 1:2], ps_pv[:, 64:65])
                        v.nop(cycle_cnt=32)
                        v.reciprocal(den1[:, 0:1], den1[:, 0:1])
                        v.tensor_scalar_mul(pv_sb[:, :], ps_pv[:, 0:64],
                                            den1[:, 1:2])
                        v.nop(cycle_cnt=32)
                        v.tensor_scalar_mul(mo_sb[:, :], ps_nd[:, 0:64],
                                            den1[:, 0:1])
                        v.tensor_sub(dv65[:, 0:64],
                                     vtmh[:, j * 64:(j + 1) * 64], pv_sb[:, :])
                        v.nop(cycle_cnt=32)
                        v.tensor_sub(t_sb[:, :], mo_sb[:, :],
                                     ps_sd[:, s4 * 64:(s4 + 1) * 64])
                        v.nop(cycle_cnt=32)
                        v.tensor_scalar_mul(t_sb[:, :], t_sb[:, :],
                                            bt32[:, :])
                        v.nop(cycle_cnt=32)
                        v.tensor_add(otm[:, :], t_sb[:, :],
                                     ps_sd[:, s4 * 64:(s4 + 1) * 64])
                        sV.inc(v.drain())
                        v.wait_ge(sM.h, M0 + J * 14 + 4 + 3 * s4)
                        v.tensor_add(mz[:, :], mz[:, :], ps_mu[:, :])
                        v.nop(cycle_cnt=32)
                        v.tensor_copy(mz_bf[:, :], mz[:, :])
                        sV.inc(v.drain())

            @blk.scalar
            def _(s, hh=hh):
                for J in range(NJG):
                    s.wait_ge(sV.h, V0 + J * 10 + 1)
                    s.activation(e_sb[:, :], sc_sb[:, :], AF.Exp)
                    sA.inc(s.drain())
                    for s4 in range(4):
                        cb = J // 4
                        if J % 4 == 0 and s4 == 0 and cb >= 2:
                            s.wait_ge(ld.h, L0 + 16 * (cb - 1))
                        s.wait_ge(sM.h, M0 + J * 14 + 5 + 3 * s4)
                        col = (J % 4) * 128 + s4 * 32
                        s.copy(obuf[cb % 2][:, col:col + 32],
                               ps_ot[s4 % 2][:, 0:32])
                        sA.inc(s.drain())

            @blk.sync
            def _(sync, hh=hh):
                for cb in range(NJG // 4):
                    sync.wait_ge(sA.h, A0 + (4 * cb + 3) * 5 + 5)
                    ld.inc(sync.dma_start(
                        out=o_d[hh * 64:hh * 64 + 64,
                                cb * 512:(cb + 1) * 512],
                        in_=obuf[cb % 2][:, :]), 16)

    with nc.Block() as blk:
        @blk.sync
        def _(sync):
            ld.wait(sync)
    es.close()










def emit_pair_allreduce(nc, D, S, ccs, src_name, dst_name):
    ld, sA, sM, sV = S
    with nc.Block() as blk:
        @blk.gpsimd
        def _(g_):
            g_.wait_ge(ld.h, ld.v)
            g_.collective_compute(
                "AllReduce", ALU.add,
                replica_groups=[[0, 1], [2, 3], [4, 5], [6, 7]],
                ins=[D[src_name].opt()],
                outs=[D[dst_name].opt()],
            ).then_inc(ccs.h)
    with nc.Block() as blk:
        @blk.sync
        def _(sync):
            sync.wait_ge(ccs.h, 1)


def emit_exchange(nc, D, S, cc):
    ld, sA, sM, sV = S
    with nc.Block() as blk:
        @blk.gpsimd
        def _(g_):
            g_.wait_ge(ld.h, ld.v)
            g_.collective_compute(
                "AllGather", ALU.bypass,
                replica_groups=[[0, 1], [2, 3], [4, 5], [6, 7]],
                ins=[D["oh"].opt()],
                outs=[D["og"].opt()],
            ).then_inc(cc.h)



def emit_s8(nc, D, S, cc, cc4=None):
    ld, sA, sM, sV = S
    og_d, x1_d, xn1bf_d = D["og"], D["x1"], D["xn1bf"]
    sg_d, slot_d = D["sgate"], D["slotint"]
    gbuf_d, gout_d = D["gbuf"], D["gout"]
    wout_d, w1_d, w2_d = D["wout"], D["mw1"], D["mw2"]
    b1_d, b2_d, idn_d = D["mb1"], D["mb2"], D["idn"]
    out_d = D["out"]
    es = contextlib.ExitStack()

    def sb(stack, name, shape, dtype=F32):
        return stack.enter_context(nc.sbuf_tensor("s8_" + name, shape, dtype))

    def psum(stack, name, shape, dtype=F32):
        return stack.enter_context(nc.psum_tensor("s8_" + name, shape, dtype))

    with contextlib.ExitStack() as ea:
        xnbfc = sb(ea, "xnbfc", [128, G * 512], BF16)
        xtok = sb(ea, "xtok", [128, 2 * C], BF16)
        islot = sb(ea, "islot", [128, 64], I32)
        idf = sb(ea, "idfa", [128, 128])
        idb = sb(ea, "idba", [128, 128], BF16)
        ps_t = [psum(ea, f"ps_ta{i}", [128, 128], BF16) for i in range(2)]

        with nc.Block() as blk:
            @blk.sync
            def _(sync):
                sync.wait_ge(ld.h, ld.v)
                ld.inc(sync.dma_start(out=islot[:, :], in_=slot_d), 16)
                ld.inc(sync.dma_start(out=idf[:, :], in_=idn_d), 16)
                ld.wait(sync)

            @blk.vector
            def _(v):
                ld.wait(v)
                v.tensor_copy(idb[:, :], idf[:, :])
                sV.inc(v.drain())
        sV_idb = sV.v

        last_sc_ld = [0, 0]
        for n in range(16):
            c0 = n * 512
            with nc.Block() as blk:
                @blk.sync
                def _(sync, c0=c0):
                    sync.wait_ge(ld.h, ld.v)
                    ld.inc(sync.dma_start(
                        out=xnbfc[:, :].rearrange("p (g t) -> p g t", t=512),
                        in_=xn1bf_d[:, c0:c0 + 512].rearrange(
                            "(g p) t -> p g t", p=128)), 16)
                    ld.wait(sync)
            ldL = ld.v
            for sl4 in range(4):
                par = sl4 % 2
                aM0, aA0 = sM.v, sA.v
                prev_ld = last_sc_ld[par]
                with nc.Block() as blk:
                    @blk.tensor
                    def _(tn, sl4=sl4, ldL=ldL, aA0=aA0):
                        tn.wait_ge(ld.h, ldL)
                        tn.wait_ge(sV.h, sV_idb)
                        for cg in range(G):
                            if cg >= 2:
                                tn.wait_ge(sA.h, aA0 + cg - 1)
                            sM.inc(tn.transpose(
                                ps_t[cg % 2][:, :],
                                xnbfc[:, cg * 512 + sl4 * 128:
                                      cg * 512 + sl4 * 128 + 128],
                                idb[:, :]))

                    @blk.scalar
                    def _(s, par=par, aM0=aM0, prev_ld=prev_ld):
                        if prev_ld:
                            s.wait_ge(ld.h, prev_ld)
                        for cg in range(G):
                            s.wait_ge(sM.h, aM0 + cg + 1)
                            s.copy(xtok[:, par * C + cg * 128:
                                        par * C + (cg + 1) * 128],
                                   ps_t[cg % 2][:, :])
                            sA.inc(s.drain())

                    @blk.gpsimd
                    def _(g_, n=n, sl4=sl4, par=par, aA0=aA0):
                        g_.wait_ge(sA.h, aA0 + 8)
                        ld.inc(g_.indirect_dma_start(
                            out=gbuf_d,
                            out_offset=bass.IndirectOffsetOnAxis(
                                islot[:, n * 4 + sl4:n * 4 + sl4 + 1], 0),
                            in_=xtok[:, par * C:(par + 1) * C],
                            in_offset=None), 16)
                last_sc_ld[par] = ld.v

    NQ = 24
    with contextlib.ExitStack() as eb:
        w1t = sb(eb, "w1t", [128, G * 2 * C], BF16)
        w2r = sb(eb, "w2r", [128, 16 * C], BF16)
        b1c = sb(eb, "b1c", [128, 16])
        gtok = sb(eb, "gtok", [128, 4 * C], BF16)
        xe = sb(eb, "xe", [128, G * 512], BF16)
        u_bf = sb(eb, "u_bf", [128, 16 * 512], BF16)
        oe = sb(eb, "oe", [128, 512])
        otok = sb(eb, "otok", [128, 4 * C])
        idfb = sb(eb, "idfb", [128, 128])
        idbb = sb(eb, "idbb", [128, 128], BF16)
        offr2 = sb(eb, "offr2", [1, 8])
        qcolt = sb(eb, "qcolt", [24, 1])
        piota = sb(eb, "piota", [128, 1])
        ge24 = sb(eb, "ge24", [24, 8])
        ec24 = sb(eb, "ec24", [24, 1])
        ecrow = sb(eb, "ecrow", [1, 24])
        ecb = sb(eb, "ecb", [128, 24])
        wf = sb(eb, "wf", [128, 24])
        w0 = sb(eb, "w0", [128, 24])
        iwidx = sb(eb, "iwidx", [128, 8 * 24], I32)
        iw2 = sb(eb, "iw2", [128, 16 * 24], I32)
        ib1 = sb(eb, "ib1", [128, 24], I32)
        ones1b = sb(eb, "ones1b", [1, 128])
        ps_h = [psum(eb, f"ps_hb{i}", [128, 512]) for i in range(2)]
        ps_o = [psum(eb, f"ps_ob{i}", [128, 512]) for i in range(2)]
        ps_tb = [psum(eb, f"ps_tbb{i}", [128, 128], BF16) for i in range(2)]
        ps_tf = [psum(eb, f"ps_tfb{i}", [128, 128]) for i in range(2)]

        kB0 = sM.v
        with nc.Block() as blk:
            @blk.sync
            def _(sync):
                sync.wait_ge(ld.h, ld.v)
                ld.inc(sync.dma_start(out=idfb[:, :], in_=idn_d), 16)
                ld.inc(sync.dma_start(out=offr2[:, :], in_=D["offs"]), 16)
                ld.inc(sync.dma_start(out=qcolt[:, :], in_=D["qcol"]), 16)
                ld.inc(sync.dma_start(out=piota[:, :], in_=D["piota"]), 16)
                ld.wait(sync)

            @blk.vector
            def _(v):
                ld.wait(v)
                v.tensor_copy(idbb[:, :], idfb[:, :])
                v.memset(otok[:, :], 0.0)
                v.memset(ones1b[:, :], 1.0)
                sV.inc(v.drain())

            @blk.gpsimd
            def _(g_):
                sV.wait(g_)
                ld.inc(g_.dma_start(
                    out=D["goutp"][NSLOT:NSLOT + 128, :],
                    in_=otok[:, 0:C]), 16)
        sV_idb2 = sV.v

        kM0, kV0 = sM.v, sV.v
        with nc.Block() as blk:
            @blk.tensor
            def _(tn):
                sM.inc(tn.matmul(ge24[0:24, 0:8] if False else ps_h[0][0:24, 0:8],
                                 ones1b[0:1, 0:24], offr2[:, :],
                                 start=True, stop=True))

            @blk.vector
            def _(v, kM0=kM0):
                v.wait_ge(sM.h, kM0 + 1)
                v.tensor_scalar(ge24[:, :], ps_h[0][0:24, 0:8],
                                qcolt[:, 0:1], None, ALU.is_le)
                v.nop(cycle_cnt=32)
                v.reduce_sum(ec24[:, :], ge24[:, :], axis=AX)
                v.nop(cycle_cnt=32)
                v.tensor_scalar(ec24[:, :], ec24[:, :], -1.0, None, ALU.add)
                sV.inc(v.drain())

            @blk.tensor
            def _(tn, kV0=kV0):
                tn.wait_ge(sV.h, kV0 + 1)
                sM.inc(tn.transpose(ps_tf[0][0:1, 0:24], ec24[:, :],
                                    idfb[0:24, 0:24]))

            @blk.scalar
            def _(s, kM0=kM0):
                s.wait_ge(sM.h, kM0 + 2)
                s.copy(ecrow[:, :], ps_tf[0][0:1, 0:24])
                sA.inc(s.drain())

            @blk.tensor
            def _(tn, kA0=sA.v):
                tn.wait_ge(sA.h, kA0)
                sM.inc(tn.matmul(ps_h[1][0:128, 0:24], ones1b[0:1, :],
                                 ecrow[:, :], start=True, stop=True))

            @blk.vector
            def _(v, kM0=kM0):
                v.wait_ge(sM.h, kM0 + 3)
                v.tensor_copy(ecb[:, :], ps_h[1][0:128, 0:24])
                sV.inc(v.drain())

            @blk.vector
            def _(v):
                v.tensor_scalar_mul(w0[:, :], ecb[:, :], 1024.0)
                v.nop(cycle_cnt=32)
                v.tensor_scalar(w0[:, :], w0[:, :], piota[:, 0:1], None,
                                ALU.add)
                v.nop(cycle_cnt=32)
                for kc in range(G):
                    v.tensor_scalar(wf[:, :], w0[:, :], float(kc * 128), None,
                                    ALU.add)
                    v.nop(cycle_cnt=32)
                    v.tensor_copy(iwidx[:, kc * 24:(kc + 1) * 24], wf[:, :])
                    v.nop(cycle_cnt=32)
                v.tensor_scalar_mul(w0[:, :], ecb[:, :], 2048.0)
                v.nop(cycle_cnt=32)
                v.tensor_scalar(w0[:, :], w0[:, :], piota[:, 0:1], None,
                                ALU.add)
                v.nop(cycle_cnt=32)
                for hc in range(16):
                    v.tensor_scalar(wf[:, :], w0[:, :], float(hc * 128), None,
                                    ALU.add)
                    v.nop(cycle_cnt=32)
                    v.tensor_copy(iw2[:, hc * 24:(hc + 1) * 24], wf[:, :])
                    v.nop(cycle_cnt=32)
                v.tensor_scalar_mul(w0[:, :], ecb[:, :], 128.0)
                v.nop(cycle_cnt=32)
                v.tensor_scalar(w0[:, :], w0[:, :], piota[:, 0:1], None,
                                ALU.add)
                v.nop(cycle_cnt=32)
                v.tensor_copy(ib1[:, :], w0[:, :])
                sV.inc(v.drain())

        for q in range(NQ):
            r0 = q * 512
            prevM = sM.v
            with nc.Block() as blk:
                @blk.gpsimd
                def _(g_, q=q, prevM=prevM):
                    g_.wait_ge(sM.h, prevM)
                    for kc in range(G):
                        ld.inc(g_.indirect_dma_start(
                            out=w1t[:, kc * 2 * C:(kc + 1) * 2 * C],
                            out_offset=None,
                            in_=w1_d,
                            in_offset=bass.IndirectOffsetOnAxis(
                                iwidx[:, kc * 24 + q:kc * 24 + q + 1], 0)),
                            16)
                    for hc in range(16):
                        ld.inc(g_.indirect_dma_start(
                            out=w2r[:, hc * C:(hc + 1) * C],
                            out_offset=None,
                            in_=w2_d,
                            in_offset=bass.IndirectOffsetOnAxis(
                                iw2[:, hc * 24 + q:hc * 24 + q + 1], 0)),
                            16)
                    ld.inc(g_.indirect_dma_start(
                        out=b1c[:, :],
                        out_offset=None,
                        in_=D["b1pr"],
                        in_offset=bass.IndirectOffsetOnAxis(
                            ib1[:, q:q + 1], 0)), 16)

                @blk.sync
                def _(sync, r0=r0):
                    sync.wait_ge(ld.h, ld.v)
                    ld.inc(sync.dma_start(
                        out=gtok[:, :].rearrange("p (gr c) -> p gr c", c=C),
                        in_=gbuf_d[r0:r0 + 512, :].rearrange(
                            "(gr p) c -> p gr c", p=128)), 16)
                    ld.wait(sync)
            ld_w = ld.v

            bM0, bV0 = sM.v, sV.v
            with nc.Block() as blk:
                @blk.tensor
                def _(tn, ld_w=ld_w, bV0=bV0):
                    tn.wait_ge(ld.h, ld_w)
                    tn.wait_ge(sV.h, sV_idb2)
                    for i in range(32):
                        gr, cg = divmod(i, 8)
                        if i >= 2:
                            tn.wait_ge(sV.h, bV0 + i - 1)
                        sM.inc(tn.transpose(
                            ps_tb[i % 2][:, :],
                            gtok[:, gr * C + cg * 128:
                                 gr * C + (cg + 1) * 128],
                            idbb[:, :]))

                @blk.vector
                def _(v, bM0=bM0):
                    for i in range(32):
                        gr, cg = divmod(i, 8)
                        v.wait_ge(sM.h, bM0 + i + 1)
                        v.tensor_copy(xe[:, cg * 512 + gr * 128:
                                         cg * 512 + (gr + 1) * 128],
                                      ps_tb[i % 2][:, :])
                        sV.inc(v.drain())

            cM0, cA0 = sM.v, sA.v
            with nc.Block() as blk:
                @blk.tensor
                def _(tn, cA0=cA0, bV0=bV0):
                    tn.wait_ge(sV.h, bV0 + 32)
                    for hc in range(16):
                        if hc >= 2:
                            tn.wait_ge(sA.h, cA0 + hc - 1)
                        for kc in range(G):
                            mm = tn.matmul(
                                ps_h[hc % 2][:, :],
                                w1t[:, kc * 2 * C + hc * 128:
                                    kc * 2 * C + hc * 128 + 128],
                                xe[:, kc * 512:(kc + 1) * 512],
                                start=(kc == 0), stop=(kc == G - 1))
                        sM.inc(mm)

                @blk.scalar
                def _(s, cM0=cM0):
                    for hc in range(16):
                        s.wait_ge(sM.h, cM0 + hc + 1)
                        s.activation(u_bf[:, hc * 512:(hc + 1) * 512],
                                     ps_h[hc % 2][:, :], AF.Gelu,
                                     bias=b1c[:, hc:hc + 1])
                        sA.inc(s.drain())

            dM0, dV0, dA0, dL0 = sM.v, sV.v, sA.v, ld.v
            with nc.Block() as blk:
                @blk.tensor
                def _(tn, dM0=dM0, dA0=dA0, dV0=dV0):
                    for m in range(G):
                        if m >= 2:
                            tn.wait_ge(sA.h, dA0 + m - 1)
                        for hc in range(16):
                            mm = tn.matmul(
                                ps_o[m % 2][:, :],
                                w2r[:, hc * C + m * 128:
                                    hc * C + (m + 1) * 128],
                                u_bf[:, hc * 512:(hc + 1) * 512],
                                start=(hc == 0), stop=(hc == 15))
                        sM.inc(mm)
                        tn.wait_ge(sA.h, dA0 + m + 1)
                        for gr in range(4):
                            if m * 4 + gr >= 2:
                                tn.wait_ge(sV.h, dV0 + m * 4 + gr - 1)
                            sM.inc(tn.transpose(
                                ps_tf[gr % 2][:, :],
                                oe[:, gr * 128:(gr + 1) * 128],
                                idfb[:, :]))

                @blk.scalar
                def _(s, dM0=dM0):
                    for m in range(G):
                        s.wait_ge(sM.h, dM0 + m * 5 + 1)
                        s.copy(oe[:, :], ps_o[m % 2][:, :])
                        sA.inc(s.drain())

                @blk.vector
                def _(v, dM0=dM0, dL0=dL0):
                    for m in range(G):
                        for gr in range(4):
                            if m == 0 and gr == 0:
                                v.wait_ge(ld.h, dL0)
                            v.wait_ge(sM.h, dM0 + m * 5 + 2 + gr)
                            v.tensor_copy(otok[:, gr * C + m * 128:
                                               gr * C + (m + 1) * 128],
                                          ps_tf[gr % 2][:, :])
                            sV.inc(v.drain())

                @blk.sync
                def _(sync, r0=r0, dV0=dV0):
                    sync.wait_ge(sV.h, dV0 + 32)
                    ld.inc(sync.dma_start(
                        out=D["goutp"][r0:r0 + 512, :].rearrange(
                            "(gr p) c -> p gr c", p=128),
                        in_=otok[:, :].rearrange("p (gr c) -> p gr c",
                                                 c=C)), 16)

    emit_pair_allreduce(nc, D, S, cc4, "goutp", "gout")

    with contextlib.ExitStack() as ec:
        wout_b = sb(ec, "wout_b", [128, G * C], BF16)
        oc = sb(ec, "oc", [128, G * 512], BF16)
        x1c = sb(ec, "x1c", [128, G * 512])
        s1_sb = sb(ec, "s1sb", [128, G * 512])
        mtok = sb(ec, "mtok", [128, 4 * C])
        macc = sb(ec, "macc", [128, G * 512])
        sgc8 = sb(ec, "sgc8", [8, 512])
        b2t = sb(ec, "b2t2", [8, C])
        rowg = sb(ec, "rowg", [1, 512])
        ones8 = sb(ec, "ones8", [8, 1])
        onesr = sb(ec, "onesr2", [1, 128])
        idf3 = sb(ec, "idf3", [128, 128])
        islot2 = sb(ec, "islot2", [128, 64], I32)
        ps_h2 = [psum(ec, f"ps_hc{i}", [128, 512]) for i in range(2)]
        ps_tf2 = [psum(ec, f"ps_tfc{i}", [128, 128]) for i in range(2)]
        ps_w = psum(ec, "ps_wc", [128, 512])
        ps_b2 = psum(ec, "ps_b2c", [128, 512])
        ps_row = psum(ec, "ps_rowc", [1, 512])

        with nc.Block() as blk:
            @blk.sync
            def _(sync):
                sync.wait_ge(ld.h, ld.v)
                sync.wait_ge(cc.h, 1)
                ld.inc(sync.dma_start(out=b2t[:, :], in_=b2_d), 16)
                ld.inc(sync.dma_start(out=islot2[:, :], in_=slot_d), 16)
                ld.inc(sync.dma_start(out=idf3[:, :], in_=idn_d), 16)
                ld.wait(sync)

            @blk.gpsimd
            def _(g_):
                ld.inc(g_.dma_start(
                    out=wout_b[:, :].rearrange("p (kc o) -> p kc o", o=C),
                    in_=wout_d.rearrange("(kc p) o -> p kc o", p=128)), 16)

            @blk.vector
            def _(v):
                v.memset(ones8[:, :], 1.0)
                v.memset(onesr[:, :], 1.0)
                sV.inc(v.drain())
        wout_ld = ld.v

        for n in range(16):
            c0 = n * 512
            eL0 = ld.v
            with nc.Block() as blk:
                @blk.sync
                def _(sync, c0=c0):
                    sync.wait_ge(ld.h, ld.v)
                    ld.inc(sync.dma_start(
                        out=oc[:, :].rearrange("p (g t) -> p g t", t=512),
                        in_=og_d[:, c0:c0 + 512].rearrange(
                            "(g p) t -> p g t", p=128)), 16)
                    ld.inc(sync.dma_start(
                        out=x1c[:, :].rearrange("p (g t) -> p g t", t=512),
                        in_=x1_d[:, c0:c0 + 512].rearrange(
                            "(g p) t -> p g t", p=128)), 16)
                    ld.inc(sync.dma_start(out=sgc8[:, :],
                                          in_=sg_d[:, c0:c0 + 512]), 16)
                    ld.wait(sync)
            ld_chunk = ld.v

            fA0, fM0 = sA.v, sM.v
            with nc.Block() as blk:
                @blk.tensor
                def _(tn, fA0=fA0, ld_chunk=ld_chunk):
                    tn.wait_ge(ld.h, wout_ld)
                    tn.wait_ge(ld.h, ld_chunk)
                    for m in range(G):
                        if m >= 2:
                            tn.wait_ge(sA.h, fA0 + m - 1)
                        for kc in range(G):
                            mm = tn.matmul(
                                ps_h2[m % 2][:, :],
                                wout_b[:, kc * C + m * 128:
                                       kc * C + m * 128 + 128],
                                oc[:, kc * 512:(kc + 1) * 512],
                                start=(kc == 0), stop=(kc == G - 1))
                        sM.inc(mm)

                @blk.scalar
                def _(s, fM0=fM0):
                    for m in range(G):
                        s.wait_ge(sM.h, fM0 + m + 1)
                        s.copy(s1_sb[:, m * 512:(m + 1) * 512],
                               ps_h2[m % 2][:, :])
                        sA.inc(s.drain())

            with nc.Block() as blk:
                @blk.gpsimd
                def _(g_, n=n):
                    for tg in range(4):
                        ld.inc(g_.indirect_dma_start(
                            out=mtok[:, tg * C:(tg + 1) * C],
                            out_offset=None,
                            in_=gout_d,
                            in_offset=bass.IndirectOffsetOnAxis(
                                islot2[:, n * 4 + tg:n * 4 + tg + 1], 0)),
                            16)
            ld_gather = ld.v

            hM0, hV0 = sM.v, sV.v
            with nc.Block() as blk:
                @blk.tensor
                def _(tn, hV0=hV0, ld_gather=ld_gather):
                    tn.wait_ge(ld.h, ld_gather)
                    for i in range(32):
                        tg, cg = divmod(i, 8)
                        if i >= 2:
                            tn.wait_ge(sV.h, hV0 + i - 1)
                        sM.inc(tn.transpose(
                            ps_tf2[i % 2][:, :],
                            mtok[:, tg * C + cg * 128:
                                 tg * C + (cg + 1) * 128],
                            idf3[:, :]))

                @blk.vector
                def _(v, hM0=hM0):
                    for i in range(32):
                        tg, cg = divmod(i, 8)
                        v.wait_ge(sM.h, hM0 + i + 1)
                        v.tensor_copy(macc[:, cg * 512 + tg * 128:
                                           cg * 512 + tg * 128 + 128],
                                      ps_tf2[i % 2][:, :])
                        sV.inc(v.drain())

            iM0, iV0, iA0 = sM.v, sV.v, sA.v
            with nc.Block() as blk:
                @blk.tensor
                def _(tn):
                    sM.inc(tn.matmul(ps_row[:, :], ones8[:, 0:1],
                                     sgc8[:, :], start=True, stop=True))

                @blk.scalar
                def _(s, iM0=iM0):
                    s.wait_ge(sM.h, iM0 + 1)
                    s.copy(rowg[:, :], ps_row[:, :])
                    sA.inc(s.drain())

                @blk.tensor
                def _(tn, iA0=iA0):
                    tn.wait_ge(sA.h, iA0 + 1)
                    sM.inc(tn.matmul(ps_w[:, :], onesr[:, :], rowg[:, :],
                                     start=True, stop=True))

                @blk.vector
                def _(v, iM0=iM0):
                    v.wait_ge(sM.h, iM0 + 2)
                    for m in range(G):
                        sl = slice(m * 512, (m + 1) * 512)
                        v.tensor_mul(macc[:, sl], macc[:, sl], ps_w[:, :])
                    sV.inc(v.drain())
            jM0, jV0, jA0 = sM.v, sV.v, sA.v
            with nc.Block() as blk:
                @blk.tensor
                def _(tn, jV0=jV0):
                    for m in range(G):
                        if m >= 1:
                            tn.wait_ge(sV.h, jV0 + m)
                        sM.inc(tn.matmul(ps_b2[:, :],
                                         b2t[:, m * 128:(m + 1) * 128],
                                         sgc8[:, :], start=True, stop=True))

                @blk.vector
                def _(v, jM0=jM0):
                    for m in range(G):
                        v.wait_ge(sM.h, jM0 + m + 1)
                        sl = slice(m * 512, (m + 1) * 512)
                        v.tensor_add(macc[:, sl], macc[:, sl], ps_b2[:, :])
                        sV.inc(v.drain())

                @blk.scalar
                def _(s, jV0=jV0):
                    for m in range(G):
                        s.wait_ge(sV.h, jV0 + m + 1)
                        sl = slice(m * 512, (m + 1) * 512)
                        s.activation(macc[:, sl], macc[:, sl], AF.Silu)
                        sA.inc(s.drain())

                @blk.vector
                def _(v, jA0=jA0):
                    for m in range(G):
                        v.wait_ge(sA.h, jA0 + m + 1)
                        sl = slice(m * 512, (m + 1) * 512)
                        v.tensor_mul(s1_sb[:, sl], s1_sb[:, sl], macc[:, sl])
                        v.nop(cycle_cnt=32)
                        v.tensor_add(x1c[:, sl], x1c[:, sl], s1_sb[:, sl])
                        sV.inc(v.drain(), 2)

                @blk.sync
                def _(sync, n=n, jV0=jV0):
                    sync.wait_ge(sV.h, jV0 + G + 2 * G)
                    ld.inc(sync.dma_start(
                        out=out_d[:, n * 512:(n + 1) * 512].rearrange(
                            "(g p) t -> p g t", p=128),
                        in_=x1c[:, :].rearrange("p (g t) -> p g t", t=512)),
                        16)

    with nc.Block() as blk:
        @blk.sync
        def _(sync):
            ld.wait(sync)
    es.close()


def emit_s8_dense(nc, D, S, cc, nchunks=T // 512, nexperts=8):
    SKIP = ''
    es = contextlib.ExitStack()
    th0 = 0

    o_d = D["og"]
    wout_d, x1_d, xn1bf_d = D["wout"], D["x1"], D["xn1bf"]
    sg_d = D["sgate"]
    w1_d, w2_d, b1_d, b2_d = D["mw1"], D["mw2"], D["mb1"], D["mb2"]
    out_d = D["out"]

    def sb(stack, name, shape, dtype=F32):
        return stack.enter_context(nc.sbuf_tensor("s8d_" + name, shape, dtype))

    def psum(stack, name, shape, dtype=F32):
        return stack.enter_context(nc.psum_tensor("s8d_" + name, shape, dtype))

    wout_b = sb(es, "wout_b", [128, G * C], BF16)
    b1t = sb(es, "b1t", [128, 8 * 32])
    b2t = sb(es, "b2t", [8, C])
    w1t = sb(es, "w1t", [128, G * 4 * C], BF16)
    w2t = [sb(es, "w2t0", [128, 32 * 128], BF16),
           sb(es, "w2t1", [128, 32 * 128], BF16)]
    x1c = sb(es, "x1c", [128, G * 512])
    xnbfc = sb(es, "xnbfc", [128, G * 512], BF16)
    s1_sb = sb(es, "s1_sb", [128, G * 512])
    acc = sb(es, "acc", [128, G * 512])
    u_bf = sb(es, "u_bf", [128, 32 * 512], BF16)
    oc = u_bf
    glt = sb(es, "glt", [128, 2 * 512])
    sgc8 = sb(es, "sgc8", [8, 512])
    sgc1 = sb(es, "sgc1", [1, 512])

    ps_h = [psum(es, f"ps_h{i}", [128, 512]) for i in range(2)]
    ps_o = [psum(es, f"ps_o{i}", [128, 512]) for i in range(2)]
    ps_w = psum(es, "ps_w", [128, 512])
    ps_b2 = psum(es, "ps_b2", [128, 512])

    ld, sA, sM, sV = S
    onesr = sb(es, "onesr", [1, 128])

    with nc.Block() as blk:
        @blk.sync
        def _(sync):
            sync.wait_ge(ld.h, ld.v)
            sync.wait_ge(cc.h, 1)
            ld.inc(sync.dma_start(out=b1t[:, :], in_=b1_d), 16)
            ld.inc(sync.dma_start(out=b2t[:, :], in_=b2_d), 16)
            ld.wait(sync)

        @blk.gpsimd
        def _(g_):
            ld.inc(g_.dma_start(
                out=wout_b[:, :].rearrange("p (kc o) -> p kc o", o=C),
                in_=wout_d.rearrange("(kc p) o -> p kc o", p=128)), 16)

        @blk.vector
        def _(v):
            v.memset(onesr[:, :], 1.0)

    for n in range(nchunks):
        c0 = th0 + n * 512
        with nc.Block() as blk:
            @blk.sync
            def _(sync, n=n, c0=c0):
                sync.wait_ge(ld.h, ld.v)
                ld.inc(sync.dma_start(
                    out=oc[:, 0:G * 512].rearrange("p (g t) -> p g t", t=512),
                    in_=o_d[:, c0:c0 + 512].rearrange("(g p) t -> p g t",
                                                      p=128)), 16)
                ld.inc(sync.dma_start(
                    out=x1c[:, :].rearrange("p (g t) -> p g t", t=512),
                    in_=x1_d[:, c0:c0 + 512].rearrange("(g p) t -> p g t",
                                                       p=128)), 16)
                ld.inc(sync.dma_start(
                    out=xnbfc[:, :].rearrange("p (g t) -> p g t", t=512),
                    in_=xn1bf_d[:, c0:c0 + 512].rearrange("(g p) t -> p g t",
                                                          p=128)), 16)
                ld.inc(sync.dma_start(out=sgc8[:, :],
                                      in_=sg_d[:, n * 512:(n + 1) * 512]), 16)
                ld.wait(sync)

        sA_b, sM_b = sA.v, sM.v
        if 'wout' in SKIP:
            pass
        else:
         with nc.Block() as blk:
            @blk.tensor
            def _(tn):
                for m in range(G):
                    if m >= 2:
                        tn.wait_ge(sA.h, sA_b + m - 1)
                    for kc in range(G):
                        mm = tn.matmul(
                            ps_h[m % 2][:, :],
                            wout_b[:, kc * C + m * 128:kc * C + m * 128 + 128],
                            oc[:, kc * 512:(kc + 1) * 512],
                            start=(kc == 0), stop=(kc == G - 1))
                    sM.inc(mm)

            @blk.scalar
            def _(s):
                for m in range(G):
                    s.wait_ge(sM.h, sM_b + m + 1)
                    s.copy(s1_sb[:, m * 512:(m + 1) * 512], ps_h[m % 2][:, :])
                    sA.inc(s.drain())

        with nc.Block() as blk:
            @blk.vector
            def _(v):
                v.memset(acc[:, :], 0.0)

        for e in range(nexperts):
            with nc.Block() as blk:
                @blk.gpsimd
                def _(g_, e=e):
                    ld.inc(g_.dma_start(
                        out=w1t[:, :].rearrange("p (kc o) -> p kc o",
                                                o=4 * C),
                        in_=w1_d[e * C:(e + 1) * C, :].rearrange(
                            "(kc p) o -> p kc o", p=128)), 16)
                    ld.wait(g_)

                @blk.sync
                def _(sync, e=e, n=n):
                    ld.inc(sync.dma_start(
                        out=sgc1[:, :],
                        in_=sg_d[e:e + 1, n * 512:(n + 1) * 512]), 16)
                    ld.wait(sync)

                @blk.tensor
                def _(tn, e=e):
                    ld.wait(tn)
                    sM.inc(tn.matmul(ps_w[:, :], onesr[:, :], sgc1[:, :],
                                     start=True, stop=True))

            sA_b, sM_b, sV_b = sA.v, sM.v, sV.v
            with nc.Block() as blk:
                @blk.tensor
                def _(tn, e=e):
                    for hc in range(32):
                        if hc >= 2:
                            tn.wait_ge(sV.h, sV_b + hc - 1)
                        for kc in range(G):
                            mm = tn.matmul(
                                ps_h[hc % 2][:, :],
                                w1t[:, kc * 4 * C + hc * 128:
                                    kc * 4 * C + hc * 128 + 128],
                                xnbfc[:, kc * 512:(kc + 1) * 512],
                                start=(kc == 0), stop=(kc == G - 1))
                        sM.inc(mm)

                @blk.scalar
                def _(s, e=e):
                    for hc in range(32):
                        s.wait_ge(sM.h, sM_b + hc + 1)
                        if hc >= 2:
                            s.wait_ge(sV.h, sV_b + hc - 1)
                        gl = glt[:, (hc % 2) * 512:(hc % 2) * 512 + 512]
                        s.activation(gl, ps_h[hc % 2][:, :], AF.Gelu,
                                     bias=b1t[:, e * 32 + hc:e * 32 + hc + 1])
                        sA.inc(s.drain())

                @blk.vector
                def _(v, e=e):
                    for hc in range(32):
                        v.wait_ge(sA.h, sA_b + hc + 1)
                        gl = glt[:, (hc % 2) * 512:(hc % 2) * 512 + 512]
                        v.tensor_mul(u_bf[:, hc * 512:(hc + 1) * 512], gl,
                                     ps_w[:, :])
                        sV.inc(v.drain())

            sM_b2, sV_b2, ld_b2 = sM.v, sV.v, ld.v
            with nc.Block() as blk:
                @blk.gpsimd
                def _(g_, e=e):
                    for m in range(G):
                        if m >= 2:
                            g_.wait_ge(sM.h, sM_b2 + m - 1)
                        ld.inc(g_.dma_start(
                            out=w2t[m % 2][:, :].rearrange(
                                "p (hc o) -> p hc o", o=128),
                            in_=w2_d[e * 4 * C:(e + 1) * 4 * C,
                                     m * 128:(m + 1) * 128].rearrange(
                                "(hc p) o -> p hc o", p=128)), 16)

                @blk.tensor
                def _(tn, e=e):
                    for m in range(G):
                        tn.wait_ge(ld.h, ld_b2 + 16 * (m + 1))
                        if m >= 2:
                            tn.wait_ge(sV.h, sV_b2 + m - 1)
                        for hc in range(32):
                            mm = tn.matmul(
                                ps_o[m % 2][:, :],
                                w2t[m % 2][:, hc * 128:(hc + 1) * 128],
                                u_bf[:, hc * 512:(hc + 1) * 512],
                                start=(hc == 0), stop=(hc == 31))
                        sM.inc(mm)

                @blk.vector
                def _(v, e=e):
                    for m in range(G):
                        v.wait_ge(sM.h, sM_b2 + m + 1)
                        sl = slice(m * 512, (m + 1) * 512)
                        v.tensor_add(acc[:, sl], acc[:, sl], ps_o[m % 2][:, :])
                        sV.inc(v.drain())

        sA_b, sM_b, sV_b = sA.v, sM.v, sV.v
        if 'comb' in SKIP:
            continue
        with nc.Block() as blk:
            @blk.tensor
            def _(tn):
                for m in range(G):
                    if m >= 1:
                        tn.wait_ge(sV.h, sV_b + m)
                    sM.inc(tn.matmul(ps_b2[:, :],
                                     b2t[:, m * 128:(m + 1) * 128],
                                     sgc8[:, :], start=True, stop=True))

            @blk.vector
            def _(v):
                for m in range(G):
                    v.wait_ge(sM.h, sM_b + m + 1)
                    sl = slice(m * 512, (m + 1) * 512)
                    v.tensor_add(acc[:, sl], acc[:, sl], ps_b2[:, :])
                    sV.inc(v.drain())

            @blk.scalar
            def _(s):
                for m in range(G):
                    s.wait_ge(sV.h, sV_b + m + 1)
                    sl = slice(m * 512, (m + 1) * 512)
                    s.activation(acc[:, sl], acc[:, sl], AF.Silu)
                    sA.inc(s.drain())

            @blk.vector
            def _(v):
                for m in range(G):
                    v.wait_ge(sA.h, sA_b + m + 1)
                    sl = slice(m * 512, (m + 1) * 512)
                    v.tensor_mul(s1_sb[:, sl], s1_sb[:, sl], acc[:, sl])
                    v.nop(cycle_cnt=32)
                    v.tensor_add(x1c[:, sl], x1c[:, sl], s1_sb[:, sl])
                    sV.inc(v.drain(), 2)

            @blk.sync
            def _(sync, n=n):
                sV.wait(sync)
                ld.inc(sync.dma_start(
                    out=out_d[:, n * 512:(n + 1) * 512].rearrange(
                        "(g p) t -> p g t", p=128),
                    in_=x1c[:, :].rearrange("p (g t) -> p g t", t=512)), 16)

    with nc.Block() as blk:
        @blk.sync
        def _(sync):
            ld.wait(sync)
    es.close()
    return nc




def stage_begin(nc, S):
    ld, sA, sM, sV = S
    with nc.Block() as blk:
        for eng in ("sync", "vector", "scalar", "tensor", "gpsimd"):
            def mk(e, _vals=(ld.v, sA.v, sM.v, sV.v)):
                e.wait_ge(ld.h, _vals[0])
                e.wait_ge(sA.h, _vals[1])
                e.wait_ge(sM.h, _vals[2])
                e.wait_ge(sV.h, _vals[3])
            getattr(blk, eng)(mk)


def build_fused():
    nc = bass.Bass("TRN2", target_bir_lowering=False)
    es = contextlib.ExitStack()
    D = {}

    def ein(name, shape, dt=F32):
        D[name] = nc.dram_tensor(name, shape, dt, kind="ExternalInput").ap()

    def intr(name, shape, dt=F32):
        D[name] = nc.dram_tensor(name, shape, dt, kind="Internal").ap()

    ein("xT", [C, T + 1])
    for nm in ("Wk", "Wv", "Wr", "Wo"):
        ein(nm, [C, C])
    for nm in ("ln1_s", "ln1_b", "tmk", "tmv", "tmr", "w_decay", "u_first",
               "w_router", "ln2_s", "ln2_b", "f_tmk", "f_tmr",
               "ln_g", "ln_b"):
        ein(nm, [128, G])
    ein("fWk", [C, 2 * C]); ein("fWr", [C, C]); ein("fWv", [2 * C, C])
    ein("W1", [C, 4 * C]); ein("W2", [4 * C, C])
    ein("b1", [128, 32]); ein("b2", [128, G]); ein("idn", [128, 128])
    ein("wq", [C, 512]); ein("wk", [C, 512]); ein("wv", [C, 512])
    ein("wr", [C, 8]); ein("br", [1, 8])
    ein("cosb", [128, T]); ein("sinb", [128, T]); ein("ps2", [128, 128])
    ein("mbeta", [1, 8]); ein("mask4", [32, 128])
    ein("wout", [C, C]); ein("mw1", [8 * C, 2 * C]); ein("mw2", [16 * C, C])
    ein("mb1", [128, 8 * 32]); ein("mb2", [8, C])
    ein("iota8", [128, 8]); ein("iotac", [128, 8]); ein("ut128", [128, 128])
    ein("trasht", [128, 1])
    ein("lt8", [8, 8]); ein("qcol", [24, 1]); ein("piota", [128, 1])
    ein("b1pr", [8 * 128, 16])

    intr("xn0", [C, T + 1]); intr("xn0bf", [C, T + 1], BF16)
    intr("ln1bf", [C, T + 1], BF16)
    intr("scores", [1, T]); intr("tmout", [C, T])
    intr("xmid", [C, T + 1]); intr("ln2bf", [C, T + 1], BF16)
    intr("cmf", [C, T]); intr("cmp", [C, T])
    intr("sout", [C, T + 1]); intr("x1", [C, T])
    intr("xn1bf", [C, T], BF16)
    intr("sgate", [8, T]); intr("slotint", [128, 64], I32)
    intr("offs", [1, 8])
    for nm in ("qrh", "krh", "sqh", "skh"):
        intr(nm, [512, T], BF16)
    intr("sktm", [T, 512], BF16); intr("vtm", [T, 512], BF16)
    intr("oh", [512, T], BF16); intr("og", [1024, T], BF16)
    intr("gbuf", [NSLOT + 128, C], BF16); intr("gout", [NSLOT + 128, C])
    intr("goutp", [NSLOT + 128, C])
    D["out"] = nc.dram_tensor("out", [C, T], F32, kind="ExternalOutput").ap()

    S = tuple(Sem(es.enter_context(nc.semaphore(n)))
              for n in ("ld", "sA", "sM", "sV"))
    cc = Sem(es.enter_context(nc.semaphore("cc")))
    cc3 = Sem(es.enter_context(nc.semaphore("cc3")))
    cc4 = Sem(es.enter_context(nc.semaphore("cc4")))

    emit_s1(nc, D, S)
    stage_begin(nc, S)
    emit_lif(nc, S, "l1_", D["tmout"], D["xn0"], D["xmid"], zero_col0=True)
    stage_begin(nc, S)
    emit_s3(nc, D, S)
    emit_pair_allreduce(nc, D, S, cc3, "cmp", "cmf")
    stage_begin(nc, S)
    emit_lif(nc, S, "l2_", D["cmf"], D["xmid"], D["sout"])
    stage_begin(nc, S)
    emit_s5(nc, D, S)
    stage_begin(nc, S)
    emit_s6(nc, D, S)
    stage_begin(nc, S)
    emit_s7(nc, D, S)
    stage_begin(nc, S)
    emit_exchange(nc, D, S, cc)
    emit_s8(nc, D, S, cc, cc4)

    ld = S[0]
    with nc.Block() as blk:
        @blk.sync
        def _(sync):
            sync.wait_ge(ld.h, ld.v)
            sync.wait_ge(cc.h, 1)
    es.close()
    return nc


def kernel(**inputs):
    inp = {k: np.asarray(v) for k, v in inputs.items()}
    nc = build_fused()

    wqkv = np.asarray(inp["L1_attn_w_qkv"], np.float32)
    cosb, sinb, ps2 = rope_tables()
    eye = np.eye(128, dtype=np.float32)
    mb = np.asarray(inp["L1_attn_mem_beta"], np.float32).reshape(1, 16)
    mask_add = np.where(np.tril(np.ones((32, 32), np.float32)) > 0, 0.0,
                        -1e30).astype(np.float32)
    iota8 = np.tile(np.arange(8, dtype=np.float32), (128, 1))
    iota8 = np.ascontiguousarray(iota8)
    iotac = np.ascontiguousarray(iota8 * float(CAP))
    ut128 = np.ascontiguousarray(
        np.triu(np.ones((128, 128), np.float32), 1))

    fWk_full = np.asarray(inp["L0_attn_fWk"], np.float32)
    fWv_full = np.asarray(inp["L0_attn_fWv"], np.float32)
    shared = {
        "fWr": np.asarray(inp["L0_attn_fWr"], np.float32),
        "ln2_s": vec_pg(inp["L0_attn_ln2_s"]),
        "ln2_b": vec_pg(inp["L0_attn_ln2_b"]),
        "f_tmk": vec_pg(inp["L0_attn_f_tmk"]),
        "f_tmr": vec_pg(inp["L0_attn_f_tmr"]),
        "W1": np.asarray(inp["L0_ffn_W1"], np.float32),
        "W2": np.asarray(inp["L0_ffn_W2"], np.float32),
        "b1": np.ascontiguousarray(
            np.asarray(inp["L0_ffn_b1"], np.float32).reshape(32, 128).T),
        "b2": vec_pg(inp["L0_ffn_b2"]),
        "idn": eye,
        "wr": np.asarray(inp["L1_ffn_wr"], np.float32),
        "br": np.asarray(inp["L1_ffn_br"], np.float32).reshape(1, 8),
        "cosb": cosb, "sinb": sinb, "ps2": ps2,
        "ln_g": vec_pg(inp["L1_attn_ln_g"]),
        "ln_b": vec_pg(inp["L1_attn_ln_b"]),
        "mask4": np.ascontiguousarray(np.tile(mask_add, (1, 4))),
        "wout": np.asarray(inp["L1_attn_w_out"], np.float32),
        "mb1": np.concatenate(
            [np.asarray(inp["L1_ffn_b1"], np.float32)[e].reshape(32, 128).T
             for e in range(8)], axis=1),
        "mb2": np.asarray(inp["L1_ffn_b2"], np.float32),
        "iota8": iota8, "iotac": iotac, "ut128": ut128,
        "trasht": np.ascontiguousarray(
            (NSLOT + np.arange(128, dtype=np.float32)).reshape(128, 1)),
        "lt8": np.ascontiguousarray(
            np.tril(np.ones((8, 8), np.float32), -1)),
        "qcol": np.ascontiguousarray(
            (512.0 * np.arange(24, dtype=np.float32)).reshape(24, 1)),
        "piota": np.ascontiguousarray(
            np.arange(128, dtype=np.float32).reshape(128, 1)),
    }
    mw1_full = np.asarray(inp["L1_ffn_w1"], np.float32).reshape(8 * C, 4 * C)
    mw2_full = np.asarray(inp["L1_ffn_w2"], np.float32)
    b1pr_full = np.ascontiguousarray(np.stack(
        [np.asarray(inp["L1_ffn_b1"], np.float32)[e].reshape(32, 128).T
         for e in range(8)], axis=0).reshape(8 * 128, 32))

    maps = []
    for c in range(8):
        b, hs = c // 2, c % 2
        m = dict(s1_inmap(inp, b))
        m.update(shared)
        m.update({
            "wq": np.ascontiguousarray(wqkv[:, hs * 512:hs * 512 + 512]),
            "wk": np.ascontiguousarray(
                wqkv[:, 1024 + hs * 512:1024 + hs * 512 + 512]),
            "wv": np.ascontiguousarray(
                wqkv[:, 2048 + hs * 512:2048 + hs * 512 + 512]),
            "mbeta": np.ascontiguousarray(mb[:, hs * 8:hs * 8 + 8]),
            "fWk": np.ascontiguousarray(fWk_full[:, hs * 2048:(hs + 1) * 2048]),
            "fWv": np.ascontiguousarray(fWv_full[hs * 2048:(hs + 1) * 2048, :]),
            "mw1": np.ascontiguousarray(
                mw1_full[:, hs * 2048:(hs + 1) * 2048]),
            "mw2": np.ascontiguousarray(
                mw2_full[:, hs * 2048:(hs + 1) * 2048, :].reshape(
                    8 * 2048, C)),
            "b1pr": np.ascontiguousarray(
                b1pr_full[:, hs * 16:(hs + 1) * 16]),
        })
        maps.append(m)

    res = run_bass_kernel_spmd(nc, maps, core_ids=list(range(8))).results
    out = np.empty((B, T, C), np.float32)
    for b in range(B):
        out[b] = res[2 * b]["out"].T
    return out


# version 08
# speedup vs baseline: 1.0555x; 1.0095x over previous
import contextlib
import numpy as np
import concourse.bass as bass
import concourse.mybir as mybir
from concourse.bass_utils import run_bass_kernel_spmd

F32 = mybir.dt.float32
BF16 = mybir.dt.bfloat16
ALU = mybir.AluOpType
AF = mybir.ActivationFunctionType
AX = mybir.AxisListType.X

B, T, C = 4, 8192, 1024
I32 = mybir.dt.int32
CAP = 1536
NSLOT = 8 * CAP
TH = T // 2
G = C // 128
NC512 = T // 512
PAIRS = [[0, 1], [2, 3], [4, 5], [6, 7]]
EPS = 1e-5


class Sem:

    def __init__(self, h):
        self.h = h
        self.v = 0

    def inc(self, instr, k=1):
        instr.then_inc(self.h, k)
        self.v += k
        return instr

    def wait(self, eng):
        eng.wait_ge(self.h, self.v)


def interleave(*chains):
    import itertools
    for ops in itertools.zip_longest(*chains):
        for op in ops:
            if op is not None:
                op()


def vec_pg(a):
    return np.ascontiguousarray(np.asarray(a, np.float32).reshape(G, 128).T)



def emit_s1(nc, D, S):
    es = contextlib.ExitStack()
    xT = D["xT"]; Wk = D["Wk"]; Wv = D["Wv"]; Wr = D["Wr"]; Wo = D["Wo"]
    vecs = {nm: D[nm] for nm in ["ln1_s", "ln1_b", "tmk", "tmv", "tmr",
                                 "w_decay", "u_first", "w_router"]}
    xn0_d, xn0bf_d, ln1bf_d = D["xn0"], D["xn0bf"], D["ln1bf"]
    scores_d, tmout_d = D["scores"], D["tmout"]

    def sb(stack, name, shape, dtype=F32):
        return stack.enter_context(nc.sbuf_tensor("s1_" + name, shape, dtype))

    def psum(stack, name, shape, dtype=F32):
        return stack.enter_context(nc.psum_tensor("s1_" + name, shape, dtype))

    vt = {nm: sb(es, "v_" + nm, [128, G]) for nm in vecs}
    aux = sb(es, "aux", [128, 4 * G])
    lam_tile = sb(es, "lam_tile", [128, G * 512])
    onesc = sb(es, "onesc", [128, 1])
    onesr = sb(es, "onesr", [1, 128])
    zeros1 = sb(es, "zeros1", [128, G], BF16)
    carry = sb(es, "carry", [128, 4 * G])
    wkf = sb(es, "wkf", [128, G * 2 * C], BF16)
    wvf = sb(es, "wvf", [128, G * 2 * C], BF16)
    wrf = sb(es, "wrf", [128, G * 2 * C], BF16)
    wo_b = sb(es, "wo_b", [128, G * C], BF16)
    cst_eps = sb(es, "cst_eps", [128, 1])

    ld, sA, sM, sV = S

    with nc.Block() as blk:
        @blk.sync
        def _(sync):
            for nm in vecs:
                ld.inc(sync.dma_start(out=vt[nm][:, :], in_=vecs[nm]), 16)
            ld.wait(sync)

        @blk.gpsimd
        def _(g_):
            ld.inc(g_.dma_start(
                out=wo_b[:, :].rearrange("p (kc o) -> p kc o", o=C),
                in_=Wo.rearrange("(kc p) o -> p kc o", p=128)), 16)

    with nc.Block() as blk:
        @blk.vector
        def _(v):
            v.memset(onesc[:, :], 1.0)
            v.memset(onesr[:, :], 1.0)
            v.memset(zeros1[:, :], 0.0)
            v.memset(carry[:, :], 0.0)
            v.memset(lam_tile[:, :], 1.0)
            v.memset(cst_eps[:, :], EPS)
            for i, nm in enumerate(["tmk", "tmv", "tmr"]):
                v.tensor_scalar(aux[:, i * G:(i + 1) * G], vt[nm][:, :], -1.0, 1.0,
                                ALU.mult, ALU.add)

        @blk.scalar
        def _(s):
            s.activation(aux[:, 3 * G:4 * G], vt["u_first"][:, :], AF.Exp)
            s.activation(vt["w_decay"][:, :], vt["w_decay"][:, :], AF.Exp)
            s.drain()
            s.activation(vt["w_decay"][:, :], vt["w_decay"][:, :], AF.Exp,
                         scale=-1.0)

    with nc.Block() as blk:
        @blk.vector
        def _(v):
            for g in range(G):
                sl = lam_tile[:, g * 512:(g + 1) * 512]
                v.tensor_scalar_mul(sl, sl, vt["w_decay"][:, g:g + 1])

    with contextlib.ExitStack() as es0:
        wtmp = sb(es0, "wtmp", [128, C])
        for proj, (wd, dst, mixv) in enumerate(
                [(Wk, wkf, "tmk"), (Wv, wvf, "tmv"), (Wr, wrf, "tmr")]):
            for kc in range(G):
                with nc.Block() as blk:
                    @blk.sync
                    def _(sync, wd=wd, kc=kc):
                        ld.inc(sync.dma_start(
                            out=wtmp[:, :],
                            in_=wd[kc * 128:(kc + 1) * 128, :]), 16)
                        ld.wait(sync)

                    @blk.vector
                    def _(v, dst=dst, mixv=mixv, proj=proj, kc=kc):
                        ld.wait(v)
                        base = kc * 2 * C
                        v.tensor_scalar_mul(dst[:, base:base + C], wtmp[:, :],
                                            vt[mixv][:, kc:kc + 1])
                        v.tensor_scalar_mul(
                            dst[:, base + C:base + 2 * C], wtmp[:, :],
                            aux[:, proj * G + kc:proj * G + kc + 1])

    with nc.Block() as blk:
        @blk.sync
        def _(sync):
            ld.inc(sync.dma_start(
                out=ln1bf_d[:, 0:1].rearrange("(g p) t -> p g t", p=128),
                in_=zeros1[:, :].rearrange("p (g o) -> p g o", o=1),
                allow_slow_non_contiguous=True), 16)

    with contextlib.ExitStack() as es1:
        xc = sb(es1, "xc", [128, G * 512])
        sqf = sb(es1, "sqf", [128, G * 512])
        xnc = sb(es1, "xnc", [128, G * 512])
        xnbfc = sb(es1, "xnbfc", [128, G * 512], BF16)
        ln1cw = sb(es1, "ln1cw", [128, G * 512], BF16)
        row = sb(es1, "row", [1, 5 * 512])
        ps_sum = psum(es1, "ps_sum", [1, 512])
        ps_sq = psum(es1, "ps_sq", [1, 512])
        ps_rb = psum(es1, "ps_rb", [128, 512])
        ps_mrb = psum(es1, "ps_mrb", [128, 512])
        ps_sc = psum(es1, "ps_sc", [1, 512])

        def ln_block(src, out_cb, scores_too):
            with nc.Block() as blk:
                @blk.scalar
                def _(s):
                    for g in range(G):
                        s.activation(sqf[:, g * 512:(g + 1) * 512],
                                     src[:, g * 512:(g + 1) * 512], AF.Square)
                    sA.inc(s.drain())

                @blk.tensor
                def _(tn):
                    for g in range(G):
                        tn.matmul(ps_sum[:, :], onesc[:, 0:1],
                                  src[:, g * 512:(g + 1) * 512],
                                  start=(g == 0), stop=(g == G - 1))
                    sA.wait(tn)
                    for g in range(G):
                        last = (g == G - 1) and not scores_too
                        mm = tn.matmul(ps_sq[:, :], onesc[:, 0:1],
                                       sqf[:, g * 512:(g + 1) * 512],
                                       start=(g == 0), stop=(g == G - 1))
                    if scores_too:
                        for g in range(G):
                            mm = tn.matmul(ps_sc[:, :],
                                           vt["w_router"][:, g:g + 1],
                                           src[:, g * 512:(g + 1) * 512],
                                           start=(g == 0), stop=(g == G - 1))
                    sM.inc(mm)

                @blk.vector
                def _(v):
                    sM.wait(v)
                    mean, var, rstd, mr = (row[:, 0:512], row[:, 512:1024],
                                           row[:, 1024:1536], row[:, 1536:2048])
                    v.tensor_scalar_mul(mean, ps_sum[:, :], 1.0 / C)
                    v.nop(cycle_cnt=16)
                    v.tensor_mul(var, mean, mean)
                    v.nop(cycle_cnt=16)
                    v.scalar_tensor_tensor(var, ps_sq[:, :], 1.0 / C, var,
                                           ALU.mult, ALU.subtract)
                    sV.inc(v.drain())

                @blk.scalar
                def _(s):
                    sV.wait(s)
                    s.activation(row[:, 1024:1536], row[:, 512:1024], AF.Sqrt,
                                 bias=cst_eps[0:1, 0:1])
                    sA.inc(s.drain())

                @blk.vector
                def _(v):
                    sA.wait(v)
                    v.reciprocal(row[:, 1024:1536], row[:, 1024:1536])
                    v.nop(cycle_cnt=32)
                    v.tensor_mul(row[:, 1536:2048], row[:, 0:512],
                                 row[:, 1024:1536])
                    sV.inc(v.drain())

                @blk.tensor
                def _(tn):
                    sV.wait(tn)
                    tn.matmul(ps_rb[:, :], onesr[:, :], row[:, 1024:1536],
                              start=True, stop=True)
                    sM.inc(tn.matmul(ps_mrb[:, :], onesr[:, :],
                                     row[:, 1536:2048], start=True, stop=True))

                @blk.vector
                def _(v):
                    sM.wait(v)
                    for g in range(0, G, 2):
                        interleave(out_cb(g), out_cb(g + 1))
                    sV.inc(v.drain())

        for n in range(NC512):
            c0 = 1 + n * 512
            with nc.Block() as blk:
                @blk.sync
                def _(sync):
                    ld.inc(sync.dma_start(
                        out=xc[:, :].rearrange("p (g t) -> p g t", t=512),
                        in_=xT[:, c0:c0 + 512].rearrange("(g p) t -> p g t",
                                                         p=128)), 16)
                    ld.wait(sync)

            def cb1(g):
                def f(v=None):
                    pass
                xg = xc[:, g * 512:(g + 1) * 512]
                og = xnc[:, g * 512:(g + 1) * 512]
                bg = xnbfc[:, g * 512:(g + 1) * 512]
                vv = nc.vector
                return [
                    lambda: vv.tensor_mul(og, xg, ps_rb[:, :]),
                    lambda: vv.tensor_sub(og, og, ps_mrb[:, :]),
                    lambda: vv.tensor_copy(bg, og),
                ]
            ln_block(xc, cb1, scores_too=False)

            def cb2(g):
                xg = xnc[:, g * 512:(g + 1) * 512]
                tg = xc[:, g * 512:(g + 1) * 512]
                og = ln1cw[:, g * 512:(g + 1) * 512]
                vv = nc.vector
                return [
                    lambda: vv.tensor_mul(tg, xg, ps_rb[:, :]),
                    lambda: vv.tensor_sub(tg, tg, ps_mrb[:, :]),
                    lambda: vv.tensor_scalar(og, tg, vt["ln1_s"][:, g:g + 1],
                                             vt["ln1_b"][:, g:g + 1],
                                             ALU.mult, ALU.add),
                ]
            ln_block(xnc, cb2, scores_too=True)

            with nc.Block() as blk:
                @blk.scalar
                def _(s):
                    s.copy(row[:, 2048:2560], ps_sc[:, :])
                    sA.inc(s.drain())

                @blk.sync
                def _(sync):
                    sA.wait(sync)
                    ld.inc(sync.dma_start(
                        out=xn0_d[:, c0:c0 + 512].rearrange(
                            "(g p) t -> p g t", p=128),
                        in_=xnc[:, :].rearrange("p (g t) -> p g t", t=512)), 16)
                    ld.inc(sync.dma_start(
                        out=xn0bf_d[:, c0:c0 + 512].rearrange(
                            "(g p) t -> p g t", p=128),
                        in_=xnbfc[:, :].rearrange("p (g t) -> p g t", t=512)), 16)
                    ld.inc(sync.dma_start(
                        out=ln1bf_d[:, c0:c0 + 512].rearrange(
                            "(g p) t -> p g t", p=128),
                        in_=ln1cw[:, :].rearrange("p (g t) -> p g t", t=512)), 16)
                    ld.inc(sync.dma_start(
                        out=scores_d[:, n * 512:(n + 1) * 512],
                        in_=row[:, 2048:2560]), 16)
                    ld.wait(sync)

    with contextlib.ExitStack() as es2:
        ln1c = sb(es2, "ln1c", [128, G * 513], BF16)
        z_sb = sb(es2, "z_sb", [128, G * 512], BF16)
        tm_sb = sb(es2, "tm_sb", [128, 2 * 512])
        wk_t = {}
        par_t = {}
        for nm in ["ek", "vv", "sr", "ekv", "euk", "ukv"]:
            par_t[nm] = [sb(es2, nm + "A", [128, 512]),
                         sb(es2, nm + "B", [128, 512])]
        PQ = {}
        for nm in ["P", "Q"]:
            PQ[nm] = [sb(es2, nm + "A", [128, 513]),
                      sb(es2, nm + "B", [128, 513])]
        ps_k = [psum(es2, f"ps_k{i}", [128, 512]) for i in range(2)]
        ps_v = [psum(es2, f"ps_v{i}", [128, 512]) for i in range(2)]
        ps_r = [psum(es2, f"ps_r{i}", [128, 512]) for i in range(2)]

        for n in range(NC512):
            c0 = 1 + n * 512
            with nc.Block() as blk:
                @blk.sync
                def _(sync):
                    ld.inc(sync.dma_start(
                        out=ln1c[:, :].rearrange("p (g t) -> p g t", t=513),
                        in_=ln1bf_d[:, c0 - 1:c0 + 512].rearrange(
                            "(g p) t -> p g t", p=128)), 16)
                    ld.wait(sync)

            sA_b, sM_b, sV_b = sA.v, sM.v, sV.v
            with nc.Block() as blk:
                @blk.tensor
                def _(tn):
                    for g in range(G):
                        if g >= 2:
                            tn.wait_ge(sA.h, sA_b + g - 1)
                        for wf, pp in [(wkf, ps_k), (wvf, ps_v), (wrf, ps_r)]:
                            for kc in range(G):
                                lw = wf[:, (kc * 2) * C + g * 128:
                                        (kc * 2) * C + g * 128 + 128]
                                ls = wf[:, (kc * 2 + 1) * C + g * 128:
                                        (kc * 2 + 1) * C + g * 128 + 128]
                                cur = ln1c[:, kc * 513 + 1:kc * 513 + 513]
                                shf = ln1c[:, kc * 513 + 0:kc * 513 + 512]
                                tn.matmul(pp[g % 2][:, :], lw, cur,
                                          start=(kc == 0), stop=False)
                                mm = tn.matmul(pp[g % 2][:, :], ls, shf,
                                               start=False, stop=(kc == G - 1))
                        sM.inc(mm)

                @blk.scalar
                def _(s):
                    for g in range(G):
                        par = g % 2
                        s.wait_ge(sM.h, sM_b + g + 1)
                        if g >= 2:
                            s.wait_ge(sV.h, sV_b + (g - 2) // 2 + 1)
                        s.activation(par_t["ek"][par][:, :], ps_k[par][:, :],
                                     AF.Exp)
                        s.copy(par_t["vv"][par][:, :], ps_v[par][:, :])
                        s.activation(par_t["sr"][par][:, :], ps_r[par][:, :],
                                     AF.Sigmoid)
                        sA.inc(s.drain())

                @blk.vector
                def _(v):
                    for g0 in range(0, G, 2):
                        v.wait_ge(sA.h, sA_b + g0 + 2)

                        def mk(g):
                            par = g % 2
                            ekg = par_t["ek"][par]
                            vg = par_t["vv"][par]
                            srg = par_t["sr"][par]
                            ekvg = par_t["ekv"][par]
                            eukg = par_t["euk"][par]
                            ukvg = par_t["ukv"][par]
                            Pg, Qg = PQ["P"][par], PQ["Q"][par]
                            lamg = lam_tile[:, g * 512:(g + 1) * 512]
                            zg = z_sb[:, g * 512:(g + 1) * 512]
                            cP = carry[:, g:g + 1]
                            cQ = carry[:, 2 * G + g:2 * G + g + 1]
                            return [
                                lambda: v.tensor_mul(ekvg[:, :], ekg[:, :],
                                                     vg[:, :]),
                                lambda: v.tensor_scalar_mul(
                                    eukg[:, :], ekg[:, :],
                                    aux[:, 3 * G + g:3 * G + g + 1]),
                                lambda: v.tensor_mul(ukvg[:, :], eukg[:, :],
                                                     vg[:, :]),
                                lambda: v.tensor_copy(Pg[:, 0:1], cP),
                                lambda: v.tensor_copy(Qg[:, 0:1], cQ),
                                lambda: v.tensor_tensor_scan(
                                    Pg[:, 1:513], lamg, ekvg[:, :], cP,
                                    ALU.mult, ALU.add),
                                lambda: v.tensor_tensor_scan(
                                    Qg[:, 1:513], lamg, ekg[:, :], cQ,
                                    ALU.mult, ALU.add),
                                lambda: v.tensor_add(ukvg[:, :], ukvg[:, :],
                                                     Pg[:, 0:512]),
                                lambda: v.tensor_add(eukg[:, :], eukg[:, :],
                                                     Qg[:, 0:512]),
                                lambda: v.tensor_copy(cP, Pg[:, 512:513]),
                                lambda: v.tensor_copy(cQ, Qg[:, 512:513]),
                                lambda: v.reciprocal(eukg[:, :], eukg[:, :]),
                                lambda: v.tensor_mul(ukvg[:, :], ukvg[:, :],
                                                     eukg[:, :]),
                                lambda: v.tensor_mul(zg, ukvg[:, :], srg[:, :]),
                            ]
                        interleave(mk(g0), mk(g0 + 1))
                        sV.inc(v.drain())

            sA_b, sM_b = sA.v, sM.v
            with nc.Block() as blk:
                @blk.tensor
                def _(tn):
                    for m in range(G):
                        if m >= 2:
                            tn.wait_ge(sA.h, sA_b + m - 1)
                        for g in range(G):
                            lw = wo_b[:, g * C + m * 128:g * C + m * 128 + 128]
                            mm = tn.matmul(ps_k[m % 2][:, :], lw,
                                           z_sb[:, g * 512:(g + 1) * 512],
                                           start=(g == 0), stop=(g == G - 1))
                        sM.inc(mm)

                @blk.scalar
                def _(s):
                    for m in range(G):
                        s.wait_ge(sM.h, sM_b + m + 1)
                        if m >= 2:
                            s.wait_ge(ld.h, ld.v + 16 * (m - 1))
                        s.copy(tm_sb[:, (m % 2) * 512:(m % 2) * 512 + 512],
                               ps_k[m % 2][:, :])
                        sA.inc(s.drain())

                @blk.sync
                def _(sync):
                    for m in range(G):
                        sync.wait_ge(sA.h, sA_b + m + 1)
                        ld.inc(sync.dma_start(
                            out=tmout_d[m * 128:(m + 1) * 128,
                                        n * 512:(n + 1) * 512],
                            in_=tm_sb[:, (m % 2) * 512:(m % 2) * 512 + 512]), 16)
                ld.v = ld.v

    with nc.Block() as blk:
        @blk.sync
        def _(sync):
            ld.wait(sync)

    es.close()


def s1_inmap(inp, b):
    x = np.zeros((C, T + 1), np.float32)
    x[:, 1:] = np.asarray(inp["L0_attn_x"][b], np.float32).T
    return {
        "xT": x,
        "Wk": np.asarray(inp["L0_attn_Wk"], np.float32),
        "Wv": np.asarray(inp["L0_attn_Wv"], np.float32),
        "Wr": np.asarray(inp["L0_attn_Wr"], np.float32),
        "Wo": np.asarray(inp["L0_attn_Wo"], np.float32),
        "ln1_s": vec_pg(inp["L0_attn_ln1_s"]),
        "ln1_b": vec_pg(inp["L0_attn_ln1_b"]),
        "tmk": vec_pg(inp["L0_attn_tmk"]),
        "tmv": vec_pg(inp["L0_attn_tmv"]),
        "tmr": vec_pg(inp["L0_attn_tmr"]),
        "w_decay": vec_pg(inp["L0_attn_w_decay"]),
        "u_first": vec_pg(inp["L0_attn_u_first"]),
        "w_router": vec_pg(inp["L0_ffn_w_router"]),
    }





def emit_lif(nc, S, base_name, src_d, base_d, out_d, zero_col0=False):
    es = contextlib.ExitStack()

    def sb(name, shape, dtype=F32):
        return es.enter_context(nc.sbuf_tensor(base_name + name, shape, dtype))

    CH = 1024
    NCH = T // CH
    tmc = [sb("tmc0", [128, G * CH]), sb("tmc1", [128, G * CH])]
    wbuf = sb("wbuf", [128, G * CH])
    bsc = sb("bsc", [128, G * CH])
    lifu = sb("lifu", [128, G])
    ld, sA, sM, sV = S
    if zero_col0:
        zt = sb("zc", [128, G])
        with nc.Block() as blk:
            @blk.vector
            def _(v):
                v.memset(zt[:, :], 0.0)
                sV.inc(v.drain())

            @blk.gpsimd
            def _(g_):
                sV.wait(g_)
                ld.inc(g_.dma_start(
                    out=out_d[:, 0:1].rearrange("(g p) t -> p g t", p=128),
                    in_=zt[:, :].rearrange("p (g o) -> p g o", o=1),
                    allow_slow_non_contiguous=True), 16)

    with nc.Block() as blk:
        @blk.vector
        def _(v):
            v.memset(lifu[:, :], 0.0)

    with nc.Block() as blk:
        @blk.sync
        def _(sync):
            ld.inc(sync.dma_start(
                out=tmc[0][:, :].rearrange("p (g t) -> p g t", t=CH),
                in_=src_d[:, 0:CH].rearrange("(g p) t -> p g t", p=128)), 16)

    for ch in range(NCH):
        buf = tmc[ch % 2]
        nxt = tmc[(ch + 1) % 2]
        ld_target = ld.v
        with nc.Block() as blk:
            @blk.sync
            def _(sync, ch=ch, nxt=nxt):
                sync.wait_ge(ld.h, ld.v)
                if ch + 1 < NCH:
                    ld.inc(sync.dma_start(
                        out=nxt[:, :].rearrange("p (g t) -> p g t", t=CH),
                        in_=src_d[:, (ch + 1) * CH:(ch + 2) * CH].rearrange(
                            "(g p) t -> p g t", p=128)), 16)
                ld.inc(sync.dma_start(
                    out=bsc[:, :].rearrange("p (g t) -> p g t", t=CH),
                    in_=base_d[:, 1 + ch * CH:1 + (ch + 1) * CH].rearrange(
                        "(g p) t -> p g t", p=128)), 16)

            @blk.vector
            def _(v, buf=buf, ch=ch, ld_target=ld_target):
                v.wait_ge(ld.h, ld_target)
                bt = buf[:, :].rearrange("p (g t) -> p g t", t=CH)
                wt = wbuf[:, :].rearrange("p (g t) -> p g t", t=CH)
                for t in range(CH):
                    v.scalar_tensor_tensor(wt[:, :, t], lifu[:, :], 0.5,
                                           bt[:, :, t], ALU.mult, ALU.add)
                    v.nop(cycle_cnt=16)
                    v.scalar_tensor_tensor(lifu[:, :], wt[:, :, t], 2.0,
                                           wt[:, :, t], ALU.is_lt, ALU.mult)
                    v.nop(cycle_cnt=16)
                ld.wait(v)
                for g in range(0, G, 2):
                    for gg in (g, g + 1):
                        v.scalar_tensor_tensor(
                            bsc[:, gg * CH:(gg + 1) * CH],
                            wbuf[:, gg * CH:(gg + 1) * CH], 2.0,
                            bsc[:, gg * CH:(gg + 1) * CH],
                            ALU.is_ge, ALU.add)
                sV.inc(v.drain())

            @blk.gpsimd
            def _(g_, ch=ch):
                sV.wait(g_)
                ld.inc(g_.dma_start(
                    out=out_d[:, 1 + ch * CH:1 + (ch + 1) * CH].rearrange(
                        "(g p) t -> p g t", p=128),
                    in_=bsc[:, :].rearrange("p (g t) -> p g t", t=CH)), 16)

    with nc.Block() as blk:
        @blk.sync
        def _(sync):
            ld.wait(sync)
    es.close()





def emit_s3(nc, D, S):
    es = contextlib.ExitStack()
    th0 = 0

    xmid_d = D["xmid"]
    fWk_d, fWr_d, fWv_d = D["fWk"], D["fWr"], D["fWv"]
    vecs = {nm: D[nm] for nm in ["ln2_s", "ln2_b", "f_tmk", "f_tmr"]}
    cmh_d = D["cmp"]
    ln2bf_d = D["ln2bf"]

    def sb(stack, name, shape, dtype=F32):
        return stack.enter_context(nc.sbuf_tensor("s3_" + name, shape, dtype))

    def psum(stack, name, shape, dtype=F32):
        return stack.enter_context(nc.psum_tensor("s3_" + name, shape, dtype))

    vt = {nm: sb(es, "v_" + nm, [128, G]) for nm in vecs}
    aux = sb(es, "aux", [128, 2 * G])
    onesc = sb(es, "onesc", [128, 1])
    onesr = sb(es, "onesr", [1, 128])
    cst_eps = sb(es, "cst_eps", [128, 1])
    zeros1 = sb(es, "zeros1", [128, G], BF16)
    fwr_b = sb(es, "fwr_b", [128, G * C], BF16)

    ld, sA, sM, sV = S

    with nc.Block() as blk:
        @blk.sync
        def _(sync):
            for nm in vecs:
                ld.inc(sync.dma_start(out=vt[nm][:, :], in_=vecs[nm]), 16)
            ld.wait(sync)

        @blk.gpsimd
        def _(g_):
            ld.inc(g_.dma_start(
                out=fwr_b[:, :].rearrange("p (kc o) -> p kc o", o=C),
                in_=fWr_d.rearrange("(kc p) o -> p kc o", p=128)), 16)

    with nc.Block() as blk:
        @blk.vector
        def _(v):
            v.memset(onesc[:, :], 1.0)
            v.memset(onesr[:, :], 1.0)
            v.memset(cst_eps[:, :], EPS)
            v.memset(zeros1[:, :], 0.0)
            for i, nm in enumerate(["f_tmk", "f_tmr"]):
                v.tensor_scalar(aux[:, i * G:(i + 1) * G], vt[nm][:, :], -1.0,
                                1.0, ALU.mult, ALU.add)

    with contextlib.ExitStack() as es1:
        xc = sb(es1, "xc", [128, G * 512])
        sqf = sb(es1, "sqf", [128, G * 512])
        lnw = sb(es1, "lnw", [128, G * 512], BF16)
        row = sb(es1, "row", [1, 4 * 512])
        ps_sum = psum(es1, "ps_sum", [1, 512])
        ps_sq = psum(es1, "ps_sq", [1, 512])
        ps_rb = psum(es1, "ps_rb", [128, 512])
        ps_mrb = psum(es1, "ps_mrb", [128, 512])

        widths = [512] * (T // 512) + [1]
        col = th0
        for n, W in enumerate(widths):
            lcol = col - th0
            with nc.Block() as blk:
                @blk.sync
                def _(sync, col=col, W=W):
                    sync.wait_ge(ld.h, ld.v)
                    ld.inc(sync.dma_start(
                        out=xc[:, 0:G * W].rearrange("p (g t) -> p g t", t=W),
                        in_=xmid_d[:, col:col + W].rearrange(
                            "(g p) t -> p g t", p=128),
                        allow_slow_non_contiguous=(W == 1)), 16)
                    ld.wait(sync)

            with nc.Block() as blk:
                @blk.scalar
                def _(s, W=W):
                    for g in range(G):
                        s.activation(sqf[:, g * W:(g + 1) * W],
                                     xc[:, g * W:(g + 1) * W], AF.Square)
                    sA.inc(s.drain())

                @blk.tensor
                def _(tn, W=W):
                    for g in range(G):
                        tn.matmul(ps_sum[:, 0:W], onesc[:, 0:1],
                                  xc[:, g * W:(g + 1) * W],
                                  start=(g == 0), stop=(g == G - 1))
                    sA.wait(tn)
                    for g in range(G):
                        mm = tn.matmul(ps_sq[:, 0:W], onesc[:, 0:1],
                                       sqf[:, g * W:(g + 1) * W],
                                       start=(g == 0), stop=(g == G - 1))
                    sM.inc(mm)

                @blk.vector
                def _(v, W=W):
                    sM.wait(v)
                    mean, var = row[:, 0:W], row[:, 512:512 + W]
                    v.tensor_scalar_mul(mean, ps_sum[:, 0:W], 1.0 / C)
                    v.nop(cycle_cnt=16)
                    v.tensor_mul(var, mean, mean)
                    v.nop(cycle_cnt=16)
                    v.scalar_tensor_tensor(var, ps_sq[:, 0:W], 1.0 / C, var,
                                           ALU.mult, ALU.subtract)
                    sV.inc(v.drain())

                @blk.scalar
                def _(s, W=W):
                    sV.wait(s)
                    s.activation(row[:, 1024:1024 + W], row[:, 512:512 + W],
                                 AF.Sqrt, bias=cst_eps[0:1, 0:1])
                    sA.inc(s.drain())

                @blk.vector
                def _(v, W=W):
                    sA.wait(v)
                    v.reciprocal(row[:, 1024:1024 + W], row[:, 1024:1024 + W])
                    v.nop(cycle_cnt=32)
                    v.tensor_mul(row[:, 1536:1536 + W], row[:, 0:W],
                                 row[:, 1024:1024 + W])
                    sV.inc(v.drain())

                @blk.tensor
                def _(tn, W=W):
                    sV.wait(tn)
                    tn.matmul(ps_rb[:, 0:W], onesr[:, :], row[:, 1024:1024 + W],
                              start=True, stop=True)
                    sM.inc(tn.matmul(ps_mrb[:, 0:W], onesr[:, :],
                                     row[:, 1536:1536 + W], start=True,
                                     stop=True))

                @blk.vector
                def _(v, W=W):
                    sM.wait(v)

                    def mk(g):
                        xg = xc[:, g * W:(g + 1) * W]
                        tg = sqf[:, g * W:(g + 1) * W]
                        og = lnw[:, g * W:(g + 1) * W]
                        return [
                            lambda: v.tensor_mul(tg, xg, ps_rb[:, 0:W]),
                            lambda: v.tensor_sub(tg, tg, ps_mrb[:, 0:W]),
                            lambda: v.tensor_scalar(og, tg,
                                                    vt["ln2_s"][:, g:g + 1],
                                                    vt["ln2_b"][:, g:g + 1],
                                                    ALU.mult, ALU.add),
                        ]
                    for g in range(0, G, 2):
                        interleave(mk(g), mk(g + 1))
                    sV.inc(v.drain())

                @blk.sync
                def _(sync, W=W, lcol=lcol):
                    sV.wait(sync)
                    ld.inc(sync.dma_start(
                        out=ln2bf_d[:, lcol:lcol + W].rearrange(
                            "(g p) t -> p g t", p=128),
                        in_=lnw[:, 0:G * W].rearrange("p (g t) -> p g t", t=W),
                        allow_slow_non_contiguous=(W == 1)), 16)
            col += W

    if True:
        with nc.Block() as blk:
            @blk.sync
            def _(sync):
                ld.inc(sync.dma_start(
                    out=ln2bf_d[:, 0:1].rearrange("(g p) t -> p g t", p=128),
                    in_=zeros1[:, :].rearrange("p (g o) -> p g o", o=1),
                    allow_slow_non_contiguous=True), 16)

    with contextlib.ExitStack() as es2:
        fwk_b = sb(es2, "fwk_b", [128, G * 2 * C], BF16)
        fwv_t = [sb(es2, "fwv_t0", [128, 16 * 128], BF16),
                 sb(es2, "fwv_t1", [128, 16 * 128], BF16)]
        ln2c = sb(es2, "ln2c", [128, G * 513], BF16)
        xk2 = sb(es2, "xk2", [128, G * 512], BF16)
        xr2 = sb(es2, "xr2", [128, G * 512], BF16)
        sig = sb(es2, "sig", [128, G * 512])
        h2 = sb(es2, "h2", [128, 16 * 512], BF16)
        relu_t = sb(es2, "relu_t", [128, 2 * 512])
        cmo = sb(es2, "cmo", [128, G * 512])
        ps_h = [psum(es2, f"ps_h{i}", [128, 512]) for i in range(2)]
        ps_o = [psum(es2, f"ps_o{i}", [128, 512]) for i in range(2)]

        with nc.Block() as blk:
            @blk.gpsimd
            def _(g_):
                ld.inc(g_.dma_start(
                    out=fwk_b[:, :].rearrange("p (kc o) -> p kc o", o=2 * C),
                    in_=fWk_d.rearrange("(kc p) o -> p kc o", p=128)), 16)
                ld.wait(g_)

        for n in range(T // 512):
            with nc.Block() as blk:
                @blk.sync
                def _(sync, n=n):
                    sync.wait_ge(ld.h, ld.v)
                    ld.inc(sync.dma_start(
                        out=ln2c[:, :].rearrange("p (g t) -> p g t", t=513),
                        in_=ln2bf_d[:, n * 512:n * 512 + 513].rearrange(
                            "(g p) t -> p g t", p=128)), 16)
                    ld.wait(sync)

            sA_b, sM_b, sV_b = sA.v, sM.v, sV.v
            with nc.Block() as blk:
                @blk.vector
                def _(v):
                    def mk(g, dst, mixv):
                        cur = ln2c[:, g * 513 + 1:g * 513 + 513]
                        shf = ln2c[:, g * 513 + 0:g * 513 + 512]
                        og = dst[:, g * 512:(g + 1) * 512]
                        tg = relu_t[:, (g % 2) * 512:(g % 2) * 512 + 512]
                        return [
                            lambda: v.tensor_tensor(tg, cur, shf, ALU.subtract),
                            lambda: v.scalar_tensor_tensor(
                                og, tg, vt[mixv][:, g:g + 1], shf,
                                ALU.mult, ALU.add),
                        ]
                    for g in range(0, G, 2):
                        interleave(mk(g, xk2, "f_tmk"), mk(g + 1, xk2, "f_tmk"))
                    for g in range(0, G, 2):
                        interleave(mk(g, xr2, "f_tmr"), mk(g + 1, xr2, "f_tmr"))
                    sV.inc(v.drain())

                @blk.tensor
                def _(tn):
                    sV.wait(tn)
                    for m in range(G):
                        if m >= 2:
                            tn.wait_ge(sA.h, sA_b + m - 1)
                        for kc in range(G):
                            mm = tn.matmul(
                                ps_o[m % 2][:, :],
                                fwr_b[:, kc * C + m * 128:kc * C + m * 128 + 128],
                                xr2[:, kc * 512:(kc + 1) * 512],
                                start=(kc == 0), stop=(kc == G - 1))
                        sM.inc(mm)

                @blk.scalar
                def _(s):
                    for m in range(G):
                        s.wait_ge(sM.h, sM_b + m + 1)
                        s.activation(sig[:, m * 512:(m + 1) * 512],
                                     ps_o[m % 2][:, :], AF.Sigmoid)
                        sA.inc(s.drain())

            sA_b, sM_b = sA.v, sM.v
            with nc.Block() as blk:
                @blk.tensor
                def _(tn):
                    for hc in range(16):
                        if hc >= 2:
                            tn.wait_ge(sA.h, sA_b + 2 * (hc - 1))
                        for kc in range(G):
                            mm = tn.matmul(
                                ps_h[hc % 2][:, :],
                                fwk_b[:, kc * 2 * C + hc * 128:
                                      kc * 2 * C + hc * 128 + 128],
                                xk2[:, kc * 512:(kc + 1) * 512],
                                start=(kc == 0), stop=(kc == G - 1))
                        sM.inc(mm)

                @blk.scalar
                def _(s):
                    for hc in range(16):
                        s.wait_ge(sM.h, sM_b + hc + 1)
                        rt = relu_t[:, (hc % 2) * 512:(hc % 2) * 512 + 512]
                        s.activation(rt, ps_h[hc % 2][:, :], AF.Relu)
                        s.drain()
                        s.activation(h2[:, hc * 512:(hc + 1) * 512], rt,
                                     AF.Square)
                        sA.inc(s.drain(), 2)

            sM_b2, sV_b2, ld_b2 = sM.v, sV.v, ld.v
            with nc.Block() as blk:
                @blk.gpsimd
                def _(g_):
                    for m in range(G):
                        if m >= 2:
                            g_.wait_ge(sM.h, sM_b2 + m - 1)
                        ld.inc(g_.dma_start(
                            out=fwv_t[m % 2][:, :].rearrange(
                                "p (hc o) -> p hc o", o=128),
                            in_=fWv_d[:, m * 128:(m + 1) * 128].rearrange(
                                "(hc p) o -> p hc o", p=128)), 16)

                @blk.tensor
                def _(tn):
                    for m in range(G):
                        tn.wait_ge(ld.h, ld_b2 + 16 * (m + 1))
                        if m >= 2:
                            tn.wait_ge(sV.h, sV_b2 + m - 1)
                        for hc in range(16):
                            mm = tn.matmul(
                                ps_o[m % 2][:, :],
                                fwv_t[m % 2][:, hc * 128:(hc + 1) * 128],
                                h2[:, hc * 512:(hc + 1) * 512],
                                start=(hc == 0), stop=(hc == 15))
                        sM.inc(mm)

                @blk.vector
                def _(v):
                    for m in range(G):
                        v.wait_ge(sM.h, sM_b2 + m + 1)
                        v.tensor_mul(cmo[:, m * 512:(m + 1) * 512],
                                     sig[:, m * 512:(m + 1) * 512],
                                     ps_o[m % 2][:, :])
                        sV.inc(v.drain())

                @blk.sync
                def _(sync, n=n):
                    sV.wait(sync)
                    ld.inc(sync.dma_start(
                        out=cmh_d[:, n * 512:(n + 1) * 512].rearrange(
                            "(g p) t -> p g t", p=128),
                        in_=cmo[:, :].rearrange("p (g t) -> p g t", t=512)), 16)

    with nc.Block() as blk:
        @blk.sync
        def _(sync):
            ld.wait(sync)
    es.close()





def emit_s5(nc, D, S):
    es = contextlib.ExitStack()
    th0 = 0

    xT_d, xn0_d, xn0bf_d = D["xT"], D["xn0"], D["xn0bf"]
    sout_d, scores_d = D["sout"], D["scores"]
    W1_d, W2_d, b1_d, b2_d = D["W1"], D["W2"], D["b1"], D["b2"]
    idn_d = D["idn"]
    x1h_d = D["x1"]

    def sb(stack, name, shape, dtype=F32):
        return stack.enter_context(nc.sbuf_tensor("s5_" + name, shape, dtype))

    def psum(stack, name, shape, dtype=F32):
        return stack.enter_context(nc.psum_tensor("s5_" + name, shape, dtype))

    b1t = sb(es, "b1t", [128, 32])
    b2t = sb(es, "b2t", [128, G])
    onesc = sb(es, "onesc", [128, 1])
    onesr = sb(es, "onesr", [1, 128])
    idf = sb(es, "idf", [128, 128])
    scp = sb(es, "scp", [128, 64])
    msk64 = sb(es, "msk64", [128, 64])
    bis = sb(es, "bis", [128, 8])
    rowx = sb(es, "rowx", [1, 132])
    w1b = sb(es, "w1b", [128, G * 4 * C], BF16)
    w2t = [sb(es, "w2t0", [128, 32 * 128], BF16),
           sb(es, "w2t1", [128, 32 * 128], BF16)]

    ld, sA, sM, sV = S

    with nc.Block() as blk:
        @blk.sync
        def _(sync):
            ld.inc(sync.dma_start(out=b1t[:, :], in_=b1_d), 16)
            ld.inc(sync.dma_start(out=b2t[:, :], in_=b2_d), 16)
            ld.inc(sync.dma_start(
                out=scp[:, :],
                in_=scores_d.rearrange("o (p f) -> p (o f)", p=128)), 16)
            ld.wait(sync)

        @blk.gpsimd
        def _(g_):
            ld.inc(g_.dma_start(
                out=w1b[:, :].rearrange("p (kc o) -> p kc o", o=4 * C),
                in_=W1_d.rearrange("(kc p) o -> p kc o", p=128)), 16)

    with nc.Block() as blk:
        @blk.vector
        def _(v):
            v.memset(onesc[:, :], 1.0)
            v.memset(onesr[:, :], 1.0)

        @blk.sync
        def _(sync):
            ld.inc(sync.dma_start(out=idf[:, :], in_=idn_d), 16)
            ld.wait(sync)

    ps_t = psum(es, "ps_t", [1, 128])
    ps_c = psum(es, "ps_c", [1, 128])
    ps_bc = psum(es, "ps_bc", [128, 64])

    with nc.Block() as blk:
        @blk.vector
        def _(v):
            v.reduce_max(bis[:, 5:6], scp[:, :], axis=AX)
            v.nop(cycle_cnt=32)
            v.tensor_reduce(bis[:, 6:7], scp[:, :], axis=AX, op=ALU.min)
            sV.inc(v.drain())

        @blk.tensor
        def _(tn):
            sV.wait(tn)
            tn.transpose(ps_t[:, :], bis[:, 5:6], idf[:, :])
            sM.inc(tn.transpose(ps_c[:, :], bis[:, 6:7], idf[:, :]))

        @blk.vector
        def _(v):
            sM.wait(v)
            v.reduce_max(rowx[0:1, 0:1], ps_t[0:1, :], axis=AX)
            v.tensor_reduce(rowx[0:1, 1:2], ps_c[0:1, :], axis=AX, op=ALU.min)
            v.nop(cycle_cnt=32)
            v.tensor_scalar(rowx[0:1, 0:1], rowx[0:1, 0:1], 1.0, None, ALU.add)
            v.tensor_scalar(rowx[0:1, 1:2], rowx[0:1, 1:2], -1.0, None, ALU.add)
            sV.inc(v.drain())

        @blk.tensor
        def _(tn):
            sV.wait(tn)
            sM.inc(tn.matmul(ps_bc[:, 0:2], onesr[:, :], rowx[0:1, 0:2],
                             start=True, stop=True))

        @blk.vector
        def _(v):
            sM.wait(v)
            v.tensor_copy(bis[:, 1:2], ps_bc[:, 0:1])
            v.tensor_copy(bis[:, 0:1], ps_bc[:, 1:2])
            v.nop(cycle_cnt=32)
            v.tensor_add(bis[:, 2:3], bis[:, 0:1], bis[:, 1:2])
            v.nop(cycle_cnt=32)
            v.tensor_scalar_mul(bis[:, 2:3], bis[:, 2:3], 0.5)

    for it in range(44):
        with nc.Block() as blk:
            @blk.vector
            def _(v):
                v.tensor_scalar(msk64[:, :], scp[:, :], bis[:, 2:3], None,
                                ALU.is_ge)
                v.nop(cycle_cnt=32)
                v.reduce_sum(bis[:, 4:5], msk64[:, :], axis=AX)
                sV.inc(v.drain())

            @blk.tensor
            def _(tn):
                sV.wait(tn)
                sM.inc(tn.matmul(ps_t[0:1, 0:1], onesc[:, 0:1], bis[:, 4:5],
                                 start=True, stop=True))

            @blk.scalar
            def _(s):
                sM.wait(s)
                s.copy(rowx[0:1, 2:3], ps_t[0:1, 0:1])
                sA.inc(s.drain())

            @blk.tensor
            def _(tn):
                sA.wait(tn)
                sM.inc(tn.matmul(ps_bc[:, 0:1], onesr[:, :], rowx[0:1, 2:3],
                                 start=True, stop=True))

            @blk.vector
            def _(v):
                sM.wait(v)
                lo, hi, th, sel = (bis[:, 0:1], bis[:, 1:2], bis[:, 2:3],
                                   bis[:, 3:4])
                t1, t2 = bis[:, 5:6], bis[:, 6:7]
                v.tensor_scalar(sel, ps_bc[:, 0:1], float(TH), None, ALU.is_ge)
                v.tensor_sub(t1, th, lo)
                v.tensor_sub(t2, th, hi)
                v.nop(cycle_cnt=32)
                v.tensor_mul(t1, t1, sel)
                v.tensor_scalar(bis[:, 7:8], sel, -1.0, 1.0, ALU.mult, ALU.add)
                v.nop(cycle_cnt=32)
                v.tensor_add(lo, lo, t1)
                v.tensor_mul(t2, t2, bis[:, 7:8])
                v.nop(cycle_cnt=32)
                v.tensor_add(hi, hi, t2)
                v.nop(cycle_cnt=32)
                v.tensor_add(th, lo, hi)
                v.nop(cycle_cnt=32)
                v.tensor_scalar_mul(th, th, 0.5)

    with nc.Block() as blk:
        @blk.vector
        def _(v):
            v.tensor_copy(msk64[:, 0:1], bis[:, 0:1])

    with contextlib.ExitStack() as es2:
        xnbfc = sb(es2, "xnbfc", [128, G * 512], BF16)
        xnc = sb(es2, "xnc", [128, G * 512])
        soc = sb(es2, "soc", [128, G * 512])
        h2 = sb(es2, "h2", [128, 32 * 512], BF16)
        x1c = sb(es2, "x1c", [128, G * 512])
        srow = sb(es2, "srow", [1, 3 * 512])
        gm = sb(es2, "gm", [128, G * 512])
        ps_h = [psum(es2, f"ps_h{i}", [128, 512]) for i in range(2)]
        ps_o = [psum(es2, f"ps_o{i}", [128, 512]) for i in range(2)]
        ps_w = psum(es2, "ps_w", [128, 512])

        for n in range(T // 512):
            c0 = 1 + th0 + n * 512
            with nc.Block() as blk:
                @blk.sync
                def _(sync, n=n, c0=c0):
                    sync.wait_ge(ld.h, ld.v)
                    ld.inc(sync.dma_start(
                        out=xnbfc[:, :].rearrange("p (g t) -> p g t", t=512),
                        in_=xn0bf_d[:, c0:c0 + 512].rearrange(
                            "(g p) t -> p g t", p=128)), 16)
                    ld.inc(sync.dma_start(
                        out=xnc[:, :].rearrange("p (g t) -> p g t", t=512),
                        in_=xn0_d[:, c0:c0 + 512].rearrange(
                            "(g p) t -> p g t", p=128)), 16)
                    ld.inc(sync.dma_start(
                        out=x1c[:, :].rearrange("p (g t) -> p g t", t=512),
                        in_=xT_d[:, c0:c0 + 512].rearrange(
                            "(g p) t -> p g t", p=128)), 16)
                    ld.inc(sync.dma_start(
                        out=soc[:, :].rearrange("p (g t) -> p g t", t=512),
                        in_=sout_d[:, c0:c0 + 512].rearrange(
                            "(g p) t -> p g t", p=128)), 16)
                    ld.inc(sync.dma_start(
                        out=srow[0:1, 0:512],
                        in_=scores_d[:, th0 + n * 512:th0 + (n + 1) * 512]), 16)
                    ld.wait(sync)

            sA_b0, sV_b0, sM_b0 = sA.v, sV.v, sM.v
            with nc.Block() as blk:
                @blk.scalar
                def _(s):
                    s.activation(srow[0:1, 512:1024], srow[0:1, 0:512],
                                 AF.Sigmoid)
                    sA.inc(s.drain())

                @blk.vector
                def _(v):
                    sA.wait(v)
                    v.tensor_scalar(srow[0:1, 1024:1536], srow[0:1, 0:512],
                                    bis[0:1, 0:1], None, ALU.is_ge)
                    v.nop(cycle_cnt=32)
                    v.tensor_mul(srow[0:1, 1024:1536], srow[0:1, 1024:1536],
                                 srow[0:1, 512:1024])
                    sV.inc(v.drain())

                @blk.tensor
                def _(tn):
                    sV.wait(tn)
                    sM.inc(tn.matmul(ps_w[:, :], onesr[:, :],
                                     srow[0:1, 1024:1536], start=True,
                                     stop=True))

            sA_b, sM_b = sA.v, sM.v
            with nc.Block() as blk:
                @blk.tensor
                def _(tn):
                    for hc in range(32):
                        if hc >= 2:
                            tn.wait_ge(sA.h, sA_b + hc - 1)
                        for kc in range(G):
                            mm = tn.matmul(
                                ps_h[hc % 2][:, :],
                                w1b[:, kc * 4 * C + hc * 128:
                                    kc * 4 * C + hc * 128 + 128],
                                xnbfc[:, kc * 512:(kc + 1) * 512],
                                start=(kc == 0), stop=(kc == G - 1))
                        sM.inc(mm)

                @blk.scalar
                def _(s):
                    for hc in range(32):
                        s.wait_ge(sM.h, sM_b + hc + 1)
                        s.activation(h2[:, hc * 512:(hc + 1) * 512],
                                     ps_h[hc % 2][:, :], AF.Gelu_apprx_tanh,
                                     bias=b1t[:, hc:hc + 1])
                        sA.inc(s.drain())

            sM_b2, sV_b2, sA_b2, ld_b2 = sM.v, sV.v, sA.v, ld.v
            with nc.Block() as blk:
                @blk.gpsimd
                def _(g_):
                    for m in range(G):
                        if m >= 2:
                            g_.wait_ge(sM.h, sM_b2 + m - 1)
                        ld.inc(g_.dma_start(
                            out=w2t[m % 2][:, :].rearrange(
                                "p (hc o) -> p hc o", o=128),
                            in_=W2_d[:, m * 128:(m + 1) * 128].rearrange(
                                "(hc p) o -> p hc o", p=128)), 16)

                @blk.tensor
                def _(tn):
                    for m in range(G):
                        tn.wait_ge(ld.h, ld_b2 + 16 * (m + 1))
                        if m >= 2:
                            tn.wait_ge(sV.h, sV_b2 + m - 1)
                        for hc in range(32):
                            mm = tn.matmul(
                                ps_o[m % 2][:, :],
                                w2t[m % 2][:, hc * 128:(hc + 1) * 128],
                                h2[:, hc * 512:(hc + 1) * 512],
                                start=(hc == 0), stop=(hc == 31))
                        sM.inc(mm)

                @blk.vector
                def _(v):
                    for m in range(G):
                        v.wait_ge(sM.h, sM_b2 + m + 1)
                        sl = slice(m * 512, (m + 1) * 512)
                        par = m % 2
                        gsl = gm[:, sl]
                        v.tensor_scalar(gsl, ps_o[par][:, :],
                                        b2t[:, m:m + 1], None, ALU.add)
                        v.nop(cycle_cnt=32)
                        v.tensor_mul(gsl, gsl, ps_w[:, :])
                        v.nop(cycle_cnt=32)
                        v.tensor_add(gsl, gsl, xnc[:, sl])
                        sV.inc(v.drain())

                @blk.scalar
                def _(s):
                    for m in range(G):
                        s.wait_ge(sV.h, sV_b2 + m + 1)
                        sl = slice(m * 512, (m + 1) * 512)
                        s.activation(gm[:, sl], gm[:, sl], AF.Silu)
                        sA.inc(s.drain())

                @blk.vector
                def _(v):
                    for m in range(G):
                        v.wait_ge(sA.h, sA_b2 + m + 1)
                        sl = slice(m * 512, (m + 1) * 512)
                        v.tensor_mul(soc[:, sl], soc[:, sl], gm[:, sl])
                        v.nop(cycle_cnt=32)
                        v.tensor_add(x1c[:, sl], x1c[:, sl], soc[:, sl])
                        sV.inc(v.drain())

                @blk.sync
                def _(sync, n=n):
                    sV.wait(sync)
                    ld.inc(sync.dma_start(
                        out=x1h_d[:, n * 512:(n + 1) * 512].rearrange(
                            "(g p) t -> p g t", p=128),
                        in_=x1c[:, :].rearrange("p (g t) -> p g t", t=512)), 16)

    with nc.Block() as blk:
        @blk.sync
        def _(sync):
            ld.wait(sync)
    es.close()





def emit_s6(nc, D, S):
    es = contextlib.ExitStack()

    x1_d = D["x1"]
    wq_d, wk_d, wv_d = D["wq"], D["wk"], D["wv"]
    wr_d, br_d = D["wr"], D["br"]
    cos_d, sin_d, ps2_d, idn_d = D["cosb"], D["sinb"], D["ps2"], D["idn"]
    vecs = {nm: D[nm] for nm in ["ln_g", "ln_b"]}

    xn1bf_d = D["xn1bf"]
    qr_d, kr_d, sq_d, sk_d = D["qrh"], D["krh"], D["sqh"], D["skh"]
    sktm_d, vtm_d = D["sktm"], D["vtm"]
    sgate_d, slot_d = D["sgate"], D["slotint"]
    iota8_d, iotac_d, ut128_d = D["iota8"], D["iotac"], D["ut128"]
    trasht_d = D["trasht"]

    def sb(stack, name, shape, dtype=F32):
        return stack.enter_context(nc.sbuf_tensor("s6_" + name, shape, dtype))

    def psum(stack, name, shape, dtype=F32):
        return stack.enter_context(nc.psum_tensor("s6_" + name, shape, dtype))

    vt = {nm: sb(es, "v_" + nm, [128, G]) for nm in vecs}
    onesc = sb(es, "onesc", [128, 1])
    onesr = sb(es, "onesr", [1, 128])
    cst_eps = sb(es, "cst_eps", [128, 1])
    idf = sb(es, "idf", [128, 128])
    ps2b = sb(es, "ps2b", [128, 128], BF16)
    wqb = sb(es, "wqb", [128, G * 512], BF16)
    wkb = sb(es, "wkb", [128, G * 512], BF16)
    wvb = sb(es, "wvb", [128, G * 512], BF16)
    wrt = sb(es, "wrt", [128, G * 8])
    brt = sb(es, "brt", [1, 8])
    iota8 = sb(es, "iota8", [128, 8])
    iotac = sb(es, "iotac", [128, 8])
    ut128 = sb(es, "ut128", [128, 128])
    trasht = sb(es, "trasht", [128, 1])
    cnt8 = sb(es, "cnt8", [1, 8])
    islot6 = sb(es, "islot6", [128, 64], I32)
    rank6 = sb(es, "rank6", [128, 64])
    e16 = sb(es, "e16", [128, 64])
    offrow = sb(es, "offrow", [1, 8])
    lt8t = sb(es, "lt8t", [8, 8])
    r8 = sb(es, "r8", [1, 32])
    i8a = sb(es, "i8a", [1, 8], I32)
    t88 = sb(es, "t88", [8, 8])
    off8c = sb(es, "off8c", [8, 1])

    ld, sA, sM, sV = S

    with nc.Block() as blk:
        @blk.sync
        def _(sync):
            for nm in vecs:
                ld.inc(sync.dma_start(out=vt[nm][:, :], in_=vecs[nm]), 16)
            ld.inc(sync.dma_start(out=idf[:, :], in_=idn_d), 16)
            ld.inc(sync.dma_start(
                out=wrt[:, :].rearrange("p (kc o) -> p kc o", o=8),
                in_=wr_d.rearrange("(kc p) o -> p kc o", p=128)), 16)
            ld.inc(sync.dma_start(out=brt[:, :], in_=br_d), 16)
            ld.inc(sync.dma_start(out=iota8[:, :], in_=iota8_d), 16)
            ld.inc(sync.dma_start(out=iotac[:, :], in_=iotac_d), 16)
            ld.inc(sync.dma_start(out=ut128[:, :], in_=ut128_d), 16)
            ld.inc(sync.dma_start(out=trasht[:, :], in_=trasht_d), 16)
            ld.inc(sync.dma_start(out=lt8t[:, :], in_=D["lt8"]), 16)
            ld.wait(sync)

        @blk.gpsimd
        def _(g_):
            ld.inc(g_.dma_start(out=ps2b[:, :], in_=ps2_d), 16)
            for wd, wb in [(wq_d, wqb), (wk_d, wkb), (wv_d, wvb)]:
                ld.inc(g_.dma_start(
                    out=wb[:, :].rearrange("p (kc o) -> p kc o", o=512),
                    in_=wd.rearrange("(kc p) o -> p kc o", p=128)), 16)

    with nc.Block() as blk:
        @blk.vector
        def _(v):
            v.memset(onesc[:, :], 1.0)
            v.memset(onesr[:, :], 1.0)
            v.memset(cst_eps[:, :], EPS)
            v.memset(cnt8[:, :], 0.0)

    with contextlib.ExitStack() as e1:
        xc = sb(e1, "xc", [128, G * 512])
        sqf = sb(e1, "sqf", [128, G * 512])
        xnc = sb(e1, "xnc", [128, G * 512])
        xnbfc = sb(e1, "xnbfc", [128, G * 512], BF16)
        lac = sb(e1, "lac", [128, G * 512], BF16)
        row = sb(e1, "row", [1, 4 * 512])
        cosc = sb(e1, "cosc", [128, 512])
        sinc = sb(e1, "sinc", [128, 512])
        qb = sb(e1, "qb", [128, 4 * 512], BF16)
        kb = sb(e1, "kb", [128, 4 * 512], BF16)
        ob = sb(e1, "ob", [128, 4 * 512], BF16)
        ob2 = sb(e1, "ob2", [128, 4 * 512], BF16)
        tt8 = sb(e1, "tt8", [128, 8 * 512])
        lg = sb(e1, "lg", [128, 48])
        sg8 = sb(e1, "sg8", [8, 128])
        ps_sum = psum(e1, "ps_sum", [1, 512])
        ps_sq = psum(e1, "ps_sq", [1, 512])
        ps_rb = psum(e1, "ps_rb", [128, 512])
        ps_mrb = psum(e1, "ps_mrb", [128, 512])
        ps_q = [psum(e1, f"ps_q{i}", [128, 512]) for i in range(2)]
        ps_g = psum(e1, "ps_g", [128, 8])
        ps_gt = psum(e1, "ps_gt", [8, 128])

        for n in range(NC512):
            c0 = n * 512
            sV_chunk_top = sV.v
            with nc.Block() as blk:
                @blk.sync
                def _(sync, c0=c0):
                    sync.wait_ge(ld.h, ld.v)
                    ld.inc(sync.dma_start(
                        out=xc[:, :].rearrange("p (g t) -> p g t", t=512),
                        in_=x1_d[:, c0:c0 + 512].rearrange(
                            "(g p) t -> p g t", p=128)), 16)
                    ld.inc(sync.dma_start(out=cosc[:, :],
                                          in_=cos_d[:, c0:c0 + 512]), 16)
                    ld.inc(sync.dma_start(out=sinc[:, :],
                                          in_=sin_d[:, c0:c0 + 512]), 16)
                    ld.wait(sync)

            with nc.Block() as blk:
                @blk.scalar
                def _(s):
                    for g in range(G):
                        s.activation(sqf[:, g * 512:(g + 1) * 512],
                                     xc[:, g * 512:(g + 1) * 512], AF.Square)
                    sA.inc(s.drain())

                @blk.tensor
                def _(tn, sV_chunk_top=sV_chunk_top):
                    tn.wait_ge(sV.h, sV_chunk_top)
                    for g in range(G):
                        tn.matmul(ps_sum[:, :], onesc[:, 0:1],
                                  xc[:, g * 512:(g + 1) * 512],
                                  start=(g == 0), stop=(g == G - 1))
                    sA.wait(tn)
                    for g in range(G):
                        mm = tn.matmul(ps_sq[:, :], onesc[:, 0:1],
                                       sqf[:, g * 512:(g + 1) * 512],
                                       start=(g == 0), stop=(g == G - 1))
                    sM.inc(mm)

                @blk.vector
                def _(v):
                    sM.wait(v)
                    mean, var = row[:, 0:512], row[:, 512:1024]
                    v.tensor_scalar_mul(mean, ps_sum[:, :], 1.0 / C)
                    v.nop(cycle_cnt=16)
                    v.tensor_mul(var, mean, mean)
                    v.nop(cycle_cnt=16)
                    v.scalar_tensor_tensor(var, ps_sq[:, :], 1.0 / C, var,
                                           ALU.mult, ALU.subtract)
                    sV.inc(v.drain())

                @blk.scalar
                def _(s):
                    sV.wait(s)
                    s.activation(row[:, 1024:1536], row[:, 512:1024], AF.Sqrt,
                                 bias=cst_eps[0:1, 0:1])
                    sA.inc(s.drain())

                @blk.vector
                def _(v):
                    sA.wait(v)
                    v.reciprocal(row[:, 1024:1536], row[:, 1024:1536])
                    v.nop(cycle_cnt=32)
                    v.tensor_mul(row[:, 1536:2048], row[:, 0:512],
                                 row[:, 1024:1536])
                    sV.inc(v.drain())

                @blk.tensor
                def _(tn):
                    sV.wait(tn)
                    tn.matmul(ps_rb[:, :], onesr[:, :], row[:, 1024:1536],
                              start=True, stop=True)
                    sM.inc(tn.matmul(ps_mrb[:, :], onesr[:, :],
                                     row[:, 1536:2048], start=True, stop=True))

                @blk.vector
                def _(v):
                    sM.wait(v)

                    def mk(g):
                        xg = xc[:, g * 512:(g + 1) * 512]
                        og = xnc[:, g * 512:(g + 1) * 512]
                        bg = xnbfc[:, g * 512:(g + 1) * 512]
                        ag = lac[:, g * 512:(g + 1) * 512]
                        return [
                            lambda: v.tensor_mul(og, xg, ps_rb[:, :]),
                            lambda: v.tensor_sub(og, og, ps_mrb[:, :]),
                            lambda: v.tensor_copy(bg, og),
                            lambda: v.tensor_scalar(ag, og,
                                                    vt["ln_g"][:, g:g + 1],
                                                    vt["ln_b"][:, g:g + 1],
                                                    ALU.mult, ALU.add),
                        ]
                    for g in range(0, G, 2):
                        interleave(mk(g), mk(g + 1))
                    sV.inc(v.drain())

                @blk.sync
                def _(sync, c0=c0):
                    sV.wait(sync)
                    ld.inc(sync.dma_start(
                        out=xn1bf_d[:, c0:c0 + 512].rearrange(
                            "(g p) t -> p g t", p=128),
                        in_=xnbfc[:, :].rearrange("p (g t) -> p g t", t=512)),
                        16)

            sA_b, sM_b = sA.v, sM.v
            with nc.Block() as blk:
                @blk.tensor
                def _(tn):
                    for idx in range(8):
                        wb = wqb if idx < 4 else wkb
                        m = idx % 4
                        if idx >= 2:
                            tn.wait_ge(sA.h, sA_b + idx - 1)
                        for kc in range(G):
                            mm = tn.matmul(
                                ps_q[idx % 2][:, :],
                                wb[:, kc * 512 + m * 128:
                                   kc * 512 + m * 128 + 128],
                                lac[:, kc * 512:(kc + 1) * 512],
                                start=(kc == 0), stop=(kc == G - 1))
                        sM.inc(mm)

                @blk.scalar
                def _(s):
                    for idx in range(8):
                        s.wait_ge(sM.h, sM_b + idx + 1)
                        dst = qb if idx < 4 else kb
                        m = idx % 4
                        s.copy(dst[:, m * 512:(m + 1) * 512],
                               ps_q[idx % 2][:, :])
                        sA.inc(s.drain())

            sM_b, sV_b = sM.v, sV.v
            with nc.Block() as blk:
                @blk.tensor
                def _(tn):
                    for idx in range(8):
                        src = qb if idx < 4 else kb
                        m = idx % 4
                        if idx >= 2:
                            tn.wait_ge(sV.h, sV_b + idx - 1)
                        sM.inc(tn.matmul(ps_q[idx % 2][:, :], ps2b[:, :],
                                         src[:, m * 512:(m + 1) * 512],
                                         start=True, stop=True))

                @blk.vector
                def _(v):
                    for idx in range(8):
                        src = qb if idx < 4 else kb
                        dst = ob if idx < 4 else ob2
                        m = idx % 4
                        ssl = slice(m * 512, (m + 1) * 512)
                        t0 = tt8[:, (idx % 2) * 512:(idx % 2) * 512 + 512]
                        t1 = tt8[:, (2 + idx % 2) * 512:(2 + idx % 2) * 512 + 512]
                        v.wait_ge(sM.h, sM_b + idx + 1)
                        v.tensor_mul(t0, ps_q[idx % 2][:, :], sinc[:, :])
                        v.tensor_mul(t1, src[:, ssl], cosc[:, :])
                        v.nop(cycle_cnt=32)
                        v.tensor_add(dst[:, ssl], t0, t1)
                        sV.inc(v.drain())

                @blk.sync
                def _(sync, c0=c0):
                    for idx in range(8):
                        sync.wait_ge(sV.h, sV_b + idx + 1)
                        dd = qr_d if idx < 4 else kr_d
                        src = ob if idx < 4 else ob2
                        m = idx % 4
                        ld.inc(sync.dma_start(
                            out=dd[m * 128:(m + 1) * 128, c0:c0 + 512],
                            in_=src[:, m * 512:(m + 1) * 512]), 16)

            sA_b, sV_b, ld_b = sA.v, sV.v, ld.v
            with nc.Block() as blk:
                @blk.vector
                def _(v):
                    for idx in range(8):
                        src = qb if idx < 4 else kb
                        m = idx % 4
                        ssl = slice(m * 512, (m + 1) * 512)
                        tt = tt8[:, idx * 512:(idx + 1) * 512]
                        v.tensor_scalar_min(tt, src[:, ssl], 0.0)
                        sV.inc(v.drain())

                @blk.scalar
                def _(s):
                    for idx in range(8):
                        s.wait_ge(sV.h, sV_b + idx + 1)
                        tt = tt8[:, idx * 512:(idx + 1) * 512]
                        s.activation(tt, tt, AF.Exp)
                        sA.inc(s.drain())

                @blk.vector
                def _(v):
                    for idx in range(8):
                        v.wait_ge(sA.h, sA_b + idx + 1)
                        src = qb if idx < 4 else kb
                        dst = ob if idx < 4 else ob2
                        m = idx % 4
                        ssl = slice(m * 512, (m + 1) * 512)
                        tt = tt8[:, idx * 512:(idx + 1) * 512]
                        v.scalar_tensor_tensor(dst[:, ssl], src[:, ssl], 1.0,
                                               tt, ALU.add, ALU.max)
                        sV.inc(v.drain())

                @blk.sync
                def _(sync, c0=c0):
                    for idx in range(8):
                        sync.wait_ge(sV.h, sV_b + 8 + idx + 1)
                        dd = sq_d if idx < 4 else sk_d
                        src = ob if idx < 4 else ob2
                        m = idx % 4
                        ld.inc(sync.dma_start(
                            out=dd[m * 128:(m + 1) * 128, c0:c0 + 512],
                            in_=src[:, m * 512:(m + 1) * 512]), 16)

            sA_b, sM_b, sV_b = sA.v, sM.v, sV.v
            with nc.Block() as blk:
                @blk.tensor
                def _(tn):
                    for idx in range(8):
                        sl4 = idx // 2
                        wb = wvb if idx % 2 == 0 else wkb
                        if idx >= 2:
                            tn.wait_ge(sA.h, sA_b + idx - 1)
                        for kc in range(G):
                            mm = tn.matmul(
                                ps_q[idx % 2][:, :],
                                lac[:, kc * 512 + sl4 * 128:
                                    kc * 512 + sl4 * 128 + 128],
                                wb[:, kc * 512:(kc + 1) * 512],
                                start=(kc == 0), stop=(kc == G - 1))
                        sM.inc(mm)

                @blk.scalar
                def _(s):
                    for idx in range(8):
                        s.wait_ge(sM.h, sM_b + idx + 1)
                        sl4 = idx // 2
                        if idx % 2 == 0:
                            s.copy(ob[:, sl4 * 512:(sl4 + 1) * 512],
                                   ps_q[idx % 2][:, :])
                        else:
                            s.copy(tt8[:, sl4 * 512:(sl4 + 1) * 512],
                                   ps_q[idx % 2][:, :])
                        sA.inc(s.drain())

                @blk.vector
                def _(v):
                    for sl4 in range(4):
                        v.wait_ge(sA.h, sA_b + 2 * sl4 + 2)
                        tt = tt8[:, sl4 * 512:(sl4 + 1) * 512]
                        mn = tt8[:, (4 + sl4) * 512:(5 + sl4) * 512]
                        v.tensor_scalar_min(mn, tt, 0.0)
                        sV.inc(v.drain())

                @blk.scalar
                def _(s):
                    for sl4 in range(4):
                        s.wait_ge(sV.h, sV_b + sl4 + 1)
                        mn = tt8[:, (4 + sl4) * 512:(5 + sl4) * 512]
                        s.activation(mn, mn, AF.Exp)
                        sA.inc(s.drain())

                @blk.vector
                def _(v):
                    for sl4 in range(4):
                        v.wait_ge(sA.h, sA_b + 8 + sl4 + 1)
                        tt = tt8[:, sl4 * 512:(sl4 + 1) * 512]
                        mn = tt8[:, (4 + sl4) * 512:(5 + sl4) * 512]
                        v.scalar_tensor_tensor(
                            ob2[:, sl4 * 512:(sl4 + 1) * 512], tt, 1.0, mn,
                            ALU.add, ALU.max)
                        sV.inc(v.drain())

                @blk.sync
                def _(sync, c0=c0):
                    for sl4 in range(4):
                        sync.wait_ge(sA.h, sA_b + 2 * sl4 + 1)
                        ld.inc(sync.dma_start(
                            out=vtm_d[c0 + sl4 * 128:c0 + (sl4 + 1) * 128, :],
                            in_=ob[:, sl4 * 512:(sl4 + 1) * 512]), 16)
                    for sl4 in range(4):
                        sync.wait_ge(sV.h, sV_b + 4 + sl4 + 1)
                        ld.inc(sync.dma_start(
                            out=sktm_d[c0 + sl4 * 128:c0 + (sl4 + 1) * 128, :],
                            in_=ob2[:, sl4 * 512:(sl4 + 1) * 512]), 16)

            for sl4 in range(4):
                gcol = c0 + sl4 * 128
                gidx = n * 4 + sl4
                rM0, rV0, rA0, rL0 = sM.v, sV.v, sA.v, ld.v
                with nc.Block() as blk:
                    @blk.tensor
                    def _(tn, sl4=sl4):
                        for kc in range(G):
                            tn.matmul(ps_g[:, :],
                                      xnc[:, kc * 512 + sl4 * 128:
                                          kc * 512 + sl4 * 128 + 128],
                                      wrt[:, kc * 8:(kc + 1) * 8],
                                      start=(kc == 0), stop=False)
                        sM.inc(tn.matmul(ps_g[:, :], onesr[:, :],
                                         brt[:, :], start=False,
                                         stop=True))

                    @blk.vector
                    def _(v, rM0=rM0):
                        v.wait_ge(sM.h, rM0 + 1)
                        v.tensor_copy(lg[:, 0:8], ps_g[:, :])
                        v.nop(cycle_cnt=32)
                        v.max(out=lg[:, 8:16], in_=lg[:, 0:8])
                        v.drain()
                        v.tensor_sub(lg[:, 16:17], lg[:, 8:9], lg[:, 9:10])
                        sV.inc(v.drain())

                    @blk.scalar
                    def _(s, rV0=rV0):
                        s.wait_ge(sV.h, rV0 + 1)
                        s.activation(lg[:, 17:18], lg[:, 16:17], AF.Sigmoid)
                        sA.inc(s.drain())

                    @blk.vector
                    def _(v, rA0=rA0):
                        v.wait_ge(sA.h, rA0 + 1)
                        v.tensor_scalar(lg[:, 20:28], lg[:, 0:8],
                                        lg[:, 8:9], None, ALU.is_equal)
                        v.nop(cycle_cnt=32)
                        v.tensor_mul(lg[:, 28:36], lg[:, 20:28], iota8[:, :])
                        v.tensor_scalar(lg[:, 36:44], lg[:, 20:28], -9.0, 9.0,
                                        ALU.mult, ALU.add)
                        v.nop(cycle_cnt=32)
                        v.tensor_add(lg[:, 28:36], lg[:, 28:36], lg[:, 36:44])
                        v.nop(cycle_cnt=32)
                        v.tensor_reduce(lg[:, 18:19], lg[:, 28:36], axis=AX,
                                        op=ALU.min)
                        v.nop(cycle_cnt=32)
                        v.tensor_scalar(lg[:, 20:28], iota8[:, :],
                                        lg[:, 18:19], None, ALU.is_equal)
                        v.nop(cycle_cnt=32)
                        v.tensor_scalar(lg[:, 28:36], lg[:, 20:28],
                                        lg[:, 17:18], None, ALU.mult)
                        sV.inc(v.drain())

                    @blk.tensor
                    def _(tn, rV0=rV0, rA0=rA0):
                        tn.wait_ge(sV.h, rV0 + 2)
                        tn.wait_ge(sA.h, rA0)
                        tn.matmul(ps_g[:, :], ut128[:, :], lg[:, 20:28],
                                  start=True, stop=False)
                        tn.matmul(ps_g[:, :], onesr[:, :], cnt8[:, :],
                                  start=False, stop=True)
                        sM.inc(tn.matmul(ps_sum[0:1, 0:8], onesc[:, 0:1],
                                         lg[:, 20:28], start=True, stop=True))
                        sM.inc(tn.transpose(ps_gt[:, :], lg[:, 28:36],
                                            idf[:, :]))

                    @blk.vector
                    def _(v, gidx=gidx, rM0=rM0):
                        v.wait_ge(sM.h, rM0 + 2)
                        v.tensor_tensor(lg[:, 36:44], ps_g[:, :],
                                        lg[:, 20:28], ALU.mult)
                        v.tensor_tensor(cnt8[:, :], cnt8[:, :],
                                        ps_sum[0:1, 0:8], ALU.add)
                        v.nop(cycle_cnt=32)
                        v.reduce_sum(rank6[:, gidx:gidx + 1], lg[:, 36:44],
                                     axis=AX)
                        v.tensor_copy(e16[:, gidx:gidx + 1], lg[:, 18:19])
                        sV.inc(v.drain())

                    @blk.scalar
                    def _(s, rM0=rM0, rL0=rL0):
                        s.wait_ge(sM.h, rM0 + 3)
                        s.wait_ge(ld.h, rL0)
                        s.copy(sg8[:, :], ps_gt[:, :])
                        sA.inc(s.drain())

                    @blk.sync
                    def _(sync, gcol=gcol, rA0=rA0):
                        sync.wait_ge(sA.h, rA0 + 2)
                        ld.inc(sync.dma_start(
                            out=sgate_d[:, gcol:gcol + 128],
                            in_=sg8[:, :]), 16)

        pM0, pV0 = sM.v, sV.v
        with nc.Block() as blk:
            @blk.vector
            def _(v):
                v.tensor_scalar_mul(r8[0:1, 0:8], cnt8[:, :], 1.0 / 512.0)
                v.nop(cycle_cnt=32)
                v.tensor_copy(i8a[:, :], r8[0:1, 0:8])
                v.nop(cycle_cnt=32)
                v.tensor_copy(r8[0:1, 8:16], i8a[:, :])
                v.nop(cycle_cnt=32)
                v.tensor_tensor(r8[0:1, 16:24], r8[0:1, 0:8], r8[0:1, 8:16],
                                ALU.is_gt)
                v.nop(cycle_cnt=32)
                v.tensor_add(r8[0:1, 8:16], r8[0:1, 8:16], r8[0:1, 16:24])
                v.nop(cycle_cnt=32)
                v.tensor_scalar_mul(r8[0:1, 8:16], r8[0:1, 8:16], 512.0)
                sV.inc(v.drain())

            @blk.tensor
            def _(tn, pV0=pV0):
                tn.wait_ge(sV.h, pV0 + 1)
                sM.inc(tn.matmul(ps_g[0:8, 0:8], onesr[0:1, 0:8],
                                 r8[0:1, 8:16], start=True, stop=True))

            @blk.vector
            def _(v, pM0=pM0):
                v.wait_ge(sM.h, pM0 + 1)
                v.tensor_tensor(t88[:, :], ps_g[0:8, 0:8], lt8t[:, :],
                                ALU.mult)
                v.nop(cycle_cnt=32)
                v.reduce_sum(off8c[:, :], t88[:, :], axis=AX)
                sV.inc(v.drain())

            @blk.tensor
            def _(tn, pV0=pV0):
                tn.wait_ge(sV.h, pV0 + 2)
                sM.inc(tn.transpose(ps_sum[0:1, 0:8], off8c[:, :],
                                    idf[0:8, 0:8]))

            @blk.vector
            def _(v, pM0=pM0):
                v.wait_ge(sM.h, pM0 + 2)
                v.tensor_copy(offrow[:, :], ps_sum[0:1, 0:8])
                sV.inc(v.drain())

            @blk.sync
            def _(sync, pV0=pV0):
                sync.wait_ge(sV.h, pV0 + 3)
                ld.inc(sync.dma_start(out=D["offs"], in_=offrow[:, :]), 16)

            @blk.tensor
            def _(tn, pV0=pV0):
                tn.wait_ge(sV.h, pV0 + 3)
                sM.inc(tn.matmul(ps_g[:, :], onesr[:, :], offrow[:, :],
                                 start=True, stop=True))

            @blk.vector
            def _(v, pM0=pM0):
                v.wait_ge(sM.h, pM0 + 3)
                for gidx in range(64):
                    v.tensor_scalar(lg[:, 20:28], iota8[:, :],
                                    e16[:, gidx:gidx + 1], None, ALU.is_equal)
                    v.nop(cycle_cnt=32)
                    v.tensor_tensor(lg[:, 28:36], ps_g[:, :], lg[:, 20:28],
                                    ALU.mult)
                    v.nop(cycle_cnt=32)
                    v.reduce_sum(lg[:, 19:20], lg[:, 28:36], axis=AX)
                    v.nop(cycle_cnt=32)
                    v.tensor_add(lg[:, 19:20], lg[:, 19:20],
                                 rank6[:, gidx:gidx + 1])
                    v.nop(cycle_cnt=32)
                    v.tensor_copy(islot6[:, gidx:gidx + 1], lg[:, 19:20])
                sV.inc(v.drain())

        with nc.Block() as blk:
            @blk.sync
            def _(sync):
                sV.wait(sync)
                ld.inc(sync.dma_start(out=slot_d, in_=islot6[:, :]), 16)

    with nc.Block() as blk:
        @blk.sync
        def _(sync):
            ld.wait(sync)
    es.close()


def rope_tables():
    d = 64
    pos = np.arange(T, dtype=np.float32)
    freqs = (10000.0 ** (-np.arange(0, d, 2, dtype=np.float32) / d))
    ang = pos[:, None] * freqs
    cos = np.repeat(np.cos(ang), 2, axis=1).T.astype(np.float32)
    sin = np.repeat(np.sin(ang), 2, axis=1).T.astype(np.float32)
    cosb = np.concatenate([cos, cos], 0)
    sinb = np.concatenate([sin, sin], 0)
    p1 = np.zeros((64, 64), np.float32)
    for i in range(32):
        p1[2 * i + 1, 2 * i] = -1.0
        p1[2 * i, 2 * i + 1] = 1.0
    ps2 = np.zeros((128, 128), np.float32)
    ps2[0:64, 0:64] = p1
    ps2[64:128, 64:128] = p1
    return cosb, sinb, ps2


def emit_s7(nc, D, S, njg=64, nheads=8):
    _pad = 0
    es = contextlib.ExitStack()

    qr_d, kr_d, sq_d, sk_d = D["qrh"], D["krh"], D["sqh"], D["skh"]
    sktm_d, vtm_d = D["sktm"], D["vtm"]
    mb_d, msk_d, idn_d = D["mbeta"], D["mask4"], D["idn"]
    o_d = D["oh"]
    dbg = False

    def sb(stack, name, shape, dtype=F32):
        return stack.enter_context(nc.sbuf_tensor("s7_" + name, shape, dtype))

    def psum(stack, name, shape, dtype=F32):
        return stack.enter_context(nc.psum_tensor("s7_" + name, shape, dtype))

    qrh = sb(es, "qrt", [64, T], BF16)
    krh = sb(es, "krt", [64, T], BF16)
    sqh = sb(es, "sqt", [64, T], BF16)
    skh = sb(es, "skt", [64, T], BF16)
    sktmh = sb(es, "sktmt", [32, 256 * 64], BF16)
    vtmh = sb(es, "vtmt", [32, 256 * 64], BF16)
    mask4 = sb(es, "mask4t", [32, 128])
    idf = sb(es, "idft", [128, 128])
    idb = sb(es, "idbt", [128, 128], BF16)
    idb32 = sb(es, "idb32", [32, 32], BF16)
    mbt = sb(es, "mbt", [1, 8])
    ones1r = sb(es, "ones1r", [1, 32])
    bt32 = sb(es, "bt32", [32, 1])
    mz = sb(es, "mz", [64, 65])
    mz_bf = sb(es, "mz_bf", [64, 65], BF16)
    sc_sb = sb(es, "sc_sb", [32, 128])
    e_sb = sb(es, "e_sb", [32, 128])
    e_bf = sb(es, "e_bf", [32, 128], BF16)
    pT = sb(es, "pT", [32, 128], BF16)
    nmx = sb(es, "nmx", [32, 4])
    rs = sb(es, "rs", [32, 4])
    den1 = sb(es, "den1", [32, 2])
    mo_sb = sb(es, "mo_sb", [32, 64])
    pv_sb = sb(es, "pv_sb", [32, 64], BF16)
    dv65 = sb(es, "dv65", [32, 65], BF16)
    t_sb = sb(es, "t_sb", [32, 64])
    otm = sb(es, "otm", [32, 64], BF16)
    obuf = [sb(es, "obuf0", [64, 512], BF16), sb(es, "obuf1", [64, 512], BF16)]

    ps_sc = psum(es, "ps_sc", [32, 128])
    ps_sd = psum(es, "ps_sd", [32, 256])
    ps_nd = psum(es, "ps_nd", [32, 65])
    ps_pv = psum(es, "ps_pv", [32, 65])
    ps_mu = psum(es, "ps_mu", [64, 65])
    ps_ot = [psum(es, "ps_ot0", [64, 32], BF16),
             psum(es, "ps_ot1", [64, 32], BF16)]
    ps_bb = psum(es, "ps_bb", [32, 1])

    ld, sA, sM, sV = S
    ld_s7_base = ld.v

    with nc.Block() as blk:
        @blk.sync
        def _(sync):
            sync.wait_ge(ld.h, ld.v)
            ld.inc(sync.dma_start(out=mask4[:, :], in_=msk_d), 16)
            ld.inc(sync.dma_start(out=idf[:, :], in_=idn_d), 16)
            ld.inc(sync.dma_start(out=mbt[:, :], in_=mb_d), 16)
            ld.wait(sync)

        @blk.vector
        def _(v):
            v.wait_ge(ld.h, ld_s7_base + 48)
            v.tensor_copy(idb[:, :], idf[:, :])
            v.tensor_copy(idb32[:, :], idf[0:32, 0:32])
            v.memset(ones1r[:, :], 1.0)

        @blk.scalar
        def _(s):
            ld.wait(s)
            for _ in range(_pad):
                s.nop(cycle_cnt=7)
            s.activation(mbt[:, :], mbt[:, :], AF.Sigmoid)

    NJG = njg
    for hh in range(nheads):
        with nc.Block() as blk:
            @blk.sync
            def _(sync, hh=hh):
                sync.wait_ge(ld.h, ld.v)
                r0 = hh * 64
                ld.inc(sync.dma_start(out=qrh[:, :], in_=qr_d[r0:r0 + 64, :]), 16)
                ld.inc(sync.dma_start(out=krh[:, :], in_=kr_d[r0:r0 + 64, :]), 16)
                ld.inc(sync.dma_start(out=sqh[:, :], in_=sq_d[r0:r0 + 64, :]), 16)
                ld.inc(sync.dma_start(out=skh[:, :], in_=sk_d[r0:r0 + 64, :]), 16)
                ld.inc(sync.dma_start(
                    out=sktmh[:, :].rearrange("p (s d) -> p s d", d=64),
                    in_=sktm_d[:, r0:r0 + 64].rearrange("(s p) d -> p s d",
                                                        p=32)), 16)
                ld.inc(sync.dma_start(
                    out=vtmh[:, :].rearrange("p (s d) -> p s d", d=64),
                    in_=vtm_d[:, r0:r0 + 64].rearrange("(s p) d -> p s d",
                                                       p=32)), 16)
                ld.wait(sync)

            @blk.tensor
            def _(tn, hh=hh):
                sM.inc(tn.matmul(ps_bb[:, :], ones1r[:, :],
                                 mbt[:, hh:hh + 1], start=True, stop=True))

            @blk.vector
            def _(v):
                sM.wait(v)
                v.memset(mz[:, 0:64], 0.0)
                v.memset(mz[:, 64:65], 1e-6)
                v.memset(mz_bf[:, 0:64], 0.0)
                v.memset(mz_bf[:, 64:65], 1e-6)
                v.memset(dv65[:, 64:65], 1.0)
                v.tensor_copy(bt32[:, :], ps_bb[:, :])

        M0, V0, A0, L0 = sM.v, sV.v, sA.v, ld.v
        with nc.Block() as blk:
            @blk.tensor
            def _(tn, hh=hh):
                for J in range(NJG):
                    if J >= 1:
                        tn.wait_ge(sV.h, V0 + (J - 1) * 10 + 1)
                    for s4 in range(4):
                        j = J * 4 + s4
                        mm = tn.matmul(ps_sc[:, s4 * 32:(s4 + 1) * 32],
                                       qrh[:, j * 32:(j + 1) * 32],
                                       krh[:, j * 32:(j + 1) * 32],
                                       start=True, stop=True)
                    sM.inc(mm)
                    tn.wait_ge(sV.h, V0 + J * 10 + 2)
                    for s4 in range(4):
                        j = J * 4 + s4
                        mm = tn.matmul(ps_sd[:, s4 * 64:(s4 + 1) * 64],
                                       pT[:, s4 * 32:(s4 + 1) * 32],
                                       vtmh[:, j * 64:(j + 1) * 64],
                                       start=True, stop=True)
                    sM.inc(mm)
                    for s4 in range(4):
                        j = J * 4 + s4
                        if s4 >= 1:
                            tn.wait_ge(sV.h, V0 + J * 10 + 2 * s4 + 2)
                        tn.matmul(ps_nd[:, :], sqh[:, j * 32:(j + 1) * 32],
                                  mz_bf[:, :], start=True, stop=True)
                        sM.inc(tn.matmul(ps_pv[:, :],
                                         skh[:, j * 32:(j + 1) * 32],
                                         mz_bf[:, :], start=True, stop=True))
                        tn.wait_ge(sV.h, V0 + J * 10 + 3 + 2 * s4)
                        sM.inc(tn.matmul(ps_mu[:, :],
                                         sktmh[:, j * 64:(j + 1) * 64],
                                         dv65[:, :], start=True, stop=True))
                        if s4 >= 2:
                            tn.wait_ge(sA.h, A0 + J * 5 + 1 + s4)
                        elif J >= 1:
                            tn.wait_ge(sA.h, A0 + (J - 1) * 5 + 4 + s4)
                        sM.inc(tn.transpose(ps_ot[s4 % 2][:, 0:32], otm[:, :],
                                            idb32[:, :]))

            @blk.vector
            def _(v, hh=hh):
                for J in range(NJG):
                    if J >= 1:
                        v.wait_ge(sA.h, A0 + (J - 1) * 5 + 1)
                    v.wait_ge(sM.h, M0 + J * 14 + 1)
                    v.scalar_tensor_tensor(sc_sb[:, :], ps_sc[:, :], 0.125,
                                           mask4[:, :], ALU.mult, ALU.add)
                    v.nop(cycle_cnt=32)
                    for s4 in range(4):
                        v.tensor_reduce(nmx[:, s4:s4 + 1],
                                        sc_sb[:, s4 * 32:(s4 + 1) * 32],
                                        axis=AX, op=ALU.max)
                    v.nop(cycle_cnt=32)
                    for s4 in range(4):
                        v.tensor_scalar(sc_sb[:, s4 * 32:(s4 + 1) * 32],
                                        sc_sb[:, s4 * 32:(s4 + 1) * 32],
                                        nmx[:, s4:s4 + 1], None, ALU.subtract)
                    sV.inc(v.drain())
                    v.wait_ge(sA.h, A0 + J * 5 + 1)
                    for s4 in range(4):
                        v.reduce_sum(rs[:, s4:s4 + 1],
                                     e_sb[:, s4 * 32:(s4 + 1) * 32], axis=AX)
                    for s4 in range(4):
                        v.reciprocal(rs[:, s4:s4 + 1], rs[:, s4:s4 + 1])
                    for s4 in range(4):
                        v.tensor_scalar(e_bf[:, s4 * 32:(s4 + 1) * 32],
                                        e_sb[:, s4 * 32:(s4 + 1) * 32],
                                        rs[:, s4:s4 + 1], None, ALU.mult)
                    v.drain()
                    v.transpose(pT[:, :], e_bf[:, :])
                    sV.inc(v.drain())
                    for s4 in range(4):
                        j = J * 4 + s4
                        v.wait_ge(sM.h, M0 + J * 14 + 3 + 3 * s4)
                        v.tensor_scalar(den1[:, 0:1], ps_nd[:, 64:65], 1e-6,
                                        None, ALU.add)
                        v.reciprocal(den1[:, 1:2], ps_pv[:, 64:65])
                        v.nop(cycle_cnt=32)
                        v.reciprocal(den1[:, 0:1], den1[:, 0:1])
                        v.tensor_scalar_mul(pv_sb[:, :], ps_pv[:, 0:64],
                                            den1[:, 1:2])
                        v.nop(cycle_cnt=32)
                        v.tensor_scalar_mul(mo_sb[:, :], ps_nd[:, 0:64],
                                            den1[:, 0:1])
                        v.tensor_sub(dv65[:, 0:64],
                                     vtmh[:, j * 64:(j + 1) * 64], pv_sb[:, :])
                        v.nop(cycle_cnt=32)
                        v.tensor_sub(t_sb[:, :], mo_sb[:, :],
                                     ps_sd[:, s4 * 64:(s4 + 1) * 64])
                        v.nop(cycle_cnt=32)
                        v.tensor_scalar_mul(t_sb[:, :], t_sb[:, :],
                                            bt32[:, :])
                        v.nop(cycle_cnt=32)
                        v.tensor_add(otm[:, :], t_sb[:, :],
                                     ps_sd[:, s4 * 64:(s4 + 1) * 64])
                        sV.inc(v.drain())
                        v.wait_ge(sM.h, M0 + J * 14 + 4 + 3 * s4)
                        v.tensor_add(mz[:, :], mz[:, :], ps_mu[:, :])
                        v.nop(cycle_cnt=32)
                        v.tensor_copy(mz_bf[:, :], mz[:, :])
                        sV.inc(v.drain())

            @blk.scalar
            def _(s, hh=hh):
                for J in range(NJG):
                    s.wait_ge(sV.h, V0 + J * 10 + 1)
                    s.activation(e_sb[:, :], sc_sb[:, :], AF.Exp)
                    sA.inc(s.drain())
                    for s4 in range(4):
                        cb = J // 4
                        if J % 4 == 0 and s4 == 0 and cb >= 2:
                            s.wait_ge(ld.h, L0 + 16 * (cb - 1))
                        s.wait_ge(sM.h, M0 + J * 14 + 5 + 3 * s4)
                        col = (J % 4) * 128 + s4 * 32
                        s.copy(obuf[cb % 2][:, col:col + 32],
                               ps_ot[s4 % 2][:, 0:32])
                        sA.inc(s.drain())

            @blk.sync
            def _(sync, hh=hh):
                for cb in range(NJG // 4):
                    sync.wait_ge(sA.h, A0 + (4 * cb + 3) * 5 + 5)
                    ld.inc(sync.dma_start(
                        out=o_d[hh * 64:hh * 64 + 64,
                                cb * 512:(cb + 1) * 512],
                        in_=obuf[cb % 2][:, :]), 16)

    with nc.Block() as blk:
        @blk.sync
        def _(sync):
            ld.wait(sync)
    es.close()










def emit_pair_allreduce(nc, D, S, ccs, src_name, dst_name):
    ld, sA, sM, sV = S
    with nc.Block() as blk:
        @blk.gpsimd
        def _(g_):
            g_.wait_ge(ld.h, ld.v)
            g_.collective_compute(
                "AllReduce", ALU.add,
                replica_groups=[[0, 1], [2, 3], [4, 5], [6, 7]],
                ins=[D[src_name].opt()],
                outs=[D[dst_name].opt()],
            ).then_inc(ccs.h)
    with nc.Block() as blk:
        @blk.sync
        def _(sync):
            sync.wait_ge(ccs.h, 1)


def emit_exchange(nc, D, S, cc):
    ld, sA, sM, sV = S
    with nc.Block() as blk:
        @blk.gpsimd
        def _(g_):
            g_.wait_ge(ld.h, ld.v)
            g_.collective_compute(
                "AllGather", ALU.bypass,
                replica_groups=[[0, 1], [2, 3], [4, 5], [6, 7]],
                ins=[D["oh"].opt()],
                outs=[D["og"].opt()],
            ).then_inc(cc.h)



def emit_s8(nc, D, S, cc, ccg=None):
    ld, sA, sM, sV = S
    og_d, x1_d, xn1bf_d = D["og"], D["x1"], D["xn1bf"]
    sg_d, slot_d = D["sgate"], D["slotint"]
    gbuf_d, gout_d = D["gbuf"], D["gout"]
    wout_d, w1_d, w2_d = D["wout"], D["mw1"], D["mw2"]
    b1_d, b2_d, idn_d = D["mb1"], D["mb2"], D["idn"]
    out_d = D["out"]
    es = contextlib.ExitStack()

    def sb(stack, name, shape, dtype=F32):
        return stack.enter_context(nc.sbuf_tensor("s8_" + name, shape, dtype))

    def psum(stack, name, shape, dtype=F32):
        return stack.enter_context(nc.psum_tensor("s8_" + name, shape, dtype))

    with contextlib.ExitStack() as ea:
        xnbfc = sb(ea, "xnbfc", [128, G * 512], BF16)
        xtok = sb(ea, "xtok", [128, 2 * C], BF16)
        islot = sb(ea, "islot", [128, 64], I32)
        idf = sb(ea, "idfa", [128, 128])
        idb = sb(ea, "idba", [128, 128], BF16)
        ps_t = [psum(ea, f"ps_ta{i}", [128, 128], BF16) for i in range(2)]

        with nc.Block() as blk:
            @blk.sync
            def _(sync):
                sync.wait_ge(ld.h, ld.v)
                ld.inc(sync.dma_start(out=islot[:, :], in_=slot_d), 16)
                ld.inc(sync.dma_start(out=idf[:, :], in_=idn_d), 16)
                ld.wait(sync)

            @blk.vector
            def _(v):
                ld.wait(v)
                v.tensor_copy(idb[:, :], idf[:, :])
                sV.inc(v.drain())
        sV_idb = sV.v

        last_sc_ld = [0, 0]
        for n in range(16):
            c0 = n * 512
            with nc.Block() as blk:
                @blk.sync
                def _(sync, c0=c0):
                    sync.wait_ge(ld.h, ld.v)
                    ld.inc(sync.dma_start(
                        out=xnbfc[:, :].rearrange("p (g t) -> p g t", t=512),
                        in_=xn1bf_d[:, c0:c0 + 512].rearrange(
                            "(g p) t -> p g t", p=128)), 16)
                    ld.wait(sync)
            ldL = ld.v
            for sl4 in range(4):
                par = sl4 % 2
                aM0, aA0 = sM.v, sA.v
                prev_ld = last_sc_ld[par]
                with nc.Block() as blk:
                    @blk.tensor
                    def _(tn, sl4=sl4, ldL=ldL, aA0=aA0):
                        tn.wait_ge(ld.h, ldL)
                        tn.wait_ge(sV.h, sV_idb)
                        for cg in range(G):
                            if cg >= 2:
                                tn.wait_ge(sA.h, aA0 + cg - 1)
                            sM.inc(tn.transpose(
                                ps_t[cg % 2][:, :],
                                xnbfc[:, cg * 512 + sl4 * 128:
                                      cg * 512 + sl4 * 128 + 128],
                                idb[:, :]))

                    @blk.scalar
                    def _(s, par=par, aM0=aM0, prev_ld=prev_ld):
                        if prev_ld:
                            s.wait_ge(ld.h, prev_ld)
                        for cg in range(G):
                            s.wait_ge(sM.h, aM0 + cg + 1)
                            s.copy(xtok[:, par * C + cg * 128:
                                        par * C + (cg + 1) * 128],
                                   ps_t[cg % 2][:, :])
                            sA.inc(s.drain())

                    @blk.gpsimd
                    def _(g_, n=n, sl4=sl4, par=par, aA0=aA0):
                        g_.wait_ge(sA.h, aA0 + 8)
                        ld.inc(g_.indirect_dma_start(
                            out=gbuf_d,
                            out_offset=bass.IndirectOffsetOnAxis(
                                islot[:, n * 4 + sl4:n * 4 + sl4 + 1], 0),
                            in_=xtok[:, par * C:(par + 1) * C],
                            in_offset=None), 16)
                last_sc_ld[par] = ld.v

    NQ = 24
    with contextlib.ExitStack() as eb:
        w1t = sb(eb, "w1t", [128, G * 2 * C], BF16)
        w2r = sb(eb, "w2r", [128, 16 * C], BF16)
        b1c = sb(eb, "b1c", [128, 16])
        gtok = sb(eb, "gtok", [128, 4 * C], BF16)
        xe = sb(eb, "xe", [128, G * 512], BF16)
        u_bf = sb(eb, "u_bf", [128, 16 * 512], BF16)
        oe = sb(eb, "oe", [128, 512])
        otok = sb(eb, "otok", [128, 4 * C])
        idfb = sb(eb, "idfb", [128, 128])
        idbb = sb(eb, "idbb", [128, 128], BF16)
        offr2 = sb(eb, "offr2", [1, 8])
        qcolt = sb(eb, "qcolt", [24, 1])
        piota = sb(eb, "piota", [128, 1])
        ge24 = sb(eb, "ge24", [24, 8])
        ec24 = sb(eb, "ec24", [24, 1])
        ecrow = sb(eb, "ecrow", [1, 24])
        ecb = sb(eb, "ecb", [128, 24])
        wf = sb(eb, "wf", [128, 24])
        w0 = sb(eb, "w0", [128, 24])
        iwidx = sb(eb, "iwidx", [128, 8 * 24], I32)
        iw2 = sb(eb, "iw2", [128, 16 * 24], I32)
        ib1 = sb(eb, "ib1", [128, 24], I32)
        ones1b = sb(eb, "ones1b", [1, 128])
        ps_h = [psum(eb, f"ps_hb{i}", [128, 512]) for i in range(2)]
        ps_o = [psum(eb, f"ps_ob{i}", [128, 512]) for i in range(2)]
        ps_tb = [psum(eb, f"ps_tbb{i}", [128, 128], BF16) for i in range(2)]
        ps_tf = [psum(eb, f"ps_tfb{i}", [128, 128]) for i in range(2)]

        kB0 = sM.v
        with nc.Block() as blk:
            @blk.sync
            def _(sync):
                sync.wait_ge(ld.h, ld.v)
                ld.inc(sync.dma_start(out=idfb[:, :], in_=idn_d), 16)
                ld.inc(sync.dma_start(out=offr2[:, :], in_=D["offs"]), 16)
                ld.inc(sync.dma_start(out=qcolt[:, :], in_=D["qcol"]), 16)
                ld.inc(sync.dma_start(out=piota[:, :], in_=D["piota"]), 16)
                ld.wait(sync)

            @blk.vector
            def _(v):
                ld.wait(v)
                v.tensor_copy(idbb[:, :], idfb[:, :])
                v.memset(otok[:, :], 0.0)
                v.memset(ones1b[:, :], 1.0)
                sV.inc(v.drain())

            @blk.gpsimd
            def _(g_):
                sV.wait(g_)
                ld.inc(g_.dma_start(
                    out=D["goutp"][NSLOT:NSLOT + 128, :],
                    in_=otok[:, 0:C]), 16)
        sV_idb2 = sV.v

        kM0, kV0 = sM.v, sV.v
        with nc.Block() as blk:
            @blk.tensor
            def _(tn):
                sM.inc(tn.matmul(ge24[0:24, 0:8] if False else ps_h[0][0:24, 0:8],
                                 ones1b[0:1, 0:24], offr2[:, :],
                                 start=True, stop=True))

            @blk.vector
            def _(v, kM0=kM0):
                v.wait_ge(sM.h, kM0 + 1)
                v.tensor_scalar(ge24[:, :], ps_h[0][0:24, 0:8],
                                qcolt[:, 0:1], None, ALU.is_le)
                v.nop(cycle_cnt=32)
                v.reduce_sum(ec24[:, :], ge24[:, :], axis=AX)
                v.nop(cycle_cnt=32)
                v.tensor_scalar(ec24[:, :], ec24[:, :], -1.0, None, ALU.add)
                sV.inc(v.drain())

            @blk.tensor
            def _(tn, kV0=kV0):
                tn.wait_ge(sV.h, kV0 + 1)
                sM.inc(tn.transpose(ps_tf[0][0:1, 0:24], ec24[:, :],
                                    idfb[0:24, 0:24]))

            @blk.scalar
            def _(s, kM0=kM0):
                s.wait_ge(sM.h, kM0 + 2)
                s.copy(ecrow[:, :], ps_tf[0][0:1, 0:24])
                sA.inc(s.drain())

            @blk.tensor
            def _(tn, kA0=sA.v):
                tn.wait_ge(sA.h, kA0)
                sM.inc(tn.matmul(ps_h[1][0:128, 0:24], ones1b[0:1, :],
                                 ecrow[:, :], start=True, stop=True))

            @blk.vector
            def _(v, kM0=kM0):
                v.wait_ge(sM.h, kM0 + 3)
                v.tensor_copy(ecb[:, :], ps_h[1][0:128, 0:24])
                sV.inc(v.drain())

            @blk.vector
            def _(v):
                v.tensor_scalar_mul(w0[:, :], ecb[:, :], 1024.0)
                v.nop(cycle_cnt=32)
                v.tensor_scalar(w0[:, :], w0[:, :], piota[:, 0:1], None,
                                ALU.add)
                v.nop(cycle_cnt=32)
                for kc in range(G):
                    v.tensor_scalar(wf[:, :], w0[:, :], float(kc * 128), None,
                                    ALU.add)
                    v.nop(cycle_cnt=32)
                    v.tensor_copy(iwidx[:, kc * 24:(kc + 1) * 24], wf[:, :])
                    v.nop(cycle_cnt=32)
                v.tensor_scalar_mul(w0[:, :], ecb[:, :], 2048.0)
                v.nop(cycle_cnt=32)
                v.tensor_scalar(w0[:, :], w0[:, :], piota[:, 0:1], None,
                                ALU.add)
                v.nop(cycle_cnt=32)
                for hc in range(16):
                    v.tensor_scalar(wf[:, :], w0[:, :], float(hc * 128), None,
                                    ALU.add)
                    v.nop(cycle_cnt=32)
                    v.tensor_copy(iw2[:, hc * 24:(hc + 1) * 24], wf[:, :])
                    v.nop(cycle_cnt=32)
                v.tensor_scalar_mul(w0[:, :], ecb[:, :], 128.0)
                v.nop(cycle_cnt=32)
                v.tensor_scalar(w0[:, :], w0[:, :], piota[:, 0:1], None,
                                ALU.add)
                v.nop(cycle_cnt=32)
                v.tensor_copy(ib1[:, :], w0[:, :])
                sV.inc(v.drain())

        for q in range(NQ):
            r0 = q * 512
            prevM = sM.v
            with nc.Block() as blk:
                @blk.gpsimd
                def _(g_, q=q, prevM=prevM):
                    g_.wait_ge(sM.h, prevM)
                    for kc in range(G):
                        ld.inc(g_.indirect_dma_start(
                            out=w1t[:, kc * 2 * C:(kc + 1) * 2 * C],
                            out_offset=None,
                            in_=w1_d,
                            in_offset=bass.IndirectOffsetOnAxis(
                                iwidx[:, kc * 24 + q:kc * 24 + q + 1], 0)),
                            16)
                    for hc in range(16):
                        ld.inc(g_.indirect_dma_start(
                            out=w2r[:, hc * C:(hc + 1) * C],
                            out_offset=None,
                            in_=w2_d,
                            in_offset=bass.IndirectOffsetOnAxis(
                                iw2[:, hc * 24 + q:hc * 24 + q + 1], 0)),
                            16)
                    ld.inc(g_.indirect_dma_start(
                        out=b1c[:, :],
                        out_offset=None,
                        in_=D["b1pr"],
                        in_offset=bass.IndirectOffsetOnAxis(
                            ib1[:, q:q + 1], 0)), 16)

                @blk.sync
                def _(sync, r0=r0):
                    sync.wait_ge(ld.h, ld.v)
                    ld.inc(sync.dma_start(
                        out=gtok[:, :].rearrange("p (gr c) -> p gr c", c=C),
                        in_=gbuf_d[r0:r0 + 512, :].rearrange(
                            "(gr p) c -> p gr c", p=128)), 16)
                    ld.wait(sync)
            ld_w = ld.v

            bM0, bV0 = sM.v, sV.v
            with nc.Block() as blk:
                @blk.tensor
                def _(tn, ld_w=ld_w, bV0=bV0):
                    tn.wait_ge(ld.h, ld_w)
                    tn.wait_ge(sV.h, sV_idb2)
                    for i in range(32):
                        gr, cg = divmod(i, 8)
                        if i >= 2:
                            tn.wait_ge(sV.h, bV0 + i - 1)
                        sM.inc(tn.transpose(
                            ps_tb[i % 2][:, :],
                            gtok[:, gr * C + cg * 128:
                                 gr * C + (cg + 1) * 128],
                            idbb[:, :]))

                @blk.vector
                def _(v, bM0=bM0):
                    for i in range(32):
                        gr, cg = divmod(i, 8)
                        v.wait_ge(sM.h, bM0 + i + 1)
                        v.tensor_copy(xe[:, cg * 512 + gr * 128:
                                         cg * 512 + (gr + 1) * 128],
                                      ps_tb[i % 2][:, :])
                        sV.inc(v.drain())

            cM0, cA0 = sM.v, sA.v
            with nc.Block() as blk:
                @blk.tensor
                def _(tn, cA0=cA0, bV0=bV0):
                    tn.wait_ge(sV.h, bV0 + 32)
                    for hc in range(16):
                        if hc >= 2:
                            tn.wait_ge(sA.h, cA0 + hc - 1)
                        for kc in range(G):
                            mm = tn.matmul(
                                ps_h[hc % 2][:, :],
                                w1t[:, kc * 2 * C + hc * 128:
                                    kc * 2 * C + hc * 128 + 128],
                                xe[:, kc * 512:(kc + 1) * 512],
                                start=(kc == 0), stop=(kc == G - 1))
                        sM.inc(mm)

                @blk.scalar
                def _(s, cM0=cM0):
                    for hc in range(16):
                        s.wait_ge(sM.h, cM0 + hc + 1)
                        s.activation(u_bf[:, hc * 512:(hc + 1) * 512],
                                     ps_h[hc % 2][:, :], AF.Gelu,
                                     bias=b1c[:, hc:hc + 1])
                        sA.inc(s.drain())

            dM0, dV0, dA0, dL0 = sM.v, sV.v, sA.v, ld.v
            with nc.Block() as blk:
                @blk.tensor
                def _(tn, dM0=dM0, dA0=dA0, dV0=dV0):
                    for m in range(G):
                        if m >= 2:
                            tn.wait_ge(sA.h, dA0 + m - 1)
                        for hc in range(16):
                            mm = tn.matmul(
                                ps_o[m % 2][:, :],
                                w2r[:, hc * C + m * 128:
                                    hc * C + (m + 1) * 128],
                                u_bf[:, hc * 512:(hc + 1) * 512],
                                start=(hc == 0), stop=(hc == 15))
                        sM.inc(mm)
                        tn.wait_ge(sA.h, dA0 + m + 1)
                        for gr in range(4):
                            if m * 4 + gr >= 2:
                                tn.wait_ge(sV.h, dV0 + m * 4 + gr - 1)
                            sM.inc(tn.transpose(
                                ps_tf[gr % 2][:, :],
                                oe[:, gr * 128:(gr + 1) * 128],
                                idfb[:, :]))

                @blk.scalar
                def _(s, dM0=dM0):
                    for m in range(G):
                        s.wait_ge(sM.h, dM0 + m * 5 + 1)
                        s.copy(oe[:, :], ps_o[m % 2][:, :])
                        sA.inc(s.drain())

                @blk.vector
                def _(v, dM0=dM0, dL0=dL0):
                    for m in range(G):
                        for gr in range(4):
                            if m == 0 and gr == 0:
                                v.wait_ge(ld.h, dL0)
                            v.wait_ge(sM.h, dM0 + m * 5 + 2 + gr)
                            v.tensor_copy(otok[:, gr * C + m * 128:
                                               gr * C + (m + 1) * 128],
                                          ps_tf[gr % 2][:, :])
                            sV.inc(v.drain())

                @blk.sync
                def _(sync, r0=r0, dV0=dV0):
                    sync.wait_ge(sV.h, dV0 + 32)
                    ld.inc(sync.dma_start(
                        out=D["goutp"][r0:r0 + 512, :].rearrange(
                            "(gr p) c -> p gr c", p=128),
                        in_=otok[:, :].rearrange("p (gr c) -> p gr c",
                                                 c=C)), 16)

            if q % 4 == 3:
                k = q // 4
                p0 = k * 2048
                p1 = p0 + 2048 if k < 5 else NSLOT + 128
                ar_ld = ld.v
                with nc.Block() as blk:
                    @blk.gpsimd
                    def _(g_, k=k, p0=p0, p1=p1, ar_ld=ar_ld):
                        g_.wait_ge(ld.h, ar_ld)
                        g_.collective_compute(
                            "AllReduce", ALU.add,
                            replica_groups=[[0, 1], [2, 3], [4, 5], [6, 7]],
                            ins=[gout_d if False else D["goutp"][p0:p1, :].opt()],
                            outs=[D["gout"][p0:p1, :].opt()],
                        ).then_inc(ccg[k].h)

    with contextlib.ExitStack() as ec:
        wout_b = sb(ec, "wout_b", [128, G * C], BF16)
        oc = sb(ec, "oc", [128, G * 512], BF16)
        x1c = sb(ec, "x1c", [128, G * 512])
        s1_sb = sb(ec, "s1sb", [128, G * 512])
        mtok = sb(ec, "mtok", [128, 4 * C])
        macc = sb(ec, "macc", [128, G * 512])
        sgc8 = sb(ec, "sgc8", [8, 512])
        b2t = sb(ec, "b2t2", [8, C])
        rowg = sb(ec, "rowg", [1, 512])
        ones8 = sb(ec, "ones8", [8, 1])
        onesr = sb(ec, "onesr2", [1, 128])
        idf3 = sb(ec, "idf3", [128, 128])
        islot2 = sb(ec, "islot2", [128, 64], I32)
        ps_h2 = [psum(ec, f"ps_hc{i}", [128, 512]) for i in range(2)]
        ps_tf2 = [psum(ec, f"ps_tfc{i}", [128, 128]) for i in range(2)]
        ps_w = psum(ec, "ps_wc", [128, 512])
        ps_b2 = psum(ec, "ps_b2c", [128, 512])
        ps_row = psum(ec, "ps_rowc", [1, 512])

        with nc.Block() as blk:
            @blk.sync
            def _(sync):
                sync.wait_ge(ld.h, ld.v)
                sync.wait_ge(cc.h, 1)
                ld.inc(sync.dma_start(out=b2t[:, :], in_=b2_d), 16)
                ld.inc(sync.dma_start(out=islot2[:, :], in_=slot_d), 16)
                ld.inc(sync.dma_start(out=idf3[:, :], in_=idn_d), 16)
                ld.wait(sync)

            @blk.gpsimd
            def _(g_):
                ld.inc(g_.dma_start(
                    out=wout_b[:, :].rearrange("p (kc o) -> p kc o", o=C),
                    in_=wout_d.rearrange("(kc p) o -> p kc o", p=128)), 16)

            @blk.vector
            def _(v):
                v.memset(ones8[:, :], 1.0)
                v.memset(onesr[:, :], 1.0)
                sV.inc(v.drain())
        wout_ld = ld.v

        for n in range(16):
            c0 = n * 512
            eL0 = ld.v
            with nc.Block() as blk:
                @blk.sync
                def _(sync, c0=c0):
                    sync.wait_ge(ld.h, ld.v)
                    ld.inc(sync.dma_start(
                        out=oc[:, :].rearrange("p (g t) -> p g t", t=512),
                        in_=og_d[:, c0:c0 + 512].rearrange(
                            "(g p) t -> p g t", p=128)), 16)
                    ld.inc(sync.dma_start(
                        out=x1c[:, :].rearrange("p (g t) -> p g t", t=512),
                        in_=x1_d[:, c0:c0 + 512].rearrange(
                            "(g p) t -> p g t", p=128)), 16)
                    ld.inc(sync.dma_start(out=sgc8[:, :],
                                          in_=sg_d[:, c0:c0 + 512]), 16)
                    ld.wait(sync)
            ld_chunk = ld.v

            fA0, fM0 = sA.v, sM.v
            with nc.Block() as blk:
                @blk.tensor
                def _(tn, fA0=fA0, ld_chunk=ld_chunk):
                    tn.wait_ge(ld.h, wout_ld)
                    tn.wait_ge(ld.h, ld_chunk)
                    for m in range(G):
                        if m >= 2:
                            tn.wait_ge(sA.h, fA0 + m - 1)
                        for kc in range(G):
                            mm = tn.matmul(
                                ps_h2[m % 2][:, :],
                                wout_b[:, kc * C + m * 128:
                                       kc * C + m * 128 + 128],
                                oc[:, kc * 512:(kc + 1) * 512],
                                start=(kc == 0), stop=(kc == G - 1))
                        sM.inc(mm)

                @blk.scalar
                def _(s, fM0=fM0):
                    for m in range(G):
                        s.wait_ge(sM.h, fM0 + m + 1)
                        s.copy(s1_sb[:, m * 512:(m + 1) * 512],
                               ps_h2[m % 2][:, :])
                        sA.inc(s.drain())

            with nc.Block() as blk:
                @blk.gpsimd
                def _(g_, n=n):
                    if n == 0:
                        for k in range(6):
                            g_.wait_ge(ccg[k].h, 1)
                    for tg in range(4):
                        ld.inc(g_.indirect_dma_start(
                            out=mtok[:, tg * C:(tg + 1) * C],
                            out_offset=None,
                            in_=gout_d,
                            in_offset=bass.IndirectOffsetOnAxis(
                                islot2[:, n * 4 + tg:n * 4 + tg + 1], 0)),
                            16)
            ld_gather = ld.v

            hM0, hV0 = sM.v, sV.v
            with nc.Block() as blk:
                @blk.tensor
                def _(tn, hV0=hV0, ld_gather=ld_gather):
                    tn.wait_ge(ld.h, ld_gather)
                    for i in range(32):
                        tg, cg = divmod(i, 8)
                        if i >= 2:
                            tn.wait_ge(sV.h, hV0 + i - 1)
                        sM.inc(tn.transpose(
                            ps_tf2[i % 2][:, :],
                            mtok[:, tg * C + cg * 128:
                                 tg * C + (cg + 1) * 128],
                            idf3[:, :]))

                @blk.vector
                def _(v, hM0=hM0):
                    for i in range(32):
                        tg, cg = divmod(i, 8)
                        v.wait_ge(sM.h, hM0 + i + 1)
                        v.tensor_copy(macc[:, cg * 512 + tg * 128:
                                           cg * 512 + tg * 128 + 128],
                                      ps_tf2[i % 2][:, :])
                        sV.inc(v.drain())

            iM0, iV0, iA0 = sM.v, sV.v, sA.v
            with nc.Block() as blk:
                @blk.tensor
                def _(tn):
                    sM.inc(tn.matmul(ps_row[:, :], ones8[:, 0:1],
                                     sgc8[:, :], start=True, stop=True))

                @blk.scalar
                def _(s, iM0=iM0):
                    s.wait_ge(sM.h, iM0 + 1)
                    s.copy(rowg[:, :], ps_row[:, :])
                    sA.inc(s.drain())

                @blk.tensor
                def _(tn, iA0=iA0):
                    tn.wait_ge(sA.h, iA0 + 1)
                    sM.inc(tn.matmul(ps_w[:, :], onesr[:, :], rowg[:, :],
                                     start=True, stop=True))

                @blk.vector
                def _(v, iM0=iM0):
                    v.wait_ge(sM.h, iM0 + 2)
                    for m in range(G):
                        sl = slice(m * 512, (m + 1) * 512)
                        v.tensor_mul(macc[:, sl], macc[:, sl], ps_w[:, :])
                    sV.inc(v.drain())
            jM0, jV0, jA0 = sM.v, sV.v, sA.v
            with nc.Block() as blk:
                @blk.tensor
                def _(tn, jV0=jV0):
                    for m in range(G):
                        if m >= 1:
                            tn.wait_ge(sV.h, jV0 + m)
                        sM.inc(tn.matmul(ps_b2[:, :],
                                         b2t[:, m * 128:(m + 1) * 128],
                                         sgc8[:, :], start=True, stop=True))

                @blk.vector
                def _(v, jM0=jM0):
                    for m in range(G):
                        v.wait_ge(sM.h, jM0 + m + 1)
                        sl = slice(m * 512, (m + 1) * 512)
                        v.tensor_add(macc[:, sl], macc[:, sl], ps_b2[:, :])
                        sV.inc(v.drain())

                @blk.scalar
                def _(s, jV0=jV0):
                    for m in range(G):
                        s.wait_ge(sV.h, jV0 + m + 1)
                        sl = slice(m * 512, (m + 1) * 512)
                        s.activation(macc[:, sl], macc[:, sl], AF.Silu)
                        sA.inc(s.drain())

                @blk.vector
                def _(v, jA0=jA0):
                    for m in range(G):
                        v.wait_ge(sA.h, jA0 + m + 1)
                        sl = slice(m * 512, (m + 1) * 512)
                        v.tensor_mul(s1_sb[:, sl], s1_sb[:, sl], macc[:, sl])
                        v.nop(cycle_cnt=32)
                        v.tensor_add(x1c[:, sl], x1c[:, sl], s1_sb[:, sl])
                        sV.inc(v.drain(), 2)

                @blk.sync
                def _(sync, n=n, jV0=jV0):
                    sync.wait_ge(sV.h, jV0 + G + 2 * G)
                    ld.inc(sync.dma_start(
                        out=out_d[:, n * 512:(n + 1) * 512].rearrange(
                            "(g p) t -> p g t", p=128),
                        in_=x1c[:, :].rearrange("p (g t) -> p g t", t=512)),
                        16)

    with nc.Block() as blk:
        @blk.sync
        def _(sync):
            ld.wait(sync)
    es.close()


def emit_s8_dense(nc, D, S, cc, nchunks=T // 512, nexperts=8):
    SKIP = ''
    es = contextlib.ExitStack()
    th0 = 0

    o_d = D["og"]
    wout_d, x1_d, xn1bf_d = D["wout"], D["x1"], D["xn1bf"]
    sg_d = D["sgate"]
    w1_d, w2_d, b1_d, b2_d = D["mw1"], D["mw2"], D["mb1"], D["mb2"]
    out_d = D["out"]

    def sb(stack, name, shape, dtype=F32):
        return stack.enter_context(nc.sbuf_tensor("s8d_" + name, shape, dtype))

    def psum(stack, name, shape, dtype=F32):
        return stack.enter_context(nc.psum_tensor("s8d_" + name, shape, dtype))

    wout_b = sb(es, "wout_b", [128, G * C], BF16)
    b1t = sb(es, "b1t", [128, 8 * 32])
    b2t = sb(es, "b2t", [8, C])
    w1t = sb(es, "w1t", [128, G * 4 * C], BF16)
    w2t = [sb(es, "w2t0", [128, 32 * 128], BF16),
           sb(es, "w2t1", [128, 32 * 128], BF16)]
    x1c = sb(es, "x1c", [128, G * 512])
    xnbfc = sb(es, "xnbfc", [128, G * 512], BF16)
    s1_sb = sb(es, "s1_sb", [128, G * 512])
    acc = sb(es, "acc", [128, G * 512])
    u_bf = sb(es, "u_bf", [128, 32 * 512], BF16)
    oc = u_bf
    glt = sb(es, "glt", [128, 2 * 512])
    sgc8 = sb(es, "sgc8", [8, 512])
    sgc1 = sb(es, "sgc1", [1, 512])

    ps_h = [psum(es, f"ps_h{i}", [128, 512]) for i in range(2)]
    ps_o = [psum(es, f"ps_o{i}", [128, 512]) for i in range(2)]
    ps_w = psum(es, "ps_w", [128, 512])
    ps_b2 = psum(es, "ps_b2", [128, 512])

    ld, sA, sM, sV = S
    onesr = sb(es, "onesr", [1, 128])

    with nc.Block() as blk:
        @blk.sync
        def _(sync):
            sync.wait_ge(ld.h, ld.v)
            sync.wait_ge(cc.h, 1)
            ld.inc(sync.dma_start(out=b1t[:, :], in_=b1_d), 16)
            ld.inc(sync.dma_start(out=b2t[:, :], in_=b2_d), 16)
            ld.wait(sync)

        @blk.gpsimd
        def _(g_):
            ld.inc(g_.dma_start(
                out=wout_b[:, :].rearrange("p (kc o) -> p kc o", o=C),
                in_=wout_d.rearrange("(kc p) o -> p kc o", p=128)), 16)

        @blk.vector
        def _(v):
            v.memset(onesr[:, :], 1.0)

    for n in range(nchunks):
        c0 = th0 + n * 512
        with nc.Block() as blk:
            @blk.sync
            def _(sync, n=n, c0=c0):
                sync.wait_ge(ld.h, ld.v)
                ld.inc(sync.dma_start(
                    out=oc[:, 0:G * 512].rearrange("p (g t) -> p g t", t=512),
                    in_=o_d[:, c0:c0 + 512].rearrange("(g p) t -> p g t",
                                                      p=128)), 16)
                ld.inc(sync.dma_start(
                    out=x1c[:, :].rearrange("p (g t) -> p g t", t=512),
                    in_=x1_d[:, c0:c0 + 512].rearrange("(g p) t -> p g t",
                                                       p=128)), 16)
                ld.inc(sync.dma_start(
                    out=xnbfc[:, :].rearrange("p (g t) -> p g t", t=512),
                    in_=xn1bf_d[:, c0:c0 + 512].rearrange("(g p) t -> p g t",
                                                          p=128)), 16)
                ld.inc(sync.dma_start(out=sgc8[:, :],
                                      in_=sg_d[:, n * 512:(n + 1) * 512]), 16)
                ld.wait(sync)

        sA_b, sM_b = sA.v, sM.v
        if 'wout' in SKIP:
            pass
        else:
         with nc.Block() as blk:
            @blk.tensor
            def _(tn):
                for m in range(G):
                    if m >= 2:
                        tn.wait_ge(sA.h, sA_b + m - 1)
                    for kc in range(G):
                        mm = tn.matmul(
                            ps_h[m % 2][:, :],
                            wout_b[:, kc * C + m * 128:kc * C + m * 128 + 128],
                            oc[:, kc * 512:(kc + 1) * 512],
                            start=(kc == 0), stop=(kc == G - 1))
                    sM.inc(mm)

            @blk.scalar
            def _(s):
                for m in range(G):
                    s.wait_ge(sM.h, sM_b + m + 1)
                    s.copy(s1_sb[:, m * 512:(m + 1) * 512], ps_h[m % 2][:, :])
                    sA.inc(s.drain())

        with nc.Block() as blk:
            @blk.vector
            def _(v):
                v.memset(acc[:, :], 0.0)

        for e in range(nexperts):
            with nc.Block() as blk:
                @blk.gpsimd
                def _(g_, e=e):
                    ld.inc(g_.dma_start(
                        out=w1t[:, :].rearrange("p (kc o) -> p kc o",
                                                o=4 * C),
                        in_=w1_d[e * C:(e + 1) * C, :].rearrange(
                            "(kc p) o -> p kc o", p=128)), 16)
                    ld.wait(g_)

                @blk.sync
                def _(sync, e=e, n=n):
                    ld.inc(sync.dma_start(
                        out=sgc1[:, :],
                        in_=sg_d[e:e + 1, n * 512:(n + 1) * 512]), 16)
                    ld.wait(sync)

                @blk.tensor
                def _(tn, e=e):
                    ld.wait(tn)
                    sM.inc(tn.matmul(ps_w[:, :], onesr[:, :], sgc1[:, :],
                                     start=True, stop=True))

            sA_b, sM_b, sV_b = sA.v, sM.v, sV.v
            with nc.Block() as blk:
                @blk.tensor
                def _(tn, e=e):
                    for hc in range(32):
                        if hc >= 2:
                            tn.wait_ge(sV.h, sV_b + hc - 1)
                        for kc in range(G):
                            mm = tn.matmul(
                                ps_h[hc % 2][:, :],
                                w1t[:, kc * 4 * C + hc * 128:
                                    kc * 4 * C + hc * 128 + 128],
                                xnbfc[:, kc * 512:(kc + 1) * 512],
                                start=(kc == 0), stop=(kc == G - 1))
                        sM.inc(mm)

                @blk.scalar
                def _(s, e=e):
                    for hc in range(32):
                        s.wait_ge(sM.h, sM_b + hc + 1)
                        if hc >= 2:
                            s.wait_ge(sV.h, sV_b + hc - 1)
                        gl = glt[:, (hc % 2) * 512:(hc % 2) * 512 + 512]
                        s.activation(gl, ps_h[hc % 2][:, :], AF.Gelu,
                                     bias=b1t[:, e * 32 + hc:e * 32 + hc + 1])
                        sA.inc(s.drain())

                @blk.vector
                def _(v, e=e):
                    for hc in range(32):
                        v.wait_ge(sA.h, sA_b + hc + 1)
                        gl = glt[:, (hc % 2) * 512:(hc % 2) * 512 + 512]
                        v.tensor_mul(u_bf[:, hc * 512:(hc + 1) * 512], gl,
                                     ps_w[:, :])
                        sV.inc(v.drain())

            sM_b2, sV_b2, ld_b2 = sM.v, sV.v, ld.v
            with nc.Block() as blk:
                @blk.gpsimd
                def _(g_, e=e):
                    for m in range(G):
                        if m >= 2:
                            g_.wait_ge(sM.h, sM_b2 + m - 1)
                        ld.inc(g_.dma_start(
                            out=w2t[m % 2][:, :].rearrange(
                                "p (hc o) -> p hc o", o=128),
                            in_=w2_d[e * 4 * C:(e + 1) * 4 * C,
                                     m * 128:(m + 1) * 128].rearrange(
                                "(hc p) o -> p hc o", p=128)), 16)

                @blk.tensor
                def _(tn, e=e):
                    for m in range(G):
                        tn.wait_ge(ld.h, ld_b2 + 16 * (m + 1))
                        if m >= 2:
                            tn.wait_ge(sV.h, sV_b2 + m - 1)
                        for hc in range(32):
                            mm = tn.matmul(
                                ps_o[m % 2][:, :],
                                w2t[m % 2][:, hc * 128:(hc + 1) * 128],
                                u_bf[:, hc * 512:(hc + 1) * 512],
                                start=(hc == 0), stop=(hc == 31))
                        sM.inc(mm)

                @blk.vector
                def _(v, e=e):
                    for m in range(G):
                        v.wait_ge(sM.h, sM_b2 + m + 1)
                        sl = slice(m * 512, (m + 1) * 512)
                        v.tensor_add(acc[:, sl], acc[:, sl], ps_o[m % 2][:, :])
                        sV.inc(v.drain())

        sA_b, sM_b, sV_b = sA.v, sM.v, sV.v
        if 'comb' in SKIP:
            continue
        with nc.Block() as blk:
            @blk.tensor
            def _(tn):
                for m in range(G):
                    if m >= 1:
                        tn.wait_ge(sV.h, sV_b + m)
                    sM.inc(tn.matmul(ps_b2[:, :],
                                     b2t[:, m * 128:(m + 1) * 128],
                                     sgc8[:, :], start=True, stop=True))

            @blk.vector
            def _(v):
                for m in range(G):
                    v.wait_ge(sM.h, sM_b + m + 1)
                    sl = slice(m * 512, (m + 1) * 512)
                    v.tensor_add(acc[:, sl], acc[:, sl], ps_b2[:, :])
                    sV.inc(v.drain())

            @blk.scalar
            def _(s):
                for m in range(G):
                    s.wait_ge(sV.h, sV_b + m + 1)
                    sl = slice(m * 512, (m + 1) * 512)
                    s.activation(acc[:, sl], acc[:, sl], AF.Silu)
                    sA.inc(s.drain())

            @blk.vector
            def _(v):
                for m in range(G):
                    v.wait_ge(sA.h, sA_b + m + 1)
                    sl = slice(m * 512, (m + 1) * 512)
                    v.tensor_mul(s1_sb[:, sl], s1_sb[:, sl], acc[:, sl])
                    v.nop(cycle_cnt=32)
                    v.tensor_add(x1c[:, sl], x1c[:, sl], s1_sb[:, sl])
                    sV.inc(v.drain(), 2)

            @blk.sync
            def _(sync, n=n):
                sV.wait(sync)
                ld.inc(sync.dma_start(
                    out=out_d[:, n * 512:(n + 1) * 512].rearrange(
                        "(g p) t -> p g t", p=128),
                    in_=x1c[:, :].rearrange("p (g t) -> p g t", t=512)), 16)

    with nc.Block() as blk:
        @blk.sync
        def _(sync):
            ld.wait(sync)
    es.close()
    return nc




def stage_begin(nc, S):
    ld, sA, sM, sV = S
    with nc.Block() as blk:
        for eng in ("sync", "vector", "scalar", "tensor", "gpsimd"):
            def mk(e, _vals=(ld.v, sA.v, sM.v, sV.v)):
                e.wait_ge(ld.h, _vals[0])
                e.wait_ge(sA.h, _vals[1])
                e.wait_ge(sM.h, _vals[2])
                e.wait_ge(sV.h, _vals[3])
            getattr(blk, eng)(mk)


def build_fused():
    nc = bass.Bass("TRN2", target_bir_lowering=False)
    es = contextlib.ExitStack()
    D = {}

    def ein(name, shape, dt=F32):
        D[name] = nc.dram_tensor(name, shape, dt, kind="ExternalInput").ap()

    def intr(name, shape, dt=F32):
        D[name] = nc.dram_tensor(name, shape, dt, kind="Internal").ap()

    ein("xT", [C, T + 1])
    for nm in ("Wk", "Wv", "Wr", "Wo"):
        ein(nm, [C, C])
    for nm in ("ln1_s", "ln1_b", "tmk", "tmv", "tmr", "w_decay", "u_first",
               "w_router", "ln2_s", "ln2_b", "f_tmk", "f_tmr",
               "ln_g", "ln_b"):
        ein(nm, [128, G])
    ein("fWk", [C, 2 * C]); ein("fWr", [C, C]); ein("fWv", [2 * C, C])
    ein("W1", [C, 4 * C]); ein("W2", [4 * C, C])
    ein("b1", [128, 32]); ein("b2", [128, G]); ein("idn", [128, 128])
    ein("wq", [C, 512]); ein("wk", [C, 512]); ein("wv", [C, 512])
    ein("wr", [C, 8]); ein("br", [1, 8])
    ein("cosb", [128, T]); ein("sinb", [128, T]); ein("ps2", [128, 128])
    ein("mbeta", [1, 8]); ein("mask4", [32, 128])
    ein("wout", [C, C]); ein("mw1", [8 * C, 2 * C]); ein("mw2", [16 * C, C])
    ein("mb1", [128, 8 * 32]); ein("mb2", [8, C])
    ein("iota8", [128, 8]); ein("iotac", [128, 8]); ein("ut128", [128, 128])
    ein("trasht", [128, 1])
    ein("lt8", [8, 8]); ein("qcol", [24, 1]); ein("piota", [128, 1])
    ein("b1pr", [8 * 128, 16])

    intr("xn0", [C, T + 1]); intr("xn0bf", [C, T + 1], BF16)
    intr("ln1bf", [C, T + 1], BF16)
    intr("scores", [1, T]); intr("tmout", [C, T])
    intr("xmid", [C, T + 1]); intr("ln2bf", [C, T + 1], BF16)
    intr("cmf", [C, T]); intr("cmp", [C, T])
    intr("sout", [C, T + 1]); intr("x1", [C, T])
    intr("xn1bf", [C, T], BF16)
    intr("sgate", [8, T]); intr("slotint", [128, 64], I32)
    intr("offs", [1, 8])
    for nm in ("qrh", "krh", "sqh", "skh"):
        intr(nm, [512, T], BF16)
    intr("sktm", [T, 512], BF16); intr("vtm", [T, 512], BF16)
    intr("oh", [512, T], BF16); intr("og", [1024, T], BF16)
    intr("gbuf", [NSLOT + 128, C], BF16); intr("gout", [NSLOT + 128, C])
    intr("goutp", [NSLOT + 128, C])
    D["out"] = nc.dram_tensor("out", [C, T], F32, kind="ExternalOutput").ap()

    S = tuple(Sem(es.enter_context(nc.semaphore(n)))
              for n in ("ld", "sA", "sM", "sV"))
    cc = Sem(es.enter_context(nc.semaphore("cc")))
    cc3 = Sem(es.enter_context(nc.semaphore("cc3")))
    ccg = [Sem(es.enter_context(nc.semaphore(f"ccg{i}"))) for i in range(6)]

    emit_s1(nc, D, S)
    stage_begin(nc, S)
    emit_lif(nc, S, "l1_", D["tmout"], D["xn0"], D["xmid"], zero_col0=True)
    stage_begin(nc, S)
    emit_s3(nc, D, S)
    emit_pair_allreduce(nc, D, S, cc3, "cmp", "cmf")
    stage_begin(nc, S)
    emit_lif(nc, S, "l2_", D["cmf"], D["xmid"], D["sout"])
    stage_begin(nc, S)
    emit_s5(nc, D, S)
    stage_begin(nc, S)
    emit_s6(nc, D, S)
    stage_begin(nc, S)
    emit_s7(nc, D, S)
    stage_begin(nc, S)
    emit_exchange(nc, D, S, cc)
    emit_s8(nc, D, S, cc, ccg)

    ld = S[0]
    with nc.Block() as blk:
        @blk.sync
        def _(sync):
            sync.wait_ge(ld.h, ld.v)
            sync.wait_ge(cc.h, 1)
    es.close()
    return nc


def kernel(**inputs):
    inp = {k: np.asarray(v) for k, v in inputs.items()}
    nc = build_fused()

    wqkv = np.asarray(inp["L1_attn_w_qkv"], np.float32)
    cosb, sinb, ps2 = rope_tables()
    eye = np.eye(128, dtype=np.float32)
    mb = np.asarray(inp["L1_attn_mem_beta"], np.float32).reshape(1, 16)
    mask_add = np.where(np.tril(np.ones((32, 32), np.float32)) > 0, 0.0,
                        -1e30).astype(np.float32)
    iota8 = np.tile(np.arange(8, dtype=np.float32), (128, 1))
    iota8 = np.ascontiguousarray(iota8)
    iotac = np.ascontiguousarray(iota8 * float(CAP))
    ut128 = np.ascontiguousarray(
        np.triu(np.ones((128, 128), np.float32), 1))

    fWk_full = np.asarray(inp["L0_attn_fWk"], np.float32)
    fWv_full = np.asarray(inp["L0_attn_fWv"], np.float32)
    shared = {
        "fWr": np.asarray(inp["L0_attn_fWr"], np.float32),
        "ln2_s": vec_pg(inp["L0_attn_ln2_s"]),
        "ln2_b": vec_pg(inp["L0_attn_ln2_b"]),
        "f_tmk": vec_pg(inp["L0_attn_f_tmk"]),
        "f_tmr": vec_pg(inp["L0_attn_f_tmr"]),
        "W1": np.asarray(inp["L0_ffn_W1"], np.float32),
        "W2": np.asarray(inp["L0_ffn_W2"], np.float32),
        "b1": np.ascontiguousarray(
            np.asarray(inp["L0_ffn_b1"], np.float32).reshape(32, 128).T),
        "b2": vec_pg(inp["L0_ffn_b2"]),
        "idn": eye,
        "wr": np.asarray(inp["L1_ffn_wr"], np.float32),
        "br": np.asarray(inp["L1_ffn_br"], np.float32).reshape(1, 8),
        "cosb": cosb, "sinb": sinb, "ps2": ps2,
        "ln_g": vec_pg(inp["L1_attn_ln_g"]),
        "ln_b": vec_pg(inp["L1_attn_ln_b"]),
        "mask4": np.ascontiguousarray(np.tile(mask_add, (1, 4))),
        "wout": np.asarray(inp["L1_attn_w_out"], np.float32),
        "mb1": np.concatenate(
            [np.asarray(inp["L1_ffn_b1"], np.float32)[e].reshape(32, 128).T
             for e in range(8)], axis=1),
        "mb2": np.asarray(inp["L1_ffn_b2"], np.float32),
        "iota8": iota8, "iotac": iotac, "ut128": ut128,
        "trasht": np.ascontiguousarray(
            (NSLOT + np.arange(128, dtype=np.float32)).reshape(128, 1)),
        "lt8": np.ascontiguousarray(
            np.tril(np.ones((8, 8), np.float32), -1)),
        "qcol": np.ascontiguousarray(
            (512.0 * np.arange(24, dtype=np.float32)).reshape(24, 1)),
        "piota": np.ascontiguousarray(
            np.arange(128, dtype=np.float32).reshape(128, 1)),
    }
    mw1_full = np.asarray(inp["L1_ffn_w1"], np.float32).reshape(8 * C, 4 * C)
    mw2_full = np.asarray(inp["L1_ffn_w2"], np.float32)
    b1pr_full = np.ascontiguousarray(np.stack(
        [np.asarray(inp["L1_ffn_b1"], np.float32)[e].reshape(32, 128).T
         for e in range(8)], axis=0).reshape(8 * 128, 32))

    maps = []
    for c in range(8):
        b, hs = c // 2, c % 2
        m = dict(s1_inmap(inp, b))
        m.update(shared)
        m.update({
            "wq": np.ascontiguousarray(wqkv[:, hs * 512:hs * 512 + 512]),
            "wk": np.ascontiguousarray(
                wqkv[:, 1024 + hs * 512:1024 + hs * 512 + 512]),
            "wv": np.ascontiguousarray(
                wqkv[:, 2048 + hs * 512:2048 + hs * 512 + 512]),
            "mbeta": np.ascontiguousarray(mb[:, hs * 8:hs * 8 + 8]),
            "fWk": np.ascontiguousarray(fWk_full[:, hs * 2048:(hs + 1) * 2048]),
            "fWv": np.ascontiguousarray(fWv_full[hs * 2048:(hs + 1) * 2048, :]),
            "mw1": np.ascontiguousarray(
                mw1_full[:, hs * 2048:(hs + 1) * 2048]),
            "mw2": np.ascontiguousarray(
                mw2_full[:, hs * 2048:(hs + 1) * 2048, :].reshape(
                    8 * 2048, C)),
            "b1pr": np.ascontiguousarray(
                b1pr_full[:, hs * 16:(hs + 1) * 16]),
        })
        maps.append(m)

    res = run_bass_kernel_spmd(nc, maps, core_ids=list(range(8))).results
    out = np.empty((B, T, C), np.float32)
    for b in range(B):
        out[b] = res[2 * b]["out"].T
    return out
